# version 05
# baseline (speedup 1.0000x reference)
import numpy as np
import ml_dtypes
import concourse.bass as bass
import concourse.mybir as mybir
from concourse.bass_utils import run_bass_kernel_spmd

F32 = mybir.dt.float32
BF16 = mybir.dt.bfloat16
AF = mybir.ActivationFunctionType
ALU = mybir.AluOpType
AX = mybir.AxisListType
NPBF16 = ml_dtypes.bfloat16

_ENGINE_OF = {"pe": "tensor", "dve": "vector", "act": "scalar", "pool": "gpsimd",
              "qsp": "sync", "qact": "scalar", "qpool": "gpsimd"}
_DMA_Q = ("qsp", "qact", "qpool")


class Plan:

    def __init__(self, nc):
        self.nc = nc
        self.ops = []
        self.last_write = {}
        self.readers = {}
        self.n_emitted = 0

    def op(self, engine, emit, reads=(), writes=(), tag=""):
        idx = len(self.ops)
        deps = set()
        for k in reads:
            if k in self.last_write:
                deps.add(self.last_write[k])
        for k in writes:
            if k in self.last_write:
                deps.add(self.last_write[k])
            for r in self.readers.get(k, ()):
                deps.add(r)
        for k in reads:
            self.readers.setdefault(k, []).append(idx)
        for k in writes:
            self.last_write[k] = idx
            self.readers[k] = []
        self.ops.append({"engine": engine, "emit": emit, "deps": deps, "tag": tag,
                         "inc": False, "seq": None})
        return idx

    def barrier(self):
        idx = len(self.ops)
        deps = set(range(idx))
        self.ops.append({"engine": "barrier", "emit": None, "deps": deps, "tag": "barrier",
                         "inc": False, "seq": None})
        self.last_write = {}
        self.readers = {}
        self._barrier_idx = idx
        return idx

    def finalize(self):
        nc = self.nc
        ops = self.ops
        n = len(ops)
        last_barrier = -1
        for i, o in enumerate(ops):
            if o["engine"] == "barrier":
                last_barrier = i
            else:
                o["deps"] = {d for d in o["deps"] if d > last_barrier}
                if last_barrier >= 0:
                    o["deps"].add(last_barrier)
        for i, o in enumerate(ops):
            if o["engine"] == "barrier":
                continue
            pruned = set()
            for d in o["deps"]:
                od = ops[d]
                if od["engine"] == "barrier":
                    pruned.add(d)
                    continue
                if od["engine"] == "pe" and o["engine"] == "pe":
                    continue
                pruned.add(d)
            o["deps"] = pruned
        for o in ops:
            if o["engine"] in _DMA_Q:
                o["inc"] = True
            for d in o["deps"]:
                ops[d]["inc"] = True
        NS = 8
        comp_sems = {e: self.nc.alloc_semaphore(f"sem_{e}")
                     for e in _ENGINE_OF if e not in _DMA_Q}
        dma_sems = {q: [self.nc.alloc_semaphore(f"sem_{q}{i}") for i in range(NS)]
                    for q in _DMA_Q}
        counters = {e: 0 for e in _ENGINE_OF if e not in _DMA_Q}
        qidx = {q: 0 for q in _DMA_Q}
        waited = {}

        def sem_of(o):
            if o["engine"] in _DMA_Q:
                q, i = o["engine"], o["qslot"]
                return dma_sems[q][i % NS], 16 * (1 + i // NS)
            return comp_sems[o["engine"]], o["seq"]

        def emit_wait(weng, sem, val):
            key = (weng, sem.name)
            if waited.get(key, 0) >= val:
                return
            waited[key] = val
            getattr(nc, _ENGINE_OF[weng]).wait_ge(sem, val)

        for i, o in enumerate(ops):
            if o["engine"] == "barrier":
                for q in _DMA_Q:
                    for j in range(max(0, qidx[q] - NS), qidx[q]):
                        emit_wait(q, dma_sems[q][j % NS], 16 * (1 + j // NS))
                nc.all_engine_barrier()
                for e, c in counters.items():
                    for w in _ENGINE_OF:
                        waited[(w, comp_sems[e].name)] = c
                for q in _DMA_Q:
                    for j in range(qidx[q]):
                        for w in _ENGINE_OF:
                            key = (w, dma_sems[q][j % NS].name)
                            waited[key] = max(waited.get(key, 0), 16 * (1 + j // NS))
                o["seq"] = None
                continue
            eng = o["engine"]
            for d in sorted(o["deps"]):
                od = ops[d]
                if od["engine"] == "barrier":
                    continue
                sem, val = sem_of(od)
                emit_wait(eng, sem, val)
            if eng in _DMA_Q:
                j = qidx[q := eng]
                if j >= NS:
                    emit_wait(q, dma_sems[q][j % NS], 16 * (j // NS))
                o["qslot"] = j
                qidx[q] += 1
            instr = o["emit"]()
            if eng in _DMA_Q:
                sem, val = sem_of(o)
                instr.then_inc(sem, 16)
                o["seq"] = val
            elif o["inc"]:
                counters[eng] += 1
                o["seq"] = counters[eng]
                if instr is None:
                    raise RuntimeError(f"op {o['tag']} needs inc but emit returned None")
                instr.then_inc(comp_sems[eng], 1)
            else:
                o["seq"] = counters[eng]
            self.n_emitted += 1
        for q in _DMA_Q:
            for j in range(max(0, qidx[q] - NS), qidx[q]):
                emit_wait(q, dma_sems[q][j % NS], 16 * (1 + j // NS))
        return self


class B:

    def __init__(self, name):
        self.nc = bass.Bass("TRN2", target_bir_lowering=False)
        self.plan = Plan(self.nc)
        self._ctx = []
        self.sbuf_bytes = 0
        self._psum_pool = []
        self._uid = 0
        self._consts = {}

    def constcol(self, value, parts, dtype=F32):
        key = ("cst", value, parts, str(dtype))
        if key not in self._consts:
            t = self.sbuf(f"cst{self.uid()}", [parts, 1], dtype)
            self.plan.op("pool", lambda t=t, v=value: self.nc.gpsimd.memset(t[:, :], v),
                         [], [key], tag="const")
            self._consts[key] = t
        return self._consts[key], key

    def uid(self):
        self._uid += 1
        return self._uid

    def dram(self, name, shape, dtype=F32, kind="Internal"):
        return self.nc.dram_tensor(name, list(shape), dtype, kind=kind).ap()

    def sbuf(self, name, shape, dtype=F32):
        cm = self.nc.sbuf_tensor(name, list(shape), dtype)
        t = cm.__enter__()
        self._ctx.append(cm)
        self.sbuf_bytes += shape[0] * int(np.prod(shape[1:])) * (2 if dtype == BF16 else 4)
        return t

    def psum(self, name, shape=(128, 512), dtype=F32):
        cm = self.nc.psum_tensor(name, list(shape), dtype)
        t = cm.__enter__()
        self._ctx.append(cm)
        return t

    def close(self):
        for cm in reversed(self._ctx):
            cm.__exit__(None, None, None)
        self._ctx = []

    def dma(self, q, out, in_, reads, writes, tag="", transpose=False):
        eng = getattr(self.nc, _ENGINE_OF[q])
        if transpose:
            return self.plan.op(q, lambda: eng.dma_start(out=out, in_=in_, transpose=True),
                                reads, writes, tag)
        return self.plan.op(q, lambda: eng.dma_start(out=out, in_=in_), reads, writes, tag)

    def mm(self, out, lhsT, rhs, start, stop, reads, writes, tag="", tile_position=None):
        t = self.nc.tensor
        return self.plan.op("pe", lambda: t.matmul(out, lhsT, rhs, start=start, stop=stop,
                                                   tile_position=tile_position),
                            reads, writes, tag)

    def transp(self, out, in_, iden, reads, writes, tag=""):
        t = self.nc.tensor
        return self.plan.op("pe", lambda: t.transpose(out, in_, iden), reads, writes, tag)

    def dve(self, fn, reads, writes, tag=""):
        return self.plan.op("dve", fn, reads, writes, tag)

    def act(self, fn, reads, writes, tag=""):
        return self.plan.op("act", fn, reads, writes, tag)

    def pool(self, fn, reads, writes, tag=""):
        return self.plan.op("pool", fn, reads, writes, tag)

    def bar(self):
        return self.plan.barrier()

    def lp(self, fn):
        def emit():
            with self.nc.allow_low_precision(reason="intentional bf16 storage"):
                return fn()
        return emit

    def done(self):
        self.plan.finalize()
        self.close()
        return self.nc



def ceil_div(a, b):
    return (a + b - 1) // b


_PROGS = {}


def _get_prog(name, builder):
    if name not in _PROGS:
        b = builder()
        b.done()
        _PROGS[name] = b.nc
    return _PROGS[name]


def kernel(**inputs):
    cores = list(range(8))

    nc = _get_prog("prep", build_prep)
    res = run_bass_kernel_spmd(nc, [stage_prep(c, inputs) for c in cores], core_ids=cores)
    prep = assemble_prep(res.results)

    nc = _get_prog("l0pre", lambda: build_l0pre(NT_FULL))
    resA = run_bass_kernel_spmd(nc, [stage_l0pre(c, inputs) for c in cores], core_ids=cores)
    A = resA.results

    nc = _get_prog("l0delta", lambda: build_l0delta(NT_FULL))
    res1 = run_bass_kernel_spmd(nc, [stage_l0delta(c, A[c], inputs) for c in cores],
                                core_ids=cores)
    maps = []
    for c in cores:
        sinit = np.asarray(res1.results[c - 1]["sfin"], np.float32) if c % 2 == 1 else None
        maps.append(stage_l0delta(c, A[c], inputs, sinit=sinit))
    resB = run_bass_kernel_spmd(nc, maps, core_ids=cores)
    Bv = resB.results

    nc = _get_prog("combine0", lambda: build_combine0(NT_FULL))
    resC = run_bass_kernel_spmd(nc, [stage_combine0(c, A[c], Bv[c], prep, inputs)
                                     for c in cores], core_ids=cores)
    C = resC.results

    z3 = np.zeros((2048, 3), dtype=NPBF16)
    def mixmaps(route):
        mm = []
        for c in cores:
            halo = np.asarray(C[c - 1]["xn1"])[:, -3:] if c % 2 == 1 else z3
            sinit = (np.asarray(route[c - 1]["s1fin"], np.float32)
                     if (route is not None and c % 2 == 1) else None)
            mm.append(stage_l1mix(c, np.asarray(C[c]["xn1"]), halo, inputs, s1init=sinit))
        return mm
    nc = _get_prog("l1mix", lambda: build_l1mix(NT_FULL))
    resD1 = run_bass_kernel_spmd(nc, mixmaps(None), core_ids=cores)
    resD = run_bass_kernel_spmd(nc, mixmaps(resD1.results), core_ids=cores)
    D = resD.results

    nc = _get_prog("l1ffn", lambda: build_l1ffn(NT_FULL))
    resE = run_bass_kernel_spmd(nc, [stage_l1ffn(c, C[c]["xn1"], C[c]["x1T"], D[c]["s1T"],
                                                 prep, inputs) for c in cores], core_ids=cores)
    out = np.empty((4, 4096, 2048), np.float32)
    for c in cores:
        bi, half = c // 2, c % 2
        out[bi, half * 2048:(half + 1) * 2048, :] = np.asarray(resE.results[c]["outT"],
                                                              np.float32).T
    return out


NT_FULL = 2048
HALO = 3


def token_tiles(nt, halo=HALO, tile=512):
    tiles = [(0, halo)] if halo else []
    off = halo
    while off < nt + halo:
        w = min(tile, nt + halo - off)
        tiles.append((off, w))
        off += w
    return tiles


def build_l0pre(nt=NT_FULL):
    T = nt + HALO
    b = B("l0pre")
    nc = b.nc
    xT = b.dram("xT", [2048, T], F32, kind="ExternalInput")
    wqkvzT = b.dram("wqkvzT", [2048, 12288], F32, kind="ExternalInput")
    wbaT = b.dram("wbaT", [2048, 64], F32, kind="ExternalInput")
    convd = b.dram("convw_diag", [64 * 4 * 128, 128], F32, kind="ExternalInput")
    mixed = b.dram("mixed", [8192, nt], BF16, kind="ExternalOutput")
    zs = b.dram("zs", [4096, nt], BF16, kind="ExternalOutput")
    xn_hbm = b.dram("xn", [2048, nt], BF16, kind="ExternalOutput")
    ba_f = b.dram("ba_f", [64, nt], F32, kind="ExternalOutput")

    xn_sb = b.sbuf("xn_sb", [128, 16 * T], BF16)
    xb = [b.sbuf(f"xb{i}", [128, 16 * 512], BF16) for i in range(2)]
    sq = b.sbuf("sq", [128, 512], BF16)
    ones128 = b.sbuf("ones128", [128, 1], BF16)
    ones1 = b.sbuf("ones1", [1, 128], BF16)
    stats = b.sbuf("stats", [1, 6 * 512], F32)
    rr = b.sbuf("rr", [1, 512], BF16)
    rm = b.sbuf("rm", [1, 512], BF16)
    wt = [b.sbuf(f"wt{i}", [128, 16 * 128], BF16) for i in range(2)]
    pre = b.sbuf("pre", [128, T], BF16)
    cw = b.sbuf("cw", [128, 4 * 128], BF16)
    stg = b.sbuf("stg", [128, 512], BF16)
    bastg = b.sbuf("bastg", [64, 512], F32)
    ps = [b.psum(f"ps{i}", (128, 512), F32) for i in range(6)]

    b.pool(lambda: nc.gpsimd.memset(ones128[:, :], 1.0), [], ["ones128"])
    b.pool(lambda: nc.gpsimd.memset(ones1[:, :], 1.0), [], ["ones1"])

    tiles = token_tiles(nt)
    for ti, (off, w) in enumerate(tiles):
        xbuf = xb[ti % 2]
        kx = ("xb", ti % 2)
        b.dma("qpool", xbuf[:, 0:16 * w].rearrange("p (c t) -> p c t", c=16),
              xT.rearrange("(c p) t -> p c t", p=128)[:, :, off:off + w],
              reads=[], writes=[kx], tag=f"ldx{ti}")
        pm, pq = ps[0], ps[1]
        for c in range(16):
            xc = xbuf[:, c * w:(c + 1) * w]
            b.mm(pm[0:1, 0:w], ones128[:, :], xc, start=(c == 0), stop=(c == 15),
                 reads=[kx, "ones128"], writes=[("ps", 0)], tag=f"sum{ti}_{c}")
            eng = b.dve if c % 2 == 0 else b.pool
            eng((lambda xc=xc, w=w: nc.vector.tensor_mul(sq[:, 0:w], xc, xc)) if c % 2 == 0 else
                (lambda xc=xc, w=w: nc.gpsimd.tensor_mul(sq[:, 0:w], xc, xc)),
                reads=[kx], writes=["sq"], tag=f"sq{ti}_{c}")
            b.mm(pq[0:1, 0:w], ones128[:, :], sq[:, 0:w], start=(c == 0), stop=(c == 15),
                 reads=["sq", "ones128"], writes=[("ps", 1)], tag=f"sumq{ti}_{c}")
        mean = stats[:, 0:w]
        ex2 = stats[:, 512:512 + w]
        msq = stats[:, 1024:1024 + w]
        var = stats[:, 1536:1536 + w]
        sd = stats[:, 2048:2048 + w]
        b.dve(lambda w=w, o=mean: nc.vector.tensor_scalar(out=o, in0=ps[0][0:1, 0:w],
              scalar1=1.0 / 2048, scalar2=None, op0=ALU.mult), [("ps", 0)], [("st", "m")])
        b.dve(lambda w=w, o=ex2: nc.vector.tensor_scalar(out=o, in0=ps[1][0:1, 0:w],
              scalar1=1.0 / 2048, scalar2=None, op0=ALU.mult), [("ps", 1)], [("st", "e")])
        b.dve(lambda o=msq, a=mean: nc.vector.tensor_mul(o, a, a), [("st", "m")], [("st", "ms")])
        b.dve(lambda o=var, a=ex2, c2=msq: nc.vector.tensor_sub(o, a, c2),
              [("st", "e"), ("st", "ms")], [("st", "v")])
        epsc, epsk = b.constcol(1e-5, 1)
        b.act(lambda o=sd, a=var, e=epsc: nc.scalar.activation(out=o, in_=a, func=AF.Sqrt, bias=e[:, :]),
              [("st", "v"), epsk], [("st", "sd")])
        b.dve(b.lp(lambda w=w, a=sd: nc.vector.reciprocal(rr[:, 0:w], a)), [("st", "sd")], ["rr"])
        b.dve(b.lp(lambda w=w, a=mean: nc.vector.tensor_mul(rm[:, 0:w], a, rr[:, 0:w])),
              [("st", "m"), "rr"], ["rm"])
        pR, pM = ps[2], ps[3]
        b.mm(pR[:, 0:w], ones1[:, :], rr[:, 0:w], True, True, ["ones1", "rr"], [("ps", 2)])
        b.mm(pM[:, 0:w], ones1[:, :], rm[:, 0:w], True, True, ["ones1", "rm"], [("ps", 3)])
        for c in range(16):
            xc = xbuf[:, c * w:(c + 1) * w]
            xnc = xn_sb[:, c * T + off:c * T + off + w]
            b.dve(lambda xc=xc, w=w: nc.vector.tensor_mul(sq[:, 0:w], xc, pR[0:128, 0:w]),
                  [kx, ("ps", 2)], ["sq"], tag=f"xnm{ti}_{c}")
            b.dve(lambda xnc=xnc, w=w: nc.vector.tensor_sub(xnc, sq[:, 0:w], pM[0:128, 0:w]),
                  ["sq", ("ps", 3)], [("xn", c)], tag=f"xns{ti}_{c}")

    for c in range(16):
        b.dma("qsp", xn_hbm[c * 128:(c + 1) * 128, :], xn_sb[:, c * T + HALO:(c + 1) * T],
              reads=[("xn", c)], writes=[], tag=f"stxn{c}")

    for f in range(97):
        wbuf = wt[f % 2]
        kw = ("wt", f % 2)
        if f < 96:
            b.dma("qpool", wbuf[:, :].rearrange("p (c n) -> p c n", c=16),
                  wqkvzT[:, f * 128:(f + 1) * 128].rearrange("(c p) n -> p c n", p=128),
                  reads=[], writes=[kw], tag=f"ldw{f}")
        else:
            b.dma("qpool", wbuf[:, 0:16 * 64].rearrange("p (c n) -> p c n", c=16),
                  wbaT.rearrange("(c p) n -> p c n", p=128),
                  reads=[], writes=[kw], tag="ldwba")
        nout = 128 if f < 96 else 64
        if f < 64:
            b.dma("qpool", cw[:, :].rearrange("p (j n) -> p j n", j=4),
                  convd.rearrange("(c j p) n -> c p j n", j=4, p=128)[f],
                  reads=[], writes=["cw"], tag=f"ldcw{f}")
        for ti, (off, w) in enumerate(tiles):
            pb = ps[ti % 2]
            kp = ("ps", ti % 2)
            for c in range(16):
                b.mm(pb[0:nout, 0:w], wbuf[:, c * nout:(c + 1) * nout],
                     xn_sb[:, c * T + off:c * T + off + w],
                     start=(c == 0), stop=(c == 15),
                     reads=[kw, ("xn", c)], writes=[kp], tag=f"pj{f}_{ti}_{c}")
            if f < 64:
                b.dve(lambda pb=pb, off=off, w=w: nc.vector.tensor_copy(pre[:, off:off + w], pb[:, 0:w]),
                      [kp], [("pre", ti)], tag=f"pc{f}_{ti}")
            elif f < 96:
                if off >= HALO:
                    b.act(lambda pb=pb, w=w: nc.scalar.activation(out=stg[:, 0:w], in_=pb[:, 0:w],
                          func=AF.Silu), [kp], ["stg"], tag=f"zs{f}_{ti}")
                    b.dma("qsp", zs[(f - 64) * 128:(f - 63) * 128, off - HALO:off - HALO + w],
                          stg[:, 0:w], reads=["stg"], writes=[], tag=f"stz{f}_{ti}")
            else:
                if off >= HALO:
                    b.dve(lambda pb=pb, w=w: nc.vector.tensor_copy(bastg[:, 0:w], pb[0:64, 0:w]),
                          [kp], ["bastg"], tag=f"bac{ti}")
                    b.dma("qsp", ba_f[:, off - HALO:off - HALO + w], bastg[:, 0:w],
                          reads=["bastg"], writes=[], tag=f"stba{ti}")
        if f < 64:
            for ti, (off, w) in enumerate(tiles):
                if off < HALO:
                    continue
                pb = ps[2 + ti % 2]
                kp = ("ps", 2 + ti % 2)
                for j in range(4):
                    b.mm(pb[:, 0:w], cw[:, j * 128:(j + 1) * 128],
                         pre[:, off - 3 + j:off - 3 + j + w], start=(j == 0), stop=(j == 3),
                         reads=["cw"] + [("pre", t2) for t2 in range(len(tiles))],
                         writes=[kp], tag=f"cv{f}_{ti}_{j}")
                b.act(lambda pb=pb, w=w: nc.scalar.activation(out=stg[:, 0:w], in_=pb[:, 0:w],
                      func=AF.Silu), [kp], ["stg"], tag=f"cvs{f}_{ti}")
                b.dma("qsp", mixed[f * 128:(f + 1) * 128, off - HALO:off - HALO + w],
                      stg[:, 0:w], reads=["stg"], writes=[], tag=f"stm{f}_{ti}")
    return b


LN128H = float(-0.5 * np.log(128.0))


def build_l0delta(nt=NT_FULL, debug_taps=False):
    nch = nt // 128
    b = B("l0delta")
    nc = b.nc
    mixed = b.dram("mixed", [8192, nt], BF16, kind="ExternalInput")
    zs_h = b.dram("zs", [4096, nt], BF16, kind="ExternalInput")
    bb_h = b.dram("bb_f", [32, nt], F32, kind="ExternalInput")
    aa_h = b.dram("aa_f", [32, nt], F32, kind="ExternalInput")
    alog = b.dram("alog", [32, 1], F32, kind="ExternalInput")
    dtb = b.dram("dtb", [32, 1], F32, kind="ExternalInput")
    nw_h = b.dram("nw", [128, 1], F32, kind="ExternalInput")
    sin_h = b.dram("sinit", [128, 4096], F32, kind="ExternalInput")
    eh_h = b.dram("eh", [2048, 32], F32, kind="ExternalInput")
    mnL_h = b.dram("mnL", [128, 128], F32, kind="ExternalInput")
    mnU_h = b.dram("mnU", [128, 128], F32, kind="ExternalInput")
    mnUI_h = b.dram("mnUI", [128, 128], F32, kind="ExternalInput")
    id32_h = b.dram("id32", [96, 32], F32, kind="ExternalInput")
    id128_h = b.dram("id128", [128, 128], F32, kind="ExternalInput")
    hg_h = b.dram("hg", [4096, nt], BF16, kind="ExternalOutput")
    sfin_h = b.dram("sfin", [128, 4096], F32, kind="ExternalOutput")

    bb = b.sbuf("bb", [32, nt], F32)
    aa = b.sbuf("aa", [32, nt], F32)
    gv = b.sbuf("gv", [32, 128], F32)
    Gc = b.sbuf("Gcc", [32, 128], F32)
    lb = b.sbuf("lbc", [32, 128], F32)
    beta = b.sbuf("betac", [32, 128], F32)
    lrq = b.sbuf("lrqc", [32, 128], F32)
    lrk = b.sbuf("lrkc", [32, 128], F32)
    uu = b.sbuf("uuc", [32, 128], F32)
    uq = b.sbuf("uqc", [32, 128], F32)
    wneg = b.sbuf("wnegc", [32, 128], F32)
    ebK = b.sbuf("ebKc", [32, 128], F32)
    ekg2 = b.sbuf("ekg2c", [32, 128], F32)
    tmp32 = b.sbuf("tmp32", [32, 256], F32)
    tmp32b = b.sbuf("tmp32b", [32, 128], F32)
    eglc = b.sbuf("eglc", [32, 1], F32)
    eglflat = b.sbuf("eglflat", [1, 32], F32)
    zero32 = b.sbuf("zero32", [32, 128], F32)
    S = b.sbuf("S", [128, 4096], BF16)
    eh = b.sbuf("ehs", [128, 16 * 32], BF16)
    mnL = b.sbuf("mnLs", [128, 128], F32)
    mnU = b.sbuf("mnUs", [128, 128], F32)
    mnUI = b.sbuf("mnUIs", [128, 128], F32)
    id32 = b.sbuf("id32s", [96, 32], F32)
    id128 = b.sbuf("id128s", [128, 128], BF16)
    nw = b.sbuf("nws", [128, 1], F32)
    ones1 = b.sbuf("ones1", [1, 128], BF16)
    ones1f = b.sbuf("ones1f", [1, 128], F32)
    ones128 = b.sbuf("ones128", [128, 1], BF16)
    sqt = b.sbuf("sqt", [128, 128], BF16)
    s32 = b.sbuf("s32", [32, 128], F32)
    mfm = b.sbuf("mfm", [128, 8192], BF16)
    mtm = b.sbuf("mtm", [128, 8192], BF16)
    zt = b.sbuf("zt", [128, 4096], BF16)
    uflat = b.sbuf("uflat", [1, 4096], F32)
    uqflat = b.sbuf("uqflat", [1, 4096], F32)
    wnflat = b.sbuf("wnflat", [1, 4096], F32)
    eqgflat = b.sbuf("eqgflat", [1, 4096], F32)
    betaT = b.sbuf("betaT", [128, 32], F32)
    ebKT = b.sbuf("ebKT", [128, 32], F32)
    ekg2T = b.sbuf("ekg2T", [128, 32], F32)
    sG = b.sbuf("sG", [128, 32], F32)
    hgstg = b.sbuf("hgstg", [128, 4096], BF16)
    kkTs = b.sbuf("kkTs", [128, 128], BF16)
    eA = b.sbuf("eA", [128, 128], BF16)
    eAT = b.sbuf("eAT", [128, 128], BF16)
    eQT = b.sbuf("eQT", [128, 128], BF16)
    attnT = b.sbuf("attnT", [128, 128], BF16)
    P = [b.sbuf(f"P{j}", [128, 128], BF16) for j in range(6)]
    Q = [b.sbuf(f"Q{j}", [128, 128], BF16) for j in range(7)]
    X = b.sbuf("X", [128, 256], BF16)
    kcf = b.sbuf("kcf", [128, 128], BF16)
    vnew = b.sbuf("vnew", [128, 128], BF16)
    qeg = b.sbuf("qeg", [128, 128], BF16)
    ofm = b.sbuf("ofm", [128, 128], BF16)
    osq = b.sbuf("osq", [128, 128], BF16)
    rstd = b.sbuf("rstd", [1, 128], BF16)
    lnv = b.sbuf("lnv", [1, 128], F32)
    t1s = b.sbuf("t1s", [128, 128], BF16)
    k2 = b.sbuf("k2", [128, 128], BF16)
    argt = b.sbuf("argt", [128, 128], F32)
    ps = [b.psum(f"ps{i}", (128, 512), F32) for i in range(6)]
    pbf = [b.psum(f"pbf{i}", (128, 1024), BF16) for i in range(2)]

    b.dma("qpool", bb[:, :], bb_h, [], ["bb"])
    b.dma("qpool", aa[:, :], aa_h, [], ["aa"])
    b.dma("qpool", eh[:, :].rearrange("p (c m) -> p c m", c=16),
          eh_h.rearrange("(c p) m -> p c m", p=128), [], ["eh"])
    b.dma("qpool", mnL[:, :], mnL_h, [], ["mnL"])
    b.dma("qpool", mnU[:, :], mnU_h, [], ["mnU"])
    b.dma("qpool", mnUI[:, :], mnUI_h, [], ["mnUI"])
    b.dma("qpool", id32[:, :], id32_h, [], ["id32"])
    b.dma("qpool", id128[:, :], id128_h, [], ["id128"])
    b.dma("qpool", nw[:, :], nw_h, [], ["nw"])
    b.dma("qpool", S[:, :], sin_h, [], [("S", h) for h in range(32)])
    alog_t = b.sbuf("alog_t", [32, 1], F32)
    dtb_t = b.sbuf("dtb_t", [32, 1], F32)
    negA = b.sbuf("negA", [32, 1], F32)
    b.dma("qpool", alog_t[:, :], alog, [], ["alog"])
    b.dma("qpool", dtb_t[:, :], dtb, [], ["dtb"])
    b.pool(lambda: nc.gpsimd.memset(ones1[:, :], 1.0), [], ["ones1"])
    b.pool(lambda: nc.gpsimd.memset(ones1f[:, :], 1.0), [], ["ones1f"])
    b.pool(lambda: nc.gpsimd.memset(ones128[:, :], 1.0), [], ["ones128"])
    b.pool(lambda: nc.gpsimd.memset(zero32[:, :], 0.0), [], ["zero32"])
    one_c, one_k = b.constcol(1.0, 32)
    eps6_c, eps6_k = b.constcol(1e-6, 32)
    eps6r_c, eps6r_k = b.constcol(1e-6, 1)
    b.act(lambda: nc.scalar.activation(out=negA[:, :], in_=alog_t[:, :], func=AF.Exp),
          ["alog"], ["negA0"])
    b.dve(lambda: nc.vector.tensor_scalar(out=negA[:, :], in0=negA[:, :], scalar1=-1.0,
          scalar2=None, op0=ALU.mult), ["negA0"], ["negA"])

    for c in range(nch):
        sl = slice(c * 128, (c + 1) * 128)
        bsl, asl = bb[:, sl], aa[:, sl]
        b.dma("qpool", mfm[:, :].rearrange("p (k t) -> p k t", k=64),
              mixed.rearrange("(k p) t -> p k t", p=128)[:, :, sl], [], ["mfm"], tag=f"mfm{c}")
        b.dma("qsp", mtm[:, :], mixed[:, sl], [], ["mtm"], tag=f"mtm{c}", transpose=True)
        b.dma("qact", zt[:, :].rearrange("p (k t) -> p k t", k=32),
              zs_h.rearrange("(k p) t -> p k t", p=128)[:, :, sl], [], ["zt"], tag=f"zt{c}")
        b.act(lambda asl=asl: nc.scalar.activation(out=tmp32[:, 0:128], in_=asl, func=AF.Abs,
              bias=dtb_t[:, :]), ["aa", "dtb"], ["t32"], tag=f"gp1{c}")
        b.act(lambda: nc.scalar.activation(out=tmp32[:, 0:128], in_=tmp32[:, 0:128], func=AF.Exp,
              scale=-1.0), ["t32"], ["t32"], tag=f"gp2{c}")
        b.act(lambda: nc.scalar.activation(out=tmp32[:, 0:128], in_=tmp32[:, 0:128], func=AF.Ln,
              bias=one_c[:, :]), ["t32", one_k], ["t32"], tag=f"gp3{c}")
        b.act(lambda asl=asl: nc.scalar.activation(out=tmp32b[:, :], in_=asl, func=AF.Relu,
              bias=dtb_t[:, :]), ["aa", "dtb"], ["t32b"], tag=f"gp4{c}")
        b.dve(lambda: nc.vector.tensor_add(tmp32[:, 0:128], tmp32[:, 0:128], tmp32b[:, :]),
              ["t32", "t32b"], ["t32"], tag=f"gp5{c}")
        b.dve(lambda: nc.vector.scalar_tensor_tensor(out=gv[:, :], in0=tmp32[:, 0:128],
              scalar=negA[:, :], in1=tmp32[:, 0:128], op0=ALU.mult, op1=ALU.bypass),
              ["t32", "negA"], ["gv"], tag=f"gp6{c}")
        b.act(lambda bsl=bsl: nc.scalar.activation(out=tmp32[:, 0:128], in_=bsl, func=AF.Abs),
              ["bb"], ["t32"], tag=f"lb1{c}")
        b.act(lambda: nc.scalar.activation(out=tmp32[:, 0:128], in_=tmp32[:, 0:128], func=AF.Exp,
              scale=-1.0), ["t32"], ["t32"], tag=f"lb2{c}")
        b.act(lambda: nc.scalar.activation(out=tmp32[:, 0:128], in_=tmp32[:, 0:128], func=AF.Ln,
              bias=one_c[:, :]), ["t32", one_k], ["t32"], tag=f"lb3{c}")
        b.act(lambda bsl=bsl: nc.scalar.activation(out=tmp32b[:, :], in_=bsl, func=AF.Relu,
              scale=-1.0), ["bb"], ["t32b"], tag=f"lb4{c}")
        b.dve(lambda: nc.vector.tensor_add(tmp32[:, 0:128], tmp32[:, 0:128], tmp32b[:, :]),
              ["t32", "t32b"], ["t32"], tag=f"lb5{c}")
        b.dve(lambda: nc.vector.tensor_scalar(out=lb[:, :], in0=tmp32[:, 0:128], scalar1=-1.0,
              scalar2=None, op0=ALU.mult), ["t32"], ["lb"], tag=f"lb6{c}")
        b.act(lambda bsl=bsl: nc.scalar.activation(out=beta[:, :], in_=bsl, func=AF.Sigmoid),
              ["bb"], ["beta"], tag=f"be{c}")
        b.dve(lambda: nc.vector.tensor_tensor_scan(out=Gc[:, :], data0=gv[:, :],
              data1=zero32[:, :], initial=0.0, op0=ALU.add, op1=ALU.add),
              ["gv", "zero32"], ["Gc"], tag=f"cums{c}")
        b.act(lambda: nc.scalar.activation(out=eglc[:, :], in_=Gc[:, 127:128], func=AF.Exp),
              ["Gc"], ["eglc"], tag=f"egl{c}")
        b.dma("qpool", eglflat[:, :], eglc[:, :], ["eglc"], ["eglflat"], tag=f"feg{c}")
        for which, base, lr in (("q", 0, lrq), ("k", 2048, lrk)):
            tag = "lq" if which == "q" else "lk"
            pq = ps[4]
            for hc in range(16):
                xc = mfm[:, base + hc * 128:base + (hc + 1) * 128]
                b.pool(lambda xc=xc: nc.gpsimd.tensor_mul(sqt[:, :], xc, xc),
                       ["mfm"], ["sqt"], tag=f"{tag}s{c}_{hc}")
                b.mm(pq[0:32, 0:128], eh[:, hc * 32:(hc + 1) * 32], sqt[:, :],
                     start=(hc == 0), stop=(hc == 15), reads=["sqt", "eh"], writes=[("ps", 4)],
                     tag=f"{tag}m{c}_{hc}")
            b.act(lambda pq=pq: nc.scalar.activation(out=s32[:, :], in_=pq[0:32, 0:128], func=AF.Sqrt),
                  [("ps", 4)], ["s32"], tag=f"{tag}sq{c}")
            b.act(lambda: nc.scalar.activation(out=s32[:, :], in_=s32[:, :], func=AF.Ln,
                  bias=eps6_c[:, :]), ["s32", eps6_k], ["s32"], tag=f"{tag}ln{c}")
            extra = LN128H if which == "q" else 0.0
            b.dve(lambda lr=lr, extra=extra: nc.vector.tensor_scalar(out=lr[:, :],
                  in0=s32[:, :], scalar1=-1.0, scalar2=extra, op0=ALU.mult, op1=ALU.add),
                  ["s32"], ["lrq" if which == "q" else "lrk"], tag=f"{tag}f{c}")
        b.dve(lambda: nc.vector.tensor_add(uu[:, :], Gc[:, :], lrk[:, :]),
              ["Gc", "lrk"], ["uu0"], tag=f"uu0{c}")
        b.dve(lambda: nc.vector.tensor_add(uu[:, :], uu[:, :], lb[:, :]),
              ["uu0", "lb"], ["uu"], tag=f"uu{c}")
        b.dve(lambda: nc.vector.tensor_add(uq[:, :], Gc[:, :], lrq[:, :]),
              ["Gc", "lrq"], ["uq"], tag=f"uq{c}")
        b.dve(lambda: nc.vector.tensor_sub(wneg[:, :], lrk[:, :], Gc[:, :]),
              ["Gc", "lrk"], ["wneg"], tag=f"wn{c}")
        b.act(lambda: nc.scalar.activation(out=ebK[:, :], in_=uu[:, :], func=AF.Exp),
              ["uu"], ["ebK"], tag=f"ebk{c}")
        b.dve(lambda: nc.vector.tensor_scalar(out=tmp32[:, 0:128], in0=Gc[:, :],
              scalar1=Gc[:, 127:128], scalar2=None, op0=ALU.subtract),
              ["Gc"], ["t32"], tag=f"ek1{c}")
        b.dve(lambda: nc.vector.tensor_sub(tmp32[:, 128:256], lrk[:, :], tmp32[:, 0:128]),
              ["lrk", "t32"], ["t32c"], tag=f"ek2{c}")
        b.act(lambda: nc.scalar.activation(out=ekg2[:, :], in_=tmp32[:, 128:256], func=AF.Exp),
              ["t32c"], ["ekg2"], tag=f"ek3{c}")
        b.dma("qpool", uflat[:, :], uu[:, :], ["uu"], ["uflat"], tag=f"fu{c}")
        b.dma("qpool", uqflat[:, :], uq[:, :], ["uq"], ["uqflat"], tag=f"fuq{c}")
        b.dma("qpool", wnflat[:, :], wneg[:, :], ["wneg"], ["wnflat"], tag=f"fw{c}")
        b.act(lambda: nc.scalar.activation(out=eqgflat[:, :], in_=uqflat[:, :], func=AF.Exp),
              ["uqflat"], ["eqgflat"], tag=f"eqg{c}")
        for srcv, dst, kk in ((beta, betaT, "betaT"), (ebK, ebKT, "ebKT"), (ekg2, ekg2T, "ekg2T")):
            rd = {"betaT": "beta", "ebKT": "ebK", "ekg2T": "ekg2"}[kk]
            b.transp(ps[5][0:128, 0:32], srcv[:, :], id32[0:32, :], [rd, "id32"],
                     [("ps", 5)], tag=f"tp{kk}{c}")
            b.dve(lambda dst=dst: nc.vector.tensor_copy(dst[:, :], ps[5][0:128, 0:32]),
                  [("ps", 5)], [kk], tag=f"tpc{kk}{c}")
        b.mm(ps[5][0:128, 32:64], ones1f[:, :], eglflat[0:1, :], True, True,
             ["ones1f", "eglflat"], [("ps", 5)], tag=f"eglb{c}")
        b.dve(lambda: nc.vector.tensor_copy(sG[:, :], ps[5][0:128, 32:64]),
              [("ps", 5)], ["sG"], tag=f"eglc{c}")

        if debug_taps and c == 0:
            hh = b.dram("tapc_mtm", [128, 8192], F32, kind="ExternalOutput")
            b.dma("qpool", hh, mtm[:, :], ["mtm"], [], tag="tapcmtm")
            hh2 = b.dram("tapc_mfm", [128, 8192], F32, kind="ExternalOutput")
            b.dma("qpool", hh2, mfm[:, :], ["mfm"], [], tag="tapcmfm")
            for nm, t, rd in (("betaT", betaT, ["betaT"]), ("ebKT", ebKT, ["ebKT"]),
                              ("ekg2T", ekg2T, ["ekg2T"]), ("eqgflat", eqgflat, ["eqgflat"]),
                              ("uflat", uflat, ["uflat"]), ("wnflat", wnflat, ["wnflat"]),
                              ("sG", sG, ["sG"])):
                hh = b.dram("tapc_" + nm, list(t.shape), F32, kind="ExternalOutput")
                b.dma("qpool", hh, t[:, :], rd, [], tag="tapc" + nm)
        for h in range(32):
            hq = h // 2
            qfm = mfm[:, hq * 128:(hq + 1) * 128]
            kfm = mfm[:, 2048 + hq * 128:2048 + (hq + 1) * 128]
            ktm = mtm[:, 2048 + hq * 128:2048 + (hq + 1) * 128]
            vtm = mtm[:, 4096 + h * 128:4096 + (h + 1) * 128]
            zsl = zt[:, h * 128:(h + 1) * 128]
            urow = uflat[0:1, h * 128:(h + 1) * 128]
            uqrow = uqflat[0:1, h * 128:(h + 1) * 128]
            wnrow = wnflat[0:1, h * 128:(h + 1) * 128]
            eqrow = eqgflat[0:1, h * 128:(h + 1) * 128]
            Ssl = S[:, h * 128:(h + 1) * 128]
            kS = ("S", h)
            I = f"{c}_{h}"
            b.mm(ps[0][:, 0:128], kfm, kfm, True, True, ["mfm"], [("ps", 0)], tag=f"kkt{I}")
            b.dve(b.lp(lambda: nc.vector.tensor_copy(kkTs[:, :], ps[0][:, 0:128])),
                  [("ps", 0)], ["kkTs"], tag=f"kkc{I}")
            b.mm(ps[1][:, 0:128], urow, ones1f[:, :], True, False, ["uflat", "ones1f"],
                 [("ps", 1)], tag=f"aA1{I}")
            b.mm(ps[1][:, 0:128], ones1f[:, :], wnrow, False, True, ["wnflat", "ones1f"],
                 [("ps", 1)], tag=f"aA2{I}")
            b.dve(lambda: nc.vector.scalar_tensor_tensor(out=argt[:, :], in0=ps[1][:, 0:128],
                  scalar=60.0, in1=mnL[:, :], op0=ALU.min, op1=ALU.add),
                  [("ps", 1), "mnL"], ["argt"], tag=f"aA3{I}")
            b.act(b.lp(lambda: nc.scalar.activation(out=eA[:, :], in_=argt[:, :], func=AF.Exp)),
                  ["argt"], ["eA"], tag=f"aA4{I}")
            b.mm(ps[2][:, 0:128], wnrow, ones1f[:, :], True, False, ["wnflat", "ones1f"],
                 [("ps", 2)], tag=f"aT1{I}")
            b.mm(ps[2][:, 0:128], ones1f[:, :], urow, False, True, ["uflat", "ones1f"],
                 [("ps", 2)], tag=f"aT2{I}")
            b.dve(lambda: nc.vector.scalar_tensor_tensor(out=argt[:, :], in0=ps[2][:, 0:128],
                  scalar=60.0, in1=mnU[:, :], op0=ALU.min, op1=ALU.add),
                  [("ps", 2), "mnU"], ["argt"], tag=f"aT3{I}")
            b.act(b.lp(lambda: nc.scalar.activation(out=eAT[:, :], in_=argt[:, :], func=AF.Exp)),
                  ["argt"], ["eAT"], tag=f"aT4{I}")
            b.dve(b.lp(lambda: nc.vector.scalar_tensor_tensor(out=P[0][:, :], in0=kkTs[:, :],
                  scalar=-1.0, in1=eA[:, :], op0=ALU.mult, op1=ALU.mult)),
                  ["kkTs", "eA"], [("P", 0)], tag=f"P0{I}")
            b.dve(b.lp(lambda: nc.vector.scalar_tensor_tensor(out=Q[0][:, :], in0=kkTs[:, :],
                  scalar=-1.0, in1=eAT[:, :], op0=ALU.mult, op1=ALU.mult)),
                  ["kkTs", "eAT"], [("Q", 0)], tag=f"Q0{I}")
            b.mm(ps[0][:, 128:256], kfm, qfm, True, True, ["mfm"], [("ps", 0)], tag=f"qkt{I}")
            b.mm(ps[1][:, 128:256], wnrow, ones1f[:, :], True, False, ["wnflat", "ones1f"],
                 [("ps", 1)], tag=f"aQ1{I}")
            b.mm(ps[1][:, 128:256], ones1f[:, :], uqrow, False, True, ["uqflat", "ones1f"],
                 [("ps", 1)], tag=f"aQ2{I}")
            b.dve(lambda: nc.vector.scalar_tensor_tensor(out=argt[:, :], in0=ps[1][:, 128:256],
                  scalar=60.0, in1=mnUI[:, :], op0=ALU.min, op1=ALU.add),
                  [("ps", 1), "mnUI"], ["argt"], tag=f"aQ3{I}")
            b.act(b.lp(lambda: nc.scalar.activation(out=eQT[:, :], in_=argt[:, :], func=AF.Exp)),
                  ["argt"], ["eQT"], tag=f"aQ4{I}")
            b.dve(b.lp(lambda: nc.vector.tensor_mul(attnT[:, :], eQT[:, :], ps[0][:, 128:256])),
                  ["eQT", ("ps", 0)], ["attnT"], tag=f"at{I}")
            b.pool(b.lp(lambda h=h, vtm=vtm: nc.gpsimd.tensor_scalar(out=X[:, 0:128], in0=vtm,
                   scalar1=betaT[:, h:h + 1], scalar2=None, op0=ALU.mult)),
                   ["mtm", "betaT"], [("X", "v")], tag=f"Xv{I}")
            b.pool(b.lp(lambda h=h, ktm=ktm: nc.gpsimd.tensor_scalar(out=X[:, 128:256], in0=ktm,
                   scalar1=ebKT[:, h:h + 1], scalar2=None, op0=ALU.mult)),
                   ["mtm", "ebKT"], [("X", "k")], tag=f"Xk{I}")
            if debug_taps and c == 0 and h == 0:
                hh = b.dram("tapi_X0", [128, 256], F32, kind="ExternalOutput")
                b.dma("qpool", hh, X[:, :], [("X", "v"), ("X", "k")], [], tag="tapiX0")
            for j in range(1, 7):
                if j <= 5:
                    b.mm(ps[3][:, 0:128], Q[j - 1][:, :], P[j - 1][:, :], True, True,
                         [("Q", j - 1), ("P", j - 1)], [("ps", 3)], tag=f"sqP{j}{I}")
                    b.dve(b.lp(lambda j=j: nc.vector.tensor_copy(P[j][:, :], ps[3][:, 0:128])),
                          [("ps", 3)], [("P", j)], tag=f"sqPc{j}{I}")
                b.mm(ps[4][:, 0:128], P[j - 1][:, :], Q[j - 1][:, :], True, True,
                     [("P", j - 1), ("Q", j - 1)], [("ps", 4)], tag=f"sqQ{j}{I}")
                b.act(b.lp(lambda j=j: nc.scalar.activation(out=Q[j][:, :], in_=ps[4][:, 0:128],
                      func=AF.Copy)), [("ps", 4)], [("Q", j)], tag=f"sqQc{j}{I}")
            for j in range(6, -1, -1):
                b.mm(ps[5][:, 0:256], Q[j][:, :], X[:, :], True, True,
                     [("Q", j), ("X", "v"), ("X", "k")], [("ps", 5)], tag=f"ap{j}{I}")
                b.dve(b.lp(lambda: nc.vector.scalar_tensor_tensor(out=X[:, :], in0=ps[5][:, 0:256],
                      scalar=1.0, in1=X[:, :], op0=ALU.mult, op1=ALU.add)),
                      [("ps", 5), ("X", "v"), ("X", "k")], [("X", "v"), ("X", "k")],
                      tag=f"apc{j}{I}")
            if debug_taps and c == 0 and h == 0:
                hh = b.dram("tapi_Q6", [128, 128], F32, kind="ExternalOutput")
                b.dma("qpool", hh, Q[6][:, :], [("Q", 6)], [], tag="tapiQ6")
            b.transp(pbf[0][:, 0:128], X[:, 128:256], id128[:, :],
                     [("X", "k"), "id128"], [("pbf", 0)], tag=f"kct{I}")
            b.dve(b.lp(lambda: nc.vector.tensor_copy(kcf[:, :], pbf[0][:, 0:128])),
                  [("pbf", 0)], ["kcf"], tag=f"kcc{I}")
            b.mm(ps[0][:, 256:384], kcf[:, :], Ssl, True, True, ["kcf", kS],
                 [("ps", 0)], tag=f"vn1{I}")
            b.dve(b.lp(lambda: nc.vector.scalar_tensor_tensor(out=vnew[:, :], in0=ps[0][:, 256:384],
                  scalar=-1.0, in1=X[:, 0:128], op0=ALU.mult, op1=ALU.add)),
                  [("ps", 0), ("X", "v")], ["vnew"], tag=f"vn2{I}")
            b.mm(ps[1][:, 256:384], ones1f[:, :], eqrow, True, True, ["eqgflat", "ones1f"],
                 [("ps", 1)], tag=f"qe1{I}")
            b.dve(b.lp(lambda qfm=qfm: nc.vector.tensor_mul(qeg[:, :], qfm, ps[1][:, 256:384])),
                  ["mfm", ("ps", 1)], ["qeg"], tag=f"qe2{I}")
            b.mm(ps[2][:, 256:384], Ssl, qeg[:, :], True, False, [kS, "qeg"],
                 [("ps", 2)], tag=f"o1{I}")
            b.mm(ps[2][:, 256:384], vnew[:, :], attnT[:, :], False, True, ["vnew", "attnT"],
                 [("ps", 2)], tag=f"o2{I}")
            b.dve(b.lp(lambda: nc.vector.tensor_copy(ofm[:, :], ps[2][:, 256:384])),
                  [("ps", 2)], ["ofm"], tag=f"oc{I}")
            b.pool(b.lp(lambda: nc.gpsimd.tensor_mul(osq[:, :], ofm[:, :], ofm[:, :])),
                   ["ofm"], ["osq"], tag=f"os{I}")
            b.mm(ps[3][0:1, 128:256], ones128[:, :], osq[:, :], True, True,
                 ["osq", "ones128"], [("ps", 3)], tag=f"rs{I}")
            b.act(lambda: nc.scalar.activation(out=lnv[:, :], in_=ps[3][0:1, 128:256],
                  func=AF.Ln, scale=1.0 / 128, bias=eps6r_c[:, :]),
                  [("ps", 3), eps6r_k], ["lnv"], tag=f"rl{I}")
            b.act(b.lp(lambda: nc.scalar.activation(out=rstd[:, :], in_=lnv[:, :],
                  func=AF.Exp, scale=-0.5)), ["lnv"], ["rstd"], tag=f"re{I}")
            b.mm(ps[4][:, 128:256], ones1[:, :], rstd[:, :], True, True,
                 ["ones1", "rstd"], [("ps", 4)], tag=f"rb{I}")
            b.dve(b.lp(lambda: nc.vector.tensor_mul(t1s[:, :], ofm[:, :], ps[4][:, 128:256])),
                  ["ofm", ("ps", 4)], ["t1s"], tag=f"g1{I}")
            b.pool(b.lp(lambda zsl=zsl: nc.gpsimd.tensor_mul(t1s[:, :], t1s[:, :], zsl)),
                   ["t1s", "zt"], ["t1s"], tag=f"g2{I}")
            b.dve(b.lp(lambda h=h: nc.vector.tensor_scalar(out=hgstg[:, h * 128:(h + 1) * 128],
                  in0=t1s[:, :], scalar1=nw[:, :], scalar2=None, op0=ALU.mult)),
                  ["t1s", "nw"], [("hgstg", h)], tag=f"g3{I}")
            if debug_taps and c == 0 and h == 0:
                for nm, t, rd in (("kkTs", kkTs, ["kkTs"]), ("eA", eA, ["eA"]),
                                  ("eAT", eAT, ["eAT"]), ("eQT", eQT, ["eQT"]),
                                  ("attnT", attnT, ["attnT"]), ("X", X, [("X", "v"), ("X", "k")]),
                                  ("P0", P[0], [("P", 0)]), ("Q1", Q[1], [("Q", 1)]),
                                  ("kcf", kcf, ["kcf"]), ("vnew", vnew, ["vnew"]),
                                  ("qeg", qeg, ["qeg"]), ("ofm", ofm, ["ofm"]),
                                  ("hg00", t1s, ["t1s"])):
                    hh = b.dram("tapi_" + nm, list(t.shape), F32, kind="ExternalOutput")
                    b.dma("qpool", hh, t[:, :], rd, [], tag="tapi" + nm)
            b.pool(b.lp(lambda h=h, ktm=ktm: nc.gpsimd.tensor_scalar(out=k2[:, :], in0=ktm,
                   scalar1=ekg2T[:, h:h + 1], scalar2=None, op0=ALU.mult)),
                   ["mtm", "ekg2T"], ["k2"], tag=f"k2{I}")
            b.mm(ps[3][:, 256:384], k2[:, :], vnew[:, :], True, True, ["k2", "vnew"],
                 [("ps", 3)], tag=f"ds{I}")
            b.dve(b.lp(lambda h=h, Ssl=Ssl: nc.vector.scalar_tensor_tensor(out=Ssl, in0=Ssl,
                  scalar=sG[:, h:h + 1], in1=ps[3][:, 256:384], op0=ALU.mult, op1=ALU.add)),
                  [kS, "sG", ("ps", 3)], [kS], tag=f"su{I}")
        b.dma("qsp", hg_h.rearrange("(h p) t -> p h t", p=128)[:, :, sl],
              hgstg[:, :].rearrange("p (h t) -> p h t", h=32),
              [("hgstg", h2) for h2 in range(32)], [], tag=f"sthg{c}")
    b.dma("qpool", sfin_h, S[:, :], [("S", h2) for h2 in range(32)], [], tag="stS")
    return b


def stage_l0delta(core, arts, inputs, nt=NT_FULL, sinit=None):
    consts = {}
    eh = np.zeros((16, 128, 32), np.float32)
    for hc in range(16):
        eh[hc, :, 2 * hc] = 1.0
        eh[hc, :, 2 * hc + 1] = 1.0
    consts["eh"] = eh.reshape(2048, 32)
    idx = np.arange(128)
    mnL = np.where(idx[:, None] > idx[None, :], 0.0, -1e9).astype(np.float32)
    mnU = np.where(idx[:, None] < idx[None, :], 0.0, -1e9).astype(np.float32)
    mnUI = np.where(idx[:, None] <= idx[None, :], 0.0, -1e9).astype(np.float32)
    consts["mnL"], consts["mnU"], consts["mnUI"] = mnL, mnU, mnUI
    consts["id32"] = np.concatenate([np.eye(32, dtype=np.float32)] * 3, 0)
    consts["id128"] = np.eye(128, dtype=np.float32)
    if sinit is None:
        sinit = np.zeros((128, 4096), np.float32)
    baf = np.asarray(arts["ba_f"], np.float32)
    return {
        "mixed": arts["mixed"], "zs": arts["zs"],
        "bb_f": np.ascontiguousarray(baf[:32]), "aa_f": np.ascontiguousarray(baf[32:]),
        "alog": inputs["L0_attn_A_log"].reshape(32, 1).astype(np.float32),
        "dtb": inputs["L0_attn_dt_bias"].reshape(32, 1).astype(np.float32),
        "nw": inputs["L0_attn_norm_weight"].reshape(128, 1).astype(np.float32),
        "sinit": sinit, **consts,
    }


def build_prep():
    b = B("prep")
    nc = b.nc
    wgu_s = b.dram("wgu_s", [2048, 2048], F32, kind="ExternalInput")
    wp_s = b.dram("wp_s", [256, 8192], F32, kind="ExternalInput")
    wup_s = b.dram("wup_s", [1024, 2048], F32, kind="ExternalInput")
    wgt_s = b.dram("wgt_s", [1024, 2048], F32, kind="ExternalInput")
    wdn_s = b.dram("wdn_s", [256, 8192], F32, kind="ExternalInput")
    su_ug = b.dram("su_ug", [16, 128], F32, kind="ExternalInput")
    su_dp = b.dram("su_dp", [64, 128], F32, kind="ExternalInput")
    H128h = b.dram("H128", [128, 128], F32, kind="ExternalInput")
    H16h = b.dram("H16", [16, 16], F32, kind="ExternalInput")
    H64h = b.dram("H64", [64, 64], F32, kind="ExternalInput")
    q_gu = b.dram("q_gu", [2048, 2048], BF16, kind="ExternalOutput")
    q_p = b.dram("q_p", [256, 8192], BF16, kind="ExternalOutput")
    fup = b.dram("fup", [2048, 1024], BF16, kind="ExternalOutput")
    fgt = b.dram("fgt", [2048, 1024], BF16, kind="ExternalOutput")
    fdn = b.dram("fdn", [8192, 256], BF16, kind="ExternalOutput")
    wub = b.dram("wub", [1024, 2048], BF16)
    wgb = b.dram("wgb", [1024, 2048], BF16)
    wdb = b.dram("wdb", [256, 8192], BF16)
    A1u = b.dram("A1u", [16 * 128, 1024], BF16)
    A1g = b.dram("A1g", [16 * 128, 1024], BF16)
    A1d = b.dram("A1d", [64 * 128, 256], BF16)

    wt = b.sbuf("wt", [128, 8192], BF16)
    grp = b.sbuf("grp", [128, 128], F32)
    scb = b.sbuf("scb", [128, 128], BF16)
    rsc = b.sbuf("rsc", [128, 128], F32)
    rat = b.sbuf("rat", [128, 8192], BF16)
    tsg = b.sbuf("tsg", [128, 8192], BF16)
    tgt = b.sbuf("tgt", [128, 8192], BF16)
    what = b.sbuf("what", [128, 8192], BF16)
    H128 = b.sbuf("H128s", [128, 128], BF16)
    H16 = b.sbuf("H16s", [16, 16], BF16)
    H64 = b.sbuf("H64s", [64, 64], BF16)
    su2u = b.sbuf("su2u", [16, 128], F32)
    su2d = b.sbuf("su2d", [64, 128], F32)
    tS = b.sbuf("tS", [128, 1024], BF16)
    t2 = b.sbuf("t2", [64, 1024], BF16)
    stg = b.sbuf("stgp", [128, 1024], BF16)
    ps = [b.psum(f"ps{i}", (128, 512), F32) for i in range(4)]

    b.dma("qpool", H128[:, :], H128h, [], ["H128"])
    b.dma("qpool", H16[:, :], H16h, [], ["H16"])
    b.dma("qpool", H64[:, :], H64h, [], ["H64"])
    b.dma("qpool", su2u[:, :], su_ug, [], ["su2u"])
    b.dma("qpool", su2d[:, :], su_dp, [], ["su2d"])

    def quant(src, dst, R, C, nm):
        G = C // 64
        for r0 in range(0, R, 128):
            b.dma("qpool", wt[:, 0:C], src[r0:r0 + 128, :], [], ["wt"], tag=f"qw{nm}{r0}")
            b.dve(lambda C=C, G=G: nc.vector.tensor_reduce(
                out=wt[:, 0:C].rearrange("p (g e) -> p g e", e=64)[:, :, 0:1] if False else grp[:, 0:G].rearrange("p (g e) -> p g e", e=1),
                in_=wt[:, 0:C].rearrange("p (g e) -> p g e", e=64), axis=AX.X, op=ALU.add,
                apply_absolute_value=True), ["wt"], ["grp"], tag=f"qr{nm}{r0}")
            b.dve(b.lp(lambda G=G: nc.vector.tensor_scalar(out=scb[:, 0:G], in0=grp[:, 0:G],
                  scalar1=1.0 / 64, scalar2=1e-8, op0=ALU.mult, op1=ALU.max)),
                  ["grp"], ["scb"], tag=f"qs{nm}{r0}")
            b.dve(lambda G=G: nc.vector.reciprocal(rsc[:, 0:G], scb[:, 0:G]), ["scb"], ["rsc"],
                  tag=f"qi{nm}{r0}")
            def bc(x3, s3):
                return bass.broadcast_tensor_aps(x3, s3)
            b.dve(b.lp(lambda C=C, G=G: nc.vector.tensor_tensor(
                  out=rat[:, 0:C].rearrange("p (g e) -> p g e", e=64),
                  in0=bc(wt[:, 0:C].rearrange("p (g e) -> p g e", e=64),
                         rsc[:, 0:G].rearrange("p (g e) -> p g e", e=1))[0],
                  in1=bc(wt[:, 0:C].rearrange("p (g e) -> p g e", e=64),
                         rsc[:, 0:G].rearrange("p (g e) -> p g e", e=1))[1],
                  op=ALU.mult)), ["wt", "rsc"], ["rat"], tag=f"qd{nm}{r0}")
            b.act(b.lp(lambda C=C: nc.scalar.activation(out=tsg[:, 0:C], in_=rat[:, 0:C],
                  func=AF.Sign)), ["rat"], ["tsg"], tag=f"qg{nm}{r0}")
            b.act(b.lp(lambda C=C: nc.scalar.activation(out=tgt[:, 0:C], in_=rat[:, 0:C],
                  func=AF.Abs)), ["rat"], ["tgt"], tag=f"qt0{nm}{r0}")
            b.dve(b.lp(lambda C=C: nc.vector.tensor_scalar(out=tgt[:, 0:C], in0=tgt[:, 0:C],
                  scalar1=0.5, scalar2=None, op0=ALU.is_gt)),
                  ["tgt"], ["tgt"], tag=f"qt{nm}{r0}")
            b.dve(b.lp(lambda C=C: nc.vector.tensor_mul(tgt[:, 0:C], tgt[:, 0:C], tsg[:, 0:C])),
                  ["tgt", "tsg"], ["tgt"], tag=f"qq{nm}{r0}")
            b.dve(b.lp(lambda C=C, G=G: nc.vector.tensor_tensor(
                  out=what[:, 0:C].rearrange("p (g e) -> p g e", e=64),
                  in0=bc(tgt[:, 0:C].rearrange("p (g e) -> p g e", e=64),
                         scb[:, 0:G].rearrange("p (g e) -> p g e", e=1))[0],
                  in1=bc(tgt[:, 0:C].rearrange("p (g e) -> p g e", e=64),
                         scb[:, 0:G].rearrange("p (g e) -> p g e", e=1))[1],
                  op=ALU.mult)), ["tgt", "scb"], ["what"], tag=f"qh{nm}{r0}")
            b.dve(b.lp(lambda C=C: nc.vector.tensor_sub(what[:, 0:C], what[:, 0:C], wt[:, 0:C])),
                  ["what", "wt"], ["what"], tag=f"qd2{nm}{r0}")
            b.dve(b.lp(lambda C=C: nc.vector.tensor_add(what[:, 0:C], wt[:, 0:C], what[:, 0:C])),
                  ["what", "wt"], ["what"], tag=f"qa{nm}{r0}")
            b.dma("qsp", dst[r0:r0 + 128, :], what[:, 0:C], ["what"], [], tag=f"qo{nm}{r0}")
    quant(wgu_s, q_gu, 2048, 2048, "gu")
    quant(wp_s, q_p, 256, 8192, "p")

    def precast(src, dst, R, C, nm):
        for r0 in range(0, R, 128):
            b.dma("qpool", wt[:, 0:C], src[r0:r0 + 128, :], [], ["wt"], tag=f"pc{nm}{r0}")
            b.dma("qsp", dst[r0:r0 + 128, :], wt[:, 0:C], ["wt"], [("pcast", nm)], tag=f"pco{nm}{r0}")
    precast(wup_s, wub, 1024, 2048, "u")
    precast(wgt_s, wgb, 1024, 2048, "g")
    precast(wdn_s, wdb, 256, 8192, "d")

    def stage1(srcb, a1, R, nchunks, nm):
        for dc in range(nchunks):
            b.dma("qsp", tS[:, 0:R], srcb[:, dc * 128:(dc + 1) * 128], [("pcast", nm)], ["tS"],
                  tag=f"s1t{nm}{dc}", transpose=True)
            for half in range(ceil_div(R, 512)):
                w = min(512, R - half * 512)
                b.mm(ps[0][:, 0:w], H128[:, :], tS[:, half * 512:half * 512 + w], True, True,
                     ["H128", "tS"], [("ps", 0)], tag=f"s1m{nm}{dc}_{half}")
                b.act(b.lp(lambda w=w, half=half: nc.scalar.activation(out=stg[:, half * 512:half * 512 + w],
                      in_=ps[0][:, 0:w], func=AF.Copy)), [("ps", 0)], ["stgp"], tag=f"s1e{nm}{dc}_{half}")
            b.dma("qsp", a1[dc * 128:(dc + 1) * 128, :], stg[:, 0:R], ["stgp"], [("a1", nm)],
                  tag=f"s1o{nm}{dc}")
    stage1(wub, A1u, 1024, 16, "u")
    stage1(wgb, A1g, 1024, 16, "g")
    stage1(wdb, A1d, 256, 64, "d")

    def stage2(a1, out, R, KH, Hs, Hk, su, suk, nm):
        a1v = a1.rearrange("(dh yl) r -> dh yl r", yl=128)
        outv = out.rearrange("(yh yl) r -> yh yl r", yl=128)
        for yl in range(128):
            b.dma("qpool", t2[0:KH, 0:R], a1v[:, yl, :], [("a1", nm)], ["t2"], tag=f"s2l{nm}{yl}")
            for half in range(ceil_div(R, 512)):
                w = min(512, R - half * 512)
                b.mm(ps[1][0:KH, 0:w], Hs[:, :], t2[0:KH, half * 512:half * 512 + w], True, True,
                     [Hk, "t2"], [("ps", 1)], tag=f"s2m{nm}{yl}_{half}")
                b.dve(b.lp(lambda w=w, half=half, yl=yl, KH=KH: nc.vector.tensor_scalar(
                      out=stg[0:KH, half * 512:half * 512 + w], in0=ps[1][0:KH, 0:w],
                      scalar1=su[:, yl:yl + 1], scalar2=None, op0=ALU.mult)),
                      [("ps", 1), suk], ["stgp"], tag=f"s2e{nm}{yl}_{half}")
            b.dma("qsp", outv[:, yl, :], stg[0:KH, 0:R], ["stgp"], [], tag=f"s2o{nm}{yl}")
    stage2(A1u, fup, 1024, 16, H16, "H16", su2u, "su2u", "u")
    stage2(A1g, fgt, 1024, 16, H16, "H16", su2u, "su2u", "g")
    stage2(A1d, fdn, 256, 64, H64, "H64", su2d, "su2d", "d")
    return b


def hadamard_mat(n):
    H = np.array([[1.0]], np.float32)
    while H.shape[0] < n:
        H = np.block([[H, H], [H, -H]]).astype(np.float32)
    return H


def stage_prep(core, inputs):
    i = core
    return {
        "wgu_s": np.ascontiguousarray(inputs["L0_ffn_w_gate_up"][2048 * i:2048 * (i + 1)]),
        "wp_s": np.ascontiguousarray(inputs["L0_ffn_w_proj"][256 * i:256 * (i + 1)]),
        "wup_s": np.ascontiguousarray(inputs["L1_ffn_W_up"][1024 * i:1024 * (i + 1)]),
        "wgt_s": np.ascontiguousarray(inputs["L1_ffn_W_gate"][1024 * i:1024 * (i + 1)]),
        "wdn_s": np.ascontiguousarray(inputs["L1_ffn_W_down"][256 * i:256 * (i + 1)]),
        "su_ug": np.ascontiguousarray(inputs["L1_ffn_SU_ug"].reshape(16, 128)),
        "su_dp": np.ascontiguousarray(inputs["L1_ffn_SU_dp"].reshape(64, 128)),
        "H128": hadamard_mat(128), "H16": hadamard_mat(16), "H64": hadamard_mat(64),
    }


def assemble_prep(res):
    q_gu = np.concatenate([np.asarray(r["q_gu"]) for r in res], 0)
    q_p = np.concatenate([np.asarray(r["q_p"]) for r in res], 0)
    wup_t = np.concatenate([np.asarray(r["fup"]) for r in res], 1)
    wgt_t = np.concatenate([np.asarray(r["fgt"]) for r in res], 1)
    wdn_t = np.concatenate([np.asarray(r["fdn"]) for r in res], 1)
    return {
        "wq_gu_t": np.ascontiguousarray(q_gu.T), "wq_p_t": np.ascontiguousarray(q_p.T),
        "wup_t": wup_t, "wgt_t": wgt_t, "wdn_t": wdn_t,
    }


EPSF32 = 1.1920929e-7


def build_combine0(nt=NT_FULL):
    b = B("combine0")
    nc = b.nc
    hg_h = b.dram("hg", [4096, nt], BF16, kind="ExternalInput")
    xn_h = b.dram("xn", [2048, nt], BF16, kind="ExternalInput")
    x0_h = b.dram("x0T", [2048, nt], F32, kind="ExternalInput")
    wout_h = b.dram("woutT", [4096, 2048], F32, kind="ExternalInput")
    wgu_h = b.dram("wq_gu_t", [2048, 16384], BF16, kind="ExternalInput")
    wp_h = b.dram("wq_p_t", [8192, 2048], BF16, kind="ExternalInput")
    x1_h = b.dram("x1T", [2048, nt], F32, kind="ExternalOutput")
    xn1_h = b.dram("xn1", [2048, nt], BF16, kind="ExternalOutput")

    xns = b.sbuf("xns", [128, 16 * 512], BF16)
    hgs = b.sbuf("hgs", [128, 32 * 512], BF16)
    s0c = b.sbuf("s0c", [128, 512], BF16)
    hsb = b.sbuf("hsb", [128, 64 * 512], BF16)
    x1sb = b.sbuf("x1sb", [128, 16 * 512], BF16)
    x1c = b.sbuf("x1c", [128, 512], F32)
    wo = [b.sbuf("wo0", [128, 32 * 128], BF16)] * 2
    wg = [b.sbuf("wg0", [128, 16 * 128], BF16)] * 2
    wu = [b.sbuf("wu0", [128, 16 * 128], BF16)] * 2
    wp = [b.sbuf("wp0", [128, 64 * 128], BF16)] * 2
    sg = b.sbuf("sg", [128, 512], BF16)
    su = b.sbuf("su", [128, 512], BF16)
    sq = b.sbuf("sq", [128, 512], BF16)
    x0c = b.sbuf("x0c", [128, 512], F32)
    g0c = b.sbuf("g0c", [128, 512], BF16)
    t2c = b.sbuf("t2c", [128, 512], BF16)
    rstdb = b.sbuf("rstdb", [128, 512], BF16)
    rsr = b.sbuf("rsr", [1, 512], BF16)
    lnv = b.sbuf("lnv", [1, 512], F32)
    stats = b.sbuf("stats", [1, 6 * 512], F32)
    rr = b.sbuf("rr", [1, 512], BF16)
    rm = b.sbuf("rm", [1, 512], BF16)
    ones1 = b.sbuf("ones1", [1, 128], BF16)
    ones128 = b.sbuf("ones128", [128, 1], BF16)
    ps = [b.psum(f"ps{i}", (128, 512), F32) for i in range(6)]
    b.pool(lambda: nc.gpsimd.memset(ones1[:, :], 1.0), [], ["ones1"])
    b.pool(lambda: nc.gpsimd.memset(ones128[:, :], 1.0), [], ["ones128"])
    epsf_c, epsf_k = b.constcol(EPSF32, 1)
    eps5_c, eps5_k = b.constcol(1e-5, 1)

    ntile = ceil_div(nt, 512)
    for ti in range(ntile):
        off = ti * 512
        w = min(512, nt - off)
        b.dma("qpool", xns[:, 0:16 * w].rearrange("p (c t) -> p c t", c=16),
              xn_h.rearrange("(c p) t -> p c t", p=128)[:, :, off:off + w],
              [], ["xns"], tag=f"lxn{ti}")
        b.dma("qact", hgs[:, 0:32 * w].rearrange("p (c t) -> p c t", c=32),
              hg_h.rearrange("(c p) t -> p c t", p=128)[:, :, off:off + w],
              [], ["hgs"], tag=f"lhg{ti}")
        for g in range(64):
            wgb_, wub_ = wg[g % 2], wu[g % 2]
            kwg, kwu = ("wg", 0), ("wu", 0)
            b.dma("qpool", wgb_[:, :].rearrange("p (c n) -> p c n", c=16),
                  wgu_h[:, g * 128:(g + 1) * 128].rearrange("(c p) n -> p c n", p=128),
                  [], [kwg], tag=f"lwg{ti}_{g}")
            b.dma("qact", wub_[:, :].rearrange("p (c n) -> p c n", c=16),
                  wgu_h[:, 8192 + g * 128:8192 + (g + 1) * 128].rearrange("(c p) n -> p c n", p=128),
                  [], [kwu], tag=f"lwu{ti}_{g}")
            pg, kpg = ps[2], ("ps", 2)
            pu, kpu = ps[3], ("ps", 3)
            for c in range(16):
                b.mm(pg[:, 0:w], wgb_[:, c * 128:(c + 1) * 128], xns[:, c * w:(c + 1) * w],
                     start=(c == 0), stop=(c == 15), reads=[kwg, "xns"], writes=[kpg],
                     tag=f"gm{ti}_{g}_{c}")
                b.mm(pu[:, 0:w], wub_[:, c * 128:(c + 1) * 128], xns[:, c * w:(c + 1) * w],
                     start=(c == 0), stop=(c == 15), reads=[kwu, "xns"], writes=[kpu],
                     tag=f"um{ti}_{g}_{c}")
            b.act(b.lp(lambda w=w: nc.scalar.activation(out=sg[:, 0:w], in_=pg[:, 0:w],
                  func=AF.Silu)), [kpg], ["sg"], tag=f"sg{ti}_{g}")
            b.dve(b.lp(lambda w=w: nc.vector.tensor_copy(su[:, 0:w], pu[:, 0:w])),
                  [kpu], ["su"], tag=f"sue{ti}_{g}")
            b.pool(b.lp(lambda g=g, w=w: nc.gpsimd.tensor_mul(hsb[:, g * w:(g + 1) * w],
                   sg[:, 0:w], su[:, 0:w])), ["sg", "su"], [("h", g)], tag=f"hm{ti}_{g}")
            b.pool(b.lp(lambda g=g, w=w: nc.gpsimd.tensor_mul(sq[:, 0:w],
                   hsb[:, g * w:(g + 1) * w], hsb[:, g * w:(g + 1) * w])),
                   [("h", g)], ["sqc"], tag=f"hs{ti}_{g}")
            b.mm(ps[4][0:1, 0:w], ones128[:, :], sq[:, 0:w], start=(g == 0), stop=(g == 63),
                 reads=["sqc", "ones128"], writes=[("ps", 4)], tag=f"hq{ti}_{g}")
        b.act(lambda w=w: nc.scalar.activation(out=lnv[:, 0:w], in_=ps[4][0:1, 0:w],
              func=AF.Ln, scale=1.0 / 8192, bias=epsf_c[:, :]),
              [("ps", 4), epsf_k], ["lnv"], tag=f"rl{ti}")
        b.act(b.lp(lambda w=w: nc.scalar.activation(out=rsr[:, 0:w], in_=lnv[:, 0:w],
              func=AF.Exp, scale=-0.5)), ["lnv"], ["rsr"], tag=f"re{ti}")
        b.mm(ps[5][:, 0:w], ones1[:, :], rsr[:, 0:w], True, True, ["ones1", "rsr"],
             [("ps", 5)], tag=f"rb{ti}")
        b.dve(b.lp(lambda w=w: nc.vector.tensor_copy(rstdb[:, 0:w], ps[5][:, 0:w])),
              [("ps", 5)], ["rstdb"], tag=f"rc{ti}")
        for f in range(16):
            wob = wo[0]
            kwo = ("wo", 0)
            b.dma("qpool", wob[:, :].rearrange("p (c n) -> p c n", c=32),
                  wout_h[:, f * 128:(f + 1) * 128].rearrange("(c p) n -> p c n", p=128),
                  [], [kwo], tag=f"lwo{ti}_{f}")
            pso, kpso = ps[2], ("ps", 2)
            for c in range(32):
                b.mm(pso[:, 0:w], wob[:, c * 128:(c + 1) * 128], hgs[:, c * w:(c + 1) * w],
                     start=(c == 0), stop=(c == 31), reads=[kwo, "hgs"], writes=[kpso],
                     tag=f"s0m{ti}_{f}_{c}")
            b.dve(b.lp(lambda w=w: nc.vector.tensor_copy(s0c[:, 0:w], ps[2][:, 0:w])),
                  [kpso], ["s0c"], tag=f"s0e{ti}_{f}")
            wpb = wp[f % 2]
            kw = ("wp", 0)
            b.dma("qpool", wpb[:, :].rearrange("p (c n) -> p c n", c=64),
                  wp_h[:, f * 128:(f + 1) * 128].rearrange("(c p) n -> p c n", p=128),
                  [], [kw], tag=f"lwp{ti}_{f}")
            pb, kp = ps[f % 2], ("ps", f % 2)
            for c in range(64):
                b.mm(pb[:, 0:w], wpb[:, c * 128:(c + 1) * 128], hsb[:, c * w:(c + 1) * w],
                     start=(c == 0), stop=(c == 63), reads=[kw, ("h", c)], writes=[kp],
                     tag=f"dm{ti}_{f}_{c}")
            b.dve(b.lp(lambda w=w, pb=pb: nc.vector.scalar_tensor_tensor(out=t2c[:, 0:w],
                  in0=pb[:, 0:w], scalar=1.0, in1=rstdb[:, 0:w], op0=ALU.mult, op1=ALU.mult)),
                  [kp, "rstdb"], ["t2c"], tag=f"dn{ti}_{f}")
            b.act(b.lp(lambda w=w: nc.scalar.activation(out=g0c[:, 0:w], in_=t2c[:, 0:w],
                  func=AF.Silu)), ["t2c"], ["g0c"], tag=f"g0{ti}_{f}")
            b.dma("qsp", x0c[:, 0:w], x0_h[f * 128:(f + 1) * 128, off:off + w], [], ["x0c"],
                  tag=f"lx0{ti}_{f}")
            b.dve(b.lp(lambda w=w: nc.vector.tensor_mul(t2c[:, 0:w],
                  s0c[:, 0:w], g0c[:, 0:w])), ["s0c", "g0c"], ["t2c"],
                  tag=f"sgm{ti}_{f}")
            b.dve(lambda w=w: nc.vector.tensor_add(x1c[:, 0:w],
                  x0c[:, 0:w], t2c[:, 0:w]), ["x0c", "t2c"], ["x1c"], tag=f"x1{ti}_{f}")
            b.pool(b.lp(lambda f=f, w=w: nc.gpsimd.tensor_copy(x1sb[:, f * w:(f + 1) * w],
                   x1c[:, 0:w])), ["x1c"], [("x1", f)], tag=f"x1b{ti}_{f}")
            b.dma("qsp", x1_h[f * 128:(f + 1) * 128, off:off + w], x1c[:, 0:w],
                  ["x1c"], [], tag=f"sx1{ti}_{f}")
        pm, pq = ps[2], ps[3]
        for c in range(16):
            xc = x1sb[:, c * w:(c + 1) * w]
            b.mm(pm[0:1, 0:w], ones128[:, :], xc, start=(c == 0), stop=(c == 15),
                 reads=[("x1", c), "ones128"], writes=[("ps", 2)], tag=f"nsum{ti}_{c}")
            b.pool(b.lp(lambda xc=xc, w=w: nc.gpsimd.tensor_mul(sq[:, 0:w], xc, xc)),
                   [("x1", c)], ["sqc"], tag=f"nsq{ti}_{c}")
            b.mm(pq[0:1, 0:w], ones128[:, :], sq[:, 0:w], start=(c == 0), stop=(c == 15),
                 reads=["sqc", "ones128"], writes=[("ps", 3)], tag=f"nsq2{ti}_{c}")
        mean = stats[:, 0:w]
        ex2 = stats[:, 512:512 + w]
        msq = stats[:, 1024:1024 + w]
        var = stats[:, 1536:1536 + w]
        sd = stats[:, 2048:2048 + w]
        b.dve(lambda w=w, o=mean: nc.vector.tensor_scalar(out=o, in0=pm[0:1, 0:w],
              scalar1=1.0 / 2048, scalar2=None, op0=ALU.mult), [("ps", 2)], [("st", "m")])
        b.dve(lambda w=w, o=ex2: nc.vector.tensor_scalar(out=o, in0=pq[0:1, 0:w],
              scalar1=1.0 / 2048, scalar2=None, op0=ALU.mult), [("ps", 3)], [("st", "e")])
        b.dve(lambda o=msq, a=mean: nc.vector.tensor_mul(o, a, a), [("st", "m")], [("st", "ms")])
        b.dve(lambda o=var, a=ex2, c2=msq: nc.vector.tensor_sub(o, a, c2),
              [("st", "e"), ("st", "ms")], [("st", "v")])
        b.act(lambda o=sd, a=var: nc.scalar.activation(out=o, in_=a, func=AF.Sqrt,
              bias=eps5_c[:, :]), [("st", "v"), eps5_k], [("st", "sd")])
        b.dve(b.lp(lambda w=w, a=sd: nc.vector.reciprocal(rr[:, 0:w], a)), [("st", "sd")], ["rr"])
        b.dve(b.lp(lambda w=w, a=mean: nc.vector.tensor_mul(rm[:, 0:w], a, rr[:, 0:w])),
              [("st", "m"), "rr"], ["rm"])
        b.mm(ps[4][:, 0:w], ones1[:, :], rr[:, 0:w], True, True, ["ones1", "rr"], [("ps", 4)],
             tag=f"nbr{ti}")
        b.mm(ps[5][:, 0:w], ones1[:, :], rm[:, 0:w], True, True, ["ones1", "rm"], [("ps", 5)],
             tag=f"nbm{ti}")
        for c in range(16):
            xc = x1sb[:, c * w:(c + 1) * w]
            b.dve(b.lp(lambda xc=xc, w=w: nc.vector.tensor_mul(sq[:, 0:w], xc, ps[4][0:128, 0:w])),
                  [("x1", c), ("ps", 4)], ["sqc"], tag=f"xnm{ti}_{c}")
            b.dve(b.lp(lambda w=w: nc.vector.tensor_sub(t2c[:, 0:w], sq[:, 0:w], ps[5][0:128, 0:w])),
                  ["sqc", ("ps", 5)], ["t2c"], tag=f"xns{ti}_{c}")
            b.dma("qsp", xn1_h[c * 128:(c + 1) * 128, off:off + w], t2c[:, 0:w],
                  ["t2c"], [], tag=f"sxn1{ti}_{c}")
    return b


def stage_combine0(core, artsA, artsB, prep, inputs, nt=NT_FULL):
    bidx, half = core // 2, core % 2
    x = inputs["L0_attn_hidden_states"][bidx]
    lo = half * nt
    return {
        "hg": artsB["hg"], "xn": artsA["xn"],
        "x0T": np.ascontiguousarray(x[lo:lo + nt].T),
        "woutT": np.ascontiguousarray(inputs["L0_attn_W_out"].T),
        "wq_gu_t": prep["wq_gu_t"], "wq_p_t": prep["wq_p_t"],
    }


def build_l1mix(nt=NT_FULL, debug_taps=False):
    nch = nt // 128
    b = B("l1mix")
    nc = b.nc
    xf_h = b.dram("xn1f", [2048, nt + HALO], BF16, kind="ExternalInput")
    convd_h = b.dram("convd", [16 * 4 * 128, 128], F32, kind="ExternalInput")
    convb_h = b.dram("convb", [128, 16], F32, kind="ExternalInput")
    inwr_h = b.dram("inwr_t", [2048, 64], F32, kind="ExternalInput")
    inwk_h = b.dram("inwk_t", [2048, 64], F32, kind="ExternalInput")
    mgws_h = b.dram("mgws_t", [2048, 64], F32, kind="ExternalInput")
    mgwt_h = b.dram("mgwt_t", [2048, 64], F32, kind="ExternalInput")
    mgbs_h = b.dram("mgbs", [64, 1], F32, kind="ExternalInput")
    mgbt_h = b.dram("mgbt", [64, 1], F32, kind="ExternalInput")
    g0w_h = b.dram("g0w_t", [2048, 128], F32, kind="ExternalInput")
    g1w_h = b.dram("g1w_t", [128, 2048], F32, kind="ExternalInput")
    b1r_h = b.dram("b1r", [1, 2048], F32, kind="ExternalInput")
    resw_h = b.dram("resw", [128, 16], F32, kind="ExternalInput")
    mask_h = b.dram("mask01", [128, 128], mybir.dt.int32, kind="ExternalInput")
    sin_h = b.dram("s1init", [64, 2048], F32, kind="ExternalInput")
    s1_h = b.dram("s1T", [2048, nt], BF16, kind="ExternalOutput")
    sfin_h = b.dram("s1fin", [64, 2048], F32, kind="ExternalOutput")

    xf = b.sbuf("xf", [128, 16 * (nt + HALO)], BF16)
    shift = b.sbuf("shift", [128, 16 * nt], BF16)
    rs = b.sbuf("rs", [64, nt], BF16)
    ks = b.sbuf("ks", [64, nt], BF16)
    sel = b.sbuf("sel", [64, 512], F32)
    tau = b.sbuf("tau", [64, 512], F32)
    itg = b.sbuf("itg", [64, 512], F32)
    gg = b.sbuf("gg", [64, nt], F32)
    kh = b.sbuf("kh", [64, nt], BF16)
    t64 = b.sbuf("t64", [64, 512], F32)
    S = b.sbuf("S1", [64, 2048], BF16)
    cw = b.sbuf("cw", [128, 4 * 128], BF16)
    cb = b.sbuf("cb", [128, 16], F32)
    inwr = b.sbuf("inwr", [128, 16 * 64], BF16)
    inwk = b.sbuf("inwk", [128, 16 * 64], BF16)
    mgws = b.sbuf("mgws", [128, 16 * 64], BF16)
    mgwt = b.sbuf("mgwt", [128, 16 * 64], BF16)
    mgbs = b.sbuf("mgbs_s", [64, 1], F32)
    mgbt = b.sbuf("mgbt_s", [64, 1], F32)
    g0w = b.sbuf("g0w", [128, 16 * 128], BF16)
    g1w = b.sbuf("g1w", [128, 2048], BF16)
    b1b = b.sbuf("b1b", [128, 2048], BF16)
    resw = b.sbuf("resw_s", [128, 16], F32)
    mask01 = b.sbuf("mask01_s", [128, 128], mybir.dt.int32)
    zeros = b.sbuf("zeros", [128, 128], BF16)
    zero64 = b.sbuf("zero64", [64, 128], F32)
    ones1 = b.sbuf("ones1", [1, 128], BF16)
    ones1f = b.sbuf("ones1f", [1, 128], F32)
    ones64 = b.sbuf("ones64", [64, 1], F32)
    Gc = b.sbuf("Gc1", [64, 128], F32)
    eGc = b.sbuf("eGc", [64, 128], F32)
    rti = b.sbuf("rti", [64, 128], BF16)
    d2 = b.sbuf("d2", [64, 128], F32)
    er = b.sbuf("er", [64, 128], F32)
    rh = b.sbuf("rh", [64, 128], BF16)
    d1 = b.sbuf("d1", [64, 32], F32)
    ek = b.sbuf("ek", [64, 32], F32)
    ktb = b.sbuf("ktb", [64, 32], BF16)
    Asb = b.sbuf("Asb", [128, 128], BF16)
    xtm = b.sbuf("xtm", [128, 2048], BF16)
    utm = b.sbuf("utm", [128, 2048], BF16)
    tmpc = b.sbuf("tmpc", [128, 128], BF16)
    chc = b.sbuf("chc", [128, 512], BF16)
    t2c = b.sbuf("t2c", [128, 512], F32)
    kbar = b.sbuf("kbar", [64, 128], BF16)
    kbt = b.sbuf("kbt", [128, 64], BF16)
    egl1 = b.sbuf("egl1", [64, 1], F32)
    s1stg = b.sbuf("s1stg", [128, 16 * 128], BF16)
    ocp = b.sbuf("ocp", [128, 128], F32)
    srow = b.sbuf("srow", [1, 512], F32)
    ps = [b.psum(f"ps{i}", (128, 512), F32) for i in range(6)]
    pbf = [b.psum(f"pbf{i}", (128, 1024), BF16) for i in range(2)]
    id128 = b.sbuf("id128s", [128, 128], BF16)
    id128_h = b.dram("id128", [128, 128], F32, kind="ExternalInput")

    b.pool(lambda: nc.gpsimd.memset(ones1[:, :], 1.0), [], ["ones1"])
    b.pool(lambda: nc.gpsimd.memset(ones1f[:, :], 1.0), [], ["ones1f"])
    b.pool(lambda: nc.gpsimd.memset(ones64[:, :], 1.0), [], ["ones64"])
    b.pool(lambda: nc.gpsimd.memset(zeros[:, :], 0.0), [], ["zeros"])
    b.pool(lambda: nc.gpsimd.memset(zero64[:, :], 0.0), [], ["zero64"])
    one_c, one_k = b.constcol(1.0, 64)
    for t, h, kk in ((cb, convb_h, "cb"), (mgbs, mgbs_h, "mgbs"), (mgbt, mgbt_h, "mgbt"),
                     (resw, resw_h, "resw"), (mask01, mask_h, "mask01"), (id128, id128_h, "id128")):
        b.dma("qpool", t[:, :], h, [], [kk])
    b.dma("qpool", xf[:, :].rearrange("p (c t) -> p c t", c=16),
          xf_h.rearrange("(c p) t -> p c t", p=128), [], ["xf"])
    b.dma("qpool", cw[:, :], convd_h.rearrange("(z j p) n -> z p j n", j=4, p=128)[0], [], ["cwz"])
    b.dma("qpool", inwr[:, :].rearrange("p (c n) -> p c n", c=16),
          inwr_h.rearrange("(c p) n -> p c n", p=128), [], ["inwr"])
    b.dma("qpool", inwk[:, :].rearrange("p (c n) -> p c n", c=16),
          inwk_h.rearrange("(c p) n -> p c n", p=128), [], ["inwk"])
    b.dma("qpool", mgws[:, :].rearrange("p (c n) -> p c n", c=16),
          mgws_h.rearrange("(c p) n -> p c n", p=128), [], ["mgws"])
    b.dma("qpool", mgwt[:, :].rearrange("p (c n) -> p c n", c=16),
          mgwt_h.rearrange("(c p) n -> p c n", p=128), [], ["mgwt"])
    b.dma("qpool", g0w[:, :].rearrange("p (c n) -> p c n", c=16),
          g0w_h.rearrange("(c p) n -> p c n", p=128), [], ["g0w"])
    b.dma("qpool", g1w[:, :], g1w_h, [], ["g1w"])
    b.dma("qpool", S[:, :], sin_h, [], [("S1k",)])
    b1r_s = b.sbuf("b1r_s", [1, 512], F32)
    for q4 in range(4):
        b.dma("qpool", b1r_s[:, :], b1r_h[0:1, q4 * 512:(q4 + 1) * 512], ["b1r"], ["b1r"],
              tag=f"b1l{q4}")
        b.mm(ps[5][:, 0:512], ones1f[:, :], b1r_s[0:1, 0:512], True, True,
             ["ones1f", "b1r"], [("ps", 5)], tag=f"b1b{q4}")
        b.dve(b.lp(lambda q4=q4: nc.vector.tensor_copy(b1b[:, q4 * 512:(q4 + 1) * 512],
              ps[5][:, 0:512])), [("ps", 5)], ["b1b"], tag=f"b1c{q4}")

    ntile = ceil_div(nt, 512)
    for ci in range(16):
        for ti in range(ntile):
            off = ti * 512
            w = min(512, nt - off)
            pb, kp = ps[ti % 2], ("ps", ti % 2)
            for j in range(4):
                b.mm(pb[:, 0:w], cw[:, (4 * ci + j) * 0 + j * 128:(j + 1) * 128] if False else cw[:, j * 128:(j + 1) * 128],
                     xf[:, ci * (nt + HALO) + off + j:ci * (nt + HALO) + off + j + w],
                     start=(j == 0), stop=(j == 3), reads=["cwz", "xf"], writes=[kp],
                     tag=f"cv{ci}_{ti}_{j}")
            b.act(b.lp(lambda pb=pb, w=w, ci=ci, off=off: nc.scalar.activation(
                  out=shift[:, ci * nt + off:ci * nt + off + w], in_=pb[:, 0:w], func=AF.Silu,
                  bias=cb[:, ci:ci + 1])), [kp, "cb"], [("sh", ci)], tag=f"cs{ci}_{ti}")
        if ci < 15:
            b.dma("qpool", cw[:, :], convd_h.rearrange("(z j p) n -> z p j n", j=4, p=128)[ci + 1],
                  ["cwz"], ["cwz"], tag=f"lcw{ci}")
    SH = [("sh", i) for i in range(16)]
    for ti in range(ntile):
        off = ti * 512
        w = min(512, nt - off)
        slt = slice(off, off + w)
        wsl = slice(0, w)
        for name, wtile, kw, dst, kd, bias in (("r", inwr, "inwr", rs, "rs", None),
                                               ("k", inwk, "inwk", ks, "ks", None),
                                               ("s", mgws, "mgws", sel, "sel0", mgbs),
                                               ("t", mgwt, "mgwt", tau, "tau0", mgbt)):
            pb, kp = ps[2], ("ps", 2)
            for c in range(16):
                b.mm(pb[0:64, 0:w], wtile[:, c * 64:(c + 1) * 64],
                     shift[:, c * nt + off:c * nt + off + w], start=(c == 0), stop=(c == 15),
                     reads=[kw] + SH, writes=[kp], tag=f"pj{name}{ti}_{c}")
            if bias is None:
                sc = 0.125 if name == "r" else 1.0
                b.dve(b.lp(lambda pb=pb, w=w, dst=dst, off=off, sc=sc: nc.vector.tensor_scalar(
                      out=dst[:, off:off + w], in0=pb[0:64, 0:w], scalar1=sc, scalar2=None,
                      op0=ALU.mult)), [kp], [(kd, ti)], tag=f"pe{name}{ti}")
            else:
                kb = "mgbs" if name == "s" else "mgbt"
                b.act(lambda pb=pb, w=w, dst=dst, bias=bias: nc.scalar.activation(
                      out=dst[:, 0:w], in_=pb[0:64, 0:w], func=AF.Identity,
                      bias=bias[:, :]), [kp, kb], [(kd, ti)], tag=f"pe{name}{ti}")
        b.act(lambda wsl=wsl: nc.scalar.activation(out=t64[:, wsl], in_=sel[:, wsl], func=AF.Abs),
              [("sel0", ti)], ["t64a"], tag=f"sp1{ti}")
        b.act(lambda wsl=wsl: nc.scalar.activation(out=t64[:, wsl], in_=t64[:, wsl], func=AF.Exp,
              scale=-1.0), ["t64a"], ["t64a"], tag=f"sp2{ti}")
        b.act(lambda wsl=wsl: nc.scalar.activation(out=t64[:, wsl], in_=t64[:, wsl], func=AF.Ln,
              bias=one_c[:, :]), ["t64a", one_k], ["t64a"], tag=f"sp3{ti}")
        b.act(lambda wsl=wsl: nc.scalar.activation(out=sel[:, wsl], in_=sel[:, wsl], func=AF.Relu),
              [("sel0", ti)], [("selr", ti)], tag=f"sp4{ti}")
        b.dve(lambda wsl=wsl: nc.vector.tensor_add(sel[:, wsl], sel[:, wsl], t64[:, wsl]),
              [("selr", ti), "t64a"], [("sel", ti)], tag=f"sp5{ti}")
        b.act(lambda wsl=wsl: nc.scalar.activation(out=tau[:, wsl], in_=tau[:, wsl], func=AF.Sigmoid),
              [("tau0", ti)], [("tau", ti)], tag=f"sg{ti}")
        b.dve(lambda wsl=wsl: nc.vector.tensor_scalar(out=t64[:, wsl], in0=sel[:, wsl],
              scalar1=1e-19, scalar2=None, op0=ALU.max), [("sel", ti)], ["t64a"], tag=f"it1{ti}")
        b.act(lambda wsl=wsl: nc.scalar.activation(out=t64[:, wsl], in_=t64[:, wsl], func=AF.Ln),
              ["t64a"], ["t64a"], tag=f"it2{ti}")
        b.dve(lambda wsl=wsl: nc.vector.tensor_mul(t64[:, wsl], t64[:, wsl], tau[:, wsl]),
              ["t64a", ("tau", ti)], ["t64a"], tag=f"it3{ti}")
        b.act(lambda wsl=wsl: nc.scalar.activation(out=itg[:, wsl], in_=t64[:, wsl], func=AF.Exp),
              ["t64a"], [("itg", ti)], tag=f"it4{ti}")
        b.dve(lambda slt=slt, wsl=wsl: nc.vector.scalar_tensor_tensor(out=gg[:, slt], in0=sel[:, wsl],
              scalar=-1.0, in1=tau[:, wsl], op0=ALU.mult, op1=ALU.mult),
              [("sel", ti), ("tau", ti)], [("gg", ti)], tag=f"gm{ti}")
        b.pool(lambda slt=slt, wsl=wsl: nc.gpsimd.tensor_mul(t64[:, wsl], ks[:, slt], ks[:, slt]),
               [("ks", ti)], ["t64a"], tag=f"kn0{ti}")
        b.mm(ps[4][0:1, 0:w], ones64[:, :], t64[:, 0:w], True, True, ["ones64", "t64a"],
             [("ps", 4)], tag=f"kn1{ti}")
        b.act(lambda w=w: nc.scalar.activation(out=srow[:, 0:w], in_=ps[4][0:1, 0:w],
              func=AF.Sqrt), [("ps", 4)], ["srow"], tag=f"kn2{ti}")
        b.dve(lambda w=w: nc.vector.tensor_scalar(out=srow[:, 0:w], in0=srow[:, 0:w],
              scalar1=1e-12, scalar2=None, op0=ALU.max), ["srow"], ["srow"], tag=f"kn3{ti}")
        b.dve(lambda w=w: nc.vector.reciprocal(srow[:, 0:w], srow[:, 0:w]), ["srow"], ["srow"],
              tag=f"kn4{ti}")
        b.mm(ps[4][0:64, 0:w], ones1f[0:1, 0:64], srow[0:1, 0:w], True, True, ["ones1f", "srow"],
             [("ps", 4)], tag=f"kn5{ti}")
        b.dve(lambda wsl=wsl, w=w: nc.vector.tensor_mul(t64[:, wsl], itg[:, wsl], ps[4][0:64, 0:w]),
              [("itg", ti), ("ps", 4)], ["t64a"], tag=f"kn6{ti}")
        b.dve(b.lp(lambda slt=slt, wsl=wsl: nc.vector.tensor_mul(kh[:, slt], ks[:, slt], t64[:, wsl])),
              [("ks", ti), "t64a"], [("kh", ti)], tag=f"kn7{ti}")

    KH = [("kh", i) for i in range(ntile)]
    GGk = [("gg", i) for i in range(ntile)]
    RS = [("rs", i) for i in range(ntile)]
    for c in range(nch):
        sl = slice(c * 128, (c + 1) * 128)
        b.dve(lambda sl=sl: nc.vector.tensor_tensor_scan(out=Gc[:, :], data0=gg[:, sl],
              data1=zero64[:, :], initial=0.0, op0=ALU.add, op1=ALU.add),
              GGk + ["zero64"], ["Gc1"], tag=f"sc{c}")
        b.act(lambda: nc.scalar.activation(out=eGc[:, :], in_=Gc[:, :], func=AF.Exp),
              ["Gc1"], ["eGc"], tag=f"eg{c}")
        b.dve(b.lp(lambda sl=sl: nc.vector.tensor_mul(rti[:, :], rs[:, sl], eGc[:, :])),
              RS + ["eGc"], ["rti"], tag=f"rt{c}")
        b.dma("qsp", xtm[:, :], xf_h[:, HALO + c * 128:HALO + (c + 1) * 128], [], ["xtm"],
              tag=f"xt{c}", transpose=True)
        pt, kpt = ps[3], ("ps", 3)
        for k16 in range(16):
            b.mm(pt[:, 0:128], g0w[:, k16 * 128:(k16 + 1) * 128],
                 xf[:, k16 * (nt + HALO) + HALO + c * 128:k16 * (nt + HALO) + HALO + (c + 1) * 128],
                 start=(k16 == 0), stop=(k16 == 15), reads=["g0w", "xf"], writes=[kpt],
                 tag=f"ch1{c}_{k16}")
        b.dve(b.lp(lambda: nc.vector.tensor_copy(tmpc[:, :], ps[3][:, 0:128])),
              [kpt], ["tmpc"], tag=f"ch2{c}")
        for q4 in range(4):
            b.mm(ps[2][:, 0:512], tmpc[:, :], g1w[:, q4 * 512:(q4 + 1) * 512], True, True,
                 ["tmpc", "g1w"], [("ps", 2)], tag=f"ch3{c}_{q4}")
            b.dve(lambda q4=q4: nc.vector.tensor_add(t2c[:, 0:512], ps[2][:, 0:512],
                  b1b[:, q4 * 512:(q4 + 1) * 512]), [("ps", 2), "b1b"], ["t2c1"], tag=f"ch4{c}_{q4}")
            b.act(b.lp(lambda q4=q4: nc.scalar.activation(out=chc[:, 0:512], in_=t2c[:, 0:512],
                  func=AF.Sigmoid)), ["t2c1"], ["chc"], tag=f"ch5{c}_{q4}")
            b.pool(b.lp(lambda q4=q4: nc.gpsimd.tensor_mul(utm[:, q4 * 512:(q4 + 1) * 512],
                   chc[:, 0:512], xtm[:, q4 * 512:(q4 + 1) * 512])),
                   ["chc", "xtm"], [("utm", q4)], tag=f"ch6{c}_{q4}")
        UT = [("utm", i) for i in range(4)]
        for blk in range(4):
            bsl = slice(blk * 32, (blk + 1) * 32)
            refc = Gc[:, blk * 32 + 16:blk * 32 + 17]
            b.dve(lambda bsl=bsl, refc=refc: nc.vector.tensor_scalar(out=d1[:, :],
                  in0=Gc[:, bsl], scalar1=refc, scalar2=-85.0, op0=ALU.subtract, op1=ALU.max),
                  ["Gc1"], ["d1"], tag=f"kb1{c}_{blk}")
            b.act(lambda: nc.scalar.activation(out=ek[:, :], in_=d1[:, :], func=AF.Exp,
                  scale=-1.0), ["d1"], ["ek"], tag=f"kb2{c}_{blk}")
            b.dve(b.lp(lambda c=c, blk=blk: nc.vector.tensor_mul(ktb[:, :],
                  kh[:, c * 128 + blk * 32:c * 128 + (blk + 1) * 32], ek[:, :])),
                  KH + ["ek"], ["ktb"], tag=f"kb3{c}_{blk}")
            b.dve(lambda refc=refc: nc.vector.tensor_scalar(out=d2[:, :], in0=Gc[:, :],
                  scalar1=refc, scalar2=85.0, op0=ALU.subtract, op1=ALU.min),
                  ["Gc1"], ["d2"], tag=f"rh1{c}_{blk}")
            b.act(lambda: nc.scalar.activation(out=er[:, :], in_=d2[:, :], func=AF.Exp),
                  ["d2"], ["er"], tag=f"rh2{c}_{blk}")
            b.dve(b.lp(lambda sl=sl: nc.vector.tensor_mul(rh[:, :], rs[:, sl], er[:, :])),
                  RS + ["er"], ["rh"], tag=f"rh3{c}_{blk}")
            if blk > 0:
                b.dve(b.lp(lambda blk=blk: nc.vector.memset(rh[:, 0:blk * 32], 0.0)),
                      ["rh"], ["rh"], tag=f"rh4{c}_{blk}")
            b.mm(ps[0][blk * 32:(blk + 1) * 32, 0:128], ktb[:, :], rh[:, :], True, True,
                 ["ktb", "rh"], [("ps", 0)], tag=f"am{c}_{blk}", tile_position=(0, blk * 32))
        b.dve(b.lp(lambda: nc.vector.tensor_copy(Asb[:, :], zeros[:, :])),
              ["zeros"], ["Asb"], tag=f"az{c}")
        b.dve(b.lp(lambda: nc.vector.copy_predicated(out=Asb[:, :], mask=mask01[:, :],
              data=ps[0][:, 0:128])), [("ps", 0), "mask01", "Asb"], ["Asb"], tag=f"asel{c}")
        if debug_taps and c == 0:
            for nm, t, rd in (("Asb0", Asb, ["Asb"]), ("utm0", utm, UT), ("rti0", rti, ["rti"]),
                              ("Gc0", Gc, ["Gc1"])):
                hh = b.dram("tap_" + nm, list(t.shape), F32, kind="ExternalOutput")
                b.dma("qpool", hh, t[:, :], rd, [], tag="tap" + nm)
        for dc in range(16):
            psl = ps[1]
            b.mm(psl[:, 0:128], utm[:, dc * 128:(dc + 1) * 128], Asb[:, :], True, False,
                 UT + ["Asb"], [("ps", 1)], tag=f"o1{c}_{dc}")
            b.mm(psl[:, 0:128], S[:, dc * 128:(dc + 1) * 128], rti[:, :], False, True,
                 [("S1k",), "rti"], [("ps", 1)], tag=f"o2{c}_{dc}")
            b.dve(b.lp(lambda: nc.vector.tensor_copy(ocp[:, :], ps[1][:, 0:128])),
                  [("ps", 1)], ["ocp"], tag=f"ocp{c}_{dc}")
            b.dve(b.lp(lambda dc=dc, c=c: nc.vector.scalar_tensor_tensor(
                  out=s1stg[:, dc * 128:(dc + 1) * 128],
                  in0=shift[:, dc * nt + c * 128:dc * nt + (c + 1) * 128],
                  scalar=resw[:, dc:dc + 1], in1=ocp[:, :], op0=ALU.mult, op1=ALU.add)),
                  [("sh", dc), "resw", "ocp"], [("s1s", dc)], tag=f"o3{c}_{dc}")
            if debug_taps and c == 0:
                otap = b.sbuf(f"otap{dc}", [128, 128], F32) if dc == 0 and not hasattr(b, "_otap") else getattr(b, "_otap", None)
                if dc == 0:
                    b._otap = otap
                    b._otap_h = b.dram("tap_ops", [128, 16 * 128], F32, kind="ExternalOutput")
                b.dve(lambda dc=dc: nc.vector.tensor_copy(b._otap[:, :], ps[1][:, 0:128]),
                      [("ps", 1)], ["otapb"], tag=f"ot{c}_{dc}")
                b.dma("qpool", b._otap_h[:, dc * 128:(dc + 1) * 128], b._otap[:, :],
                      ["otapb"], [], tag=f"otd{c}_{dc}")
                if dc == 0:
                    b._octap_h = b.dram("tap_ocp", [128, 128], F32, kind="ExternalOutput")
                    b.dma("qpool", b._octap_h, ocp[:, :], ["ocp"], [], tag="ocptap")
                    b._shtap_h = b.dram("tap_shslice", [128, 128], F32, kind="ExternalOutput")
                    b.dma("qpool", b._shtap_h, shift[:, 0 * nt + c * 128:0 * nt + (c + 1) * 128],
                          [("sh", 0)], [], tag="shslicetap")
        b.dma("qsp", s1_h.rearrange("(dc p) t -> p dc t", p=128)[:, :, sl],
              s1stg[:, :].rearrange("p (dc t) -> p dc t", dc=16),
              [("s1s", i) for i in range(16)], [], tag=f"so{c}")
        if debug_taps and c == 0:
            hh = b.dram("tap_s1stg", [128, 2048], F32, kind="ExternalOutput")
            b.dma("qpool", hh, s1stg[:, :], [("s1s", i) for i in range(16)], [], tag="taps1stg")
        b.dve(lambda: nc.vector.tensor_scalar(out=d2[:, :], in0=Gc[:, :],
              scalar1=Gc[:, 127:128], scalar2=None, op0=ALU.subtract), ["Gc1"], ["d2"],
              tag=f"kb5{c}")
        b.act(lambda: nc.scalar.activation(out=er[:, :], in_=d2[:, :], func=AF.Exp, scale=-1.0),
              ["d2"], ["er"], tag=f"kb6{c}")
        b.dve(b.lp(lambda sl=sl: nc.vector.tensor_mul(kbar[:, :], kh[:, sl], er[:, :])),
              KH + ["er"], ["kbar"], tag=f"kb7{c}")
        b.transp(pbf[0][0:128, 0:64], kbar[:, :], id128[0:64, 0:64],
                 ["kbar", "id128"], [("pbf", 0)], tag=f"kt{c}")
        b.dve(b.lp(lambda: nc.vector.tensor_copy(kbt[:, :], pbf[0][0:128, 0:64])),
              [("pbf", 0)], ["kbt"], tag=f"ktc{c}")
        b.act(lambda: nc.scalar.activation(out=egl1[:, :], in_=Gc[:, 127:128], func=AF.Exp),
              ["Gc1"], ["eglc1"], tag=f"egl{c}")
        for q4 in range(4):
            b.mm(ps[2][0:64, 0:512], kbt[:, :], utm[:, q4 * 512:(q4 + 1) * 512], True, True,
                 ["kbt"] + UT, [("ps", 2)], tag=f"ds{c}_{q4}")
            b.dve(b.lp(lambda q4=q4: nc.vector.scalar_tensor_tensor(
                  out=S[:, q4 * 512:(q4 + 1) * 512], in0=S[:, q4 * 512:(q4 + 1) * 512],
                  scalar=egl1[:, :], in1=ps[2][0:64, 0:512], op0=ALU.mult, op1=ALU.add)),
                  [("S1k",), "eglc1", ("ps", 2)], [("S1k",)], tag=f"su{c}_{q4}")
    b.dma("qpool", sfin_h, S[:, :], [("S1k",)], [], tag="stS1")
    if debug_taps:
        for nm, t, rd in (("shift", shift, SH), ("rs", rs, RS), ("ks", ks, [("ks", i) for i in range(ntile)]),
                          ("kh", kh, KH), ("gg", gg, GGk), ("utm", utm, UT),
                          ("Asb", Asb, ["Asb"]), ("Gc", Gc, ["Gc1"]), ("rti", rti, ["rti"])):
            hh = b.dram("tap_" + nm, list(t.shape), F32, kind="ExternalOutput")
            b.dma("qpool", hh, t[:, :], rd, [], tag="tap" + nm)
    return b


def stage_l1mix(core, xn1, xn1_halo, inputs, nt=NT_FULL, s1init=None):
    cw = inputs["L1_attn_conv_w"]
    cd = np.zeros((16, 4, 128, 128), np.float32)
    idx = np.arange(128)
    for ci in range(16):
        for j in range(4):
            cd[ci, j, idx, idx] = cw[ci * 128:(ci + 1) * 128, j]
    inw = inputs["L1_attn_in_proj_w"]
    mgw = inputs["L1_attn_mem_gate_w"]
    mgb = inputs["L1_attn_mem_gate_b"]
    xn1f = np.concatenate([np.asarray(xn1_halo), np.asarray(xn1)], 1)
    return {
        "xn1f": np.ascontiguousarray(xn1f),
        "convd": cd.reshape(16 * 4 * 128, 128),
        "convb": np.ascontiguousarray(inputs["L1_attn_conv_b"].reshape(16, 128).T),
        "inwr_t": np.ascontiguousarray(inw[:64].T), "inwk_t": np.ascontiguousarray(inw[64:].T),
        "mgws_t": np.ascontiguousarray(mgw[:64].T), "mgwt_t": np.ascontiguousarray(mgw[64:].T),
        "mgbs": mgb[:64].reshape(64, 1).astype(np.float32),
        "mgbt": mgb[64:].reshape(64, 1).astype(np.float32),
        "g0w_t": np.ascontiguousarray(inputs["L1_attn_ch_gate_w0"].T),
        "g1w_t": np.ascontiguousarray(inputs["L1_attn_ch_gate_w1"].T),
        "b1r": inputs["L1_attn_ch_gate_b1"].reshape(1, 2048).astype(np.float32),
        "resw": np.ascontiguousarray(inputs["L1_attn_residual_weight"].reshape(16, 128).T),
        "mask01": np.where(np.arange(128)[:, None] <= np.arange(128)[None, :], 1, 0).astype(np.int32),
        "id128": np.eye(128, dtype=np.float32),
        "s1init": np.zeros((64, 2048), np.float32) if s1init is None else np.asarray(s1init, np.float32),
    }


def build_l1ffn(nt=NT_FULL):
    b = B("l1ffn")
    nc = b.nc
    xn_h = b.dram("xn1", [2048, nt], BF16, kind="ExternalInput")
    x1_h = b.dram("x1T", [2048, nt], F32, kind="ExternalInput")
    s1_h = b.dram("s1T", [2048, nt], BF16, kind="ExternalInput")
    wgt_h = b.dram("wgt_t", [2048, 8192], BF16, kind="ExternalInput")
    wup_h = b.dram("wup_t", [2048, 8192], BF16, kind="ExternalInput")
    wdn_h = b.dram("wdn_t", [8192, 2048], BF16, kind="ExternalInput")
    wscg_h = b.dram("wscg", [128, 64], F32, kind="ExternalInput")
    wscu_h = b.dram("wscu", [128, 64], F32, kind="ExternalInput")
    wscd_h = b.dram("wscd", [128, 16], F32, kind="ExternalInput")
    out_h = b.dram("outT", [2048, nt], F32, kind="ExternalOutput")

    xns = b.sbuf("xns", [128, 16 * 512], BF16)
    hsb = b.sbuf("hsb", [128, 64 * 512], BF16)
    wg = [b.sbuf("wg0", [128, 16 * 128], BF16)] * 2
    wu = [b.sbuf("wu0", [128, 16 * 128], BF16)] * 2
    wp = [b.sbuf("wp0", [128, 64 * 128], BF16)] * 2
    wscg = b.sbuf("wscg_s", [128, 64], F32)
    wscu = b.sbuf("wscu_s", [128, 64], F32)
    wscd = b.sbuf("wscd_s", [128, 16], F32)
    sg = b.sbuf("sg", [128, 512], BF16)
    su = b.sbuf("su", [128, 512], BF16)
    g1c = b.sbuf("g1c", [128, 512], BF16)
    s1c = b.sbuf("s1c", [128, 512], BF16)
    x1c = b.sbuf("x1c", [128, 512], F32)
    oc = b.sbuf("oc", [128, 512], F32)
    t2c = b.sbuf("t2c", [128, 512], BF16)
    ps = [b.psum(f"ps{i}", (128, 512), F32) for i in range(6)]
    b.dma("qpool", wscg[:, :], wscg_h, [], ["wscg"])
    b.dma("qpool", wscu[:, :], wscu_h, [], ["wscu"])
    b.dma("qpool", wscd[:, :], wscd_h, [], ["wscd"])

    ntile = ceil_div(nt, 512)
    for ti in range(ntile):
        off = ti * 512
        w = min(512, nt - off)
        b.dma("qpool", xns[:, 0:16 * w].rearrange("p (c t) -> p c t", c=16),
              xn_h.rearrange("(c p) t -> p c t", p=128)[:, :, off:off + w],
              [], ["xns"], tag=f"lxn{ti}")
        for g in range(64):
            b.dma("qsp", wg[0][:, :].rearrange("p (c n) -> p c n", c=16),
                  wgt_h[:, g * 128:(g + 1) * 128].rearrange("(c p) n -> p c n", p=128),
                  [], [("wg", 0)], tag=f"lwg{ti}_{g}")
            b.dma("qact", wu[0][:, :].rearrange("p (c n) -> p c n", c=16),
                  wup_h[:, g * 128:(g + 1) * 128].rearrange("(c p) n -> p c n", p=128),
                  [], [("wu", 0)], tag=f"lwu{ti}_{g}")
            for cc in range(16):
                b.mm(ps[2][:, 0:w], wg[0][:, cc * 128:(cc + 1) * 128], xns[:, cc * w:(cc + 1) * w],
                     start=(cc == 0), stop=(cc == 15), reads=[("wg", 0), "xns"], writes=[("ps", 2)],
                     tag=f"gm{ti}_{g}_{cc}")
                b.mm(ps[3][:, 0:w], wu[0][:, cc * 128:(cc + 1) * 128], xns[:, cc * w:(cc + 1) * w],
                     start=(cc == 0), stop=(cc == 15), reads=[("wu", 0), "xns"], writes=[("ps", 3)],
                     tag=f"um{ti}_{g}_{cc}")
            b.dve(b.lp(lambda w=w, g=g: nc.vector.tensor_scalar(out=sg[:, 0:w], in0=ps[2][:, 0:w],
                  scalar1=wscg[:, g:g + 1], scalar2=None, op0=ALU.mult)),
                  [("ps", 2), "wscg"], ["sg"], tag=f"gs{ti}_{g}")
            b.act(b.lp(lambda w=w: nc.scalar.activation(out=sg[:, 0:w], in_=sg[:, 0:w],
                  func=AF.Silu)), ["sg"], ["sg"], tag=f"gsl{ti}_{g}")
            b.dve(b.lp(lambda w=w, g=g: nc.vector.tensor_scalar(out=su[:, 0:w], in0=ps[3][:, 0:w],
                  scalar1=wscu[:, g:g + 1], scalar2=None, op0=ALU.mult)),
                  [("ps", 3), "wscu"], ["su"], tag=f"us{ti}_{g}")
            b.pool(b.lp(lambda g=g, w=w: nc.gpsimd.tensor_mul(hsb[:, g * w:(g + 1) * w],
                   sg[:, 0:w], su[:, 0:w])), ["sg", "su"], [("h", g)], tag=f"hm{ti}_{g}")
        for f in range(16):
            b.dma("qpool", wp[0][:, :].rearrange("p (c n) -> p c n", c=64),
                  wdn_h[:, f * 128:(f + 1) * 128].rearrange("(c p) n -> p c n", p=128),
                  [], [("wp", 0)], tag=f"lwp{ti}_{f}")
            pb, kp = ps[f % 2], ("ps", f % 2)
            for cc in range(64):
                b.mm(pb[:, 0:w], wp[0][:, cc * 128:(cc + 1) * 128], hsb[:, cc * w:(cc + 1) * w],
                     start=(cc == 0), stop=(cc == 63), reads=[("wp", 0), ("h", cc)], writes=[kp],
                     tag=f"dm{ti}_{f}_{cc}")
            b.dve(b.lp(lambda w=w, pb=pb, f=f: nc.vector.tensor_scalar(out=t2c[:, 0:w],
                  in0=pb[:, 0:w], scalar1=wscd[:, f:f + 1], scalar2=None, op0=ALU.mult)),
                  [kp, "wscd"], ["t2cp"], tag=f"dn{ti}_{f}")
            b.act(b.lp(lambda w=w: nc.scalar.activation(out=g1c[:, 0:w], in_=t2c[:, 0:w],
                  func=AF.Silu)), ["t2cp"], ["g1c"], tag=f"g1{ti}_{f}")
            b.dma("qsp", s1c[:, 0:w], s1_h[f * 128:(f + 1) * 128, off:off + w], [], ["s1c"],
                  tag=f"ls1{ti}_{f}")
            b.dma("qact", x1c[:, 0:w], x1_h[f * 128:(f + 1) * 128, off:off + w], [], ["x1c"],
                  tag=f"lx1{ti}_{f}")
            b.dve(b.lp(lambda w=w: nc.vector.tensor_mul(t2c[:, 0:w], s1c[:, 0:w], g1c[:, 0:w])),
                  ["s1c", "g1c"], ["t2cp"], tag=f"sm{ti}_{f}")
            b.dve(lambda w=w: nc.vector.tensor_add(oc[:, 0:w], x1c[:, 0:w], t2c[:, 0:w]),
                  ["x1c", "t2cp"], ["oc"], tag=f"oa{ti}_{f}")
            b.dma("qsp", out_h[f * 128:(f + 1) * 128, off:off + w], oc[:, 0:w], ["oc"], [],
                  tag=f"so{ti}_{f}")
    return b


def stage_l1ffn(core, xn1, x1T, s1T, prep, inputs, nt=NT_FULL):
    wsc = inputs["L1_ffn_Wscale_ug"].astype(np.float32)
    return {
        "xn1": np.asarray(xn1), "x1T": np.asarray(x1T), "s1T": np.asarray(s1T),
        "wgt_t": prep["wgt_t"], "wup_t": prep["wup_t"], "wdn_t": prep["wdn_t"],
        "wscg": np.ascontiguousarray((wsc[8192:] / np.sqrt(2048.0)).reshape(64, 128).T),
        "wscu": np.ascontiguousarray((wsc[:8192] / np.sqrt(2048.0)).reshape(64, 128).T),
        "wscd": np.ascontiguousarray((inputs["L1_ffn_Wscale_dp"].astype(np.float32)
                                      / np.sqrt(8192.0)).reshape(16, 128).T),
    }


def stage_l0pre(core, inputs, nt=NT_FULL):
    bidx, half = core // 2, core % 2
    x = inputs["L0_attn_hidden_states"][bidx]
    lo = half * nt
    sl = x[lo:lo + nt]
    if half == 0:
        halov = np.zeros((HALO, 2048), np.float32)
    else:
        halov = x[lo - HALO:lo]
    xT = np.ascontiguousarray(np.concatenate([halov, sl], 0).T)
    cw = inputs["L0_attn_conv_w"][:, 0, :]
    cd = np.zeros((64, 4, 128, 128), np.float32)
    idx = np.arange(128)
    for c in range(64):
        for j in range(4):
            cd[c, j, idx, idx] = cw[c * 128:(c + 1) * 128, j]
    return {
        "xT": xT,
        "wqkvzT": np.ascontiguousarray(inputs["L0_attn_W_qkvz"].T),
        "wbaT": np.ascontiguousarray(inputs["L0_attn_W_ba"].T),
        "convw_diag": cd.reshape(64 * 4 * 128, 128),
    }


# version 07
# speedup vs baseline: 1.4107x; 1.4107x over previous
import numpy as np
import ml_dtypes
import concourse.bass as bass
import concourse.mybir as mybir
from concourse.bass_utils import run_bass_kernel_spmd

F32 = mybir.dt.float32
BF16 = mybir.dt.bfloat16
AF = mybir.ActivationFunctionType
ALU = mybir.AluOpType
AX = mybir.AxisListType
NPBF16 = ml_dtypes.bfloat16

_ENGINE_OF = {"pe": "tensor", "dve": "vector", "act": "scalar", "pool": "gpsimd",
              "qsp": "sync", "qact": "scalar", "qpool": "gpsimd"}
_DMA_Q = ("qsp", "qact", "qpool")


class Plan:

    def __init__(self, nc):
        self.nc = nc
        self.ops = []
        self.last_write = {}
        self.readers = {}
        self.n_emitted = 0

    def op(self, engine, emit, reads=(), writes=(), tag=""):
        idx = len(self.ops)
        deps = set()
        for k in reads:
            if k in self.last_write:
                deps.add(self.last_write[k])
        for k in writes:
            if k in self.last_write:
                deps.add(self.last_write[k])
            for r in self.readers.get(k, ()):
                deps.add(r)
        for k in reads:
            self.readers.setdefault(k, []).append(idx)
        for k in writes:
            self.last_write[k] = idx
            self.readers[k] = []
        self.ops.append({"engine": engine, "emit": emit, "deps": deps, "tag": tag,
                         "inc": False, "seq": None})
        return idx

    def barrier(self):
        idx = len(self.ops)
        deps = set(range(idx))
        self.ops.append({"engine": "barrier", "emit": None, "deps": deps, "tag": "barrier",
                         "inc": False, "seq": None})
        self.last_write = {}
        self.readers = {}
        self._barrier_idx = idx
        return idx

    def finalize(self):
        nc = self.nc
        ops = self.ops
        n = len(ops)
        last_barrier = -1
        for i, o in enumerate(ops):
            if o["engine"] == "barrier":
                last_barrier = i
            else:
                o["deps"] = {d for d in o["deps"] if d > last_barrier}
                if last_barrier >= 0:
                    o["deps"].add(last_barrier)
        for i, o in enumerate(ops):
            if o["engine"] == "barrier":
                continue
            pruned = set()
            for d in o["deps"]:
                od = ops[d]
                if od["engine"] == "barrier":
                    pruned.add(d)
                    continue
                if od["engine"] == "pe" and o["engine"] == "pe":
                    continue
                pruned.add(d)
            o["deps"] = pruned
        for o in ops:
            if o["engine"] in _DMA_Q:
                o["inc"] = True
            if o["engine"] == "barrier":
                continue
            for d in o["deps"]:
                ops[d]["inc"] = True
        NS = 8
        comp_sems = {e: self.nc.alloc_semaphore(f"sem_{e}")
                     for e in _ENGINE_OF if e not in _DMA_Q}
        dma_sems = {q: [self.nc.alloc_semaphore(f"sem_{q}{i}") for i in range(NS)]
                    for q in _DMA_Q}
        counters = {e: 0 for e in _ENGINE_OF if e not in _DMA_Q}
        qidx = {q: 0 for q in _DMA_Q}
        waited = {}

        def sem_of(o):
            if o["engine"] in _DMA_Q:
                q, i = o["engine"], o["qslot"]
                return dma_sems[q][i % NS], 16 * (1 + i // NS)
            return comp_sems[o["engine"]], o["seq"]

        def emit_wait(weng, sem, val):
            key = (weng, sem.name)
            if waited.get(key, 0) >= val:
                return
            waited[key] = val
            getattr(nc, _ENGINE_OF[weng]).wait_ge(sem, val)

        for i, o in enumerate(ops):
            if o["engine"] == "barrier":
                for q in _DMA_Q:
                    for j in range(max(0, qidx[q] - NS), qidx[q]):
                        emit_wait(q, dma_sems[q][j % NS], 16 * (1 + j // NS))
                nc.all_engine_barrier()
                for e, c in counters.items():
                    for w in _ENGINE_OF:
                        waited[(w, comp_sems[e].name)] = c
                for q in _DMA_Q:
                    for j in range(qidx[q]):
                        for w in _ENGINE_OF:
                            key = (w, dma_sems[q][j % NS].name)
                            waited[key] = max(waited.get(key, 0), 16 * (1 + j // NS))
                o["seq"] = None
                continue
            eng = o["engine"]
            for d in sorted(o["deps"]):
                od = ops[d]
                if od["engine"] == "barrier":
                    continue
                sem, val = sem_of(od)
                emit_wait(eng, sem, val)
            if eng in _DMA_Q:
                j = qidx[q := eng]
                if j >= NS:
                    emit_wait(q, dma_sems[q][j % NS], 16 * (j // NS))
                o["qslot"] = j
                qidx[q] += 1
            instr = o["emit"]()
            if eng in _DMA_Q:
                sem, val = sem_of(o)
                instr.then_inc(sem, 16)
                o["seq"] = val
            elif o["inc"]:
                counters[eng] += 1
                o["seq"] = counters[eng]
                if instr is None:
                    raise RuntimeError(f"op {o['tag']} needs inc but emit returned None")
                instr.then_inc(comp_sems[eng], 1)
            else:
                o["seq"] = counters[eng]
            self.n_emitted += 1
        for q in _DMA_Q:
            for j in range(max(0, qidx[q] - NS), qidx[q]):
                emit_wait(q, dma_sems[q][j % NS], 16 * (1 + j // NS))
        return self


class B:

    def __init__(self, name):
        self.nc = bass.Bass("TRN2", target_bir_lowering=False)
        self.plan = Plan(self.nc)
        self._ctx = []
        self.sbuf_bytes = 0
        self._psum_pool = []
        self._uid = 0
        self._consts = {}

    def constcol(self, value, parts, dtype=F32):
        key = ("cst", value, parts, str(dtype))
        if key not in self._consts:
            t = self.sbuf(f"cst{self.uid()}", [parts, 1], dtype)
            self.plan.op("pool", lambda t=t, v=value: self.nc.gpsimd.memset(t[:, :], v),
                         [], [key], tag="const")
            self._consts[key] = t
        return self._consts[key], key

    def uid(self):
        self._uid += 1
        return self._uid

    def dram(self, name, shape, dtype=F32, kind="Internal"):
        return self.nc.dram_tensor(name, list(shape), dtype, kind=kind).ap()

    def sbuf(self, name, shape, dtype=F32):
        cm = self.nc.sbuf_tensor(getattr(self, "prefix", "") + name, list(shape), dtype)
        t = cm.__enter__()
        self._ctx.append(cm)
        self.sbuf_bytes += shape[0] * int(np.prod(shape[1:])) * (2 if dtype == BF16 else 4)
        return t

    def psum(self, name, shape=(128, 512), dtype=F32):
        cm = self.nc.psum_tensor(getattr(self, "prefix", "") + name, list(shape), dtype)
        t = cm.__enter__()
        self._ctx.append(cm)
        return t

    def close(self):
        for cm in reversed(self._ctx):
            cm.__exit__(None, None, None)
        self._ctx = []

    def dma(self, q, out, in_, reads, writes, tag="", transpose=False):
        eng = getattr(self.nc, _ENGINE_OF[q])
        if transpose:
            return self.plan.op(q, lambda: eng.dma_start(out=out, in_=in_, transpose=True),
                                reads, writes, tag)
        return self.plan.op(q, lambda: eng.dma_start(out=out, in_=in_), reads, writes, tag)

    def mm(self, out, lhsT, rhs, start, stop, reads, writes, tag="", tile_position=None):
        t = self.nc.tensor
        return self.plan.op("pe", lambda: t.matmul(out, lhsT, rhs, start=start, stop=stop,
                                                   tile_position=tile_position),
                            reads, writes, tag)

    def transp(self, out, in_, iden, reads, writes, tag=""):
        t = self.nc.tensor
        return self.plan.op("pe", lambda: t.transpose(out, in_, iden), reads, writes, tag)

    def dve(self, fn, reads, writes, tag=""):
        return self.plan.op("dve", fn, reads, writes, tag)

    def act(self, fn, reads, writes, tag=""):
        return self.plan.op("act", fn, reads, writes, tag)

    def pool(self, fn, reads, writes, tag=""):
        return self.plan.op("pool", fn, reads, writes, tag)

    def bar(self):
        return self.plan.barrier()

    def lp(self, fn):
        def emit():
            with self.nc.allow_low_precision(reason="intentional bf16 storage"):
                return fn()
        return emit

    def phase_start(self):
        self._phase_mark = len(self._ctx)

    def phase_end(self):
        self.plan.barrier()
        for cm in reversed(self._ctx[self._phase_mark:]):
            cm.__exit__(None, None, None)
        del self._ctx[self._phase_mark:]
        self._consts = {}

    def cc(self, kind, op, groups, ins, outs, reads, writes, tag=""):
        nc = self.nc
        return self.plan.op("pool", lambda: nc.gpsimd.collective_compute(
            kind, op, replica_groups=groups, ins=ins, outs=outs), reads, writes, tag)

    def done(self):
        self.plan.finalize()
        self.close()
        return self.nc



def ceil_div(a, b):
    return (a + b - 1) // b


NT_FULL = 2048
HALO = 3


def token_tiles(nt, halo=HALO, tile=512):
    tiles = [(0, halo)] if halo else []
    off = halo
    while off < nt + halo:
        w = min(tile, nt + halo - off)
        tiles.append((off, w))
        off += w
    return tiles


def decl_l0pre(b, nt, fused):
    T = nt + HALO
    mid = "Internal" if fused else "ExternalOutput"
    return {
        "xT": b.dram("xT", [2048, T], F32, kind="ExternalInput"),
        "wqkvzT": b.dram("wqkvzT", [2048, 12288], F32, kind="ExternalInput"),
        "wbaT": b.dram("wbaT", [2048, 64], F32, kind="ExternalInput"),
        "convd": b.dram("convw_diag", [64 * 4 * 128, 128], F32, kind="ExternalInput"),
        "mixed": b.dram("mixed", [8192, nt], BF16, kind=mid),
        "zs": b.dram("zs", [4096, nt], BF16, kind=mid),
        "xn": b.dram("xn", [2048, nt], BF16, kind=mid),
        "bb_f": b.dram("bb_f", [32, nt], F32, kind=mid),
        "aa_f": b.dram("aa_f", [32, nt], F32, kind=mid),
    }


def emit_l0pre(b, H, nt):
    T = nt + HALO
    nc = b.nc
    xT, wqkvzT, wbaT, convd = H["xT"], H["wqkvzT"], H["wbaT"], H["convd"]
    mixed, zs, xn_hbm = H["mixed"], H["zs"], H["xn"]
    bb_f, aa_f = H["bb_f"], H["aa_f"]

    xn_sb = b.sbuf("xn_sb", [128, 16 * T], BF16)
    xb = [b.sbuf(f"xb{i}", [128, 16 * 512], BF16) for i in range(2)]
    sq = b.sbuf("sq", [128, 512], BF16)
    ones128 = b.sbuf("ones128", [128, 1], BF16)
    ones1 = b.sbuf("ones1", [1, 128], BF16)
    stats = b.sbuf("stats", [1, 6 * 512], F32)
    rr = b.sbuf("rr", [1, 512], BF16)
    rm = b.sbuf("rm", [1, 512], BF16)
    wt = [b.sbuf(f"wt{i}", [128, 16 * 128], BF16) for i in range(2)]
    pre = b.sbuf("pre", [128, T], BF16)
    cw = b.sbuf("cw", [128, 4 * 128], BF16)
    stg = b.sbuf("stg", [128, 512], BF16)
    bastg = b.sbuf("bastg", [64, 512], F32)
    ps = [b.psum(f"ps{i}", (128, 512), F32) for i in range(6)]

    b.pool(lambda: nc.gpsimd.memset(ones128[:, :], 1.0), [], ["ones128"])
    b.pool(lambda: nc.gpsimd.memset(ones1[:, :], 1.0), [], ["ones1"])

    tiles = token_tiles(nt)
    for ti, (off, w) in enumerate(tiles):
        xbuf = xb[ti % 2]
        kx = ("xb", ti % 2)
        b.dma("qpool", xbuf[:, 0:16 * w].rearrange("p (c t) -> p c t", c=16),
              xT.rearrange("(c p) t -> p c t", p=128)[:, :, off:off + w],
              reads=[], writes=[kx], tag=f"ldx{ti}")
        pm, pq = ps[0], ps[1]
        for c in range(16):
            xc = xbuf[:, c * w:(c + 1) * w]
            b.mm(pm[0:1, 0:w], ones128[:, :], xc, start=(c == 0), stop=(c == 15),
                 reads=[kx, "ones128"], writes=[("ps", 0)], tag=f"sum{ti}_{c}")
            eng = b.dve if c % 2 == 0 else b.pool
            eng((lambda xc=xc, w=w: nc.vector.tensor_mul(sq[:, 0:w], xc, xc)) if c % 2 == 0 else
                (lambda xc=xc, w=w: nc.gpsimd.tensor_mul(sq[:, 0:w], xc, xc)),
                reads=[kx], writes=["sq"], tag=f"sq{ti}_{c}")
            b.mm(pq[0:1, 0:w], ones128[:, :], sq[:, 0:w], start=(c == 0), stop=(c == 15),
                 reads=["sq", "ones128"], writes=[("ps", 1)], tag=f"sumq{ti}_{c}")
        mean = stats[:, 0:w]
        ex2 = stats[:, 512:512 + w]
        msq = stats[:, 1024:1024 + w]
        var = stats[:, 1536:1536 + w]
        sd = stats[:, 2048:2048 + w]
        b.dve(lambda w=w, o=mean: nc.vector.tensor_scalar(out=o, in0=ps[0][0:1, 0:w],
              scalar1=1.0 / 2048, scalar2=None, op0=ALU.mult), [("ps", 0)], [("st", "m")])
        b.dve(lambda w=w, o=ex2: nc.vector.tensor_scalar(out=o, in0=ps[1][0:1, 0:w],
              scalar1=1.0 / 2048, scalar2=None, op0=ALU.mult), [("ps", 1)], [("st", "e")])
        b.dve(lambda o=msq, a=mean: nc.vector.tensor_mul(o, a, a), [("st", "m")], [("st", "ms")])
        b.dve(lambda o=var, a=ex2, c2=msq: nc.vector.tensor_sub(o, a, c2),
              [("st", "e"), ("st", "ms")], [("st", "v")])
        epsc, epsk = b.constcol(1e-5, 1)
        b.act(lambda o=sd, a=var, e=epsc: nc.scalar.activation(out=o, in_=a, func=AF.Sqrt, bias=e[:, :]),
              [("st", "v"), epsk], [("st", "sd")])
        b.dve(b.lp(lambda w=w, a=sd: nc.vector.reciprocal(rr[:, 0:w], a)), [("st", "sd")], ["rr"])
        b.dve(b.lp(lambda w=w, a=mean: nc.vector.tensor_mul(rm[:, 0:w], a, rr[:, 0:w])),
              [("st", "m"), "rr"], ["rm"])
        pR, pM = ps[2], ps[3]
        b.mm(pR[:, 0:w], ones1[:, :], rr[:, 0:w], True, True, ["ones1", "rr"], [("ps", 2)])
        b.mm(pM[:, 0:w], ones1[:, :], rm[:, 0:w], True, True, ["ones1", "rm"], [("ps", 3)])
        for c in range(16):
            xc = xbuf[:, c * w:(c + 1) * w]
            xnc = xn_sb[:, c * T + off:c * T + off + w]
            b.dve(lambda xc=xc, w=w: nc.vector.tensor_mul(sq[:, 0:w], xc, pR[0:128, 0:w]),
                  [kx, ("ps", 2)], ["sq"], tag=f"xnm{ti}_{c}")
            b.dve(lambda xnc=xnc, w=w: nc.vector.tensor_sub(xnc, sq[:, 0:w], pM[0:128, 0:w]),
                  ["sq", ("ps", 3)], [("xn", c)], tag=f"xns{ti}_{c}")

    for c in range(16):
        b.dma("qsp", xn_hbm[c * 128:(c + 1) * 128, :], xn_sb[:, c * T + HALO:(c + 1) * T],
              reads=[("xn", c)], writes=[], tag=f"stxn{c}")

    for f in range(97):
        wbuf = wt[f % 2]
        kw = ("wt", f % 2)
        if f < 96:
            b.dma("qpool", wbuf[:, :].rearrange("p (c n) -> p c n", c=16),
                  wqkvzT[:, f * 128:(f + 1) * 128].rearrange("(c p) n -> p c n", p=128),
                  reads=[], writes=[kw], tag=f"ldw{f}")
        else:
            b.dma("qpool", wbuf[:, 0:16 * 64].rearrange("p (c n) -> p c n", c=16),
                  wbaT.rearrange("(c p) n -> p c n", p=128),
                  reads=[], writes=[kw], tag="ldwba")
        nout = 128 if f < 96 else 64
        if f < 64:
            b.dma("qpool", cw[:, :].rearrange("p (j n) -> p j n", j=4),
                  convd.rearrange("(c j p) n -> c p j n", j=4, p=128)[f],
                  reads=[], writes=["cw"], tag=f"ldcw{f}")
        for ti, (off, w) in enumerate(tiles):
            pb = ps[ti % 2]
            kp = ("ps", ti % 2)
            for c in range(16):
                b.mm(pb[0:nout, 0:w], wbuf[:, c * nout:(c + 1) * nout],
                     xn_sb[:, c * T + off:c * T + off + w],
                     start=(c == 0), stop=(c == 15),
                     reads=[kw, ("xn", c)], writes=[kp], tag=f"pj{f}_{ti}_{c}")
            if f < 64:
                b.dve(lambda pb=pb, off=off, w=w: nc.vector.tensor_copy(pre[:, off:off + w], pb[:, 0:w]),
                      [kp], [("pre", ti)], tag=f"pc{f}_{ti}")
            elif f < 96:
                if off >= HALO:
                    b.act(lambda pb=pb, w=w: nc.scalar.activation(out=stg[:, 0:w], in_=pb[:, 0:w],
                          func=AF.Silu), [kp], ["stg"], tag=f"zs{f}_{ti}")
                    b.dma("qsp", zs[(f - 64) * 128:(f - 63) * 128, off - HALO:off - HALO + w],
                          stg[:, 0:w], reads=["stg"], writes=[], tag=f"stz{f}_{ti}")
            else:
                if off >= HALO:
                    b.dve(lambda pb=pb, w=w: nc.vector.tensor_copy(bastg[:, 0:w], pb[0:64, 0:w]),
                          [kp], ["bastg"], tag=f"bac{ti}")
                    b.dma("qsp", bb_f[:, off - HALO:off - HALO + w], bastg[0:32, 0:w],
                          reads=["bastg"], writes=[("dram", "bb")], tag=f"stbb{ti}")
                    b.dma("qsp", aa_f[:, off - HALO:off - HALO + w], bastg[32:64, 0:w],
                          reads=["bastg"], writes=[("dram", "aa")], tag=f"stba{ti}")
        if f < 64:
            for ti, (off, w) in enumerate(tiles):
                if off < HALO:
                    continue
                pb = ps[2 + ti % 2]
                kp = ("ps", 2 + ti % 2)
                for j in range(4):
                    b.mm(pb[:, 0:w], cw[:, j * 128:(j + 1) * 128],
                         pre[:, off - 3 + j:off - 3 + j + w], start=(j == 0), stop=(j == 3),
                         reads=["cw"] + [("pre", t2) for t2 in range(len(tiles))],
                         writes=[kp], tag=f"cv{f}_{ti}_{j}")
                b.act(lambda pb=pb, w=w: nc.scalar.activation(out=stg[:, 0:w], in_=pb[:, 0:w],
                      func=AF.Silu), [kp], ["stg"], tag=f"cvs{f}_{ti}")
                b.dma("qsp", mixed[f * 128:(f + 1) * 128, off - HALO:off - HALO + w],
                      stg[:, 0:w], reads=["stg"], writes=[("dram", "mixed")], tag=f"stm{f}_{ti}")


def build_l0pre(nt=NT_FULL):
    b = B("l0pre")
    H = decl_l0pre(b, nt, fused=False)
    emit_l0pre(b, H, nt)
    return b


LN128H = float(-0.5 * np.log(128.0))


def decl_l0delta(b, nt, fused):
    H = {
        "alog": b.dram("alog", [32, 1], F32, kind="ExternalInput"),
        "dtb": b.dram("dtb", [32, 1], F32, kind="ExternalInput"),
        "nw": b.dram("nw", [128, 1], F32, kind="ExternalInput"),
        "eh": b.dram("eh", [2048, 32], F32, kind="ExternalInput"),
        "mnL": b.dram("mnL", [128, 128], F32, kind="ExternalInput"),
        "mnU": b.dram("mnU", [128, 128], F32, kind="ExternalInput"),
        "mnUI": b.dram("mnUI", [128, 128], F32, kind="ExternalInput"),
        "id32": b.dram("id32", [96, 32], F32, kind="ExternalInput"),
        "id128": b.dram("id128", [128, 128], F32, kind="ExternalInput"),
    }
    if fused:
        H["spill"] = b.dram("dspill", [nt // 128 * 32 * 128, 640], BF16)
        H["ccin"] = b.dram("ccin", [128, 4096], F32)
        H["ccout"] = b.dram("ccout", [256, 4096], F32)
        H["hg"] = b.dram("hg", [4096, nt], BF16, kind="Internal")
        H["sfin"] = b.dram("sfin", [128, 4096], F32, kind="Internal")
    else:
        H["sinit"] = b.dram("sinit", [128, 4096], F32, kind="ExternalInput"),
        H["sinit"] = H["sinit"][0]
        H["hg"] = b.dram("hg", [4096, nt], BF16, kind="ExternalOutput")
        H["sfin"] = b.dram("sfin", [128, 4096], F32, kind="ExternalOutput")
    return H


def emit_l0delta(b, H, nt, fused, groups=None):
    nch = nt // 128
    nc = b.nc
    mixed, zs_h = H["mixed"], H["zs"]
    bb_h, aa_h = H["bb_f"], H["aa_f"]
    alog, dtb, nw_h = H["alog"], H["dtb"], H["nw"]
    eh_h, mnL_h, mnU_h, mnUI_h = H["eh"], H["mnL"], H["mnU"], H["mnUI"]
    id32_h, id128_h = H["id32"], H["id128"]
    hg_h, sfin_h = H["hg"], H["sfin"]

    bb = b.sbuf("bb", [32, nt], F32)
    aa = b.sbuf("aa", [32, nt], F32)
    gv = b.sbuf("gv", [32, 128], F32)
    Gc = b.sbuf("Gcc", [32, 128], F32)
    lb = b.sbuf("lbc", [32, 128], F32)
    beta = b.sbuf("betac", [32, 128], F32)
    lrq = b.sbuf("lrqc", [32, 128], F32)
    lrk = b.sbuf("lrkc", [32, 128], F32)
    uu = b.sbuf("uuc", [32, 128], F32)
    uq = b.sbuf("uqc", [32, 128], F32)
    wneg = b.sbuf("wnegc", [32, 128], F32)
    ebK = b.sbuf("ebKc", [32, 128], F32)
    ekg2 = b.sbuf("ekg2c", [32, 128], F32)
    tmp32 = b.sbuf("tmp32", [32, 256], F32)
    tmp32b = b.sbuf("tmp32b", [32, 128], F32)
    eglc = b.sbuf("eglc", [32, 1], F32)
    eglflat = b.sbuf("eglflat", [1, 32], F32)
    zero32 = b.sbuf("zero32", [32, 128], F32)
    S = b.sbuf("S", [128, 4096], BF16)
    eh = b.sbuf("ehs", [128, 16 * 32], BF16)
    mnL = b.sbuf("mnLs", [128, 128], F32)
    mnU = b.sbuf("mnUs", [128, 128], F32)
    mnUI = b.sbuf("mnUIs", [128, 128], F32)
    id32 = b.sbuf("id32s", [96, 32], F32)
    id128 = b.sbuf("id128s", [128, 128], BF16)
    nw = b.sbuf("nws", [128, 1], F32)
    ones1 = b.sbuf("ones1", [1, 128], BF16)
    ones1f = b.sbuf("ones1f", [1, 128], F32)
    ones128 = b.sbuf("ones128", [128, 1], BF16)
    sqt = b.sbuf("sqt", [128, 128], BF16)
    s32 = b.sbuf("s32", [32, 128], F32)
    mfm = b.sbuf("mfm", [128, 8192], BF16)
    mtm = b.sbuf("mtm", [128, 8192], BF16)
    zt = b.sbuf("zt", [128, 4096], BF16)
    uflat = b.sbuf("uflat", [1, 4096], F32)
    uqflat = b.sbuf("uqflat", [1, 4096], F32)
    wnflat = b.sbuf("wnflat", [1, 4096], F32)
    eqgflat = b.sbuf("eqgflat", [1, 4096], F32)
    betaT = b.sbuf("betaT", [128, 32], F32)
    ebKT = b.sbuf("ebKT", [128, 32], F32)
    ekg2T = b.sbuf("ekg2T", [128, 32], F32)
    sG = b.sbuf("sG", [128, 32], F32)
    sGall = b.sbuf("sGall", [128, 32 * nch], F32) if fused else None
    hgstg = b.sbuf("hgstg", [128, 4096], BF16)
    kkTs = b.sbuf("kkTs", [128, 128], BF16)
    eA = b.sbuf("eA", [128, 128], BF16)
    eAT = b.sbuf("eAT", [128, 128], BF16)
    P = [b.sbuf(f"P{j}", [128, 128], BF16) for j in range(6)]
    Q = [b.sbuf(f"Q{j}", [128, 128], BF16) for j in range(7)]
    X = b.sbuf("X", [128, 256], BF16)
    vnew = b.sbuf("vnew", [128, 128], BF16)
    ofm = b.sbuf("ofm", [128, 128], BF16)
    osq = b.sbuf("osq", [128, 128], BF16)
    rstd = b.sbuf("rstd", [1, 128], BF16)
    lnv = b.sbuf("lnv", [1, 128], F32)
    t1s = b.sbuf("t1s", [128, 128], BF16)
    argt = b.sbuf("argt", [128, 128], F32)
    if fused:
        spillstg = b.sbuf("spillstg", [128, 640], BF16)
        ldg = [b.sbuf(f"ldg{i}", [128, 640], BF16) for i in range(2)]
        selc = b.sbuf("selc", [128, 1], F32)
        sldg = b.sbuf("sldg", [128, 1024], F32)
        kcf = spillstg[:, 0:128]
        attnT = spillstg[:, 128:256]
        eQT = b.sbuf("eQT", [128, 128], BF16)
        qeg = spillstg[:, 256:384]
        k2 = spillstg[:, 384:512]
    else:
        kcf = b.sbuf("kcf", [128, 128], BF16)[:, :]
        attnT = b.sbuf("attnT", [128, 128], BF16)[:, :]
        eQT = b.sbuf("eQT", [128, 128], BF16)
        qeg = b.sbuf("qeg", [128, 128], BF16)[:, :]
        k2 = b.sbuf("k2", [128, 128], BF16)[:, :]
    ps = [b.psum(f"ps{i}", (128, 512), F32) for i in range(6)]
    pbf = [b.psum(f"pbf{i}", (128, 1024), BF16) for i in range(2)]

    b.dma("qpool", bb[:, :], bb_h, [("dram", "bb")], ["bb"])
    b.dma("qpool", aa[:, :], aa_h, [("dram", "aa")], ["aa"])
    b.dma("qpool", eh[:, :].rearrange("p (c m) -> p c m", c=16),
          eh_h.rearrange("(c p) m -> p c m", p=128), [], ["eh"])
    b.dma("qpool", mnL[:, :], mnL_h, [], ["mnL"])
    b.dma("qpool", mnU[:, :], mnU_h, [], ["mnU"])
    b.dma("qpool", mnUI[:, :], mnUI_h, [], ["mnUI"])
    b.dma("qpool", id32[:, :], id32_h, [], ["id32"])
    b.dma("qpool", id128[:, :], id128_h, [], ["id128"])
    b.dma("qpool", nw[:, :], nw_h, [], ["nw"])
    if fused:
        b.pool(lambda: nc.gpsimd.memset(S[:, :], 0.0), [], [("S", h) for h in range(32)])
        b.dma("qpool", selc[:, :], H["sel"], [], ["selc"])
    else:
        b.dma("qpool", S[:, :], H["sinit"], [], [("S", h) for h in range(32)])
    alog_t = b.sbuf("alog_t", [32, 1], F32)
    dtb_t = b.sbuf("dtb_t", [32, 1], F32)
    negA = b.sbuf("negA", [32, 1], F32)
    b.dma("qpool", alog_t[:, :], alog, [], ["alog"])
    b.dma("qpool", dtb_t[:, :], dtb, [], ["dtb"])
    b.pool(lambda: nc.gpsimd.memset(ones1[:, :], 1.0), [], ["ones1"])
    b.pool(lambda: nc.gpsimd.memset(ones1f[:, :], 1.0), [], ["ones1f"])
    b.pool(lambda: nc.gpsimd.memset(ones128[:, :], 1.0), [], ["ones128"])
    b.pool(lambda: nc.gpsimd.memset(zero32[:, :], 0.0), [], ["zero32"])
    one_c, one_k = b.constcol(1.0, 32)
    eps6_c, eps6_k = b.constcol(1e-6, 32)
    eps6r_c, eps6r_k = b.constcol(1e-6, 1)
    b.act(lambda: nc.scalar.activation(out=negA[:, :], in_=alog_t[:, :], func=AF.Exp),
          ["alog"], ["negA0"])
    b.dve(lambda: nc.vector.tensor_scalar(out=negA[:, :], in0=negA[:, :], scalar1=-1.0,
          scalar2=None, op0=ALU.mult), ["negA0"], ["negA"])

    spillv = H["spill"].rearrange("(c h p) n -> c h p n", h=32, p=128) if fused else None

    def chain_vnew(I, kS, Ssl, Xv_t, Xv_k, kcf_t, kcf_k):
        b.mm(ps[0][:, 256:384], kcf_t, Ssl, True, True, kcf_k + [kS], [("ps", 0)], tag=f"vn1{I}")
        b.dve(b.lp(lambda Xv_t=Xv_t: nc.vector.scalar_tensor_tensor(out=vnew[:, :],
              in0=ps[0][:, 256:384], scalar=-1.0, in1=Xv_t, op0=ALU.mult, op1=ALU.add)),
              [("ps", 0)] + Xv_k, ["vnew"], tag=f"vn2{I}")

    def chain_update(I, kS, Ssl, k2_t, k2_k, sG_t, sG_col, sG_k):
        b.mm(ps[3][:, 256:384], k2_t, vnew[:, :], True, True, k2_k + ["vnew"],
             [("ps", 3)], tag=f"ds{I}")
        b.dve(b.lp(lambda Ssl=Ssl, sG_t=sG_t, sG_col=sG_col: nc.vector.scalar_tensor_tensor(
              out=Ssl, in0=Ssl, scalar=sG_t[:, sG_col:sG_col + 1], in1=ps[3][:, 256:384],
              op0=ALU.mult, op1=ALU.add)),
              [kS, sG_k, ("ps", 3)], [kS], tag=f"su{I}")

    def out_stage(I, h, kS, Ssl, qeg_t, qeg_k, attnT_t, attnT_k, zsl, zk):
        b.mm(ps[2][:, 256:384], Ssl, qeg_t, True, False, [kS] + qeg_k, [("ps", 2)], tag=f"o1{I}")
        b.mm(ps[2][:, 256:384], vnew[:, :], attnT_t, False, True, ["vnew"] + attnT_k,
             [("ps", 2)], tag=f"o2{I}")
        b.dve(b.lp(lambda: nc.vector.tensor_copy(ofm[:, :], ps[2][:, 256:384])),
              [("ps", 2)], ["ofm"], tag=f"oc{I}")
        b.pool(b.lp(lambda: nc.gpsimd.tensor_mul(osq[:, :], ofm[:, :], ofm[:, :])),
               ["ofm"], ["osq"], tag=f"os{I}")
        b.mm(ps[3][0:1, 128:256], ones128[:, :], osq[:, :], True, True,
             ["osq", "ones128"], [("ps", 3)], tag=f"rs{I}")
        b.act(lambda: nc.scalar.activation(out=lnv[:, :], in_=ps[3][0:1, 128:256],
              func=AF.Ln, scale=1.0 / 128, bias=eps6r_c[:, :]),
              [("ps", 3), eps6r_k], ["lnv"], tag=f"rl{I}")
        b.act(b.lp(lambda: nc.scalar.activation(out=rstd[:, :], in_=lnv[:, :],
              func=AF.Exp, scale=-0.5)), ["lnv"], ["rstd"], tag=f"re{I}")
        b.mm(ps[4][:, 128:256], ones1[:, :], rstd[:, :], True, True,
             ["ones1", "rstd"], [("ps", 4)], tag=f"rb{I}")
        b.dve(b.lp(lambda: nc.vector.tensor_mul(t1s[:, :], ofm[:, :], ps[4][:, 128:256])),
              ["ofm", ("ps", 4)], ["t1s"], tag=f"g1{I}")
        b.pool(b.lp(lambda zsl=zsl: nc.gpsimd.tensor_mul(t1s[:, :], t1s[:, :], zsl)),
               ["t1s", zk], ["t1s"], tag=f"g2{I}")
        b.dve(b.lp(lambda h=h: nc.vector.tensor_scalar(out=hgstg[:, h * 128:(h + 1) * 128],
              in0=t1s[:, :], scalar1=nw[:, :], scalar2=None, op0=ALU.mult)),
              ["t1s", "nw"], [("hgstg", h)], tag=f"g3{I}")

    for c in range(nch):
        sl = slice(c * 128, (c + 1) * 128)
        bsl, asl = bb[:, sl], aa[:, sl]
        b.dma("qpool", mfm[:, :].rearrange("p (k t) -> p k t", k=64),
              mixed.rearrange("(k p) t -> p k t", p=128)[:, :, sl], [("dram", "mixed")],
              ["mfm"], tag=f"mfm{c}")
        b.dma("qsp", mtm[:, :], mixed[:, sl], [("dram", "mixed")], ["mtm"], tag=f"mtm{c}",
              transpose=True)
        if not fused:
            b.dma("qact", zt[:, :].rearrange("p (k t) -> p k t", k=32),
                  zs_h.rearrange("(k p) t -> p k t", p=128)[:, :, sl], [("dram", "zs")],
                  ["zt"], tag=f"zt{c}")
        b.act(lambda asl=asl: nc.scalar.activation(out=tmp32[:, 0:128], in_=asl, func=AF.Abs,
              bias=dtb_t[:, :]), ["aa", "dtb"], ["t32"], tag=f"gp1{c}")
        b.act(lambda: nc.scalar.activation(out=tmp32[:, 0:128], in_=tmp32[:, 0:128], func=AF.Exp,
              scale=-1.0), ["t32"], ["t32"], tag=f"gp2{c}")
        b.act(lambda: nc.scalar.activation(out=tmp32[:, 0:128], in_=tmp32[:, 0:128], func=AF.Ln,
              bias=one_c[:, :]), ["t32", one_k], ["t32"], tag=f"gp3{c}")
        b.act(lambda asl=asl: nc.scalar.activation(out=tmp32b[:, :], in_=asl, func=AF.Relu,
              bias=dtb_t[:, :]), ["aa", "dtb"], ["t32b"], tag=f"gp4{c}")
        b.dve(lambda: nc.vector.tensor_add(tmp32[:, 0:128], tmp32[:, 0:128], tmp32b[:, :]),
              ["t32", "t32b"], ["t32"], tag=f"gp5{c}")
        b.dve(lambda: nc.vector.scalar_tensor_tensor(out=gv[:, :], in0=tmp32[:, 0:128],
              scalar=negA[:, :], in1=tmp32[:, 0:128], op0=ALU.mult, op1=ALU.bypass),
              ["t32", "negA"], ["gv"], tag=f"gp6{c}")
        b.act(lambda bsl=bsl: nc.scalar.activation(out=tmp32[:, 0:128], in_=bsl, func=AF.Abs),
              ["bb"], ["t32"], tag=f"lb1{c}")
        b.act(lambda: nc.scalar.activation(out=tmp32[:, 0:128], in_=tmp32[:, 0:128], func=AF.Exp,
              scale=-1.0), ["t32"], ["t32"], tag=f"lb2{c}")
        b.act(lambda: nc.scalar.activation(out=tmp32[:, 0:128], in_=tmp32[:, 0:128], func=AF.Ln,
              bias=one_c[:, :]), ["t32", one_k], ["t32"], tag=f"lb3{c}")
        b.act(lambda bsl=bsl: nc.scalar.activation(out=tmp32b[:, :], in_=bsl, func=AF.Relu,
              scale=-1.0), ["bb"], ["t32b"], tag=f"lb4{c}")
        b.dve(lambda: nc.vector.tensor_add(tmp32[:, 0:128], tmp32[:, 0:128], tmp32b[:, :]),
              ["t32", "t32b"], ["t32"], tag=f"lb5{c}")
        b.dve(lambda: nc.vector.tensor_scalar(out=lb[:, :], in0=tmp32[:, 0:128], scalar1=-1.0,
              scalar2=None, op0=ALU.mult), ["t32"], ["lb"], tag=f"lb6{c}")
        b.act(lambda bsl=bsl: nc.scalar.activation(out=beta[:, :], in_=bsl, func=AF.Sigmoid),
              ["bb"], ["beta"], tag=f"be{c}")
        b.dve(lambda: nc.vector.tensor_tensor_scan(out=Gc[:, :], data0=gv[:, :],
              data1=zero32[:, :], initial=0.0, op0=ALU.add, op1=ALU.add),
              ["gv", "zero32"], ["Gc"], tag=f"cums{c}")
        b.act(lambda: nc.scalar.activation(out=eglc[:, :], in_=Gc[:, 127:128], func=AF.Exp),
              ["Gc"], ["eglc"], tag=f"egl{c}")
        b.dma("qpool", eglflat[:, :], eglc[:, :], ["eglc"], ["eglflat"], tag=f"feg{c}")
        for which, base, lr in (("q", 0, lrq), ("k", 2048, lrk)):
            tag = "lq" if which == "q" else "lk"
            pq = ps[4]
            for hc in range(16):
                xc = mfm[:, base + hc * 128:base + (hc + 1) * 128]
                b.pool(lambda xc=xc: nc.gpsimd.tensor_mul(sqt[:, :], xc, xc),
                       ["mfm"], ["sqt"], tag=f"{tag}s{c}_{hc}")
                b.mm(pq[0:32, 0:128], eh[:, hc * 32:(hc + 1) * 32], sqt[:, :],
                     start=(hc == 0), stop=(hc == 15), reads=["sqt", "eh"], writes=[("ps", 4)],
                     tag=f"{tag}m{c}_{hc}")
            b.act(lambda pq=pq: nc.scalar.activation(out=s32[:, :], in_=pq[0:32, 0:128], func=AF.Sqrt),
                  [("ps", 4)], ["s32"], tag=f"{tag}sq{c}")
            b.act(lambda: nc.scalar.activation(out=s32[:, :], in_=s32[:, :], func=AF.Ln,
                  bias=eps6_c[:, :]), ["s32", eps6_k], ["s32"], tag=f"{tag}ln{c}")
            extra = LN128H if which == "q" else 0.0
            b.dve(lambda lr=lr, extra=extra: nc.vector.tensor_scalar(out=lr[:, :],
                  in0=s32[:, :], scalar1=-1.0, scalar2=extra, op0=ALU.mult, op1=ALU.add),
                  ["s32"], ["lrq" if which == "q" else "lrk"], tag=f"{tag}f{c}")
        b.dve(lambda: nc.vector.tensor_add(uu[:, :], Gc[:, :], lrk[:, :]),
              ["Gc", "lrk"], ["uu0"], tag=f"uu0{c}")
        b.dve(lambda: nc.vector.tensor_add(uu[:, :], uu[:, :], lb[:, :]),
              ["uu0", "lb"], ["uu"], tag=f"uu{c}")
        b.dve(lambda: nc.vector.tensor_add(uq[:, :], Gc[:, :], lrq[:, :]),
              ["Gc", "lrq"], ["uq"], tag=f"uq{c}")
        b.dve(lambda: nc.vector.tensor_sub(wneg[:, :], lrk[:, :], Gc[:, :]),
              ["Gc", "lrk"], ["wneg"], tag=f"wn{c}")
        b.act(lambda: nc.scalar.activation(out=ebK[:, :], in_=uu[:, :], func=AF.Exp),
              ["uu"], ["ebK"], tag=f"ebk{c}")
        b.dve(lambda: nc.vector.tensor_scalar(out=tmp32[:, 0:128], in0=Gc[:, :],
              scalar1=Gc[:, 127:128], scalar2=None, op0=ALU.subtract),
              ["Gc"], ["t32"], tag=f"ek1{c}")
        b.dve(lambda: nc.vector.tensor_sub(tmp32[:, 128:256], lrk[:, :], tmp32[:, 0:128]),
              ["lrk", "t32"], ["t32c"], tag=f"ek2{c}")
        b.act(lambda: nc.scalar.activation(out=ekg2[:, :], in_=tmp32[:, 128:256], func=AF.Exp),
              ["t32c"], ["ekg2"], tag=f"ek3{c}")
        b.dma("qpool", uflat[:, :], uu[:, :], ["uu"], ["uflat"], tag=f"fu{c}")
        b.dma("qpool", uqflat[:, :], uq[:, :], ["uq"], ["uqflat"], tag=f"fuq{c}")
        b.dma("qpool", wnflat[:, :], wneg[:, :], ["wneg"], ["wnflat"], tag=f"fw{c}")
        b.act(lambda: nc.scalar.activation(out=eqgflat[:, :], in_=uqflat[:, :], func=AF.Exp),
              ["uqflat"], ["eqgflat"], tag=f"eqg{c}")
        for srcv, dst, kk in ((beta, betaT, "betaT"), (ebK, ebKT, "ebKT"), (ekg2, ekg2T, "ekg2T")):
            rd = {"betaT": "beta", "ebKT": "ebK", "ekg2T": "ekg2"}[kk]
            b.transp(ps[5][0:128, 0:32], srcv[:, :], id32[0:32, :], [rd, "id32"],
                     [("ps", 5)], tag=f"tp{kk}{c}")
            b.dve(lambda dst=dst: nc.vector.tensor_copy(dst[:, :], ps[5][0:128, 0:32]),
                  [("ps", 5)], [kk], tag=f"tpc{kk}{c}")
        b.mm(ps[5][0:128, 32:64], ones1f[:, :], eglflat[0:1, :], True, True,
             ["ones1f", "eglflat"], [("ps", 5)], tag=f"eglb{c}")
        if fused:
            b.dve(lambda c=c: nc.vector.tensor_copy(sGall[:, c * 32:(c + 1) * 32],
                  ps[5][0:128, 32:64]), [("ps", 5)], [("sGall", c)], tag=f"eglc{c}")
        else:
            b.dve(lambda: nc.vector.tensor_copy(sG[:, :], ps[5][0:128, 32:64]),
                  [("ps", 5)], ["sG"], tag=f"eglc{c}")

        for h in range(32):
            hq = h // 2
            qfm = mfm[:, hq * 128:(hq + 1) * 128]
            kfm = mfm[:, 2048 + hq * 128:2048 + (hq + 1) * 128]
            ktm = mtm[:, 2048 + hq * 128:2048 + (hq + 1) * 128]
            vtm = mtm[:, 4096 + h * 128:4096 + (h + 1) * 128]
            urow = uflat[0:1, h * 128:(h + 1) * 128]
            uqrow = uqflat[0:1, h * 128:(h + 1) * 128]
            wnrow = wnflat[0:1, h * 128:(h + 1) * 128]
            eqrow = eqgflat[0:1, h * 128:(h + 1) * 128]
            Ssl = S[:, h * 128:(h + 1) * 128]
            kS = ("S", h)
            I = f"{c}_{h}"
            b.mm(ps[0][:, 0:128], kfm, kfm, True, True, ["mfm"], [("ps", 0)], tag=f"kkt{I}")
            b.dve(b.lp(lambda: nc.vector.tensor_copy(kkTs[:, :], ps[0][:, 0:128])),
                  [("ps", 0)], ["kkTs"], tag=f"kkc{I}")
            b.mm(ps[1][:, 0:128], urow, ones1f[:, :], True, False, ["uflat", "ones1f"],
                 [("ps", 1)], tag=f"aA1{I}")
            b.mm(ps[1][:, 0:128], ones1f[:, :], wnrow, False, True, ["wnflat", "ones1f"],
                 [("ps", 1)], tag=f"aA2{I}")
            b.dve(lambda: nc.vector.scalar_tensor_tensor(out=argt[:, :], in0=ps[1][:, 0:128],
                  scalar=60.0, in1=mnL[:, :], op0=ALU.min, op1=ALU.add),
                  [("ps", 1), "mnL"], ["argt"], tag=f"aA3{I}")
            b.act(b.lp(lambda: nc.scalar.activation(out=eA[:, :], in_=argt[:, :], func=AF.Exp)),
                  ["argt"], ["eA"], tag=f"aA4{I}")
            b.mm(ps[2][:, 0:128], wnrow, ones1f[:, :], True, False, ["wnflat", "ones1f"],
                 [("ps", 2)], tag=f"aT1{I}")
            b.mm(ps[2][:, 0:128], ones1f[:, :], urow, False, True, ["uflat", "ones1f"],
                 [("ps", 2)], tag=f"aT2{I}")
            b.dve(lambda: nc.vector.scalar_tensor_tensor(out=argt[:, :], in0=ps[2][:, 0:128],
                  scalar=60.0, in1=mnU[:, :], op0=ALU.min, op1=ALU.add),
                  [("ps", 2), "mnU"], ["argt"], tag=f"aT3{I}")
            b.act(b.lp(lambda: nc.scalar.activation(out=eAT[:, :], in_=argt[:, :], func=AF.Exp)),
                  ["argt"], ["eAT"], tag=f"aT4{I}")
            b.dve(b.lp(lambda: nc.vector.scalar_tensor_tensor(out=P[0][:, :], in0=kkTs[:, :],
                  scalar=-1.0, in1=eA[:, :], op0=ALU.mult, op1=ALU.mult)),
                  ["kkTs", "eA"], [("P", 0)], tag=f"P0{I}")
            b.dve(b.lp(lambda: nc.vector.scalar_tensor_tensor(out=Q[0][:, :], in0=kkTs[:, :],
                  scalar=-1.0, in1=eAT[:, :], op0=ALU.mult, op1=ALU.mult)),
                  ["kkTs", "eAT"], [("Q", 0)], tag=f"Q0{I}")
            b.mm(ps[0][:, 128:256], kfm, qfm, True, True, ["mfm"], [("ps", 0)], tag=f"qkt{I}")
            b.mm(ps[1][:, 128:256], wnrow, ones1f[:, :], True, False, ["wnflat", "ones1f"],
                 [("ps", 1)], tag=f"aQ1{I}")
            b.mm(ps[1][:, 128:256], ones1f[:, :], uqrow, False, True, ["uqflat", "ones1f"],
                 [("ps", 1)], tag=f"aQ2{I}")
            b.dve(lambda: nc.vector.scalar_tensor_tensor(out=argt[:, :], in0=ps[1][:, 128:256],
                  scalar=60.0, in1=mnUI[:, :], op0=ALU.min, op1=ALU.add),
                  [("ps", 1), "mnUI"], ["argt"], tag=f"aQ3{I}")
            b.act(b.lp(lambda: nc.scalar.activation(out=eQT[:, :], in_=argt[:, :], func=AF.Exp)),
                  ["argt"], ["eQT"], tag=f"aQ4{I}")
            b.dve(b.lp(lambda attnT=attnT: nc.vector.tensor_mul(attnT, eQT[:, :],
                  ps[0][:, 128:256])), ["eQT", ("ps", 0)], ["attnT"], tag=f"at{I}")
            b.pool(b.lp(lambda h=h, vtm=vtm: nc.gpsimd.tensor_scalar(out=X[:, 0:128], in0=vtm,
                   scalar1=betaT[:, h:h + 1], scalar2=None, op0=ALU.mult)),
                   ["mtm", "betaT"], [("X", "v")], tag=f"Xv{I}")
            b.pool(b.lp(lambda h=h, ktm=ktm: nc.gpsimd.tensor_scalar(out=X[:, 128:256], in0=ktm,
                   scalar1=ebKT[:, h:h + 1], scalar2=None, op0=ALU.mult)),
                   ["mtm", "ebKT"], [("X", "k")], tag=f"Xk{I}")
            for j in range(1, 7):
                if j <= 5:
                    b.mm(ps[3][:, 0:128], Q[j - 1][:, :], P[j - 1][:, :], True, True,
                         [("Q", j - 1), ("P", j - 1)], [("ps", 3)], tag=f"sqP{j}{I}")
                    b.dve(b.lp(lambda j=j: nc.vector.tensor_copy(P[j][:, :], ps[3][:, 0:128])),
                          [("ps", 3)], [("P", j)], tag=f"sqPc{j}{I}")
                b.mm(ps[4][:, 0:128], P[j - 1][:, :], Q[j - 1][:, :], True, True,
                     [("P", j - 1), ("Q", j - 1)], [("ps", 4)], tag=f"sqQ{j}{I}")
                b.act(b.lp(lambda j=j: nc.scalar.activation(out=Q[j][:, :], in_=ps[4][:, 0:128],
                      func=AF.Copy)), [("ps", 4)], [("Q", j)], tag=f"sqQc{j}{I}")
            for j in range(6, -1, -1):
                b.mm(ps[5][:, 0:256], Q[j][:, :], X[:, :], True, True,
                     [("Q", j), ("X", "v"), ("X", "k")], [("ps", 5)], tag=f"ap{j}{I}")
                b.dve(b.lp(lambda: nc.vector.scalar_tensor_tensor(out=X[:, :], in0=ps[5][:, 0:256],
                      scalar=1.0, in1=X[:, :], op0=ALU.mult, op1=ALU.add)),
                      [("ps", 5), ("X", "v"), ("X", "k")], [("X", "v"), ("X", "k")],
                      tag=f"apc{j}{I}")
            b.transp(pbf[0][:, 0:128], X[:, 128:256], id128[:, :],
                     [("X", "k"), "id128"], [("pbf", 0)], tag=f"kct{I}")
            b.dve(b.lp(lambda kcf=kcf: nc.vector.tensor_copy(kcf, pbf[0][:, 0:128])),
                  [("pbf", 0)], ["kcf"], tag=f"kcc{I}")
            b.mm(ps[1][:, 256:384], ones1f[:, :], eqrow, True, True, ["eqgflat", "ones1f"],
                 [("ps", 1)], tag=f"qe1{I}")
            b.dve(b.lp(lambda qfm=qfm, qeg=qeg: nc.vector.tensor_mul(qeg, qfm,
                  ps[1][:, 256:384])), ["mfm", ("ps", 1)], ["qeg"], tag=f"qe2{I}")
            b.pool(b.lp(lambda h=h, ktm=ktm, k2=k2: nc.gpsimd.tensor_scalar(out=k2, in0=ktm,
                   scalar1=ekg2T[:, h:h + 1], scalar2=None, op0=ALU.mult)),
                   ["mtm", "ekg2T"], ["k2"], tag=f"k2{I}")
            if fused:
                b.pool(b.lp(lambda: nc.gpsimd.tensor_copy(spillstg[:, 512:640], X[:, 0:128])),
                       [("X", "v")], ["spXv"], tag=f"spx{I}")
                b.dma("qact", spillv[c, h], spillstg[:, :],
                      ["kcf", "attnT", "qeg", "k2", "spXv"], [("dsp", c, h)], tag=f"spd{I}")
                chain_vnew(I, kS, Ssl, X[:, 0:128], [("X", "v")], kcf, ["kcf"])
                chain_update(I, kS, Ssl, k2, ["k2"], sGall, c * 32 + h, ("sGall", c))
            else:
                chain_vnew(I, kS, Ssl, X[:, 0:128], [("X", "v")], kcf, ["kcf"])
                zsl = zt[:, h * 128:(h + 1) * 128]
                out_stage(I, h, kS, Ssl, qeg, ["qeg"], attnT, ["attnT"], zsl, "zt")
                chain_update(I, kS, Ssl, k2, ["k2"], sG, h, "sG")
        if not fused:
            b.dma("qsp", hg_h.rearrange("(h p) t -> p h t", p=128)[:, :, sl],
                  hgstg[:, :].rearrange("p (h t) -> p h t", h=32),
                  [("hgstg", h2) for h2 in range(32)], [("dram", "hg")], tag=f"sthg{c}")
    b.dma("qpool", H["ccin"] if fused else sfin_h, S[:, :],
          [("S", h2) for h2 in range(32)],
          [("dram", "ccin") if fused else ("dram", "sfin")], tag="stS")

    if fused:
        b.cc("AllGather", ALU.bypass, groups, [H["ccin"].opt()], [H["ccout"].opt()],
             [("dram", "ccin")], [("dram", "ccout")], tag="ccS0")
        for q4 in range(4):
            b.dma("qpool", sldg[:, :], H["ccout"][0:128, q4 * 1024:(q4 + 1) * 1024],
                  [("dram", "ccout")], ["sldg"], tag=f"sgl{q4}")
            b.dve(b.lp(lambda q4=q4: nc.vector.tensor_scalar(
                  out=S[:, q4 * 1024:(q4 + 1) * 1024], in0=sldg[:, :], scalar1=selc[:, :],
                  scalar2=None, op0=ALU.mult)), ["sldg", "selc"],
                  [("S", hh) for hh in range(q4 * 8, (q4 + 1) * 8)], tag=f"sgm{q4}")
        for c in range(nch):
            sl = slice(c * 128, (c + 1) * 128)
            b.dma("qact", zt[:, :].rearrange("p (k t) -> p k t", k=32),
                  zs_h.rearrange("(k p) t -> p k t", p=128)[:, :, sl], [("dram", "zs")],
                  ["zt"], tag=f"Bzt{c}")
            for h in range(32):
                I = f"B{c}_{h}"
                lt = ldg[h % 2]
                klt = ("ldg", h % 2)
                b.dma("qpool", lt[:, :], spillv[c, h], [("dsp", c, h)], [klt], tag=f"ld{I}")
                Ssl = S[:, h * 128:(h + 1) * 128]
                kS = ("S", h)
                chain_vnew(I, kS, Ssl, lt[:, 512:640], [klt], lt[:, 0:128], [klt])
                zsl = zt[:, h * 128:(h + 1) * 128]
                out_stage(I, h, kS, Ssl, lt[:, 256:384], [klt], lt[:, 128:256], [klt],
                          zsl, "zt")
                chain_update(I, kS, Ssl, lt[:, 384:512], [klt], sGall, c * 32 + h, ("sGall", c))
            b.dma("qsp", hg_h.rearrange("(h p) t -> p h t", p=128)[:, :, sl],
                  hgstg[:, :].rearrange("p (h t) -> p h t", h=32),
                  [("hgstg", h2) for h2 in range(32)], [("dram", "hg")], tag=f"Bsthg{c}")


def build_l0delta(nt=NT_FULL, debug_taps=False):
    b = B("l0delta")
    H = {
        "mixed": b.dram("mixed", [8192, nt], BF16, kind="ExternalInput"),
        "zs": b.dram("zs", [4096, nt], BF16, kind="ExternalInput"),
        "bb_f": b.dram("bb_f", [32, nt], F32, kind="ExternalInput"),
        "aa_f": b.dram("aa_f", [32, nt], F32, kind="ExternalInput"),
    }
    H.update(decl_l0delta(b, nt, fused=False))
    emit_l0delta(b, H, nt, fused=False)
    return b


def stage_l0delta(core, arts, inputs, nt=NT_FULL, sinit=None):
    consts = {}
    eh = np.zeros((16, 128, 32), np.float32)
    for hc in range(16):
        eh[hc, :, 2 * hc] = 1.0
        eh[hc, :, 2 * hc + 1] = 1.0
    consts["eh"] = eh.reshape(2048, 32)
    idx = np.arange(128)
    mnL = np.where(idx[:, None] > idx[None, :], 0.0, -1e9).astype(np.float32)
    mnU = np.where(idx[:, None] < idx[None, :], 0.0, -1e9).astype(np.float32)
    mnUI = np.where(idx[:, None] <= idx[None, :], 0.0, -1e9).astype(np.float32)
    consts["mnL"], consts["mnU"], consts["mnUI"] = mnL, mnU, mnUI
    consts["id32"] = np.concatenate([np.eye(32, dtype=np.float32)] * 3, 0)
    consts["id128"] = np.eye(128, dtype=np.float32)
    if sinit is None:
        sinit = np.zeros((128, 4096), np.float32)
    return {
        "mixed": arts["mixed"], "zs": arts["zs"],
        "bb_f": np.asarray(arts["bb_f"]), "aa_f": np.asarray(arts["aa_f"]),
        "alog": inputs["L0_attn_A_log"].reshape(32, 1).astype(np.float32),
        "dtb": inputs["L0_attn_dt_bias"].reshape(32, 1).astype(np.float32),
        "nw": inputs["L0_attn_norm_weight"].reshape(128, 1).astype(np.float32),
        "sinit": sinit, **consts,
    }


def decl_prep(b, fused):
    H = {
        "wgu_s": b.dram("wgu_s", [2048, 2048], F32, kind="ExternalInput"),
        "wp_s": b.dram("wp_s", [256, 8192], F32, kind="ExternalInput"),
        "wup_s": b.dram("wup_s", [1024, 2048], F32, kind="ExternalInput"),
        "wgt_s": b.dram("wgt_s", [1024, 2048], F32, kind="ExternalInput"),
        "wdn_s": b.dram("wdn_s", [256, 8192], F32, kind="ExternalInput"),
        "su_ug4": b.dram("su_ug4", [64, 32], F32, kind="ExternalInput"),
        "su_dp2": b.dram("su_dp2", [128, 64], F32, kind="ExternalInput"),
        "H128": b.dram("H128", [128, 128], F32, kind="ExternalInput"),
        "HK16I4": b.dram("HK16I4", [64, 64], F32, kind="ExternalInput"),
        "HK64I2": b.dram("HK64I2", [128, 128], F32, kind="ExternalInput"),
        "q_gu": b.dram("q_gu", [2048, 2048], BF16),
        "q_p": b.dram("q_p", [256, 8192], BF16),
        "tq_gu": b.dram("tq_gu", [2048, 2048], BF16),
        "tq_p": b.dram("tq_p", [8192, 256], BF16),
        "fup_sl": b.dram("fup_sl", [2048, 1024], BF16),
        "fgt_sl": b.dram("fgt_sl", [2048, 1024], BF16),
        "fdn_sl": b.dram("fdn_sl", [8192, 256], BF16),
        "wub": b.dram("wub", [1024, 2048], BF16),
        "wgb": b.dram("wgb", [1024, 2048], BF16),
        "wdb": b.dram("wdb", [256, 8192], BF16),
        "A1u": b.dram("A1u", [2048, 1024], BF16),
        "A1g": b.dram("A1g", [2048, 1024], BF16),
        "A1d": b.dram("A1d", [8192, 256], BF16),
        "wq_gu_t": b.dram("wq_gu_t", [8 * 2048, 2048], BF16),
        "wq_p_t": b.dram("wq_p_t", [8 * 8192, 256], BF16),
        "fup": b.dram("fup", [8 * 2048, 1024], BF16),
        "fgt": b.dram("fgt", [8 * 2048, 1024], BF16),
        "fdn": b.dram("fdn", [8 * 8192, 256], BF16),
    }
    return H


GROUPS8 = [[0, 1, 2, 3, 4, 5, 6, 7]]


def prep_gate(b):
    scr = b.sbuf(f"gate{b.uid()}", [1, 1], F32)
    sem, nval = b.prep_ccsem, b.prep_ccn

    def emit():
        b.nc.gpsimd.wait_ge(sem, nval)
        return b.nc.gpsimd.memset(scr[:, :], 0.0)
    b.plan.op("pool", emit, [], ["prepgate"], tag="gate")


def emit_prep(b, H, fused=False):
    nc = b.nc
    QW = 4096
    qb = [dict(wt=b.sbuf(f"p_wt{p}", [128, QW], BF16),
               grp=b.sbuf(f"p_grp{p}", [128, 64], F32),
               scb=b.sbuf(f"p_scb{p}", [128, 64], BF16),
               rsc=b.sbuf(f"p_rsc{p}", [128, 64], F32),
               rat=b.sbuf(f"p_rat{p}", [128, QW], BF16),
               tsg=b.sbuf(f"p_tsg{p}", [128, QW], BF16),
               tgt=b.sbuf(f"p_tgt{p}", [128, QW], BF16),
               what=b.sbuf(f"p_what{p}", [128, QW], BF16)) for p in range(2)]
    H128 = b.sbuf("p_H128", [128, 128], BF16)
    HK16I4 = b.sbuf("p_HK16I4", [64, 64], BF16)
    HK64I2 = b.sbuf("p_HK64I2", [128, 128], BF16)
    su4 = b.sbuf("p_su4", [64, 32], F32)
    su2 = b.sbuf("p_su2", [128, 64], F32)
    tTs = [b.sbuf(f"p_tT{p}", [128, 2048], BF16) for p in range(2)]
    tSs = [b.sbuf(f"p_tS{p}", [128, 1024], BF16) for p in range(2)]
    pcb = [b.sbuf(f"p_pc{p}", [128, 8192], BF16) for p in range(2)]
    t2s = [b.sbuf(f"p_t2{p}", [128, 1024], BF16) for p in range(2)]
    stg1 = [b.sbuf(f"p_st1{p}", [128, 1024], BF16) for p in range(2)]
    stg2 = [b.sbuf(f"p_st2{p}", [128, 1024], BF16) for p in range(2)]
    ps = [b.psum(f"p_ps{i}", (128, 512), F32) for i in range(4)]
    sk = {}

    b.dma("qpool", H128[:, :], H["H128"], [], ["H128"])
    b.dma("qpool", HK16I4[:, :], H["HK16I4"], [], ["HK16I4"])
    b.dma("qpool", HK64I2[:, :], H["HK64I2"], [], ["HK64I2"])
    b.dma("qpool", su4[:, :], H["su_ug4"], [], ["su4"])
    b.dma("qpool", su2[:, :], H["su_dp2"], [], ["su2"])

    def bc3(x3, s3):
        return bass.broadcast_tensor_aps(x3, s3)

    qu = [0]

    def quant_unit(srcap, dstap, C, nm):
        u = qu[0]
        qu[0] += 1
        t = qb[u % 2]
        G = C // 64
        kw, kg, ks, ki = ("wt", u % 2), ("grp", u % 2), ("scb", u % 2), ("rsc", u % 2)
        kr, ksg, kt = ("rat", u % 2), ("tsg", u % 2), ("tgt", u % 2)
        kh = ("what", u % 2)
        wt, grp, scb, rsc = t["wt"], t["grp"], t["scb"], t["rsc"]
        rat, tsg, tgt, what = t["rat"], t["tsg"], t["tgt"], t["what"]
        b.dma("qpool", wt[:, 0:C], srcap, [], [kw], tag=f"qw{nm}{u}")
        b.dve(lambda C=C, G=G, wt=wt, grp=grp: nc.vector.tensor_reduce(
            out=grp[:, 0:G].rearrange("p (g e) -> p g e", e=1),
            in_=wt[:, 0:C].rearrange("p (g e) -> p g e", e=64), axis=AX.X, op=ALU.add,
            apply_absolute_value=True), [kw], [kg], tag=f"qr{nm}{u}")
        b.dve(b.lp(lambda G=G, grp=grp, scb=scb: nc.vector.tensor_scalar(
              out=scb[:, 0:G], in0=grp[:, 0:G],
              scalar1=1.0 / 64, scalar2=1e-8, op0=ALU.mult, op1=ALU.max)),
              [kg], [ks], tag=f"qs{nm}{u}")
        b.dve(lambda G=G, rsc=rsc, scb=scb: nc.vector.reciprocal(rsc[:, 0:G], scb[:, 0:G]),
              [ks], [ki], tag=f"qi{nm}{u}")
        b.dve(b.lp(lambda C=C, G=G, wt=wt, rsc=rsc, rat=rat: nc.vector.tensor_tensor(
              out=rat[:, 0:C].rearrange("p (g e) -> p g e", e=64),
              in0=bc3(wt[:, 0:C].rearrange("p (g e) -> p g e", e=64),
                      rsc[:, 0:G].rearrange("p (g e) -> p g e", e=1))[0],
              in1=bc3(wt[:, 0:C].rearrange("p (g e) -> p g e", e=64),
                      rsc[:, 0:G].rearrange("p (g e) -> p g e", e=1))[1],
              op=ALU.mult)), [kw, ki], [kr], tag=f"qd{nm}{u}")
        b.act(b.lp(lambda C=C, rat=rat, tsg=tsg: nc.scalar.activation(
              out=tsg[:, 0:C], in_=rat[:, 0:C], func=AF.Sign)), [kr], [ksg], tag=f"qg{nm}{u}")
        b.act(b.lp(lambda C=C, rat=rat, tgt=tgt: nc.scalar.activation(
              out=tgt[:, 0:C], in_=rat[:, 0:C], func=AF.Abs)), [kr], [kt], tag=f"qt0{nm}{u}")
        b.dve(b.lp(lambda C=C, tgt=tgt: nc.vector.tensor_scalar(out=tgt[:, 0:C], in0=tgt[:, 0:C],
              scalar1=0.5, scalar2=None, op0=ALU.is_gt)), [kt], [kt], tag=f"qt{nm}{u}")
        b.pool(b.lp(lambda C=C, tgt=tgt, tsg=tsg: nc.gpsimd.tensor_mul(
              tgt[:, 0:C], tgt[:, 0:C], tsg[:, 0:C])), [kt, ksg], [kt], tag=f"qq{nm}{u}")
        b.dve(b.lp(lambda C=C, G=G, tgt=tgt, scb=scb, what=what: nc.vector.tensor_tensor(
              out=what[:, 0:C].rearrange("p (g e) -> p g e", e=64),
              in0=bc3(tgt[:, 0:C].rearrange("p (g e) -> p g e", e=64),
                      scb[:, 0:G].rearrange("p (g e) -> p g e", e=1))[0],
              in1=bc3(tgt[:, 0:C].rearrange("p (g e) -> p g e", e=64),
                      scb[:, 0:G].rearrange("p (g e) -> p g e", e=1))[1],
              op=ALU.mult)), [kt, ks], [kh], tag=f"qh{nm}{u}")
        b.pool(b.lp(lambda C=C, what=what, wt=wt: nc.gpsimd.tensor_sub(
              what[:, 0:C], what[:, 0:C], wt[:, 0:C])), [kh, kw], [kh], tag=f"qd2{nm}{u}")
        b.pool(b.lp(lambda C=C, what=what, wt=wt: nc.gpsimd.tensor_add(
              what[:, 0:C], wt[:, 0:C], what[:, 0:C])), [kh, kw], [kh], tag=f"qa{nm}{u}")
        skey = ("dram", nm, u)
        sk.setdefault(nm, []).append(skey)
        b.dma("qsp" if u % 2 == 0 else "qact", dstap, what[:, 0:C], [kh], [skey],
              tag=f"qo{nm}{u}")

    for i, r0 in enumerate(range(0, 2048, 128)):
        quant_unit(H["wgu_s"][r0:r0 + 128, :], H["q_gu"][r0:r0 + 128, :], 2048, "qgu")
    for r0 in (0, 128):
        for ch in range(2):
            quant_unit(H["wp_s"][r0:r0 + 128, ch * 4096:(ch + 1) * 4096],
                       H["q_p"][r0:r0 + 128, ch * 4096:(ch + 1) * 4096], 4096, "qp")

    def tpass(srcq, dstq, R, C, nm, rk_all):
        for kc in range(C // 128):
            tb = tTs[kc % 2]
            kt = ("tT", kc % 2)
            b.dma("qsp", tb[:, 0:R], srcq[:, kc * 128:(kc + 1) * 128], rk_all, [kt],
                  tag=f"tp{nm}{kc}", transpose=True)
            skey = ("tq", nm, kc)
            sk.setdefault("t" + nm, []).append(skey)
            b.dma("qact", dstq[kc * 128:(kc + 1) * 128, 0:R], tb[:, 0:R], [kt], [skey],
                  tag=f"tpo{nm}{kc}")
    tpass(H["q_gu"], H["tq_gu"], 2048, 2048, "gu", sk["qgu"])
    tpass(H["q_p"], H["tq_p"], 256, 8192, "p", sk["qp"])

    def precast(src, dst, R, C, nm):
        for i, r0 in enumerate(range(0, R, 128)):
            kb = ("pcb", i % 2)
            b.dma("qpool", pcb[i % 2][:, 0:C], src[r0:r0 + 128, :], [], [kb], tag=f"pc{nm}{r0}")
            skey = ("pcast", nm, i)
            sk.setdefault("pc" + nm, []).append(skey)
            b.dma("qsp" if i % 2 == 0 else "qact", dst[r0:r0 + 128, :], pcb[i % 2][:, 0:C],
                  [kb], [skey], tag=f"pco{nm}{r0}")
    precast(H["wup_s"], H["wub"], 1024, 2048, "u")
    precast(H["wgt_s"], H["wgb"], 1024, 2048, "g")
    precast(H["wdn_s"], H["wdb"], 256, 8192, "d")

    def stage1(srcb, a1, R, nchunks, nm):
        a1v = a1.rearrange("(yl dh) r -> dh yl r", dh=nchunks)
        for dc in range(nchunks):
            tb = tSs[dc % 2]
            kt = ("tS", dc % 2)
            b.dma("qsp", tb[:, 0:R], srcb[:, dc * 128:(dc + 1) * 128], sk["pc" + nm], [kt],
                  tag=f"s1t{nm}{dc}", transpose=True)
            sb = stg1[dc % 2]
            kst = ("stg1", dc % 2)
            for half in range(ceil_div(R, 512)):
                w = min(512, R - half * 512)
                b.mm(ps[half % 2][:, 0:w], H128[:, :], tb[:, half * 512:half * 512 + w],
                     True, True, ["H128", kt], [("ps", half % 2)], tag=f"s1m{nm}{dc}_{half}")
                b.act(b.lp(lambda w=w, half=half, sb=sb: nc.scalar.activation(
                      out=sb[:, half * 512:half * 512 + w], in_=ps[half % 2][:, 0:w],
                      func=AF.Copy)), [("ps", half % 2)], [kst], tag=f"s1e{nm}{dc}_{half}")
            skey = ("a1", nm, dc)
            sk.setdefault("a1" + nm, []).append(skey)
            b.dma("qact", a1v[dc], sb[:, 0:R], [kst], [skey], tag=f"s1o{nm}{dc}")
    stage1(H["wub"], H["A1u"], 1024, 16, "u")
    stage1(H["wgb"], H["A1g"], 1024, 16, "g")
    stage1(H["wdb"], H["A1d"], 256, 64, "d")

    def stage2_ug(a1, out, nm, a1k):
        R = 1024
        outv = out.rearrange("(yh yg j) r -> j yg yh r", yg=32, j=4)
        for ylg in range(32):
            t2 = t2s[ylg % 2]
            k2 = ("t2", ylg % 2)
            b.dma("qpool", t2[0:64, 0:R], a1[ylg * 64:(ylg + 1) * 64, :],
                  sk["a1" + a1k], [k2], tag=f"s2l{nm}{ylg}")
            sb = stg2[ylg % 2]
            kst = ("stg2", ylg % 2)
            for half in range(ceil_div(R, 512)):
                w = min(512, R - half * 512)
                b.mm(ps[2 + half % 2][0:64, 0:w], HK16I4[:, :],
                     t2[0:64, half * 512:half * 512 + w], True, True,
                     ["HK16I4", k2], [("ps", 2 + half % 2)], tag=f"s2m{nm}{ylg}_{half}")
                b.dve(b.lp(lambda w=w, half=half, ylg=ylg, sb=sb: nc.vector.tensor_scalar(
                      out=sb[0:64, half * 512:half * 512 + w],
                      in0=ps[2 + half % 2][0:64, 0:w],
                      scalar1=su4[:, ylg:ylg + 1], scalar2=None, op0=ALU.mult)),
                      [("ps", 2 + half % 2), "su4"], [kst], tag=f"s2e{nm}{ylg}_{half}")
            for j in range(4):
                skey = ("f", nm, ylg, j)
                sk.setdefault(nm, []).append(skey)
                b.dma("qsp" if ylg % 2 == 0 else "qact", outv[j, ylg],
                      sb[j * 16:(j + 1) * 16, 0:R], [kst], [skey], tag=f"s2o{nm}{ylg}_{j}")
    stage2_ug(H["A1u"], H["fup_sl"], "fu", "u")
    stage2_ug(H["A1g"], H["fgt_sl"], "fg", "g")

    def stage2_dn(a1, out, nm):
        R = 256
        outv = out.rearrange("(yh yg j) r -> j yg yh r", yg=64, j=2)
        for ylp in range(64):
            t2 = t2s[ylp % 2]
            k2 = ("t2", ylp % 2)
            b.dma("qpool", t2[:, 0:R], a1[ylp * 128:(ylp + 1) * 128, :],
                  sk["a1d"], [k2], tag=f"s2l{nm}{ylp}")
            sb = stg2[ylp % 2]
            kst = ("stg2", ylp % 2)
            b.mm(ps[2 + ylp % 2][:, 0:R], HK64I2[:, :], t2[:, 0:R], True, True,
                 ["HK64I2", k2], [("ps", 2 + ylp % 2)], tag=f"s2m{nm}{ylp}")
            b.dve(b.lp(lambda ylp=ylp, sb=sb: nc.vector.tensor_scalar(
                  out=sb[:, 0:R], in0=ps[2 + ylp % 2][:, 0:R],
                  scalar1=su2[:, ylp:ylp + 1], scalar2=None, op0=ALU.mult)),
                  [("ps", 2 + ylp % 2), "su2"], [kst], tag=f"s2e{nm}{ylp}")
            for j in range(2):
                skey = ("f", nm, ylp, j)
                sk.setdefault(nm, []).append(skey)
                b.dma("qsp" if ylp % 2 == 0 else "qact", outv[j, ylp],
                      sb[j * 64:(j + 1) * 64, 0:R], [kst], [skey], tag=f"s2o{nm}{ylp}_{j}")
    stage2_dn(H["A1d"], H["fdn_sl"], "fd")

    if fused:
        ccsem = nc.alloc_semaphore("sem_prepcc")
        b.prep_ccsem, b.prep_ccn = ccsem, 5

        def gather(in_h, out_h, rkeys, tag):
            def emit(in_h=in_h, out_h=out_h):
                instr = nc.gpsimd.collective_compute(
                    "AllGather", ALU.bypass, replica_groups=GROUPS8,
                    ins=[in_h.opt()], outs=[out_h.opt()])
                instr.then_inc(ccsem, 1)
                return instr
            b.plan.op("pool", emit, rkeys, [("ccg", tag)], tag=f"ccp{tag}")
        gather(H["tq_gu"], H["wq_gu_t"], sk["tgu"], "gu")
        gather(H["tq_p"], H["wq_p_t"], sk["tp"], "p")
        gather(H["fup_sl"], H["fup"], sk["fu"], "fu")
        gather(H["fgt_sl"], H["fgt"], sk["fg"], "fg")
        gather(H["fdn_sl"], H["fdn"], sk["fd"], "fd")


def build_prep():
    b = B("prep")
    H = decl_prep(b, fused=False)
    emit_prep(b, H)
    return b


def hadamard_mat(n):
    H = np.array([[1.0]], np.float32)
    while H.shape[0] < n:
        H = np.block([[H, H], [H, -H]]).astype(np.float32)
    return H


def stage_prep(core, inputs):
    su_ug = inputs["L1_ffn_SU_ug"].astype(np.float32)
    su_dp = inputs["L1_ffn_SU_dp"].astype(np.float32)
    s4 = su_ug.reshape(16, 32, 4)
    su_ug4 = np.ascontiguousarray(s4.transpose(2, 0, 1).reshape(64, 32))
    s2 = su_dp.reshape(64, 64, 2)
    su_dp2 = np.ascontiguousarray(s2.transpose(2, 0, 1).reshape(128, 64))
    H16 = hadamard_mat(16)
    H64 = hadamard_mat(64)
    HK16I4 = np.kron(np.eye(4, dtype=np.float32), H16).astype(np.float32)
    HK64I2 = np.kron(np.eye(2, dtype=np.float32), H64).astype(np.float32)
    c = core
    return {
        "wgu_s": np.ascontiguousarray(inputs["L0_ffn_w_gate_up"][c * 2048:(c + 1) * 2048]),
        "wp_s": np.ascontiguousarray(inputs["L0_ffn_w_proj"][c * 256:(c + 1) * 256]),
        "wup_s": np.ascontiguousarray(inputs["L1_ffn_W_up"][c * 1024:(c + 1) * 1024]),
        "wgt_s": np.ascontiguousarray(inputs["L1_ffn_W_gate"][c * 1024:(c + 1) * 1024]),
        "wdn_s": np.ascontiguousarray(inputs["L1_ffn_W_down"][c * 256:(c + 1) * 256]),
        "su_ug4": su_ug4, "su_dp2": su_dp2,
        "H128": hadamard_mat(128), "HK16I4": HK16I4, "HK64I2": HK64I2,
    }


def assemble_prep(res):
    r = res[0]
    return {
        "wq_gu_t": np.asarray(r["wq_gu_t"]), "wq_p_t": np.asarray(r["wq_p_t"]),
        "wup_t": np.asarray(r["fup"]), "wgt_t": np.asarray(r["fgt"]),
        "wdn_t": np.asarray(r["fdn"]),
    }


EPSF32 = 1.1920929e-7


def decl_combine0(b, nt, fused):
    mid = "Internal" if fused else "ExternalOutput"
    H = {
        "x0T": b.dram("x0T", [2048, nt], F32, kind="ExternalInput"),
        "woutT": b.dram("woutT", [4096, 2048], F32, kind="ExternalInput"),
        "x1T": b.dram("x1T", [2048, nt], F32, kind=mid),
        "xn1": b.dram("xn1", [2048, nt], BF16, kind=mid),
    }
    if not fused:
        H["hg"] = b.dram("hg", [4096, nt], BF16, kind="ExternalInput")
        H["xn"] = b.dram("xn", [2048, nt], BF16, kind="ExternalInput")
        H["wq_gu_t"] = b.dram("wq_gu_t", [2048, 16384], BF16, kind="ExternalInput")
        H["wq_p_t"] = b.dram("wq_p_t", [8192, 2048], BF16, kind="ExternalInput")
    return H


def emit_combine0(b, H, nt):
    nc = b.nc
    hg_h, xn_h, x0_h = H["hg"], H["xn"], H["x0T"]
    wout_h, wgu_h, wp_h = H["woutT"], H["wq_gu_t"], H["wq_p_t"]
    x1_h, xn1_h = H["x1T"], H["xn1"]
    gk = list(getattr(b, "gatekeys", []))

    def gu_src(m0):
        if wgu_h.shape[0] == 2048:
            return wgu_h[:, m0:m0 + 128]
        s, oc = divmod(m0, 2048)
        return wgu_h[s * 2048:(s + 1) * 2048, oc:oc + 128]

    def wp_src(m0):
        if wp_h.shape[0] == 8192:
            return wp_h[:, m0:m0 + 128]
        s, oc = divmod(m0, 256)
        return wp_h[s * 8192:(s + 1) * 8192, oc:oc + 128]

    xns = b.sbuf("xns", [128, 16 * 512], BF16)
    hgs = b.sbuf("hgs", [128, 32 * 512], BF16)
    s0c = b.sbuf("s0c", [128, 512], BF16)
    hsb = b.sbuf("hsb", [128, 64 * 512], BF16)
    x1sb = b.sbuf("x1sb", [128, 16 * 512], BF16)
    x1c = b.sbuf("x1c", [128, 512], F32)
    wo = [b.sbuf("wo0", [128, 32 * 128], BF16)] * 2
    wg = [b.sbuf("wg0", [128, 16 * 128], BF16)] * 2
    wu = [b.sbuf("wu0", [128, 16 * 128], BF16)] * 2
    wp = [b.sbuf("wp0", [128, 64 * 128], BF16)] * 2
    sg = b.sbuf("sg", [128, 512], BF16)
    su = b.sbuf("su", [128, 512], BF16)
    sq = b.sbuf("sq", [128, 512], BF16)
    x0c = b.sbuf("x0c", [128, 512], F32)
    g0c = b.sbuf("g0c", [128, 512], BF16)
    t2c = b.sbuf("t2c", [128, 512], BF16)
    rstdb = b.sbuf("rstdb", [128, 512], BF16)
    rsr = b.sbuf("rsr", [1, 512], BF16)
    lnv = b.sbuf("lnv", [1, 512], F32)
    stats = b.sbuf("stats", [1, 6 * 512], F32)
    rr = b.sbuf("rr", [1, 512], BF16)
    rm = b.sbuf("rm", [1, 512], BF16)
    ones1 = b.sbuf("ones1", [1, 128], BF16)
    ones128 = b.sbuf("ones128", [128, 1], BF16)
    ps = [b.psum(f"ps{i}", (128, 512), F32) for i in range(6)]
    b.pool(lambda: nc.gpsimd.memset(ones1[:, :], 1.0), [], ["ones1"])
    b.pool(lambda: nc.gpsimd.memset(ones128[:, :], 1.0), [], ["ones128"])
    epsf_c, epsf_k = b.constcol(EPSF32, 1)
    eps5_c, eps5_k = b.constcol(1e-5, 1)

    ntile = ceil_div(nt, 512)
    for ti in range(ntile):
        off = ti * 512
        w = min(512, nt - off)
        b.dma("qpool", xns[:, 0:16 * w].rearrange("p (c t) -> p c t", c=16),
              xn_h.rearrange("(c p) t -> p c t", p=128)[:, :, off:off + w],
              [("dram", "xn")], ["xns"], tag=f"lxn{ti}")
        b.dma("qact", hgs[:, 0:32 * w].rearrange("p (c t) -> p c t", c=32),
              hg_h.rearrange("(c p) t -> p c t", p=128)[:, :, off:off + w],
              [("dram", "hg")], ["hgs"], tag=f"lhg{ti}")
        for g in range(64):
            wgb_, wub_ = wg[g % 2], wu[g % 2]
            kwg, kwu = ("wg", 0), ("wu", 0)
            b.dma("qpool", wgb_[:, :].rearrange("p (c n) -> p c n", c=16),
                  gu_src(g * 128).rearrange("(c p) n -> p c n", p=128),
                  gk, [kwg], tag=f"lwg{ti}_{g}")
            b.dma("qact", wub_[:, :].rearrange("p (c n) -> p c n", c=16),
                  gu_src(8192 + g * 128).rearrange("(c p) n -> p c n", p=128),
                  gk, [kwu], tag=f"lwu{ti}_{g}")
            pg, kpg = ps[2], ("ps", 2)
            pu, kpu = ps[3], ("ps", 3)
            for c in range(16):
                b.mm(pg[:, 0:w], wgb_[:, c * 128:(c + 1) * 128], xns[:, c * w:(c + 1) * w],
                     start=(c == 0), stop=(c == 15), reads=[kwg, "xns"], writes=[kpg],
                     tag=f"gm{ti}_{g}_{c}")
                b.mm(pu[:, 0:w], wub_[:, c * 128:(c + 1) * 128], xns[:, c * w:(c + 1) * w],
                     start=(c == 0), stop=(c == 15), reads=[kwu, "xns"], writes=[kpu],
                     tag=f"um{ti}_{g}_{c}")
            b.act(b.lp(lambda w=w: nc.scalar.activation(out=sg[:, 0:w], in_=pg[:, 0:w],
                  func=AF.Silu)), [kpg], ["sg"], tag=f"sg{ti}_{g}")
            b.dve(b.lp(lambda w=w: nc.vector.tensor_copy(su[:, 0:w], pu[:, 0:w])),
                  [kpu], ["su"], tag=f"sue{ti}_{g}")
            b.pool(b.lp(lambda g=g, w=w: nc.gpsimd.tensor_mul(hsb[:, g * w:(g + 1) * w],
                   sg[:, 0:w], su[:, 0:w])), ["sg", "su"], [("h", g)], tag=f"hm{ti}_{g}")
            b.pool(b.lp(lambda g=g, w=w: nc.gpsimd.tensor_mul(sq[:, 0:w],
                   hsb[:, g * w:(g + 1) * w], hsb[:, g * w:(g + 1) * w])),
                   [("h", g)], ["sqc"], tag=f"hs{ti}_{g}")
            b.mm(ps[4][0:1, 0:w], ones128[:, :], sq[:, 0:w], start=(g == 0), stop=(g == 63),
                 reads=["sqc", "ones128"], writes=[("ps", 4)], tag=f"hq{ti}_{g}")
        b.act(lambda w=w: nc.scalar.activation(out=lnv[:, 0:w], in_=ps[4][0:1, 0:w],
              func=AF.Ln, scale=1.0 / 8192, bias=epsf_c[:, :]),
              [("ps", 4), epsf_k], ["lnv"], tag=f"rl{ti}")
        b.act(b.lp(lambda w=w: nc.scalar.activation(out=rsr[:, 0:w], in_=lnv[:, 0:w],
              func=AF.Exp, scale=-0.5)), ["lnv"], ["rsr"], tag=f"re{ti}")
        b.mm(ps[5][:, 0:w], ones1[:, :], rsr[:, 0:w], True, True, ["ones1", "rsr"],
             [("ps", 5)], tag=f"rb{ti}")
        b.dve(b.lp(lambda w=w: nc.vector.tensor_copy(rstdb[:, 0:w], ps[5][:, 0:w])),
              [("ps", 5)], ["rstdb"], tag=f"rc{ti}")
        for f in range(16):
            wob = wo[0]
            kwo = ("wo", 0)
            b.dma("qpool", wob[:, :].rearrange("p (c n) -> p c n", c=32),
                  wout_h[:, f * 128:(f + 1) * 128].rearrange("(c p) n -> p c n", p=128),
                  [], [kwo], tag=f"lwo{ti}_{f}")
            pso, kpso = ps[2], ("ps", 2)
            for c in range(32):
                b.mm(pso[:, 0:w], wob[:, c * 128:(c + 1) * 128], hgs[:, c * w:(c + 1) * w],
                     start=(c == 0), stop=(c == 31), reads=[kwo, "hgs"], writes=[kpso],
                     tag=f"s0m{ti}_{f}_{c}")
            b.dve(b.lp(lambda w=w: nc.vector.tensor_copy(s0c[:, 0:w], ps[2][:, 0:w])),
                  [kpso], ["s0c"], tag=f"s0e{ti}_{f}")
            wpb = wp[f % 2]
            kw = ("wp", 0)
            b.dma("qpool", wpb[:, :].rearrange("p (c n) -> p c n", c=64),
                  wp_src(f * 128).rearrange("(c p) n -> p c n", p=128),
                  gk, [kw], tag=f"lwp{ti}_{f}")
            pb, kp = ps[f % 2], ("ps", f % 2)
            for c in range(64):
                b.mm(pb[:, 0:w], wpb[:, c * 128:(c + 1) * 128], hsb[:, c * w:(c + 1) * w],
                     start=(c == 0), stop=(c == 63), reads=[kw, ("h", c)], writes=[kp],
                     tag=f"dm{ti}_{f}_{c}")
            b.dve(b.lp(lambda w=w, pb=pb: nc.vector.scalar_tensor_tensor(out=t2c[:, 0:w],
                  in0=pb[:, 0:w], scalar=1.0, in1=rstdb[:, 0:w], op0=ALU.mult, op1=ALU.mult)),
                  [kp, "rstdb"], ["t2c"], tag=f"dn{ti}_{f}")
            b.act(b.lp(lambda w=w: nc.scalar.activation(out=g0c[:, 0:w], in_=t2c[:, 0:w],
                  func=AF.Silu)), ["t2c"], ["g0c"], tag=f"g0{ti}_{f}")
            b.dma("qsp", x0c[:, 0:w], x0_h[f * 128:(f + 1) * 128, off:off + w], [], ["x0c"],
                  tag=f"lx0{ti}_{f}")
            b.dve(b.lp(lambda w=w: nc.vector.tensor_mul(t2c[:, 0:w],
                  s0c[:, 0:w], g0c[:, 0:w])), ["s0c", "g0c"], ["t2c"],
                  tag=f"sgm{ti}_{f}")
            b.dve(lambda w=w: nc.vector.tensor_add(x1c[:, 0:w],
                  x0c[:, 0:w], t2c[:, 0:w]), ["x0c", "t2c"], ["x1c"], tag=f"x1{ti}_{f}")
            b.pool(b.lp(lambda f=f, w=w: nc.gpsimd.tensor_copy(x1sb[:, f * w:(f + 1) * w],
                   x1c[:, 0:w])), ["x1c"], [("x1", f)], tag=f"x1b{ti}_{f}")
            b.dma("qsp", x1_h[f * 128:(f + 1) * 128, off:off + w], x1c[:, 0:w],
                  ["x1c"], [("dram", "x1")], tag=f"sx1{ti}_{f}")
        pm, pq = ps[2], ps[3]
        for c in range(16):
            xc = x1sb[:, c * w:(c + 1) * w]
            b.mm(pm[0:1, 0:w], ones128[:, :], xc, start=(c == 0), stop=(c == 15),
                 reads=[("x1", c), "ones128"], writes=[("ps", 2)], tag=f"nsum{ti}_{c}")
            b.pool(b.lp(lambda xc=xc, w=w: nc.gpsimd.tensor_mul(sq[:, 0:w], xc, xc)),
                   [("x1", c)], ["sqc"], tag=f"nsq{ti}_{c}")
            b.mm(pq[0:1, 0:w], ones128[:, :], sq[:, 0:w], start=(c == 0), stop=(c == 15),
                 reads=["sqc", "ones128"], writes=[("ps", 3)], tag=f"nsq2{ti}_{c}")
        mean = stats[:, 0:w]
        ex2 = stats[:, 512:512 + w]
        msq = stats[:, 1024:1024 + w]
        var = stats[:, 1536:1536 + w]
        sd = stats[:, 2048:2048 + w]
        b.dve(lambda w=w, o=mean: nc.vector.tensor_scalar(out=o, in0=pm[0:1, 0:w],
              scalar1=1.0 / 2048, scalar2=None, op0=ALU.mult), [("ps", 2)], [("st", "m")])
        b.dve(lambda w=w, o=ex2: nc.vector.tensor_scalar(out=o, in0=pq[0:1, 0:w],
              scalar1=1.0 / 2048, scalar2=None, op0=ALU.mult), [("ps", 3)], [("st", "e")])
        b.dve(lambda o=msq, a=mean: nc.vector.tensor_mul(o, a, a), [("st", "m")], [("st", "ms")])
        b.dve(lambda o=var, a=ex2, c2=msq: nc.vector.tensor_sub(o, a, c2),
              [("st", "e"), ("st", "ms")], [("st", "v")])
        b.act(lambda o=sd, a=var: nc.scalar.activation(out=o, in_=a, func=AF.Sqrt,
              bias=eps5_c[:, :]), [("st", "v"), eps5_k], [("st", "sd")])
        b.dve(b.lp(lambda w=w, a=sd: nc.vector.reciprocal(rr[:, 0:w], a)), [("st", "sd")], ["rr"])
        b.dve(b.lp(lambda w=w, a=mean: nc.vector.tensor_mul(rm[:, 0:w], a, rr[:, 0:w])),
              [("st", "m"), "rr"], ["rm"])
        b.mm(ps[4][:, 0:w], ones1[:, :], rr[:, 0:w], True, True, ["ones1", "rr"], [("ps", 4)],
             tag=f"nbr{ti}")
        b.mm(ps[5][:, 0:w], ones1[:, :], rm[:, 0:w], True, True, ["ones1", "rm"], [("ps", 5)],
             tag=f"nbm{ti}")
        for c in range(16):
            xc = x1sb[:, c * w:(c + 1) * w]
            b.dve(b.lp(lambda xc=xc, w=w: nc.vector.tensor_mul(sq[:, 0:w], xc, ps[4][0:128, 0:w])),
                  [("x1", c), ("ps", 4)], ["sqc"], tag=f"xnm{ti}_{c}")
            b.dve(b.lp(lambda w=w: nc.vector.tensor_sub(t2c[:, 0:w], sq[:, 0:w], ps[5][0:128, 0:w])),
                  ["sqc", ("ps", 5)], ["t2c"], tag=f"xns{ti}_{c}")
            b.dma("qsp", xn1_h[c * 128:(c + 1) * 128, off:off + w], t2c[:, 0:w],
                  ["t2c"], [("dram", "xn1")], tag=f"sxn1{ti}_{c}")


def build_combine0(nt=NT_FULL):
    b = B("combine0")
    H = decl_combine0(b, nt, fused=False)
    emit_combine0(b, H, nt)
    return b


def stage_combine0(core, artsA, artsB, prep, inputs, nt=NT_FULL):
    bidx, half = core // 2, core % 2
    x = inputs["L0_attn_hidden_states"][bidx]
    lo = half * nt
    return {
        "hg": artsB["hg"], "xn": artsA["xn"],
        "x0T": np.ascontiguousarray(x[lo:lo + nt].T),
        "woutT": np.ascontiguousarray(inputs["L0_attn_W_out"].T),
        "wq_gu_t": prep["wq_gu_t"], "wq_p_t": prep["wq_p_t"],
    }


def decl_l1mix(b, nt, fused):
    mid = "Internal" if fused else "ExternalOutput"
    H = {
        "convd1": b.dram("convd1", [16 * 4 * 128, 128], F32, kind="ExternalInput"),
        "convb": b.dram("convb", [128, 16], F32, kind="ExternalInput"),
        "inwr_t": b.dram("inwr_t", [2048, 64], F32, kind="ExternalInput"),
        "inwk_t": b.dram("inwk_t", [2048, 64], F32, kind="ExternalInput"),
        "mgws_t": b.dram("mgws_t", [2048, 64], F32, kind="ExternalInput"),
        "mgwt_t": b.dram("mgwt_t", [2048, 64], F32, kind="ExternalInput"),
        "mgbs": b.dram("mgbs", [64, 1], F32, kind="ExternalInput"),
        "mgbt": b.dram("mgbt", [64, 1], F32, kind="ExternalInput"),
        "g0w_t": b.dram("g0w_t", [2048, 128], F32, kind="ExternalInput"),
        "g1w_t": b.dram("g1w_t", [128, 2048], F32, kind="ExternalInput"),
        "b1r": b.dram("b1r", [1, 2048], F32, kind="ExternalInput"),
        "resw": b.dram("resw", [128, 16], F32, kind="ExternalInput"),
        "mask01": b.dram("mask01", [128, 128], mybir.dt.int32, kind="ExternalInput"),
        "id128b": b.dram("id128b", [128, 128], F32, kind="ExternalInput"),
        "s1T": b.dram("s1T", [2048, nt], BF16, kind=mid),
        "s1fin": b.dram("s1fin", [64, 2048], F32, kind=mid),
    }
    if fused:
        H["rdall"] = b.dram("rdall", [64, nt], BF16)
        H["halo3"] = b.dram("halo3", [2048, HALO], BF16)
        H["cc3in"] = b.dram("cc3in", [64, 2048], F32)
        H["cc3out"] = b.dram("cc3out", [128, 2048], F32)
        H["sprevd"] = b.dram("sprevd", [64, 2048], F32)
    else:
        H["xn1f"] = b.dram("xn1f", [2048, nt + HALO], BF16, kind="ExternalInput")
        H["s1init"] = b.dram("s1init", [64, 2048], F32, kind="ExternalInput")
    return H


def emit_l1mix(b, H, nt, fused):
    nch = nt // 128
    nc = b.nc
    convd_h, convb_h = H["convd1"], H["convb"]
    inwr_h, inwk_h = H["inwr_t"], H["inwk_t"]
    mgws_h, mgwt_h, mgbs_h, mgbt_h = H["mgws_t"], H["mgwt_t"], H["mgbs"], H["mgbt"]
    g0w_h, g1w_h, b1r_h = H["g0w_t"], H["g1w_t"], H["b1r"]
    resw_h, mask_h, id128_h = H["resw"], H["mask01"], H["id128b"]
    s1_h, sfin_h = H["s1T"], H["s1fin"]
    if not fused:
        xf_h = H["xn1f"]
        sin_h = H["s1init"]
    else:
        xf_h = H["xn1"]
    xf = b.sbuf("xf", [128, 16 * (nt + HALO)], BF16)
    shift = b.sbuf("shift", [128, 16 * nt], BF16)
    rs = b.sbuf("rs", [64, nt], BF16)
    ks = b.sbuf("ks", [64, nt], BF16)
    sel = b.sbuf("sel", [64, 512], F32)
    tau = b.sbuf("tau", [64, 512], F32)
    itg = b.sbuf("itg", [64, 512], F32)
    gg = b.sbuf("gg", [64, nt], F32)
    kh = b.sbuf("kh", [64, nt], BF16)
    t64 = b.sbuf("t64", [64, 512], F32)
    S = b.sbuf("S1", [64, 2048], BF16)
    cw = b.sbuf("cw", [128, 4 * 128], BF16)
    cb = b.sbuf("cb", [128, 16], F32)
    inwr = b.sbuf("inwr", [128, 16 * 64], BF16)
    inwk = b.sbuf("inwk", [128, 16 * 64], BF16)
    mgws = b.sbuf("mgws", [128, 16 * 64], BF16)
    mgwt = b.sbuf("mgwt", [128, 16 * 64], BF16)
    mgbs = b.sbuf("mgbs_s", [64, 1], F32)
    mgbt = b.sbuf("mgbt_s", [64, 1], F32)
    g0w = b.sbuf("g0w", [128, 16 * 128], BF16)
    g1w = b.sbuf("g1w", [128, 2048], BF16)
    b1b = b.sbuf("b1b", [128, 2048], BF16)
    resw = b.sbuf("resw_s", [128, 16], F32)
    mask01 = b.sbuf("mask01_s", [128, 128], mybir.dt.int32)
    zeros = b.sbuf("zeros", [128, 128], BF16)
    zero64 = b.sbuf("zero64", [64, 128], F32)
    ones1 = b.sbuf("ones1", [1, 128], BF16)
    ones1f = b.sbuf("ones1f", [1, 128], F32)
    ones64 = b.sbuf("ones64", [64, 1], F32)
    Gc = b.sbuf("Gc1", [64, 128], F32)
    eGc = b.sbuf("eGc", [64, 128], F32)
    rti = b.sbuf("rti", [64, 128], BF16)
    d2 = b.sbuf("d2", [64, 128], F32)
    er = b.sbuf("er", [64, 128], F32)
    rh = b.sbuf("rh", [64, 128], BF16)
    d1 = b.sbuf("d1", [64, 32], F32)
    ek = b.sbuf("ek", [64, 32], F32)
    ktb = b.sbuf("ktb", [64, 32], BF16)
    Asb = b.sbuf("Asb", [128, 128], BF16)
    xtm = b.sbuf("xtm", [128, 2048], BF16)
    utm = b.sbuf("utm", [128, 2048], BF16)
    tmpc = b.sbuf("tmpc", [128, 128], BF16)
    chc = b.sbuf("chc", [128, 512], BF16)
    t2c = b.sbuf("t2c", [128, 512], F32)
    kbar = b.sbuf("kbar", [64, 128], BF16)
    kbt = b.sbuf("kbt", [128, 64], BF16)
    egl1 = b.sbuf("egl1", [64, 1], F32)
    s1stg = b.sbuf("s1stg", [128, 16 * 128], BF16)
    ocp = b.sbuf("ocp", [128, 128], F32)
    srow = b.sbuf("srow", [1, 512], F32)
    ps = [b.psum(f"ps{i}", (128, 512), F32) for i in range(6)]
    pbf = [b.psum(f"pbf{i}", (128, 1024), BF16) for i in range(2)]
    id128 = b.sbuf("id128s", [128, 128], BF16)

    b.pool(lambda: nc.gpsimd.memset(ones1[:, :], 1.0), [], ["ones1"])
    b.pool(lambda: nc.gpsimd.memset(ones1f[:, :], 1.0), [], ["ones1f"])
    b.pool(lambda: nc.gpsimd.memset(ones64[:, :], 1.0), [], ["ones64"])
    b.pool(lambda: nc.gpsimd.memset(zeros[:, :], 0.0), [], ["zeros"])
    b.pool(lambda: nc.gpsimd.memset(zero64[:, :], 0.0), [], ["zero64"])
    one_c, one_k = b.constcol(1.0, 64)
    for t, h, kk in ((cb, convb_h, "cb"), (mgbs, mgbs_h, "mgbs"), (mgbt, mgbt_h, "mgbt"),
                     (resw, resw_h, "resw"), (mask01, mask_h, "mask01"), (id128, id128_h, "id128")):
        b.dma("qpool", t[:, :], h, [], [kk])
    if fused:
        b.dma("qpool", xf[:, :].rearrange("p (c t) -> p c t", c=16)[:, :, 0:HALO],
              H["halo3"].rearrange("(c p) t -> p c t", p=128), [("dram", "halo3")],
              ["xf"], tag="lxfh")
        b.dma("qact", xf[:, :].rearrange("p (c t) -> p c t", c=16)[:, :, HALO:],
              H["xn1"].rearrange("(c p) t -> p c t", p=128), [("dram", "xn1")],
              ["xf"], tag="lxfm")
    else:
        b.dma("qpool", xf[:, :].rearrange("p (c t) -> p c t", c=16),
              xf_h.rearrange("(c p) t -> p c t", p=128), [], ["xf"])
    b.dma("qpool", cw[:, :], convd_h.rearrange("(z j p) n -> z p j n", j=4, p=128)[0], [], ["cwz"])
    b.dma("qpool", inwr[:, :].rearrange("p (c n) -> p c n", c=16),
          inwr_h.rearrange("(c p) n -> p c n", p=128), [], ["inwr"])
    b.dma("qpool", inwk[:, :].rearrange("p (c n) -> p c n", c=16),
          inwk_h.rearrange("(c p) n -> p c n", p=128), [], ["inwk"])
    b.dma("qpool", mgws[:, :].rearrange("p (c n) -> p c n", c=16),
          mgws_h.rearrange("(c p) n -> p c n", p=128), [], ["mgws"])
    b.dma("qpool", mgwt[:, :].rearrange("p (c n) -> p c n", c=16),
          mgwt_h.rearrange("(c p) n -> p c n", p=128), [], ["mgwt"])
    b.dma("qpool", g0w[:, :].rearrange("p (c n) -> p c n", c=16),
          g0w_h.rearrange("(c p) n -> p c n", p=128), [], ["g0w"])
    b.dma("qpool", g1w[:, :], g1w_h, [], ["g1w"])
    if fused:
        b.pool(lambda: nc.gpsimd.memset(S[:, :], 0.0), [], [("S1k",)])
        glsum = b.sbuf("glsum", [64, 1], F32)
        dpref = b.sbuf("dpref", [64, 1], F32)
        rdc = b.sbuf("rdc", [64, 128], BF16)
        b.pool(lambda: nc.gpsimd.memset(glsum[:, :], 0.0), [], ["glsum"])
    else:
        b.dma("qpool", S[:, :], sin_h, [], [("S1k",)])
    b1r_s = b.sbuf("b1r_s", [1, 512], F32)
    for q4 in range(4):
        b.dma("qpool", b1r_s[:, :], b1r_h[0:1, q4 * 512:(q4 + 1) * 512], ["b1r"], ["b1r"],
              tag=f"b1l{q4}")
        b.mm(ps[5][:, 0:512], ones1f[:, :], b1r_s[0:1, 0:512], True, True,
             ["ones1f", "b1r"], [("ps", 5)], tag=f"b1b{q4}")
        b.dve(b.lp(lambda q4=q4: nc.vector.tensor_copy(b1b[:, q4 * 512:(q4 + 1) * 512],
              ps[5][:, 0:512])), [("ps", 5)], ["b1b"], tag=f"b1c{q4}")

    ntile = ceil_div(nt, 512)
    for ci in range(16):
        for ti in range(ntile):
            off = ti * 512
            w = min(512, nt - off)
            pb, kp = ps[ti % 2], ("ps", ti % 2)
            for j in range(4):
                b.mm(pb[:, 0:w], cw[:, (4 * ci + j) * 0 + j * 128:(j + 1) * 128] if False else cw[:, j * 128:(j + 1) * 128],
                     xf[:, ci * (nt + HALO) + off + j:ci * (nt + HALO) + off + j + w],
                     start=(j == 0), stop=(j == 3), reads=["cwz", "xf"], writes=[kp],
                     tag=f"cv{ci}_{ti}_{j}")
            b.act(b.lp(lambda pb=pb, w=w, ci=ci, off=off: nc.scalar.activation(
                  out=shift[:, ci * nt + off:ci * nt + off + w], in_=pb[:, 0:w], func=AF.Silu,
                  bias=cb[:, ci:ci + 1])), [kp, "cb"], [("sh", ci)], tag=f"cs{ci}_{ti}")
        if ci < 15:
            b.dma("qpool", cw[:, :], convd_h.rearrange("(z j p) n -> z p j n", j=4, p=128)[ci + 1],
                  ["cwz"], ["cwz"], tag=f"lcw{ci}")
    SH = [("sh", i) for i in range(16)]
    for ti in range(ntile):
        off = ti * 512
        w = min(512, nt - off)
        slt = slice(off, off + w)
        wsl = slice(0, w)
        for name, wtile, kw, dst, kd, bias in (("r", inwr, "inwr", rs, "rs", None),
                                               ("k", inwk, "inwk", ks, "ks", None),
                                               ("s", mgws, "mgws", sel, "sel0", mgbs),
                                               ("t", mgwt, "mgwt", tau, "tau0", mgbt)):
            pb, kp = ps[2], ("ps", 2)
            for c in range(16):
                b.mm(pb[0:64, 0:w], wtile[:, c * 64:(c + 1) * 64],
                     shift[:, c * nt + off:c * nt + off + w], start=(c == 0), stop=(c == 15),
                     reads=[kw] + SH, writes=[kp], tag=f"pj{name}{ti}_{c}")
            if bias is None:
                sc = 0.125 if name == "r" else 1.0
                b.dve(b.lp(lambda pb=pb, w=w, dst=dst, off=off, sc=sc: nc.vector.tensor_scalar(
                      out=dst[:, off:off + w], in0=pb[0:64, 0:w], scalar1=sc, scalar2=None,
                      op0=ALU.mult)), [kp], [(kd, ti)], tag=f"pe{name}{ti}")
            else:
                kb = "mgbs" if name == "s" else "mgbt"
                b.act(lambda pb=pb, w=w, dst=dst, bias=bias: nc.scalar.activation(
                      out=dst[:, 0:w], in_=pb[0:64, 0:w], func=AF.Identity,
                      bias=bias[:, :]), [kp, kb], [(kd, ti)], tag=f"pe{name}{ti}")
        b.act(lambda wsl=wsl: nc.scalar.activation(out=t64[:, wsl], in_=sel[:, wsl], func=AF.Abs),
              [("sel0", ti)], ["t64a"], tag=f"sp1{ti}")
        b.act(lambda wsl=wsl: nc.scalar.activation(out=t64[:, wsl], in_=t64[:, wsl], func=AF.Exp,
              scale=-1.0), ["t64a"], ["t64a"], tag=f"sp2{ti}")
        b.act(lambda wsl=wsl: nc.scalar.activation(out=t64[:, wsl], in_=t64[:, wsl], func=AF.Ln,
              bias=one_c[:, :]), ["t64a", one_k], ["t64a"], tag=f"sp3{ti}")
        b.act(lambda wsl=wsl: nc.scalar.activation(out=sel[:, wsl], in_=sel[:, wsl], func=AF.Relu),
              [("sel0", ti)], [("selr", ti)], tag=f"sp4{ti}")
        b.dve(lambda wsl=wsl: nc.vector.tensor_add(sel[:, wsl], sel[:, wsl], t64[:, wsl]),
              [("selr", ti), "t64a"], [("sel", ti)], tag=f"sp5{ti}")
        b.act(lambda wsl=wsl: nc.scalar.activation(out=tau[:, wsl], in_=tau[:, wsl], func=AF.Sigmoid),
              [("tau0", ti)], [("tau", ti)], tag=f"sg{ti}")
        b.dve(lambda wsl=wsl: nc.vector.tensor_scalar(out=t64[:, wsl], in0=sel[:, wsl],
              scalar1=1e-19, scalar2=None, op0=ALU.max), [("sel", ti)], ["t64a"], tag=f"it1{ti}")
        b.act(lambda wsl=wsl: nc.scalar.activation(out=t64[:, wsl], in_=t64[:, wsl], func=AF.Ln),
              ["t64a"], ["t64a"], tag=f"it2{ti}")
        b.dve(lambda wsl=wsl: nc.vector.tensor_mul(t64[:, wsl], t64[:, wsl], tau[:, wsl]),
              ["t64a", ("tau", ti)], ["t64a"], tag=f"it3{ti}")
        b.act(lambda wsl=wsl: nc.scalar.activation(out=itg[:, wsl], in_=t64[:, wsl], func=AF.Exp),
              ["t64a"], [("itg", ti)], tag=f"it4{ti}")
        b.dve(lambda slt=slt, wsl=wsl: nc.vector.scalar_tensor_tensor(out=gg[:, slt], in0=sel[:, wsl],
              scalar=-1.0, in1=tau[:, wsl], op0=ALU.mult, op1=ALU.mult),
              [("sel", ti), ("tau", ti)], [("gg", ti)], tag=f"gm{ti}")
        b.pool(lambda slt=slt, wsl=wsl: nc.gpsimd.tensor_mul(t64[:, wsl], ks[:, slt], ks[:, slt]),
               [("ks", ti)], ["t64a"], tag=f"kn0{ti}")
        b.mm(ps[4][0:1, 0:w], ones64[:, :], t64[:, 0:w], True, True, ["ones64", "t64a"],
             [("ps", 4)], tag=f"kn1{ti}")
        b.act(lambda w=w: nc.scalar.activation(out=srow[:, 0:w], in_=ps[4][0:1, 0:w],
              func=AF.Sqrt), [("ps", 4)], ["srow"], tag=f"kn2{ti}")
        b.dve(lambda w=w: nc.vector.tensor_scalar(out=srow[:, 0:w], in0=srow[:, 0:w],
              scalar1=1e-12, scalar2=None, op0=ALU.max), ["srow"], ["srow"], tag=f"kn3{ti}")
        b.dve(lambda w=w: nc.vector.reciprocal(srow[:, 0:w], srow[:, 0:w]), ["srow"], ["srow"],
              tag=f"kn4{ti}")
        b.mm(ps[4][0:64, 0:w], ones1f[0:1, 0:64], srow[0:1, 0:w], True, True, ["ones1f", "srow"],
             [("ps", 4)], tag=f"kn5{ti}")
        b.dve(lambda wsl=wsl, w=w: nc.vector.tensor_mul(t64[:, wsl], itg[:, wsl], ps[4][0:64, 0:w]),
              [("itg", ti), ("ps", 4)], ["t64a"], tag=f"kn6{ti}")
        b.dve(b.lp(lambda slt=slt, wsl=wsl: nc.vector.tensor_mul(kh[:, slt], ks[:, slt], t64[:, wsl])),
              [("ks", ti), "t64a"], [("kh", ti)], tag=f"kn7{ti}")

    KH = [("kh", i) for i in range(ntile)]
    GGk = [("gg", i) for i in range(ntile)]
    RS = [("rs", i) for i in range(ntile)]
    for c in range(nch):
        sl = slice(c * 128, (c + 1) * 128)
        b.dve(lambda sl=sl: nc.vector.tensor_tensor_scan(out=Gc[:, :], data0=gg[:, sl],
              data1=zero64[:, :], initial=0.0, op0=ALU.add, op1=ALU.add),
              GGk + ["zero64"], ["Gc1"], tag=f"sc{c}")
        b.act(lambda: nc.scalar.activation(out=eGc[:, :], in_=Gc[:, :], func=AF.Exp),
              ["Gc1"], ["eGc"], tag=f"eg{c}")
        b.dve(b.lp(lambda sl=sl: nc.vector.tensor_mul(rti[:, :], rs[:, sl], eGc[:, :])),
              RS + ["eGc"], ["rti"], tag=f"rt{c}")
        if fused:
            b.act(lambda: nc.scalar.activation(out=dpref[:, :], in_=glsum[:, :], func=AF.Exp),
                  ["glsum"], ["dpref"], tag=f"dp{c}")
            b.dve(b.lp(lambda: nc.vector.tensor_scalar(out=rdc[:, :], in0=rti[:, :],
                  scalar1=dpref[:, :], scalar2=None, op0=ALU.mult)),
                  ["rti", "dpref"], ["rdc"], tag=f"rd{c}")
            b.dma("qsp", H["rdall"][:, sl], rdc[:, :], ["rdc"], [("dram", "rdall")],
                  tag=f"srd{c}")
            b.dve(lambda: nc.vector.tensor_add(glsum[:, :], glsum[:, :], Gc[:, 127:128]),
                  ["glsum", "Gc1"], ["glsum"], tag=f"gs{c}")
        hoff = 0 if fused else HALO
        b.dma("qsp", xtm[:, :], xf_h[:, hoff + c * 128:hoff + (c + 1) * 128],
              [("dram", "xn1")] if fused else [], ["xtm"], tag=f"xt{c}", transpose=True)
        pt, kpt = ps[3], ("ps", 3)
        for k16 in range(16):
            b.mm(pt[:, 0:128], g0w[:, k16 * 128:(k16 + 1) * 128],
                 xf[:, k16 * (nt + HALO) + HALO + c * 128:k16 * (nt + HALO) + HALO + (c + 1) * 128],
                 start=(k16 == 0), stop=(k16 == 15), reads=["g0w", "xf"], writes=[kpt],
                 tag=f"ch1{c}_{k16}")
        b.dve(b.lp(lambda: nc.vector.tensor_copy(tmpc[:, :], ps[3][:, 0:128])),
              [kpt], ["tmpc"], tag=f"ch2{c}")
        for q4 in range(4):
            b.mm(ps[2][:, 0:512], tmpc[:, :], g1w[:, q4 * 512:(q4 + 1) * 512], True, True,
                 ["tmpc", "g1w"], [("ps", 2)], tag=f"ch3{c}_{q4}")
            b.dve(lambda q4=q4: nc.vector.tensor_add(t2c[:, 0:512], ps[2][:, 0:512],
                  b1b[:, q4 * 512:(q4 + 1) * 512]), [("ps", 2), "b1b"], ["t2c1"], tag=f"ch4{c}_{q4}")
            b.act(b.lp(lambda q4=q4: nc.scalar.activation(out=chc[:, 0:512], in_=t2c[:, 0:512],
                  func=AF.Sigmoid)), ["t2c1"], ["chc"], tag=f"ch5{c}_{q4}")
            b.pool(b.lp(lambda q4=q4: nc.gpsimd.tensor_mul(utm[:, q4 * 512:(q4 + 1) * 512],
                   chc[:, 0:512], xtm[:, q4 * 512:(q4 + 1) * 512])),
                   ["chc", "xtm"], [("utm", q4)], tag=f"ch6{c}_{q4}")
        UT = [("utm", i) for i in range(4)]
        for blk in range(4):
            bsl = slice(blk * 32, (blk + 1) * 32)
            refc = Gc[:, blk * 32 + 16:blk * 32 + 17]
            b.dve(lambda bsl=bsl, refc=refc: nc.vector.tensor_scalar(out=d1[:, :],
                  in0=Gc[:, bsl], scalar1=refc, scalar2=-85.0, op0=ALU.subtract, op1=ALU.max),
                  ["Gc1"], ["d1"], tag=f"kb1{c}_{blk}")
            b.act(lambda: nc.scalar.activation(out=ek[:, :], in_=d1[:, :], func=AF.Exp,
                  scale=-1.0), ["d1"], ["ek"], tag=f"kb2{c}_{blk}")
            b.dve(b.lp(lambda c=c, blk=blk: nc.vector.tensor_mul(ktb[:, :],
                  kh[:, c * 128 + blk * 32:c * 128 + (blk + 1) * 32], ek[:, :])),
                  KH + ["ek"], ["ktb"], tag=f"kb3{c}_{blk}")
            b.dve(lambda refc=refc: nc.vector.tensor_scalar(out=d2[:, :], in0=Gc[:, :],
                  scalar1=refc, scalar2=85.0, op0=ALU.subtract, op1=ALU.min),
                  ["Gc1"], ["d2"], tag=f"rh1{c}_{blk}")
            b.act(lambda: nc.scalar.activation(out=er[:, :], in_=d2[:, :], func=AF.Exp),
                  ["d2"], ["er"], tag=f"rh2{c}_{blk}")
            b.dve(b.lp(lambda sl=sl: nc.vector.tensor_mul(rh[:, :], rs[:, sl], er[:, :])),
                  RS + ["er"], ["rh"], tag=f"rh3{c}_{blk}")
            if blk > 0:
                b.dve(b.lp(lambda blk=blk: nc.vector.memset(rh[:, 0:blk * 32], 0.0)),
                      ["rh"], ["rh"], tag=f"rh4{c}_{blk}")
            b.mm(ps[0][blk * 32:(blk + 1) * 32, 0:128], ktb[:, :], rh[:, :], True, True,
                 ["ktb", "rh"], [("ps", 0)], tag=f"am{c}_{blk}", tile_position=(0, blk * 32))
        b.dve(b.lp(lambda: nc.vector.tensor_copy(Asb[:, :], zeros[:, :])),
              ["zeros"], ["Asb"], tag=f"az{c}")
        b.dve(b.lp(lambda: nc.vector.copy_predicated(out=Asb[:, :], mask=mask01[:, :],
              data=ps[0][:, 0:128])), [("ps", 0), "mask01", "Asb"], ["Asb"], tag=f"asel{c}")
        if False and c == 0:
            for nm, t, rd in (("Asb0", Asb, ["Asb"]), ("utm0", utm, UT), ("rti0", rti, ["rti"]),
                              ("Gc0", Gc, ["Gc1"])):
                hh = b.dram("tap_" + nm, list(t.shape), F32, kind="ExternalOutput")
                b.dma("qpool", hh, t[:, :], rd, [], tag="tap" + nm)
        for dc in range(16):
            psl = ps[1]
            b.mm(psl[:, 0:128], utm[:, dc * 128:(dc + 1) * 128], Asb[:, :], True, False,
                 UT + ["Asb"], [("ps", 1)], tag=f"o1{c}_{dc}")
            b.mm(psl[:, 0:128], S[:, dc * 128:(dc + 1) * 128], rti[:, :], False, True,
                 [("S1k",), "rti"], [("ps", 1)], tag=f"o2{c}_{dc}")
            b.dve(b.lp(lambda: nc.vector.tensor_copy(ocp[:, :], ps[1][:, 0:128])),
                  [("ps", 1)], ["ocp"], tag=f"ocp{c}_{dc}")
            b.dve(b.lp(lambda dc=dc, c=c: nc.vector.scalar_tensor_tensor(
                  out=s1stg[:, dc * 128:(dc + 1) * 128],
                  in0=shift[:, dc * nt + c * 128:dc * nt + (c + 1) * 128],
                  scalar=resw[:, dc:dc + 1], in1=ocp[:, :], op0=ALU.mult, op1=ALU.add)),
                  [("sh", dc), "resw", "ocp"], [("s1s", dc)], tag=f"o3{c}_{dc}")
            if False and c == 0:
                otap = b.sbuf(f"otap{dc}", [128, 128], F32) if dc == 0 and not hasattr(b, "_otap") else getattr(b, "_otap", None)
                if dc == 0:
                    b._otap = otap
                    b._otap_h = b.dram("tap_ops", [128, 16 * 128], F32, kind="ExternalOutput")
                b.dve(lambda dc=dc: nc.vector.tensor_copy(b._otap[:, :], ps[1][:, 0:128]),
                      [("ps", 1)], ["otapb"], tag=f"ot{c}_{dc}")
                b.dma("qpool", b._otap_h[:, dc * 128:(dc + 1) * 128], b._otap[:, :],
                      ["otapb"], [], tag=f"otd{c}_{dc}")
                if dc == 0:
                    b._octap_h = b.dram("tap_ocp", [128, 128], F32, kind="ExternalOutput")
                    b.dma("qpool", b._octap_h, ocp[:, :], ["ocp"], [], tag="ocptap")
                    b._shtap_h = b.dram("tap_shslice", [128, 128], F32, kind="ExternalOutput")
                    b.dma("qpool", b._shtap_h, shift[:, 0 * nt + c * 128:0 * nt + (c + 1) * 128],
                          [("sh", 0)], [], tag="shslicetap")
        b.dma("qsp", s1_h.rearrange("(dc p) t -> p dc t", p=128)[:, :, sl],
              s1stg[:, :].rearrange("p (dc t) -> p dc t", dc=16),
              [("s1s", i) for i in range(16)], [("dram", "s1T")], tag=f"so{c}")
        if False and c == 0:
            hh = b.dram("tap_s1stg", [128, 2048], F32, kind="ExternalOutput")
            b.dma("qpool", hh, s1stg[:, :], [("s1s", i) for i in range(16)], [], tag="taps1stg")
        b.dve(lambda: nc.vector.tensor_scalar(out=d2[:, :], in0=Gc[:, :],
              scalar1=Gc[:, 127:128], scalar2=None, op0=ALU.subtract), ["Gc1"], ["d2"],
              tag=f"kb5{c}")
        b.act(lambda: nc.scalar.activation(out=er[:, :], in_=d2[:, :], func=AF.Exp, scale=-1.0),
              ["d2"], ["er"], tag=f"kb6{c}")
        b.dve(b.lp(lambda sl=sl: nc.vector.tensor_mul(kbar[:, :], kh[:, sl], er[:, :])),
              KH + ["er"], ["kbar"], tag=f"kb7{c}")
        b.transp(pbf[0][0:128, 0:64], kbar[:, :], id128[0:64, 0:64],
                 ["kbar", "id128"], [("pbf", 0)], tag=f"kt{c}")
        b.dve(b.lp(lambda: nc.vector.tensor_copy(kbt[:, :], pbf[0][0:128, 0:64])),
              [("pbf", 0)], ["kbt"], tag=f"ktc{c}")
        b.act(lambda: nc.scalar.activation(out=egl1[:, :], in_=Gc[:, 127:128], func=AF.Exp),
              ["Gc1"], ["eglc1"], tag=f"egl{c}")
        for q4 in range(4):
            b.mm(ps[2][0:64, 0:512], kbt[:, :], utm[:, q4 * 512:(q4 + 1) * 512], True, True,
                 ["kbt"] + UT, [("ps", 2)], tag=f"ds{c}_{q4}")
            b.dve(b.lp(lambda q4=q4: nc.vector.scalar_tensor_tensor(
                  out=S[:, q4 * 512:(q4 + 1) * 512], in0=S[:, q4 * 512:(q4 + 1) * 512],
                  scalar=egl1[:, :], in1=ps[2][0:64, 0:512], op0=ALU.mult, op1=ALU.add)),
                  [("S1k",), "eglc1", ("ps", 2)], [("S1k",)], tag=f"su{c}_{q4}")
    b.dma("qpool", H["cc3in"] if fused else sfin_h, S[:, :], [("S1k",)],
          [("dram", "cc3in" if fused else "s1fin")], tag="stS1")


def build_l1mix(nt=NT_FULL, debug_taps=False):
    b = B("l1mix")
    H = decl_l1mix(b, nt, fused=False)
    emit_l1mix(b, H, nt, fused=False)
    return b


def stage_l1mix(core, xn1, xn1_halo, inputs, nt=NT_FULL, s1init=None):
    cw = inputs["L1_attn_conv_w"]
    cd = np.zeros((16, 4, 128, 128), np.float32)
    idx = np.arange(128)
    for ci in range(16):
        for j in range(4):
            cd[ci, j, idx, idx] = cw[ci * 128:(ci + 1) * 128, j]
    inw = inputs["L1_attn_in_proj_w"]
    mgw = inputs["L1_attn_mem_gate_w"]
    mgb = inputs["L1_attn_mem_gate_b"]
    xn1f = np.concatenate([np.asarray(xn1_halo), np.asarray(xn1)], 1)
    return {
        "xn1f": np.ascontiguousarray(xn1f),
        "convd1": cd.reshape(16 * 4 * 128, 128),
        "convb": np.ascontiguousarray(inputs["L1_attn_conv_b"].reshape(16, 128).T),
        "inwr_t": np.ascontiguousarray(inw[:64].T), "inwk_t": np.ascontiguousarray(inw[64:].T),
        "mgws_t": np.ascontiguousarray(mgw[:64].T), "mgwt_t": np.ascontiguousarray(mgw[64:].T),
        "mgbs": mgb[:64].reshape(64, 1).astype(np.float32),
        "mgbt": mgb[64:].reshape(64, 1).astype(np.float32),
        "g0w_t": np.ascontiguousarray(inputs["L1_attn_ch_gate_w0"].T),
        "g1w_t": np.ascontiguousarray(inputs["L1_attn_ch_gate_w1"].T),
        "b1r": inputs["L1_attn_ch_gate_b1"].reshape(1, 2048).astype(np.float32),
        "resw": np.ascontiguousarray(inputs["L1_attn_residual_weight"].reshape(16, 128).T),
        "mask01": np.where(np.arange(128)[:, None] <= np.arange(128)[None, :], 1, 0).astype(np.int32),
        "id128b": np.eye(128, dtype=np.float32),
        "s1init": np.zeros((64, 2048), np.float32) if s1init is None else np.asarray(s1init, np.float32),
    }


def decl_l1ffn(b, nt, fused):
    H = {
        "wscg": b.dram("wscg", [128, 64], F32, kind="ExternalInput"),
        "wscu": b.dram("wscu", [128, 64], F32, kind="ExternalInput"),
        "wscd": b.dram("wscd", [128, 16], F32, kind="ExternalInput"),
        "outT": b.dram("outT", [2048, nt], F32, kind="ExternalOutput"),
    }
    if not fused:
        H["xn1"] = b.dram("xn1", [2048, nt], BF16, kind="ExternalInput")
        H["x1T"] = b.dram("x1T", [2048, nt], F32, kind="ExternalInput")
        H["s1T"] = b.dram("s1T", [2048, nt], BF16, kind="ExternalInput")
        H["fgt"] = b.dram("wgt_t", [2048, 8192], BF16, kind="ExternalInput")
        H["fup"] = b.dram("wup_t", [2048, 8192], BF16, kind="ExternalInput")
        H["fdn"] = b.dram("wdn_t", [8192, 2048], BF16, kind="ExternalInput")
    return H


def emit_l1ffn(b, H, nt, fused):
    nc = b.nc
    xn_h, x1_h, s1_h = H["xn1"], H["x1T"], H["s1T"]
    wgt_h, wup_h, wdn_h = H["fgt"], H["fup"], H["fdn"]
    wscg_h, wscu_h, wscd_h = H["wscg"], H["wscu"], H["wscd"]
    out_h = H["outT"]
    gk = list(getattr(b, "gatekeys", [])) if fused else []

    def ug_src(h, m0):
        if h.shape[0] == 2048:
            return h[:, m0:m0 + 128]
        s, oc = divmod(m0, 1024)
        return h[s * 2048:(s + 1) * 2048, oc:oc + 128]

    def dn_src(m0):
        if wdn_h.shape[0] == 8192:
            return wdn_h[:, m0:m0 + 128]
        s, oc = divmod(m0, 256)
        return wdn_h[s * 8192:(s + 1) * 8192, oc:oc + 128]

    xns = b.sbuf("xns", [128, 16 * 512], BF16)
    hsb = b.sbuf("hsb", [128, 64 * 512], BF16)
    wg = [b.sbuf("wg0", [128, 16 * 128], BF16)] * 2
    wu = [b.sbuf("wu0", [128, 16 * 128], BF16)] * 2
    wp = [b.sbuf("wp0", [128, 64 * 128], BF16)] * 2
    wscg = b.sbuf("wscg_s", [128, 64], F32)
    wscu = b.sbuf("wscu_s", [128, 64], F32)
    wscd = b.sbuf("wscd_s", [128, 16], F32)
    sg = b.sbuf("sg", [128, 512], BF16)
    su = b.sbuf("su", [128, 512], BF16)
    g1c = b.sbuf("g1c", [128, 512], BF16)
    s1c = b.sbuf("s1c", [128, 512], BF16)
    x1c = b.sbuf("x1c", [128, 512], F32)
    oc = b.sbuf("oc", [128, 512], F32)
    t2c = b.sbuf("t2c", [128, 512], BF16)
    ps = [b.psum(f"ps{i}", (128, 512), F32) for i in range(6)]
    b.dma("qpool", wscg[:, :], wscg_h, [], ["wscg"])
    b.dma("qpool", wscu[:, :], wscu_h, [], ["wscu"])
    b.dma("qpool", wscd[:, :], wscd_h, [], ["wscd"])
    if fused:
        sprev = b.sbuf("sprev", [64, 2048], BF16)
        rdt = b.sbuf("rdt", [64, 512], BF16)
        s1u = b.sbuf("s1u", [128, 512], BF16)
        b.dma("qpool", sprev[:, :], H["sprevd"], [("dram", "sprevd")], ["sprev"])

    ntile = ceil_div(nt, 512)
    for ti in range(ntile):
        off = ti * 512
        w = min(512, nt - off)
        b.dma("qpool", xns[:, 0:16 * w].rearrange("p (c t) -> p c t", c=16),
              xn_h.rearrange("(c p) t -> p c t", p=128)[:, :, off:off + w],
              [("dram", "xn1")], ["xns"], tag=f"lxn{ti}")
        for g in range(64):
            b.dma("qsp", wg[0][:, :].rearrange("p (c n) -> p c n", c=16),
                  ug_src(wgt_h, g * 128).rearrange("(c p) n -> p c n", p=128),
                  gk, [("wg", 0)], tag=f"lwg{ti}_{g}")
            b.dma("qact", wu[0][:, :].rearrange("p (c n) -> p c n", c=16),
                  ug_src(wup_h, g * 128).rearrange("(c p) n -> p c n", p=128),
                  gk, [("wu", 0)], tag=f"lwu{ti}_{g}")
            for cc in range(16):
                b.mm(ps[2][:, 0:w], wg[0][:, cc * 128:(cc + 1) * 128], xns[:, cc * w:(cc + 1) * w],
                     start=(cc == 0), stop=(cc == 15), reads=[("wg", 0), "xns"], writes=[("ps", 2)],
                     tag=f"gm{ti}_{g}_{cc}")
                b.mm(ps[3][:, 0:w], wu[0][:, cc * 128:(cc + 1) * 128], xns[:, cc * w:(cc + 1) * w],
                     start=(cc == 0), stop=(cc == 15), reads=[("wu", 0), "xns"], writes=[("ps", 3)],
                     tag=f"um{ti}_{g}_{cc}")
            b.dve(b.lp(lambda w=w, g=g: nc.vector.tensor_scalar(out=sg[:, 0:w], in0=ps[2][:, 0:w],
                  scalar1=wscg[:, g:g + 1], scalar2=None, op0=ALU.mult)),
                  [("ps", 2), "wscg"], ["sg"], tag=f"gs{ti}_{g}")
            b.act(b.lp(lambda w=w: nc.scalar.activation(out=sg[:, 0:w], in_=sg[:, 0:w],
                  func=AF.Silu)), ["sg"], ["sg"], tag=f"gsl{ti}_{g}")
            b.dve(b.lp(lambda w=w, g=g: nc.vector.tensor_scalar(out=su[:, 0:w], in0=ps[3][:, 0:w],
                  scalar1=wscu[:, g:g + 1], scalar2=None, op0=ALU.mult)),
                  [("ps", 3), "wscu"], ["su"], tag=f"us{ti}_{g}")
            b.pool(b.lp(lambda g=g, w=w: nc.gpsimd.tensor_mul(hsb[:, g * w:(g + 1) * w],
                   sg[:, 0:w], su[:, 0:w])), ["sg", "su"], [("h", g)], tag=f"hm{ti}_{g}")
        for f in range(16):
            b.dma("qpool", wp[0][:, :].rearrange("p (c n) -> p c n", c=64),
                  dn_src(f * 128).rearrange("(c p) n -> p c n", p=128),
                  gk, [("wp", 0)], tag=f"lwp{ti}_{f}")
            pb, kp = ps[f % 2], ("ps", f % 2)
            for cc in range(64):
                b.mm(pb[:, 0:w], wp[0][:, cc * 128:(cc + 1) * 128], hsb[:, cc * w:(cc + 1) * w],
                     start=(cc == 0), stop=(cc == 63), reads=[("wp", 0), ("h", cc)], writes=[kp],
                     tag=f"dm{ti}_{f}_{cc}")
            b.dve(b.lp(lambda w=w, pb=pb, f=f: nc.vector.tensor_scalar(out=t2c[:, 0:w],
                  in0=pb[:, 0:w], scalar1=wscd[:, f:f + 1], scalar2=None, op0=ALU.mult)),
                  [kp, "wscd"], ["t2cp"], tag=f"dn{ti}_{f}")
            b.act(b.lp(lambda w=w: nc.scalar.activation(out=g1c[:, 0:w], in_=t2c[:, 0:w],
                  func=AF.Silu)), ["t2cp"], ["g1c"], tag=f"g1{ti}_{f}")
            b.dma("qsp", s1c[:, 0:w], s1_h[f * 128:(f + 1) * 128, off:off + w],
                  [("dram", "s1T")], ["s1c"], tag=f"ls1{ti}_{f}")
            b.dma("qact", x1c[:, 0:w], x1_h[f * 128:(f + 1) * 128, off:off + w],
                  [("dram", "x1")], ["x1c"], tag=f"lx1{ti}_{f}")
            if fused:
                if f == 0:
                    b.dma("qpool", rdt[:, 0:w], H["rdall"][:, off:off + w],
                          [("dram", "rdall")], ["rdt"], tag=f"lrd{ti}")
                b.mm(ps[4][:, 0:w], sprev[:, f * 128:(f + 1) * 128], rdt[:, 0:w], True, True,
                     ["sprev", "rdt"], [("ps", 4)], tag=f"cor{ti}_{f}")
                b.dve(b.lp(lambda w=w: nc.vector.scalar_tensor_tensor(out=s1u[:, 0:w],
                      in0=ps[4][:, 0:w], scalar=1.0, in1=s1c[:, 0:w], op0=ALU.mult,
                      op1=ALU.add)), [("ps", 4), "s1c"], ["s1uk"], tag=f"corr{ti}_{f}")
                s1use, s1key = s1u, "s1uk"
            else:
                s1use, s1key = s1c, "s1c"
            b.dve(b.lp(lambda w=w, s1use=s1use: nc.vector.tensor_mul(t2c[:, 0:w],
                  s1use[:, 0:w], g1c[:, 0:w])), [s1key, "g1c"], ["t2cp"], tag=f"sm{ti}_{f}")
            b.dve(lambda w=w: nc.vector.tensor_add(oc[:, 0:w], x1c[:, 0:w], t2c[:, 0:w]),
                  ["x1c", "t2cp"], ["oc"], tag=f"oa{ti}_{f}")
            b.dma("qsp", out_h[f * 128:(f + 1) * 128, off:off + w], oc[:, 0:w], ["oc"], [],
                  tag=f"so{ti}_{f}")


def build_l1ffn(nt=NT_FULL):
    b = B("l1ffn")
    H = decl_l1ffn(b, nt, fused=False)
    emit_l1ffn(b, H, nt, fused=False)
    return b


def stage_l1ffn(core, xn1, x1T, s1T, prep, inputs, nt=NT_FULL):
    wsc = inputs["L1_ffn_Wscale_ug"].astype(np.float32)
    wd = {} if prep is None else {"wgt_t": prep["wgt_t"], "wup_t": prep["wup_t"],
                                  "wdn_t": prep["wdn_t"],
                                  "xn1": np.asarray(xn1), "x1T": np.asarray(x1T),
                                  "s1T": np.asarray(s1T)}
    return {
        **wd,
        "wscg": np.ascontiguousarray((wsc[8192:] / np.sqrt(2048.0)).reshape(64, 128).T),
        "wscu": np.ascontiguousarray((wsc[:8192] / np.sqrt(2048.0)).reshape(64, 128).T),
        "wscd": np.ascontiguousarray((inputs["L1_ffn_Wscale_dp"].astype(np.float32)
                                      / np.sqrt(8192.0)).reshape(16, 128).T),
    }


def stage_l0pre(core, inputs, nt=NT_FULL):
    bidx, half = core // 2, core % 2
    x = inputs["L0_attn_hidden_states"][bidx]
    lo = half * nt
    sl = x[lo:lo + nt]
    if half == 0:
        halov = np.zeros((HALO, 2048), np.float32)
    else:
        halov = x[lo - HALO:lo]
    xT = np.ascontiguousarray(np.concatenate([halov, sl], 0).T)
    cw = inputs["L0_attn_conv_w"][:, 0, :]
    cd = np.zeros((64, 4, 128, 128), np.float32)
    idx = np.arange(128)
    for c in range(64):
        for j in range(4):
            cd[c, j, idx, idx] = cw[c * 128:(c + 1) * 128, j]
    return {
        "xT": xT,
        "wqkvzT": np.ascontiguousarray(inputs["L0_attn_W_qkvz"].T),
        "wbaT": np.ascontiguousarray(inputs["L0_attn_W_ba"].T),
        "convw_diag": cd.reshape(64 * 4 * 128, 128),
    }


_PROGS = {}

GROUPS = [[0, 1], [2, 3], [4, 5], [6, 7]]


def build_fused(nt=NT_FULL):
    b = B("fused")
    nc = b.nc
    H = {"sel": b.dram("sel", [128, 1], F32, kind="ExternalInput")}
    b.prefix = "d_"
    H.update(decl_prep(b, fused=True))
    H.update(decl_l0pre(b, nt, fused=True))
    H.update(decl_l0delta(b, nt, fused=True))
    H.update(decl_combine0(b, nt, fused=True))
    H.update(decl_l1mix(b, nt, fused=True))
    H.update(decl_l1ffn(b, nt, fused=True))
    H["cc2in"] = b.dram("cc2in", [2048, HALO], F32)
    H["cc2out"] = b.dram("cc2out", [4096, HALO], F32)

    b.prefix = "p0_"
    b.phase_start()
    emit_prep(b, H, fused=True)
    b.phase_end()
    b.prefix = "p1_"
    b.phase_start()
    emit_l0pre(b, H, nt)
    b.phase_end()
    b.prefix = "p2_"
    b.phase_start()
    emit_l0delta(b, H, nt, fused=True, groups=GROUPS)
    b.phase_end()
    b.prefix = "p3_"
    b.phase_start()
    b.gatekeys = ["prepgate"]
    prep_gate(b)
    emit_combine0(b, H, nt)
    t3 = b.sbuf("t3", [128, 48], F32)
    selg = b.sbuf("selg", [128, 1], F32)
    b.dma("qpool", selg[:, :], H["sel"], [], ["selg"])
    b.dma("qpool", t3[:, :].rearrange("p (c t) -> p c t", c=16),
          H["xn1"].rearrange("(c p) t -> p c t", p=128)[:, :, nt - HALO:nt],
          [("dram", "xn1")], ["t3"], tag="g2l")
    b.dma("qsp", H["cc2in"].rearrange("(c p) t -> p c t", p=128), t3[:, :].rearrange(
          "p (c t) -> p c t", c=16), ["t3"], [("dram", "cc2in")], tag="g2s")
    b.cc("AllGather", ALU.bypass, GROUPS, [H["cc2in"].opt()], [H["cc2out"].opt()],
         [("dram", "cc2in")], [("dram", "cc2out")], tag="ccG2")
    b.dma("qpool", t3[:, :].rearrange("p (c t) -> p c t", c=16),
          H["cc2out"][0:2048, :].rearrange("(c p) t -> p c t", p=128),
          [("dram", "cc2out")], ["t3"], tag="g2l2")
    b.dve(b.lp(lambda: nc.vector.tensor_scalar(out=t3[:, :], in0=t3[:, :], scalar1=selg[:, :],
          scalar2=None, op0=ALU.mult)), ["t3", "selg"], ["t3"], tag="g2m")
    b.dma("qpool", H["halo3"].rearrange("(c p) t -> p c t", p=128),
          t3[:, :].rearrange("p (c t) -> p c t", c=16), ["t3"], [("dram", "halo3")], tag="g2o")
    b.phase_end()
    b.prefix = "p4_"
    b.phase_start()
    emit_l1mix(b, H, nt, fused=True)
    b.phase_end()
    b.prefix = "p4b_"
    b.phase_start()
    sst = b.sbuf("sst", [64, 512], F32)
    selg3 = b.sbuf("selg3", [128, 1], F32)
    b.dma("qpool", selg3[:, :], H["sel"], [], ["selg3"])
    b.cc("AllGather", ALU.bypass, GROUPS, [H["cc3in"].opt()], [H["cc3out"].opt()],
         [("dram", "cc3in")], [("dram", "cc3out")], tag="ccG3")
    for q4 in range(4):
        b.dma("qpool", sst[:, :], H["cc3out"][0:64, q4 * 512:(q4 + 1) * 512],
              [("dram", "cc3out")], ["sst"], tag=f"g3l{q4}")
        b.dve(lambda: nc.vector.tensor_scalar(out=sst[:, :], in0=sst[:, :], scalar1=selg3[0:64, :],
              scalar2=None, op0=ALU.mult), ["sst", "selg3"], ["sst"], tag=f"g3m{q4}")
        b.dma("qsp", H["sprevd"][:, q4 * 512:(q4 + 1) * 512], sst[:, :], ["sst"],
              [("dram", "sprevd")], tag=f"g3o{q4}")
    b.phase_end()
    b.prefix = "p5_"
    b.phase_start()
    prep_gate(b)
    emit_l1ffn(b, H, nt, fused=True)
    b.phase_end()
    return b


def stage_fused(core, inputs, nt=NT_FULL):
    d = {}
    d.update(stage_prep(core, inputs))
    d.update(stage_l0pre(core, inputs, nt=nt))
    dd = stage_l0delta(core, {"mixed": 0, "zs": 0, "bb_f": 0, "aa_f": 0}, inputs, nt=nt)
    for k in ("mixed", "zs", "bb_f", "aa_f", "sinit"):
        dd.pop(k, None)
    d.update(dd)
    bidx, half = core // 2, core % 2
    x = inputs["L0_attn_hidden_states"][bidx]
    lo = half * nt
    d.update({"woutT": np.ascontiguousarray(inputs["L0_attn_W_out"].T),
              "x0T": np.ascontiguousarray(x[lo:lo + nt].T)})
    dm = stage_l1mix(core, np.zeros((2048, 1), NPBF16), np.zeros((2048, 1), NPBF16),
                     inputs, nt=nt)
    for k in ("xn1f", "s1init"):
        dm.pop(k, None)
    d.update(dm)
    df = stage_l1ffn(core, 0, 0, 0, None, inputs, nt=nt)
    for k in ("xn1", "x1T", "s1T", "wgt_t", "wup_t", "wdn_t"):
        df.pop(k, None)
    d.update(df)
    d["sel"] = np.full((128, 1), float(core % 2), np.float32)
    return d


def kernel(**inputs):
    cores = list(range(8))
    if "fused" not in _PROGS:
        b = build_fused(NT_FULL)
        b.done()
        _PROGS["fused"] = b.nc
    nc = _PROGS["fused"]
    res = run_bass_kernel_spmd(nc, [stage_fused(c, inputs) for c in cores], core_ids=cores)
    out = np.empty((4, 4096, 2048), np.float32)
    for c in cores:
        bi, half = c // 2, c % 2
        out[bi, half * 2048:(half + 1) * 2048, :] = np.asarray(res.results[c]["outT"],
                                                              np.float32).T
    return out




# version 08
# speedup vs baseline: 1.4287x; 1.0128x over previous
import numpy as np
import ml_dtypes
import concourse.bass as bass
import concourse.mybir as mybir
from concourse.bass_utils import run_bass_kernel_spmd

F32 = mybir.dt.float32
BF16 = mybir.dt.bfloat16
AF = mybir.ActivationFunctionType
ALU = mybir.AluOpType
AX = mybir.AxisListType
NPBF16 = ml_dtypes.bfloat16

_ENGINE_OF = {"pe": "tensor", "dve": "vector", "act": "scalar", "pool": "gpsimd",
              "qsp": "sync", "qact": "scalar", "qpool": "gpsimd"}
_DMA_Q = ("qsp", "qact", "qpool")


class Plan:

    def __init__(self, nc):
        self.nc = nc
        self.ops = []
        self.last_write = {}
        self.readers = {}
        self.n_emitted = 0

    def op(self, engine, emit, reads=(), writes=(), tag=""):
        idx = len(self.ops)
        deps = set()
        for k in reads:
            if k in self.last_write:
                deps.add(self.last_write[k])
        for k in writes:
            if k in self.last_write:
                deps.add(self.last_write[k])
            for r in self.readers.get(k, ()):
                deps.add(r)
        for k in reads:
            self.readers.setdefault(k, []).append(idx)
        for k in writes:
            self.last_write[k] = idx
            self.readers[k] = []
        self.ops.append({"engine": engine, "emit": emit, "deps": deps, "tag": tag,
                         "inc": False, "seq": None})
        return idx

    def barrier(self):
        idx = len(self.ops)
        deps = set(range(idx))
        self.ops.append({"engine": "barrier", "emit": None, "deps": deps, "tag": "barrier",
                         "inc": False, "seq": None})
        self.last_write = {}
        self.readers = {}
        self._barrier_idx = idx
        return idx

    def finalize(self):
        nc = self.nc
        ops = self.ops
        n = len(ops)
        last_barrier = -1
        for i, o in enumerate(ops):
            if o["engine"] == "barrier":
                last_barrier = i
            else:
                o["deps"] = {d for d in o["deps"] if d > last_barrier}
                if last_barrier >= 0:
                    o["deps"].add(last_barrier)
        for i, o in enumerate(ops):
            if o["engine"] == "barrier":
                continue
            pruned = set()
            for d in o["deps"]:
                od = ops[d]
                if od["engine"] == "barrier":
                    pruned.add(d)
                    continue
                if od["engine"] == "pe" and o["engine"] == "pe":
                    continue
                pruned.add(d)
            o["deps"] = pruned
        for o in ops:
            if o["engine"] in _DMA_Q:
                o["inc"] = True
            if o["engine"] == "barrier":
                continue
            for d in o["deps"]:
                ops[d]["inc"] = True
        NS = 8
        comp_sems = {e: self.nc.alloc_semaphore(f"sem_{e}")
                     for e in _ENGINE_OF if e not in _DMA_Q}
        dma_sems = {q: [self.nc.alloc_semaphore(f"sem_{q}{i}") for i in range(NS)]
                    for q in _DMA_Q}
        counters = {e: 0 for e in _ENGINE_OF if e not in _DMA_Q}
        qidx = {q: 0 for q in _DMA_Q}
        waited = {}

        def sem_of(o):
            if o["engine"] in _DMA_Q:
                q, i = o["engine"], o["qslot"]
                return dma_sems[q][i % NS], 16 * (1 + i // NS)
            return comp_sems[o["engine"]], o["seq"]

        def emit_wait(weng, sem, val):
            key = (weng, sem.name)
            if waited.get(key, 0) >= val:
                return
            waited[key] = val
            getattr(nc, _ENGINE_OF[weng]).wait_ge(sem, val)

        for i, o in enumerate(ops):
            if o["engine"] == "barrier":
                for q in _DMA_Q:
                    for j in range(max(0, qidx[q] - NS), qidx[q]):
                        emit_wait(q, dma_sems[q][j % NS], 16 * (1 + j // NS))
                nc.all_engine_barrier()
                for e, c in counters.items():
                    for w in _ENGINE_OF:
                        waited[(w, comp_sems[e].name)] = c
                for q in _DMA_Q:
                    for j in range(qidx[q]):
                        for w in _ENGINE_OF:
                            key = (w, dma_sems[q][j % NS].name)
                            waited[key] = max(waited.get(key, 0), 16 * (1 + j // NS))
                o["seq"] = None
                continue
            eng = o["engine"]
            for d in sorted(o["deps"]):
                od = ops[d]
                if od["engine"] == "barrier":
                    continue
                sem, val = sem_of(od)
                emit_wait(eng, sem, val)
            if eng in _DMA_Q:
                j = qidx[q := eng]
                if j >= NS:
                    emit_wait(q, dma_sems[q][j % NS], 16 * (j // NS))
                o["qslot"] = j
                qidx[q] += 1
            instr = o["emit"]()
            if eng in _DMA_Q:
                sem, val = sem_of(o)
                instr.then_inc(sem, 16)
                o["seq"] = val
            elif o["inc"]:
                counters[eng] += 1
                o["seq"] = counters[eng]
                if instr is None:
                    raise RuntimeError(f"op {o['tag']} needs inc but emit returned None")
                instr.then_inc(comp_sems[eng], 1)
            else:
                o["seq"] = counters[eng]
            self.n_emitted += 1
        for q in _DMA_Q:
            for j in range(max(0, qidx[q] - NS), qidx[q]):
                emit_wait(q, dma_sems[q][j % NS], 16 * (1 + j // NS))
        return self


class B:

    def __init__(self, name):
        self.nc = bass.Bass("TRN2", target_bir_lowering=False)
        self.plan = Plan(self.nc)
        self._ctx = []
        self.sbuf_bytes = 0
        self._psum_pool = []
        self._uid = 0
        self._consts = {}

    def constcol(self, value, parts, dtype=F32):
        key = ("cst", value, parts, str(dtype))
        if key not in self._consts:
            t = self.sbuf(f"cst{self.uid()}", [parts, 1], dtype)
            self.plan.op("pool", lambda t=t, v=value: self.nc.gpsimd.memset(t[:, :], v),
                         [], [key], tag="const")
            self._consts[key] = t
        return self._consts[key], key

    def uid(self):
        self._uid += 1
        return self._uid

    def dram(self, name, shape, dtype=F32, kind="Internal"):
        return self.nc.dram_tensor(name, list(shape), dtype, kind=kind).ap()

    def sbuf(self, name, shape, dtype=F32):
        cm = self.nc.sbuf_tensor(getattr(self, "prefix", "") + name, list(shape), dtype)
        t = cm.__enter__()
        self._ctx.append(cm)
        self.sbuf_bytes += shape[0] * int(np.prod(shape[1:])) * (2 if dtype == BF16 else 4)
        return t

    def psum(self, name, shape=(128, 512), dtype=F32):
        cm = self.nc.psum_tensor(getattr(self, "prefix", "") + name, list(shape), dtype)
        t = cm.__enter__()
        self._ctx.append(cm)
        return t

    def close(self):
        for cm in reversed(self._ctx):
            cm.__exit__(None, None, None)
        self._ctx = []

    def dma(self, q, out, in_, reads, writes, tag="", transpose=False):
        eng = getattr(self.nc, _ENGINE_OF[q])
        if transpose:
            return self.plan.op(q, lambda: eng.dma_start(out=out, in_=in_, transpose=True),
                                reads, writes, tag)
        return self.plan.op(q, lambda: eng.dma_start(out=out, in_=in_), reads, writes, tag)

    def mm(self, out, lhsT, rhs, start, stop, reads, writes, tag="", tile_position=None):
        t = self.nc.tensor
        return self.plan.op("pe", lambda: t.matmul(out, lhsT, rhs, start=start, stop=stop,
                                                   tile_position=tile_position),
                            reads, writes, tag)

    def transp(self, out, in_, iden, reads, writes, tag=""):
        t = self.nc.tensor
        return self.plan.op("pe", lambda: t.transpose(out, in_, iden), reads, writes, tag)

    def dve(self, fn, reads, writes, tag=""):
        return self.plan.op("dve", fn, reads, writes, tag)

    def act(self, fn, reads, writes, tag=""):
        return self.plan.op("act", fn, reads, writes, tag)

    def pool(self, fn, reads, writes, tag=""):
        return self.plan.op("pool", fn, reads, writes, tag)

    def bar(self):
        return self.plan.barrier()

    def lp(self, fn):
        def emit():
            with self.nc.allow_low_precision(reason="intentional bf16 storage"):
                return fn()
        return emit

    def phase_start(self):
        self._phase_mark = len(self._ctx)

    def phase_end(self):
        self.plan.barrier()
        for cm in reversed(self._ctx[self._phase_mark:]):
            cm.__exit__(None, None, None)
        del self._ctx[self._phase_mark:]
        self._consts = {}

    def cc(self, kind, op, groups, ins, outs, reads, writes, tag=""):
        nc = self.nc
        return self.plan.op("pool", lambda: nc.gpsimd.collective_compute(
            kind, op, replica_groups=groups, ins=ins, outs=outs), reads, writes, tag)

    def done(self):
        self.plan.finalize()
        self.close()
        return self.nc



def ceil_div(a, b):
    return (a + b - 1) // b


NT_FULL = 2048
HALO = 3


def token_tiles(nt, halo=HALO, tile=512):
    tiles = [(0, halo)] if halo else []
    off = halo
    while off < nt + halo:
        w = min(tile, nt + halo - off)
        tiles.append((off, w))
        off += w
    return tiles


def decl_l0pre(b, nt, fused):
    T = nt + HALO
    mid = "Internal" if fused else "ExternalOutput"
    return {
        "xT": b.dram("xT", [2048, T], F32, kind="ExternalInput"),
        "wqkvzT": b.dram("wqkvzT", [2048, 12288], F32, kind="ExternalInput"),
        "wbaT": b.dram("wbaT", [2048, 64], F32, kind="ExternalInput"),
        "convd": b.dram("convw_diag", [64 * 4 * 128, 128], F32, kind="ExternalInput"),
        "mixed": b.dram("mixed", [8192, nt], BF16, kind=mid),
        "zs": b.dram("zs", [4096, nt], BF16, kind=mid),
        "xn": b.dram("xn", [2048, nt], BF16, kind=mid),
        "bb_f": b.dram("bb_f", [32, nt], F32, kind=mid),
        "aa_f": b.dram("aa_f", [32, nt], F32, kind=mid),
    }


def emit_l0pre(b, H, nt):
    T = nt + HALO
    nc = b.nc
    xT, wqkvzT, wbaT, convd = H["xT"], H["wqkvzT"], H["wbaT"], H["convd"]
    mixed, zs, xn_hbm = H["mixed"], H["zs"], H["xn"]
    bb_f, aa_f = H["bb_f"], H["aa_f"]

    xn_sb = b.sbuf("xn_sb", [128, 16 * T], BF16)
    xb = [b.sbuf(f"xb{i}", [128, 16 * 512], BF16) for i in range(2)]
    sq = b.sbuf("sq", [128, 512], BF16)
    ones128 = b.sbuf("ones128", [128, 1], BF16)
    ones1 = b.sbuf("ones1", [1, 128], BF16)
    stats = b.sbuf("stats", [1, 6 * 512], F32)
    rr = b.sbuf("rr", [1, 512], BF16)
    rm = b.sbuf("rm", [1, 512], BF16)
    wt = [b.sbuf(f"wt{i}", [128, 16 * 128], BF16) for i in range(2)]
    pre = b.sbuf("pre", [128, T], BF16)
    cw = b.sbuf("cw", [128, 4 * 128], BF16)
    stg = b.sbuf("stg", [128, 512], BF16)
    bastg = b.sbuf("bastg", [64, 512], F32)
    ps = [b.psum(f"ps{i}", (128, 512), F32) for i in range(6)]

    b.pool(lambda: nc.gpsimd.memset(ones128[:, :], 1.0), [], ["ones128"])
    b.pool(lambda: nc.gpsimd.memset(ones1[:, :], 1.0), [], ["ones1"])

    tiles = token_tiles(nt)
    for ti, (off, w) in enumerate(tiles):
        xbuf = xb[ti % 2]
        kx = ("xb", ti % 2)
        b.dma("qpool", xbuf[:, 0:16 * w].rearrange("p (c t) -> p c t", c=16),
              xT.rearrange("(c p) t -> p c t", p=128)[:, :, off:off + w],
              reads=[], writes=[kx], tag=f"ldx{ti}")
        pm, pq = ps[0], ps[1]
        for c in range(16):
            xc = xbuf[:, c * w:(c + 1) * w]
            b.mm(pm[0:1, 0:w], ones128[:, :], xc, start=(c == 0), stop=(c == 15),
                 reads=[kx, "ones128"], writes=[("ps", 0)], tag=f"sum{ti}_{c}")
            eng = b.dve if c % 2 == 0 else b.pool
            eng((lambda xc=xc, w=w: nc.vector.tensor_mul(sq[:, 0:w], xc, xc)) if c % 2 == 0 else
                (lambda xc=xc, w=w: nc.gpsimd.tensor_mul(sq[:, 0:w], xc, xc)),
                reads=[kx], writes=["sq"], tag=f"sq{ti}_{c}")
            b.mm(pq[0:1, 0:w], ones128[:, :], sq[:, 0:w], start=(c == 0), stop=(c == 15),
                 reads=["sq", "ones128"], writes=[("ps", 1)], tag=f"sumq{ti}_{c}")
        mean = stats[:, 0:w]
        ex2 = stats[:, 512:512 + w]
        msq = stats[:, 1024:1024 + w]
        var = stats[:, 1536:1536 + w]
        sd = stats[:, 2048:2048 + w]
        b.dve(lambda w=w, o=mean: nc.vector.tensor_scalar(out=o, in0=ps[0][0:1, 0:w],
              scalar1=1.0 / 2048, scalar2=None, op0=ALU.mult), [("ps", 0)], [("st", "m")])
        b.dve(lambda w=w, o=ex2: nc.vector.tensor_scalar(out=o, in0=ps[1][0:1, 0:w],
              scalar1=1.0 / 2048, scalar2=None, op0=ALU.mult), [("ps", 1)], [("st", "e")])
        b.dve(lambda o=msq, a=mean: nc.vector.tensor_mul(o, a, a), [("st", "m")], [("st", "ms")])
        b.dve(lambda o=var, a=ex2, c2=msq: nc.vector.tensor_sub(o, a, c2),
              [("st", "e"), ("st", "ms")], [("st", "v")])
        epsc, epsk = b.constcol(1e-5, 1)
        b.act(lambda o=sd, a=var, e=epsc: nc.scalar.activation(out=o, in_=a, func=AF.Sqrt, bias=e[:, :]),
              [("st", "v"), epsk], [("st", "sd")])
        b.dve(b.lp(lambda w=w, a=sd: nc.vector.reciprocal(rr[:, 0:w], a)), [("st", "sd")], ["rr"])
        b.dve(b.lp(lambda w=w, a=mean: nc.vector.tensor_mul(rm[:, 0:w], a, rr[:, 0:w])),
              [("st", "m"), "rr"], ["rm"])
        pR, pM = ps[2], ps[3]
        b.mm(pR[:, 0:w], ones1[:, :], rr[:, 0:w], True, True, ["ones1", "rr"], [("ps", 2)])
        b.mm(pM[:, 0:w], ones1[:, :], rm[:, 0:w], True, True, ["ones1", "rm"], [("ps", 3)])
        for c in range(16):
            xc = xbuf[:, c * w:(c + 1) * w]
            xnc = xn_sb[:, c * T + off:c * T + off + w]
            b.dve(lambda xc=xc, w=w: nc.vector.tensor_mul(sq[:, 0:w], xc, pR[0:128, 0:w]),
                  [kx, ("ps", 2)], ["sq"], tag=f"xnm{ti}_{c}")
            b.dve(lambda xnc=xnc, w=w: nc.vector.tensor_sub(xnc, sq[:, 0:w], pM[0:128, 0:w]),
                  ["sq", ("ps", 3)], [("xn", c)], tag=f"xns{ti}_{c}")

    for c in range(16):
        b.dma("qsp", xn_hbm[c * 128:(c + 1) * 128, :], xn_sb[:, c * T + HALO:(c + 1) * T],
              reads=[("xn", c)], writes=[], tag=f"stxn{c}")

    for f in range(97):
        wbuf = wt[f % 2]
        kw = ("wt", f % 2)
        if f < 96:
            b.dma("qpool", wbuf[:, :].rearrange("p (c n) -> p c n", c=16),
                  wqkvzT[:, f * 128:(f + 1) * 128].rearrange("(c p) n -> p c n", p=128),
                  reads=[], writes=[kw], tag=f"ldw{f}")
        else:
            b.dma("qpool", wbuf[:, 0:16 * 64].rearrange("p (c n) -> p c n", c=16),
                  wbaT.rearrange("(c p) n -> p c n", p=128),
                  reads=[], writes=[kw], tag="ldwba")
        nout = 128 if f < 96 else 64
        if f < 64:
            b.dma("qpool", cw[:, :].rearrange("p (j n) -> p j n", j=4),
                  convd.rearrange("(c j p) n -> c p j n", j=4, p=128)[f],
                  reads=[], writes=["cw"], tag=f"ldcw{f}")
        for ti, (off, w) in enumerate(tiles):
            pb = ps[ti % 2]
            kp = ("ps", ti % 2)
            for c in range(16):
                b.mm(pb[0:nout, 0:w], wbuf[:, c * nout:(c + 1) * nout],
                     xn_sb[:, c * T + off:c * T + off + w],
                     start=(c == 0), stop=(c == 15),
                     reads=[kw, ("xn", c)], writes=[kp], tag=f"pj{f}_{ti}_{c}")
            if f < 64:
                b.dve(lambda pb=pb, off=off, w=w: nc.vector.tensor_copy(pre[:, off:off + w], pb[:, 0:w]),
                      [kp], [("pre", ti)], tag=f"pc{f}_{ti}")
            elif f < 96:
                if off >= HALO:
                    b.act(lambda pb=pb, w=w: nc.scalar.activation(out=stg[:, 0:w], in_=pb[:, 0:w],
                          func=AF.Silu), [kp], ["stg"], tag=f"zs{f}_{ti}")
                    b.dma("qsp", zs[(f - 64) * 128:(f - 63) * 128, off - HALO:off - HALO + w],
                          stg[:, 0:w], reads=["stg"], writes=[], tag=f"stz{f}_{ti}")
            else:
                if off >= HALO:
                    b.dve(lambda pb=pb, w=w: nc.vector.tensor_copy(bastg[:, 0:w], pb[0:64, 0:w]),
                          [kp], ["bastg"], tag=f"bac{ti}")
                    b.dma("qsp", bb_f[:, off - HALO:off - HALO + w], bastg[0:32, 0:w],
                          reads=["bastg"], writes=[("dram", "bb")], tag=f"stbb{ti}")
                    b.dma("qsp", aa_f[:, off - HALO:off - HALO + w], bastg[32:64, 0:w],
                          reads=["bastg"], writes=[("dram", "aa")], tag=f"stba{ti}")
        if f < 64:
            for ti, (off, w) in enumerate(tiles):
                if off < HALO:
                    continue
                pb = ps[2 + ti % 2]
                kp = ("ps", 2 + ti % 2)
                for j in range(4):
                    b.mm(pb[:, 0:w], cw[:, j * 128:(j + 1) * 128],
                         pre[:, off - 3 + j:off - 3 + j + w], start=(j == 0), stop=(j == 3),
                         reads=["cw"] + [("pre", t2) for t2 in range(len(tiles))],
                         writes=[kp], tag=f"cv{f}_{ti}_{j}")
                b.act(lambda pb=pb, w=w: nc.scalar.activation(out=stg[:, 0:w], in_=pb[:, 0:w],
                      func=AF.Silu), [kp], ["stg"], tag=f"cvs{f}_{ti}")
                b.dma("qsp", mixed[f * 128:(f + 1) * 128, off - HALO:off - HALO + w],
                      stg[:, 0:w], reads=["stg"], writes=[("dram", "mixed")], tag=f"stm{f}_{ti}")


def build_l0pre(nt=NT_FULL):
    b = B("l0pre")
    H = decl_l0pre(b, nt, fused=False)
    emit_l0pre(b, H, nt)
    return b


LN128H = float(-0.5 * np.log(128.0))


def decl_l0delta(b, nt, fused):
    H = {
        "alog": b.dram("alog", [32, 1], F32, kind="ExternalInput"),
        "dtb": b.dram("dtb", [32, 1], F32, kind="ExternalInput"),
        "nw": b.dram("nw", [128, 1], F32, kind="ExternalInput"),
        "eh": b.dram("eh", [2048, 32], F32, kind="ExternalInput"),
        "mnL": b.dram("mnL", [128, 128], F32, kind="ExternalInput"),
        "mnU": b.dram("mnU", [128, 128], F32, kind="ExternalInput"),
        "mnUI": b.dram("mnUI", [128, 128], F32, kind="ExternalInput"),
        "id32": b.dram("id32", [96, 32], F32, kind="ExternalInput"),
        "id128": b.dram("id128", [128, 128], F32, kind="ExternalInput"),
    }
    if fused:
        H["spill"] = b.dram("dspill", [nt // 128 * 32 * 128, 640], BF16)
        H["ccin"] = b.dram("ccin", [128, 4096], F32)
        H["ccout"] = b.dram("ccout", [256, 4096], F32)
        H["hg"] = b.dram("hg", [4096, nt], BF16, kind="Internal")
        H["sfin"] = b.dram("sfin", [128, 4096], F32, kind="Internal")
    else:
        H["sinit"] = b.dram("sinit", [128, 4096], F32, kind="ExternalInput"),
        H["sinit"] = H["sinit"][0]
        H["hg"] = b.dram("hg", [4096, nt], BF16, kind="ExternalOutput")
        H["sfin"] = b.dram("sfin", [128, 4096], F32, kind="ExternalOutput")
    return H


def emit_l0delta(b, H, nt, fused, groups=None):
    nch = nt // 128
    nc = b.nc
    mixed, zs_h = H["mixed"], H["zs"]
    bb_h, aa_h = H["bb_f"], H["aa_f"]
    alog, dtb, nw_h = H["alog"], H["dtb"], H["nw"]
    eh_h, mnL_h, mnU_h, mnUI_h = H["eh"], H["mnL"], H["mnU"], H["mnUI"]
    id32_h, id128_h = H["id32"], H["id128"]
    hg_h, sfin_h = H["hg"], H["sfin"]

    bb = b.sbuf("bb", [32, nt], F32)
    aa = b.sbuf("aa", [32, nt], F32)
    gv = b.sbuf("gv", [32, 128], F32)
    Gc = b.sbuf("Gcc", [32, 128], F32)
    lb = b.sbuf("lbc", [32, 128], F32)
    beta = b.sbuf("betac", [32, 128], F32)
    lrq = b.sbuf("lrqc", [32, 128], F32)
    lrk = b.sbuf("lrkc", [32, 128], F32)
    uu = b.sbuf("uuc", [32, 128], F32)
    uq = b.sbuf("uqc", [32, 128], F32)
    wneg = b.sbuf("wnegc", [32, 128], F32)
    ebK = b.sbuf("ebKc", [32, 128], F32)
    ekg2 = b.sbuf("ekg2c", [32, 128], F32)
    tmp32 = b.sbuf("tmp32", [32, 256], F32)
    tmp32b = b.sbuf("tmp32b", [32, 128], F32)
    eglc = b.sbuf("eglc", [32, 1], F32)
    eglflat = b.sbuf("eglflat", [1, 32], F32)
    zero32 = b.sbuf("zero32", [32, 128], F32)
    S = b.sbuf("S", [128, 4096], BF16)
    eh = b.sbuf("ehs", [128, 16 * 32], BF16)
    mnL = b.sbuf("mnLs", [128, 128], F32)
    mnU = b.sbuf("mnUs", [128, 128], F32)
    mnUI = b.sbuf("mnUIs", [128, 128], F32)
    id32 = b.sbuf("id32s", [96, 32], F32)
    id128 = b.sbuf("id128s", [128, 128], BF16)
    nw = b.sbuf("nws", [128, 1], F32)
    ones1 = b.sbuf("ones1", [1, 128], BF16)
    ones1f = b.sbuf("ones1f", [1, 128], F32)
    ones128 = b.sbuf("ones128", [128, 1], BF16)
    sqt = b.sbuf("sqt", [128, 128], BF16)
    s32 = b.sbuf("s32", [32, 128], F32)
    mfm = b.sbuf("mfm", [128, 8192], BF16)
    mtm = b.sbuf("mtm", [128, 8192], BF16)
    zt = b.sbuf("zt", [128, 4096], BF16)
    uflat = b.sbuf("uflat", [1, 4096], F32)
    uqflat = b.sbuf("uqflat", [1, 4096], F32)
    wnflat = b.sbuf("wnflat", [1, 4096], F32)
    eqgflat = b.sbuf("eqgflat", [1, 4096], F32)
    betaT = b.sbuf("betaT", [128, 32], F32)
    ebKT = b.sbuf("ebKT", [128, 32], F32)
    ekg2T = b.sbuf("ekg2T", [128, 32], F32)
    sG = b.sbuf("sG", [128, 32], F32)
    sGall = b.sbuf("sGall", [128, 32 * nch], F32) if fused else None
    hgstg = b.sbuf("hgstg", [128, 4096], BF16)
    kkTs = b.sbuf("kkTs", [128, 128], BF16)
    eA = b.sbuf("eA", [128, 128], BF16)
    eAT = b.sbuf("eAT", [128, 128], BF16)
    P = [b.sbuf(f"P{j}", [128, 128], BF16) for j in range(6)]
    Q = [b.sbuf(f"Q{j}", [128, 128], BF16) for j in range(7)]
    X = b.sbuf("X", [128, 256], BF16)
    vnew = b.sbuf("vnew", [128, 128], BF16)
    ofm = b.sbuf("ofm", [128, 128], BF16)
    osq = b.sbuf("osq", [128, 128], BF16)
    rstd = b.sbuf("rstd", [1, 128], BF16)
    lnv = b.sbuf("lnv", [1, 128], F32)
    t1s = b.sbuf("t1s", [128, 128], BF16)
    argt = b.sbuf("argt", [128, 128], F32)
    if fused:
        spillstg = b.sbuf("spillstg", [128, 640], BF16)
        ldg = [b.sbuf(f"ldg{i}", [128, 640], BF16) for i in range(2)]
        selc = b.sbuf("selc", [128, 1], F32)
        sldg = b.sbuf("sldg", [128, 1024], F32)
        kcf = spillstg[:, 0:128]
        attnT = spillstg[:, 128:256]
        eQT = b.sbuf("eQT", [128, 128], BF16)
        qeg = spillstg[:, 256:384]
        k2 = spillstg[:, 384:512]
    else:
        kcf = b.sbuf("kcf", [128, 128], BF16)[:, :]
        attnT = b.sbuf("attnT", [128, 128], BF16)[:, :]
        eQT = b.sbuf("eQT", [128, 128], BF16)
        qeg = b.sbuf("qeg", [128, 128], BF16)[:, :]
        k2 = b.sbuf("k2", [128, 128], BF16)[:, :]
    ps = [b.psum(f"ps{i}", (128, 512), F32) for i in range(6)]
    pbf = [b.psum(f"pbf{i}", (128, 1024), BF16) for i in range(2)]

    b.dma("qpool", bb[:, :], bb_h, [("dram", "bb")], ["bb"])
    b.dma("qpool", aa[:, :], aa_h, [("dram", "aa")], ["aa"])
    b.dma("qpool", eh[:, :].rearrange("p (c m) -> p c m", c=16),
          eh_h.rearrange("(c p) m -> p c m", p=128), [], ["eh"])
    b.dma("qpool", mnL[:, :], mnL_h, [], ["mnL"])
    b.dma("qpool", mnU[:, :], mnU_h, [], ["mnU"])
    b.dma("qpool", mnUI[:, :], mnUI_h, [], ["mnUI"])
    b.dma("qpool", id32[:, :], id32_h, [], ["id32"])
    b.dma("qpool", id128[:, :], id128_h, [], ["id128"])
    b.dma("qpool", nw[:, :], nw_h, [], ["nw"])
    if fused:
        b.pool(lambda: nc.gpsimd.memset(S[:, :], 0.0), [], [("S", h) for h in range(32)])
        b.dma("qpool", selc[:, :], H["sel"], [], ["selc"])
    else:
        b.dma("qpool", S[:, :], H["sinit"], [], [("S", h) for h in range(32)])
    alog_t = b.sbuf("alog_t", [32, 1], F32)
    dtb_t = b.sbuf("dtb_t", [32, 1], F32)
    negA = b.sbuf("negA", [32, 1], F32)
    b.dma("qpool", alog_t[:, :], alog, [], ["alog"])
    b.dma("qpool", dtb_t[:, :], dtb, [], ["dtb"])
    b.pool(lambda: nc.gpsimd.memset(ones1[:, :], 1.0), [], ["ones1"])
    b.pool(lambda: nc.gpsimd.memset(ones1f[:, :], 1.0), [], ["ones1f"])
    b.pool(lambda: nc.gpsimd.memset(ones128[:, :], 1.0), [], ["ones128"])
    b.pool(lambda: nc.gpsimd.memset(zero32[:, :], 0.0), [], ["zero32"])
    one_c, one_k = b.constcol(1.0, 32)
    eps6_c, eps6_k = b.constcol(1e-6, 32)
    eps6r_c, eps6r_k = b.constcol(1e-6, 1)
    b.act(lambda: nc.scalar.activation(out=negA[:, :], in_=alog_t[:, :], func=AF.Exp),
          ["alog"], ["negA0"])
    b.dve(lambda: nc.vector.tensor_scalar(out=negA[:, :], in0=negA[:, :], scalar1=-1.0,
          scalar2=None, op0=ALU.mult), ["negA0"], ["negA"])

    spillv = H["spill"].rearrange("(c h p) n -> c h p n", h=32, p=128) if fused else None

    def chain_vnew(I, kS, Ssl, Xv_t, Xv_k, kcf_t, kcf_k):
        b.mm(ps[0][:, 256:384], kcf_t, Ssl, True, True, kcf_k + [kS], [("ps", 0)], tag=f"vn1{I}")
        b.dve(b.lp(lambda Xv_t=Xv_t: nc.vector.scalar_tensor_tensor(out=vnew[:, :],
              in0=ps[0][:, 256:384], scalar=-1.0, in1=Xv_t, op0=ALU.mult, op1=ALU.add)),
              [("ps", 0)] + Xv_k, ["vnew"], tag=f"vn2{I}")

    def chain_update(I, kS, Ssl, k2_t, k2_k, sG_t, sG_col, sG_k):
        b.mm(ps[3][:, 256:384], k2_t, vnew[:, :], True, True, k2_k + ["vnew"],
             [("ps", 3)], tag=f"ds{I}")
        b.dve(b.lp(lambda Ssl=Ssl, sG_t=sG_t, sG_col=sG_col: nc.vector.scalar_tensor_tensor(
              out=Ssl, in0=Ssl, scalar=sG_t[:, sG_col:sG_col + 1], in1=ps[3][:, 256:384],
              op0=ALU.mult, op1=ALU.add)),
              [kS, sG_k, ("ps", 3)], [kS], tag=f"su{I}")

    def out_stage(I, h, kS, Ssl, qeg_t, qeg_k, attnT_t, attnT_k, zsl, zk):
        b.mm(ps[2][:, 256:384], Ssl, qeg_t, True, False, [kS] + qeg_k, [("ps", 2)], tag=f"o1{I}")
        b.mm(ps[2][:, 256:384], vnew[:, :], attnT_t, False, True, ["vnew"] + attnT_k,
             [("ps", 2)], tag=f"o2{I}")
        b.dve(b.lp(lambda: nc.vector.tensor_copy(ofm[:, :], ps[2][:, 256:384])),
              [("ps", 2)], ["ofm"], tag=f"oc{I}")
        b.pool(b.lp(lambda: nc.gpsimd.tensor_mul(osq[:, :], ofm[:, :], ofm[:, :])),
               ["ofm"], ["osq"], tag=f"os{I}")
        b.mm(ps[3][0:1, 128:256], ones128[:, :], osq[:, :], True, True,
             ["osq", "ones128"], [("ps", 3)], tag=f"rs{I}")
        b.act(lambda: nc.scalar.activation(out=lnv[:, :], in_=ps[3][0:1, 128:256],
              func=AF.Ln, scale=1.0 / 128, bias=eps6r_c[:, :]),
              [("ps", 3), eps6r_k], ["lnv"], tag=f"rl{I}")
        b.act(b.lp(lambda: nc.scalar.activation(out=rstd[:, :], in_=lnv[:, :],
              func=AF.Exp, scale=-0.5)), ["lnv"], ["rstd"], tag=f"re{I}")
        b.mm(ps[4][:, 128:256], ones1[:, :], rstd[:, :], True, True,
             ["ones1", "rstd"], [("ps", 4)], tag=f"rb{I}")
        b.dve(b.lp(lambda: nc.vector.tensor_mul(t1s[:, :], ofm[:, :], ps[4][:, 128:256])),
              ["ofm", ("ps", 4)], ["t1s"], tag=f"g1{I}")
        b.pool(b.lp(lambda zsl=zsl: nc.gpsimd.tensor_mul(t1s[:, :], t1s[:, :], zsl)),
               ["t1s", zk], ["t1s"], tag=f"g2{I}")
        b.dve(b.lp(lambda h=h: nc.vector.tensor_scalar(out=hgstg[:, h * 128:(h + 1) * 128],
              in0=t1s[:, :], scalar1=nw[:, :], scalar2=None, op0=ALU.mult)),
              ["t1s", "nw"], [("hgstg", h)], tag=f"g3{I}")

    for c in range(nch):
        sl = slice(c * 128, (c + 1) * 128)
        bsl, asl = bb[:, sl], aa[:, sl]
        b.dma("qpool", mfm[:, :].rearrange("p (k t) -> p k t", k=64),
              mixed.rearrange("(k p) t -> p k t", p=128)[:, :, sl], [("dram", "mixed")],
              ["mfm"], tag=f"mfm{c}")
        b.dma("qsp", mtm[:, :], mixed[:, sl], [("dram", "mixed")], ["mtm"], tag=f"mtm{c}",
              transpose=True)
        if not fused:
            b.dma("qact", zt[:, :].rearrange("p (k t) -> p k t", k=32),
                  zs_h.rearrange("(k p) t -> p k t", p=128)[:, :, sl], [("dram", "zs")],
                  ["zt"], tag=f"zt{c}")
        b.act(lambda asl=asl: nc.scalar.activation(out=tmp32[:, 0:128], in_=asl, func=AF.Abs,
              bias=dtb_t[:, :]), ["aa", "dtb"], ["t32"], tag=f"gp1{c}")
        b.act(lambda: nc.scalar.activation(out=tmp32[:, 0:128], in_=tmp32[:, 0:128], func=AF.Exp,
              scale=-1.0), ["t32"], ["t32"], tag=f"gp2{c}")
        b.act(lambda: nc.scalar.activation(out=tmp32[:, 0:128], in_=tmp32[:, 0:128], func=AF.Ln,
              bias=one_c[:, :]), ["t32", one_k], ["t32"], tag=f"gp3{c}")
        b.act(lambda asl=asl: nc.scalar.activation(out=tmp32b[:, :], in_=asl, func=AF.Relu,
              bias=dtb_t[:, :]), ["aa", "dtb"], ["t32b"], tag=f"gp4{c}")
        b.dve(lambda: nc.vector.tensor_add(tmp32[:, 0:128], tmp32[:, 0:128], tmp32b[:, :]),
              ["t32", "t32b"], ["t32"], tag=f"gp5{c}")
        b.dve(lambda: nc.vector.scalar_tensor_tensor(out=gv[:, :], in0=tmp32[:, 0:128],
              scalar=negA[:, :], in1=tmp32[:, 0:128], op0=ALU.mult, op1=ALU.bypass),
              ["t32", "negA"], ["gv"], tag=f"gp6{c}")
        b.act(lambda bsl=bsl: nc.scalar.activation(out=tmp32[:, 0:128], in_=bsl, func=AF.Abs),
              ["bb"], ["t32"], tag=f"lb1{c}")
        b.act(lambda: nc.scalar.activation(out=tmp32[:, 0:128], in_=tmp32[:, 0:128], func=AF.Exp,
              scale=-1.0), ["t32"], ["t32"], tag=f"lb2{c}")
        b.act(lambda: nc.scalar.activation(out=tmp32[:, 0:128], in_=tmp32[:, 0:128], func=AF.Ln,
              bias=one_c[:, :]), ["t32", one_k], ["t32"], tag=f"lb3{c}")
        b.act(lambda bsl=bsl: nc.scalar.activation(out=tmp32b[:, :], in_=bsl, func=AF.Relu,
              scale=-1.0), ["bb"], ["t32b"], tag=f"lb4{c}")
        b.dve(lambda: nc.vector.tensor_add(tmp32[:, 0:128], tmp32[:, 0:128], tmp32b[:, :]),
              ["t32", "t32b"], ["t32"], tag=f"lb5{c}")
        b.dve(lambda: nc.vector.tensor_scalar(out=lb[:, :], in0=tmp32[:, 0:128], scalar1=-1.0,
              scalar2=None, op0=ALU.mult), ["t32"], ["lb"], tag=f"lb6{c}")
        b.act(lambda bsl=bsl: nc.scalar.activation(out=beta[:, :], in_=bsl, func=AF.Sigmoid),
              ["bb"], ["beta"], tag=f"be{c}")
        b.dve(lambda: nc.vector.tensor_tensor_scan(out=Gc[:, :], data0=gv[:, :],
              data1=zero32[:, :], initial=0.0, op0=ALU.add, op1=ALU.add),
              ["gv", "zero32"], ["Gc"], tag=f"cums{c}")
        b.act(lambda: nc.scalar.activation(out=eglc[:, :], in_=Gc[:, 127:128], func=AF.Exp),
              ["Gc"], ["eglc"], tag=f"egl{c}")
        b.dma("qpool", eglflat[:, :], eglc[:, :], ["eglc"], ["eglflat"], tag=f"feg{c}")
        for which, base, lr in (("q", 0, lrq), ("k", 2048, lrk)):
            tag = "lq" if which == "q" else "lk"
            pq = ps[4]
            for hc in range(16):
                xc = mfm[:, base + hc * 128:base + (hc + 1) * 128]
                b.pool(lambda xc=xc: nc.gpsimd.tensor_mul(sqt[:, :], xc, xc),
                       ["mfm"], ["sqt"], tag=f"{tag}s{c}_{hc}")
                b.mm(pq[0:32, 0:128], eh[:, hc * 32:(hc + 1) * 32], sqt[:, :],
                     start=(hc == 0), stop=(hc == 15), reads=["sqt", "eh"], writes=[("ps", 4)],
                     tag=f"{tag}m{c}_{hc}")
            b.act(lambda pq=pq: nc.scalar.activation(out=s32[:, :], in_=pq[0:32, 0:128], func=AF.Sqrt),
                  [("ps", 4)], ["s32"], tag=f"{tag}sq{c}")
            b.act(lambda: nc.scalar.activation(out=s32[:, :], in_=s32[:, :], func=AF.Ln,
                  bias=eps6_c[:, :]), ["s32", eps6_k], ["s32"], tag=f"{tag}ln{c}")
            extra = LN128H if which == "q" else 0.0
            b.dve(lambda lr=lr, extra=extra: nc.vector.tensor_scalar(out=lr[:, :],
                  in0=s32[:, :], scalar1=-1.0, scalar2=extra, op0=ALU.mult, op1=ALU.add),
                  ["s32"], ["lrq" if which == "q" else "lrk"], tag=f"{tag}f{c}")
        b.dve(lambda: nc.vector.tensor_add(uu[:, :], Gc[:, :], lrk[:, :]),
              ["Gc", "lrk"], ["uu0"], tag=f"uu0{c}")
        b.dve(lambda: nc.vector.tensor_add(uu[:, :], uu[:, :], lb[:, :]),
              ["uu0", "lb"], ["uu"], tag=f"uu{c}")
        b.dve(lambda: nc.vector.tensor_add(uq[:, :], Gc[:, :], lrq[:, :]),
              ["Gc", "lrq"], ["uq"], tag=f"uq{c}")
        b.dve(lambda: nc.vector.tensor_sub(wneg[:, :], lrk[:, :], Gc[:, :]),
              ["Gc", "lrk"], ["wneg"], tag=f"wn{c}")
        b.act(lambda: nc.scalar.activation(out=ebK[:, :], in_=uu[:, :], func=AF.Exp),
              ["uu"], ["ebK"], tag=f"ebk{c}")
        b.dve(lambda: nc.vector.tensor_scalar(out=tmp32[:, 0:128], in0=Gc[:, :],
              scalar1=Gc[:, 127:128], scalar2=None, op0=ALU.subtract),
              ["Gc"], ["t32"], tag=f"ek1{c}")
        b.dve(lambda: nc.vector.tensor_sub(tmp32[:, 128:256], lrk[:, :], tmp32[:, 0:128]),
              ["lrk", "t32"], ["t32c"], tag=f"ek2{c}")
        b.act(lambda: nc.scalar.activation(out=ekg2[:, :], in_=tmp32[:, 128:256], func=AF.Exp),
              ["t32c"], ["ekg2"], tag=f"ek3{c}")
        b.dma("qpool", uflat[:, :], uu[:, :], ["uu"], ["uflat"], tag=f"fu{c}")
        b.dma("qpool", uqflat[:, :], uq[:, :], ["uq"], ["uqflat"], tag=f"fuq{c}")
        b.dma("qpool", wnflat[:, :], wneg[:, :], ["wneg"], ["wnflat"], tag=f"fw{c}")
        b.act(lambda: nc.scalar.activation(out=eqgflat[:, :], in_=uqflat[:, :], func=AF.Exp),
              ["uqflat"], ["eqgflat"], tag=f"eqg{c}")
        for srcv, dst, kk in ((beta, betaT, "betaT"), (ebK, ebKT, "ebKT"), (ekg2, ekg2T, "ekg2T")):
            rd = {"betaT": "beta", "ebKT": "ebK", "ekg2T": "ekg2"}[kk]
            b.transp(ps[5][0:128, 0:32], srcv[:, :], id32[0:32, :], [rd, "id32"],
                     [("ps", 5)], tag=f"tp{kk}{c}")
            b.dve(lambda dst=dst: nc.vector.tensor_copy(dst[:, :], ps[5][0:128, 0:32]),
                  [("ps", 5)], [kk], tag=f"tpc{kk}{c}")
        b.mm(ps[5][0:128, 32:64], ones1f[:, :], eglflat[0:1, :], True, True,
             ["ones1f", "eglflat"], [("ps", 5)], tag=f"eglb{c}")
        if fused:
            b.dve(lambda c=c: nc.vector.tensor_copy(sGall[:, c * 32:(c + 1) * 32],
                  ps[5][0:128, 32:64]), [("ps", 5)], [("sGall", c)], tag=f"eglc{c}")
        else:
            b.dve(lambda: nc.vector.tensor_copy(sG[:, :], ps[5][0:128, 32:64]),
                  [("ps", 5)], ["sG"], tag=f"eglc{c}")

        for h in range(32):
            hq = h // 2
            qfm = mfm[:, hq * 128:(hq + 1) * 128]
            kfm = mfm[:, 2048 + hq * 128:2048 + (hq + 1) * 128]
            ktm = mtm[:, 2048 + hq * 128:2048 + (hq + 1) * 128]
            vtm = mtm[:, 4096 + h * 128:4096 + (h + 1) * 128]
            urow = uflat[0:1, h * 128:(h + 1) * 128]
            uqrow = uqflat[0:1, h * 128:(h + 1) * 128]
            wnrow = wnflat[0:1, h * 128:(h + 1) * 128]
            eqrow = eqgflat[0:1, h * 128:(h + 1) * 128]
            Ssl = S[:, h * 128:(h + 1) * 128]
            kS = ("S", h)
            I = f"{c}_{h}"
            b.mm(ps[0][:, 0:128], kfm, kfm, True, True, ["mfm"], [("ps", 0)], tag=f"kkt{I}")
            b.dve(b.lp(lambda: nc.vector.tensor_copy(kkTs[:, :], ps[0][:, 0:128])),
                  [("ps", 0)], ["kkTs"], tag=f"kkc{I}")
            b.mm(ps[1][:, 0:128], urow, ones1f[:, :], True, False, ["uflat", "ones1f"],
                 [("ps", 1)], tag=f"aA1{I}")
            b.mm(ps[1][:, 0:128], ones1f[:, :], wnrow, False, True, ["wnflat", "ones1f"],
                 [("ps", 1)], tag=f"aA2{I}")
            b.dve(lambda: nc.vector.scalar_tensor_tensor(out=argt[:, :], in0=ps[1][:, 0:128],
                  scalar=60.0, in1=mnL[:, :], op0=ALU.min, op1=ALU.add),
                  [("ps", 1), "mnL"], ["argt"], tag=f"aA3{I}")
            b.act(b.lp(lambda: nc.scalar.activation(out=eA[:, :], in_=argt[:, :], func=AF.Exp)),
                  ["argt"], ["eA"], tag=f"aA4{I}")
            b.mm(ps[2][:, 0:128], wnrow, ones1f[:, :], True, False, ["wnflat", "ones1f"],
                 [("ps", 2)], tag=f"aT1{I}")
            b.mm(ps[2][:, 0:128], ones1f[:, :], urow, False, True, ["uflat", "ones1f"],
                 [("ps", 2)], tag=f"aT2{I}")
            b.dve(lambda: nc.vector.scalar_tensor_tensor(out=argt[:, :], in0=ps[2][:, 0:128],
                  scalar=60.0, in1=mnU[:, :], op0=ALU.min, op1=ALU.add),
                  [("ps", 2), "mnU"], ["argt"], tag=f"aT3{I}")
            b.act(b.lp(lambda: nc.scalar.activation(out=eAT[:, :], in_=argt[:, :], func=AF.Exp)),
                  ["argt"], ["eAT"], tag=f"aT4{I}")
            b.dve(b.lp(lambda: nc.vector.scalar_tensor_tensor(out=P[0][:, :], in0=kkTs[:, :],
                  scalar=-1.0, in1=eA[:, :], op0=ALU.mult, op1=ALU.mult)),
                  ["kkTs", "eA"], [("P", 0)], tag=f"P0{I}")
            b.dve(b.lp(lambda: nc.vector.scalar_tensor_tensor(out=Q[0][:, :], in0=kkTs[:, :],
                  scalar=-1.0, in1=eAT[:, :], op0=ALU.mult, op1=ALU.mult)),
                  ["kkTs", "eAT"], [("Q", 0)], tag=f"Q0{I}")
            b.mm(ps[0][:, 128:256], kfm, qfm, True, True, ["mfm"], [("ps", 0)], tag=f"qkt{I}")
            b.mm(ps[1][:, 128:256], wnrow, ones1f[:, :], True, False, ["wnflat", "ones1f"],
                 [("ps", 1)], tag=f"aQ1{I}")
            b.mm(ps[1][:, 128:256], ones1f[:, :], uqrow, False, True, ["uqflat", "ones1f"],
                 [("ps", 1)], tag=f"aQ2{I}")
            b.dve(lambda: nc.vector.scalar_tensor_tensor(out=argt[:, :], in0=ps[1][:, 128:256],
                  scalar=60.0, in1=mnUI[:, :], op0=ALU.min, op1=ALU.add),
                  [("ps", 1), "mnUI"], ["argt"], tag=f"aQ3{I}")
            b.act(b.lp(lambda: nc.scalar.activation(out=eQT[:, :], in_=argt[:, :], func=AF.Exp)),
                  ["argt"], ["eQT"], tag=f"aQ4{I}")
            b.dve(b.lp(lambda attnT=attnT: nc.vector.tensor_mul(attnT, eQT[:, :],
                  ps[0][:, 128:256])), ["eQT", ("ps", 0)], ["attnT"], tag=f"at{I}")
            b.pool(b.lp(lambda h=h, vtm=vtm: nc.gpsimd.tensor_scalar(out=X[:, 0:128], in0=vtm,
                   scalar1=betaT[:, h:h + 1], scalar2=None, op0=ALU.mult)),
                   ["mtm", "betaT"], [("X", "v")], tag=f"Xv{I}")
            b.pool(b.lp(lambda h=h, ktm=ktm: nc.gpsimd.tensor_scalar(out=X[:, 128:256], in0=ktm,
                   scalar1=ebKT[:, h:h + 1], scalar2=None, op0=ALU.mult)),
                   ["mtm", "ebKT"], [("X", "k")], tag=f"Xk{I}")
            for j in range(1, 7):
                if j <= 5:
                    b.mm(ps[3][:, 0:128], Q[j - 1][:, :], P[j - 1][:, :], True, True,
                         [("Q", j - 1), ("P", j - 1)], [("ps", 3)], tag=f"sqP{j}{I}")
                    b.dve(b.lp(lambda j=j: nc.vector.tensor_copy(P[j][:, :], ps[3][:, 0:128])),
                          [("ps", 3)], [("P", j)], tag=f"sqPc{j}{I}")
                b.mm(ps[4][:, 0:128], P[j - 1][:, :], Q[j - 1][:, :], True, True,
                     [("P", j - 1), ("Q", j - 1)], [("ps", 4)], tag=f"sqQ{j}{I}")
                b.act(b.lp(lambda j=j: nc.scalar.activation(out=Q[j][:, :], in_=ps[4][:, 0:128],
                      func=AF.Copy)), [("ps", 4)], [("Q", j)], tag=f"sqQc{j}{I}")
            for j in range(6, -1, -1):
                b.mm(ps[5][:, 0:256], Q[j][:, :], X[:, :], True, True,
                     [("Q", j), ("X", "v"), ("X", "k")], [("ps", 5)], tag=f"ap{j}{I}")
                b.dve(b.lp(lambda: nc.vector.scalar_tensor_tensor(out=X[:, :], in0=ps[5][:, 0:256],
                      scalar=1.0, in1=X[:, :], op0=ALU.mult, op1=ALU.add)),
                      [("ps", 5), ("X", "v"), ("X", "k")], [("X", "v"), ("X", "k")],
                      tag=f"apc{j}{I}")
            b.transp(pbf[0][:, 0:128], X[:, 128:256], id128[:, :],
                     [("X", "k"), "id128"], [("pbf", 0)], tag=f"kct{I}")
            b.dve(b.lp(lambda kcf=kcf: nc.vector.tensor_copy(kcf, pbf[0][:, 0:128])),
                  [("pbf", 0)], ["kcf"], tag=f"kcc{I}")
            b.mm(ps[1][:, 256:384], ones1f[:, :], eqrow, True, True, ["eqgflat", "ones1f"],
                 [("ps", 1)], tag=f"qe1{I}")
            b.dve(b.lp(lambda qfm=qfm, qeg=qeg: nc.vector.tensor_mul(qeg, qfm,
                  ps[1][:, 256:384])), ["mfm", ("ps", 1)], ["qeg"], tag=f"qe2{I}")
            b.pool(b.lp(lambda h=h, ktm=ktm, k2=k2: nc.gpsimd.tensor_scalar(out=k2, in0=ktm,
                   scalar1=ekg2T[:, h:h + 1], scalar2=None, op0=ALU.mult)),
                   ["mtm", "ekg2T"], ["k2"], tag=f"k2{I}")
            if fused:
                b.pool(b.lp(lambda: nc.gpsimd.tensor_copy(spillstg[:, 512:640], X[:, 0:128])),
                       [("X", "v")], ["spXv"], tag=f"spx{I}")
                b.dma("qact", spillv[c, h], spillstg[:, :],
                      ["kcf", "attnT", "qeg", "k2", "spXv"], [("dsp", c, h)], tag=f"spd{I}")
                chain_vnew(I, kS, Ssl, X[:, 0:128], [("X", "v")], kcf, ["kcf"])
                chain_update(I, kS, Ssl, k2, ["k2"], sGall, c * 32 + h, ("sGall", c))
            else:
                chain_vnew(I, kS, Ssl, X[:, 0:128], [("X", "v")], kcf, ["kcf"])
                zsl = zt[:, h * 128:(h + 1) * 128]
                out_stage(I, h, kS, Ssl, qeg, ["qeg"], attnT, ["attnT"], zsl, "zt")
                chain_update(I, kS, Ssl, k2, ["k2"], sG, h, "sG")
        if not fused:
            b.dma("qsp", hg_h.rearrange("(h p) t -> p h t", p=128)[:, :, sl],
                  hgstg[:, :].rearrange("p (h t) -> p h t", h=32),
                  [("hgstg", h2) for h2 in range(32)], [("dram", "hg")], tag=f"sthg{c}")
    b.dma("qpool", H["ccin"] if fused else sfin_h, S[:, :],
          [("S", h2) for h2 in range(32)],
          [("dram", "ccin") if fused else ("dram", "sfin")], tag="stS")

    if fused:
        b.cc("AllGather", ALU.bypass, groups, [H["ccin"].opt()], [H["ccout"].opt()],
             [("dram", "ccin")], [("dram", "ccout")], tag="ccS0")
        for q4 in range(4):
            b.dma("qpool", sldg[:, :], H["ccout"][0:128, q4 * 1024:(q4 + 1) * 1024],
                  [("dram", "ccout")], ["sldg"], tag=f"sgl{q4}")
            b.dve(b.lp(lambda q4=q4: nc.vector.tensor_scalar(
                  out=S[:, q4 * 1024:(q4 + 1) * 1024], in0=sldg[:, :], scalar1=selc[:, :],
                  scalar2=None, op0=ALU.mult)), ["sldg", "selc"],
                  [("S", hh) for hh in range(q4 * 8, (q4 + 1) * 8)], tag=f"sgm{q4}")
        for c in range(nch):
            sl = slice(c * 128, (c + 1) * 128)
            b.dma("qact", zt[:, :].rearrange("p (k t) -> p k t", k=32),
                  zs_h.rearrange("(k p) t -> p k t", p=128)[:, :, sl], [("dram", "zs")],
                  ["zt"], tag=f"Bzt{c}")
            for h in range(32):
                I = f"B{c}_{h}"
                lt = ldg[h % 2]
                klt = ("ldg", h % 2)
                b.dma("qpool", lt[:, :], spillv[c, h], [("dsp", c, h)], [klt], tag=f"ld{I}")
                Ssl = S[:, h * 128:(h + 1) * 128]
                kS = ("S", h)
                chain_vnew(I, kS, Ssl, lt[:, 512:640], [klt], lt[:, 0:128], [klt])
                zsl = zt[:, h * 128:(h + 1) * 128]
                out_stage(I, h, kS, Ssl, lt[:, 256:384], [klt], lt[:, 128:256], [klt],
                          zsl, "zt")
                chain_update(I, kS, Ssl, lt[:, 384:512], [klt], sGall, c * 32 + h, ("sGall", c))
            b.dma("qsp", hg_h.rearrange("(h p) t -> p h t", p=128)[:, :, sl],
                  hgstg[:, :].rearrange("p (h t) -> p h t", h=32),
                  [("hgstg", h2) for h2 in range(32)], [("dram", "hg")], tag=f"Bsthg{c}")


def build_l0delta(nt=NT_FULL, debug_taps=False):
    b = B("l0delta")
    H = {
        "mixed": b.dram("mixed", [8192, nt], BF16, kind="ExternalInput"),
        "zs": b.dram("zs", [4096, nt], BF16, kind="ExternalInput"),
        "bb_f": b.dram("bb_f", [32, nt], F32, kind="ExternalInput"),
        "aa_f": b.dram("aa_f", [32, nt], F32, kind="ExternalInput"),
    }
    H.update(decl_l0delta(b, nt, fused=False))
    emit_l0delta(b, H, nt, fused=False)
    return b


def stage_l0delta(core, arts, inputs, nt=NT_FULL, sinit=None):
    consts = {}
    eh = np.zeros((16, 128, 32), np.float32)
    for hc in range(16):
        eh[hc, :, 2 * hc] = 1.0
        eh[hc, :, 2 * hc + 1] = 1.0
    consts["eh"] = eh.reshape(2048, 32)
    idx = np.arange(128)
    mnL = np.where(idx[:, None] > idx[None, :], 0.0, -1e9).astype(np.float32)
    mnU = np.where(idx[:, None] < idx[None, :], 0.0, -1e9).astype(np.float32)
    mnUI = np.where(idx[:, None] <= idx[None, :], 0.0, -1e9).astype(np.float32)
    consts["mnL"], consts["mnU"], consts["mnUI"] = mnL, mnU, mnUI
    consts["id32"] = np.concatenate([np.eye(32, dtype=np.float32)] * 3, 0)
    consts["id128"] = np.eye(128, dtype=np.float32)
    if sinit is None:
        sinit = np.zeros((128, 4096), np.float32)
    return {
        "mixed": arts["mixed"], "zs": arts["zs"],
        "bb_f": np.asarray(arts["bb_f"]), "aa_f": np.asarray(arts["aa_f"]),
        "alog": inputs["L0_attn_A_log"].reshape(32, 1).astype(np.float32),
        "dtb": inputs["L0_attn_dt_bias"].reshape(32, 1).astype(np.float32),
        "nw": inputs["L0_attn_norm_weight"].reshape(128, 1).astype(np.float32),
        "sinit": sinit, **consts,
    }


def decl_prep(b, fused):
    H = {
        "wgu_s": b.dram("wgu_s", [2048, 2048], F32, kind="ExternalInput"),
        "wp_s": b.dram("wp_s", [256, 8192], F32, kind="ExternalInput"),
        "wup_s": b.dram("wup_s", [1024, 2048], F32, kind="ExternalInput"),
        "wgt_s": b.dram("wgt_s", [1024, 2048], F32, kind="ExternalInput"),
        "wdn_s": b.dram("wdn_s", [256, 8192], F32, kind="ExternalInput"),
        "wout_s": b.dram("wout_s", [512, 2048], F32, kind="ExternalInput"),
        "woutb_sl": b.dram("woutb_sl", [512, 2048], BF16),
        "woutb": b.dram("woutb", [4096, 2048], BF16),
        "su_ug4": b.dram("su_ug4", [64, 32], F32, kind="ExternalInput"),
        "su_dp2": b.dram("su_dp2", [128, 64], F32, kind="ExternalInput"),
        "H128": b.dram("H128", [128, 128], F32, kind="ExternalInput"),
        "HK16I4": b.dram("HK16I4", [64, 64], F32, kind="ExternalInput"),
        "HK64I2": b.dram("HK64I2", [128, 128], F32, kind="ExternalInput"),
        "q_gu": b.dram("q_gu", [2048, 2048], BF16),
        "q_p": b.dram("q_p", [256, 8192], BF16),
        "tq_gu": b.dram("tq_gu", [2048, 2048], BF16),
        "tq_p": b.dram("tq_p", [8192, 256], BF16),
        "fup_sl": b.dram("fup_sl", [2048, 1024], BF16),
        "fgt_sl": b.dram("fgt_sl", [2048, 1024], BF16),
        "fdn_sl": b.dram("fdn_sl", [8192, 256], BF16),
        "wub": b.dram("wub", [1024, 2048], BF16),
        "wgb": b.dram("wgb", [1024, 2048], BF16),
        "wdb": b.dram("wdb", [256, 8192], BF16),
        "A1u": b.dram("A1u", [2048, 1024], BF16),
        "A1g": b.dram("A1g", [2048, 1024], BF16),
        "A1d": b.dram("A1d", [8192, 256], BF16),
        "wq_gu_t": b.dram("wq_gu_t", [8 * 2048, 2048], BF16),
        "wq_p_t": b.dram("wq_p_t", [8 * 8192, 256], BF16),
        "fup": b.dram("fup", [8 * 2048, 1024], BF16),
        "fgt": b.dram("fgt", [8 * 2048, 1024], BF16),
        "fdn": b.dram("fdn", [8 * 8192, 256], BF16),
    }
    return H


GROUPS8 = [[0, 1, 2, 3, 4, 5, 6, 7]]


def prep_gate(b):
    scr = b.sbuf(f"gate{b.uid()}", [1, 1], F32)
    sem, nval = b.prep_ccsem, b.prep_ccn

    def emit():
        b.nc.gpsimd.wait_ge(sem, nval)
        return b.nc.gpsimd.memset(scr[:, :], 0.0)
    b.plan.op("pool", emit, [], ["prepgate"], tag="gate")


def emit_prep(b, H, fused=False):
    nc = b.nc
    QW = 4096
    qb = [dict(wt=b.sbuf(f"p_wt{p}", [128, QW], BF16),
               grp=b.sbuf(f"p_grp{p}", [128, 64], F32),
               scb=b.sbuf(f"p_scb{p}", [128, 64], BF16),
               rsc=b.sbuf(f"p_rsc{p}", [128, 64], F32),
               rat=b.sbuf(f"p_rat{p}", [128, QW], BF16),
               tsg=b.sbuf(f"p_tsg{p}", [128, QW], BF16),
               tgt=b.sbuf(f"p_tgt{p}", [128, QW], BF16),
               what=b.sbuf(f"p_what{p}", [128, QW], BF16)) for p in range(2)]
    H128 = b.sbuf("p_H128", [128, 128], BF16)
    HK16I4 = b.sbuf("p_HK16I4", [64, 64], BF16)
    HK64I2 = b.sbuf("p_HK64I2", [128, 128], BF16)
    su4 = b.sbuf("p_su4", [64, 32], F32)
    su2 = b.sbuf("p_su2", [128, 64], F32)
    tTs = [b.sbuf(f"p_tT{p}", [128, 2048], BF16) for p in range(2)]
    tSs = [b.sbuf(f"p_tS{p}", [128, 1024], BF16) for p in range(2)]
    pcb = [b.sbuf(f"p_pc{p}", [128, 8192], BF16) for p in range(2)]
    t2s = [b.sbuf(f"p_t2{p}", [128, 1024], BF16) for p in range(2)]
    stg1 = [b.sbuf(f"p_st1{p}", [128, 1024], BF16) for p in range(2)]
    stg2 = [b.sbuf(f"p_st2{p}", [128, 1024], BF16) for p in range(2)]
    ps = [b.psum(f"p_ps{i}", (128, 512), F32) for i in range(4)]
    sk = {}

    b.dma("qpool", H128[:, :], H["H128"], [], ["H128"])
    b.dma("qpool", HK16I4[:, :], H["HK16I4"], [], ["HK16I4"])
    b.dma("qpool", HK64I2[:, :], H["HK64I2"], [], ["HK64I2"])
    b.dma("qpool", su4[:, :], H["su_ug4"], [], ["su4"])
    b.dma("qpool", su2[:, :], H["su_dp2"], [], ["su2"])

    def bc3(x3, s3):
        return bass.broadcast_tensor_aps(x3, s3)

    qu = [0]

    def quant_unit(srcap, dstap, C, nm):
        u = qu[0]
        qu[0] += 1
        t = qb[u % 2]
        G = C // 64
        kw, kg, ks, ki = ("wt", u % 2), ("grp", u % 2), ("scb", u % 2), ("rsc", u % 2)
        kr, ksg, kt = ("rat", u % 2), ("tsg", u % 2), ("tgt", u % 2)
        kh = ("what", u % 2)
        wt, grp, scb, rsc = t["wt"], t["grp"], t["scb"], t["rsc"]
        rat, tsg, tgt, what = t["rat"], t["tsg"], t["tgt"], t["what"]
        b.dma("qpool", wt[:, 0:C], srcap, [], [kw], tag=f"qw{nm}{u}")
        b.dve(lambda C=C, G=G, wt=wt, grp=grp: nc.vector.tensor_reduce(
            out=grp[:, 0:G].rearrange("p (g e) -> p g e", e=1),
            in_=wt[:, 0:C].rearrange("p (g e) -> p g e", e=64), axis=AX.X, op=ALU.add,
            apply_absolute_value=True), [kw], [kg], tag=f"qr{nm}{u}")
        b.dve(b.lp(lambda G=G, grp=grp, scb=scb: nc.vector.tensor_scalar(
              out=scb[:, 0:G], in0=grp[:, 0:G],
              scalar1=1.0 / 64, scalar2=1e-8, op0=ALU.mult, op1=ALU.max)),
              [kg], [ks], tag=f"qs{nm}{u}")
        b.dve(lambda G=G, rsc=rsc, scb=scb: nc.vector.reciprocal(rsc[:, 0:G], scb[:, 0:G]),
              [ks], [ki], tag=f"qi{nm}{u}")
        b.dve(b.lp(lambda C=C, G=G, wt=wt, rsc=rsc, rat=rat: nc.vector.tensor_tensor(
              out=rat[:, 0:C].rearrange("p (g e) -> p g e", e=64),
              in0=bc3(wt[:, 0:C].rearrange("p (g e) -> p g e", e=64),
                      rsc[:, 0:G].rearrange("p (g e) -> p g e", e=1))[0],
              in1=bc3(wt[:, 0:C].rearrange("p (g e) -> p g e", e=64),
                      rsc[:, 0:G].rearrange("p (g e) -> p g e", e=1))[1],
              op=ALU.mult)), [kw, ki], [kr], tag=f"qd{nm}{u}")
        b.act(b.lp(lambda C=C, rat=rat, tsg=tsg: nc.scalar.activation(
              out=tsg[:, 0:C], in_=rat[:, 0:C], func=AF.Sign)), [kr], [ksg], tag=f"qg{nm}{u}")
        b.act(b.lp(lambda C=C, rat=rat, tgt=tgt: nc.scalar.activation(
              out=tgt[:, 0:C], in_=rat[:, 0:C], func=AF.Abs)), [kr], [kt], tag=f"qt0{nm}{u}")
        b.dve(b.lp(lambda C=C, tgt=tgt: nc.vector.tensor_scalar(out=tgt[:, 0:C], in0=tgt[:, 0:C],
              scalar1=0.5, scalar2=None, op0=ALU.is_gt)), [kt], [kt], tag=f"qt{nm}{u}")
        b.pool(b.lp(lambda C=C, tgt=tgt, tsg=tsg: nc.gpsimd.tensor_mul(
              tgt[:, 0:C], tgt[:, 0:C], tsg[:, 0:C])), [kt, ksg], [kt], tag=f"qq{nm}{u}")
        b.dve(b.lp(lambda C=C, G=G, tgt=tgt, scb=scb, what=what: nc.vector.tensor_tensor(
              out=what[:, 0:C].rearrange("p (g e) -> p g e", e=64),
              in0=bc3(tgt[:, 0:C].rearrange("p (g e) -> p g e", e=64),
                      scb[:, 0:G].rearrange("p (g e) -> p g e", e=1))[0],
              in1=bc3(tgt[:, 0:C].rearrange("p (g e) -> p g e", e=64),
                      scb[:, 0:G].rearrange("p (g e) -> p g e", e=1))[1],
              op=ALU.mult)), [kt, ks], [kh], tag=f"qh{nm}{u}")
        b.pool(b.lp(lambda C=C, what=what, wt=wt: nc.gpsimd.tensor_sub(
              what[:, 0:C], what[:, 0:C], wt[:, 0:C])), [kh, kw], [kh], tag=f"qd2{nm}{u}")
        b.pool(b.lp(lambda C=C, what=what, wt=wt: nc.gpsimd.tensor_add(
              what[:, 0:C], wt[:, 0:C], what[:, 0:C])), [kh, kw], [kh], tag=f"qa{nm}{u}")
        skey = ("dram", nm, u)
        sk.setdefault(nm, []).append(skey)
        b.dma("qsp" if u % 2 == 0 else "qact", dstap, what[:, 0:C], [kh], [skey],
              tag=f"qo{nm}{u}")

    for i, r0 in enumerate(range(0, 2048, 128)):
        quant_unit(H["wgu_s"][r0:r0 + 128, :], H["q_gu"][r0:r0 + 128, :], 2048, "qgu")
    for r0 in (0, 128):
        for ch in range(2):
            quant_unit(H["wp_s"][r0:r0 + 128, ch * 4096:(ch + 1) * 4096],
                       H["q_p"][r0:r0 + 128, ch * 4096:(ch + 1) * 4096], 4096, "qp")

    def tpass(srcq, dstq, R, C, nm, rk_all):
        for kc in range(C // 128):
            tb = tTs[kc % 2]
            kt = ("tT", kc % 2)
            b.dma("qsp", tb[:, 0:R], srcq[:, kc * 128:(kc + 1) * 128], rk_all, [kt],
                  tag=f"tp{nm}{kc}", transpose=True)
            skey = ("tq", nm, kc)
            sk.setdefault("t" + nm, []).append(skey)
            b.dma("qact", dstq[kc * 128:(kc + 1) * 128, 0:R], tb[:, 0:R], [kt], [skey],
                  tag=f"tpo{nm}{kc}")
    tpass(H["q_gu"], H["tq_gu"], 2048, 2048, "gu", sk["qgu"])
    tpass(H["q_p"], H["tq_p"], 256, 8192, "p", sk["qp"])

    def precast(src, dst, R, C, nm):
        for i, r0 in enumerate(range(0, R, 128)):
            kb = ("pcb", i % 2)
            b.dma("qpool", pcb[i % 2][:, 0:C], src[r0:r0 + 128, :], [], [kb], tag=f"pc{nm}{r0}")
            skey = ("pcast", nm, i)
            sk.setdefault("pc" + nm, []).append(skey)
            b.dma("qsp" if i % 2 == 0 else "qact", dst[r0:r0 + 128, :], pcb[i % 2][:, 0:C],
                  [kb], [skey], tag=f"pco{nm}{r0}")
    precast(H["wup_s"], H["wub"], 1024, 2048, "u")
    precast(H["wgt_s"], H["wgb"], 1024, 2048, "g")
    precast(H["wdn_s"], H["wdb"], 256, 8192, "d")
    precast(H["wout_s"], H["woutb_sl"], 512, 2048, "o")

    def stage1(srcb, a1, R, nchunks, nm):
        a1v = a1.rearrange("(yl dh) r -> dh yl r", dh=nchunks)
        for dc in range(nchunks):
            tb = tSs[dc % 2]
            kt = ("tS", dc % 2)
            b.dma("qsp", tb[:, 0:R], srcb[:, dc * 128:(dc + 1) * 128], sk["pc" + nm], [kt],
                  tag=f"s1t{nm}{dc}", transpose=True)
            sb = stg1[dc % 2]
            kst = ("stg1", dc % 2)
            for half in range(ceil_div(R, 512)):
                w = min(512, R - half * 512)
                b.mm(ps[half % 2][:, 0:w], H128[:, :], tb[:, half * 512:half * 512 + w],
                     True, True, ["H128", kt], [("ps", half % 2)], tag=f"s1m{nm}{dc}_{half}")
                b.act(b.lp(lambda w=w, half=half, sb=sb: nc.scalar.activation(
                      out=sb[:, half * 512:half * 512 + w], in_=ps[half % 2][:, 0:w],
                      func=AF.Copy)), [("ps", half % 2)], [kst], tag=f"s1e{nm}{dc}_{half}")
            skey = ("a1", nm, dc)
            sk.setdefault("a1" + nm, []).append(skey)
            b.dma("qact", a1v[dc], sb[:, 0:R], [kst], [skey], tag=f"s1o{nm}{dc}")
    stage1(H["wub"], H["A1u"], 1024, 16, "u")
    stage1(H["wgb"], H["A1g"], 1024, 16, "g")
    stage1(H["wdb"], H["A1d"], 256, 64, "d")

    def stage2_ug(a1, out, nm, a1k):
        R = 1024
        outv = out.rearrange("(yh yg j) r -> j yg yh r", yg=32, j=4)
        for ylg in range(32):
            t2 = t2s[ylg % 2]
            k2 = ("t2", ylg % 2)
            b.dma("qpool", t2[0:64, 0:R], a1[ylg * 64:(ylg + 1) * 64, :],
                  sk["a1" + a1k], [k2], tag=f"s2l{nm}{ylg}")
            sb = stg2[ylg % 2]
            kst = ("stg2", ylg % 2)
            for half in range(ceil_div(R, 512)):
                w = min(512, R - half * 512)
                b.mm(ps[2 + half % 2][0:64, 0:w], HK16I4[:, :],
                     t2[0:64, half * 512:half * 512 + w], True, True,
                     ["HK16I4", k2], [("ps", 2 + half % 2)], tag=f"s2m{nm}{ylg}_{half}")
                b.dve(b.lp(lambda w=w, half=half, ylg=ylg, sb=sb: nc.vector.tensor_scalar(
                      out=sb[0:64, half * 512:half * 512 + w],
                      in0=ps[2 + half % 2][0:64, 0:w],
                      scalar1=su4[:, ylg:ylg + 1], scalar2=None, op0=ALU.mult)),
                      [("ps", 2 + half % 2), "su4"], [kst], tag=f"s2e{nm}{ylg}_{half}")
            for j in range(4):
                skey = ("f", nm, ylg, j)
                sk.setdefault(nm, []).append(skey)
                b.dma("qsp" if ylg % 2 == 0 else "qact", outv[j, ylg],
                      sb[j * 16:(j + 1) * 16, 0:R], [kst], [skey], tag=f"s2o{nm}{ylg}_{j}")
    stage2_ug(H["A1u"], H["fup_sl"], "fu", "u")
    stage2_ug(H["A1g"], H["fgt_sl"], "fg", "g")

    def stage2_dn(a1, out, nm):
        R = 256
        outv = out.rearrange("(yh yg j) r -> j yg yh r", yg=64, j=2)
        for ylp in range(64):
            t2 = t2s[ylp % 2]
            k2 = ("t2", ylp % 2)
            b.dma("qpool", t2[:, 0:R], a1[ylp * 128:(ylp + 1) * 128, :],
                  sk["a1d"], [k2], tag=f"s2l{nm}{ylp}")
            sb = stg2[ylp % 2]
            kst = ("stg2", ylp % 2)
            b.mm(ps[2 + ylp % 2][:, 0:R], HK64I2[:, :], t2[:, 0:R], True, True,
                 ["HK64I2", k2], [("ps", 2 + ylp % 2)], tag=f"s2m{nm}{ylp}")
            b.dve(b.lp(lambda ylp=ylp, sb=sb: nc.vector.tensor_scalar(
                  out=sb[:, 0:R], in0=ps[2 + ylp % 2][:, 0:R],
                  scalar1=su2[:, ylp:ylp + 1], scalar2=None, op0=ALU.mult)),
                  [("ps", 2 + ylp % 2), "su2"], [kst], tag=f"s2e{nm}{ylp}")
            for j in range(2):
                skey = ("f", nm, ylp, j)
                sk.setdefault(nm, []).append(skey)
                b.dma("qsp" if ylp % 2 == 0 else "qact", outv[j, ylp],
                      sb[j * 64:(j + 1) * 64, 0:R], [kst], [skey], tag=f"s2o{nm}{ylp}_{j}")
    stage2_dn(H["A1d"], H["fdn_sl"], "fd")

    if fused:
        ccsem = nc.alloc_semaphore("sem_prepcc")
        b.prep_ccsem, b.prep_ccn = ccsem, 6

        def gather(in_h, out_h, rkeys, tag):
            def emit(in_h=in_h, out_h=out_h):
                instr = nc.gpsimd.collective_compute(
                    "AllGather", ALU.bypass, replica_groups=GROUPS8,
                    ins=[in_h.opt()], outs=[out_h.opt()])
                instr.then_inc(ccsem, 1)
                return instr
            b.plan.op("pool", emit, rkeys, [("ccg", tag)], tag=f"ccp{tag}")
        gather(H["tq_gu"], H["wq_gu_t"], sk["tgu"], "gu")
        gather(H["tq_p"], H["wq_p_t"], sk["tp"], "p")
        gather(H["fup_sl"], H["fup"], sk["fu"], "fu")
        gather(H["fgt_sl"], H["fgt"], sk["fg"], "fg")
        gather(H["fdn_sl"], H["fdn"], sk["fd"], "fd")
        gather(H["woutb_sl"], H["woutb"], sk["pco"], "wo")


def build_prep():
    b = B("prep")
    H = decl_prep(b, fused=False)
    emit_prep(b, H)
    return b


def hadamard_mat(n):
    H = np.array([[1.0]], np.float32)
    while H.shape[0] < n:
        H = np.block([[H, H], [H, -H]]).astype(np.float32)
    return H


def stage_prep(core, inputs):
    su_ug = inputs["L1_ffn_SU_ug"].astype(np.float32)
    su_dp = inputs["L1_ffn_SU_dp"].astype(np.float32)
    s4 = su_ug.reshape(16, 32, 4)
    su_ug4 = np.ascontiguousarray(s4.transpose(2, 0, 1).reshape(64, 32))
    s2 = su_dp.reshape(64, 64, 2)
    su_dp2 = np.ascontiguousarray(s2.transpose(2, 0, 1).reshape(128, 64))
    H16 = hadamard_mat(16)
    H64 = hadamard_mat(64)
    HK16I4 = np.kron(np.eye(4, dtype=np.float32), H16).astype(np.float32)
    HK64I2 = np.kron(np.eye(2, dtype=np.float32), H64).astype(np.float32)
    c = core
    return {
        "wout_s": np.ascontiguousarray(inputs["L0_attn_W_out"].T[c * 512:(c + 1) * 512]),
        "wgu_s": np.ascontiguousarray(inputs["L0_ffn_w_gate_up"][c * 2048:(c + 1) * 2048]),
        "wp_s": np.ascontiguousarray(inputs["L0_ffn_w_proj"][c * 256:(c + 1) * 256]),
        "wup_s": np.ascontiguousarray(inputs["L1_ffn_W_up"][c * 1024:(c + 1) * 1024]),
        "wgt_s": np.ascontiguousarray(inputs["L1_ffn_W_gate"][c * 1024:(c + 1) * 1024]),
        "wdn_s": np.ascontiguousarray(inputs["L1_ffn_W_down"][c * 256:(c + 1) * 256]),
        "su_ug4": su_ug4, "su_dp2": su_dp2,
        "H128": hadamard_mat(128), "HK16I4": HK16I4, "HK64I2": HK64I2,
    }


def assemble_prep(res):
    r = res[0]
    return {
        "wq_gu_t": np.asarray(r["wq_gu_t"]), "wq_p_t": np.asarray(r["wq_p_t"]),
        "wup_t": np.asarray(r["fup"]), "wgt_t": np.asarray(r["fgt"]),
        "wdn_t": np.asarray(r["fdn"]),
    }


EPSF32 = 1.1920929e-7


def decl_combine0(b, nt, fused):
    mid = "Internal" if fused else "ExternalOutput"
    H = {
        "x0T": b.dram("x0T", [2048, nt], F32, kind="ExternalInput"),
        "x1T": b.dram("x1T", [2048, nt], F32, kind=mid),
        "xn1": b.dram("xn1", [2048, nt], BF16, kind=mid),
        "s0d": b.dram("s0d", [2048, nt], BF16),
    }
    if not fused:
        H["woutT"] = b.dram("woutT", [4096, 2048], F32, kind="ExternalInput")
        H["hg"] = b.dram("hg", [4096, nt], BF16, kind="ExternalInput")
        H["xn"] = b.dram("xn", [2048, nt], BF16, kind="ExternalInput")
        H["wq_gu_t"] = b.dram("wq_gu_t", [2048, 16384], BF16, kind="ExternalInput")
        H["wq_p_t"] = b.dram("wq_p_t", [8192, 2048], BF16, kind="ExternalInput")
    return H


def emit_combine0_s0(b, H, nt):
    nc = b.nc
    hg_h = H["hg"]
    wout_h = H.get("woutb") or H["woutT"]
    s0d = H["s0d"]
    gk = list(getattr(b, "gatekeys", []))
    hgs = [b.sbuf(f"hgs{i}", [128, 32 * 512], BF16) for i in range(2)]
    wo = [b.sbuf(f"wo{i}", [128, 32 * 128], BF16) for i in range(2)]
    s0c = [b.sbuf(f"s0c{i}", [128, 512], BF16) for i in range(2)]
    ps = [b.psum(f"s0ps{i}", (128, 512), F32) for i in range(2)]
    ntile = ceil_div(nt, 512)
    for ti in range(ntile):
        off = ti * 512
        w = min(512, nt - off)
        hgb, khg = hgs[ti % 2], ("hgs", ti % 2)
        b.dma("qpool", hgb[:, 0:32 * w].rearrange("p (c t) -> p c t", c=32),
              hg_h.rearrange("(c p) t -> p c t", p=128)[:, :, off:off + w],
              [("dram", "hg")], [khg], tag=f"lhg{ti}")
        for f in range(16):
            wob, kwo = wo[f % 2], ("wo", f % 2)
            b.dma("qsp" if f % 2 == 0 else "qact",
                  wob[:, :].rearrange("p (c n) -> p c n", c=32),
                  wout_h[:, f * 128:(f + 1) * 128].rearrange("(c p) n -> p c n", p=128),
                  gk, [kwo], tag=f"lwo{ti}_{f}")
            pso, kpso = ps[f % 2], ("ps", f % 2)
            for c in range(32):
                b.mm(pso[:, 0:w], wob[:, c * 128:(c + 1) * 128], hgb[:, c * w:(c + 1) * w],
                     start=(c == 0), stop=(c == 31), reads=[kwo, khg], writes=[kpso],
                     tag=f"s0m{ti}_{f}_{c}")
            sc, ksc = s0c[f % 2], ("s0c", f % 2)
            b.dve(b.lp(lambda w=w, pso=pso, sc=sc: nc.vector.tensor_copy(sc[:, 0:w],
                  pso[:, 0:w])), [kpso], [ksc], tag=f"s0e{ti}_{f}")
            b.dma("qpool", s0d[f * 128:(f + 1) * 128, off:off + w], sc[:, 0:w],
                  [ksc], [("dram", "s0")], tag=f"ss0{ti}_{f}")


def emit_combine0(b, H, nt):
    nc = b.nc
    xn_h, x0_h = H["xn"], H["x0T"]
    wgu_h, wp_h = H["wq_gu_t"], H["wq_p_t"]
    x1_h, xn1_h, s0d = H["x1T"], H["xn1"], H["s0d"]
    gk = list(getattr(b, "gatekeys", []))

    def gu_src(m0):
        if wgu_h.shape[0] == 2048:
            return wgu_h[:, m0:m0 + 128]
        s, oc = divmod(m0, 2048)
        return wgu_h[s * 2048:(s + 1) * 2048, oc:oc + 128]

    def wp_src(m0, r0, rn):
        if wp_h.shape[0] == 8192:
            return wp_h[r0:r0 + rn, m0:m0 + 128]
        s, oc = divmod(m0, 256)
        return wp_h[s * 8192 + r0:s * 8192 + r0 + rn, oc:oc + 128]

    xns = b.sbuf("xns", [128, 16 * 512], BF16)
    s0c = b.sbuf("s0c", [128, 512], BF16)
    hsb = b.sbuf("hsb", [128, 64 * 512], BF16)
    x1sb = b.sbuf("x1sb", [128, 16 * 512], BF16)
    x1c = b.sbuf("x1c", [128, 512], F32)
    wg = [b.sbuf(f"wg{i}", [128, 16 * 128], BF16) for i in range(2)]
    wu = [b.sbuf(f"wu{i}", [128, 16 * 128], BF16) for i in range(2)]
    wp = [b.sbuf(f"wp{i}", [128, 64 * 128], BF16) for i in range(2)]
    sg = b.sbuf("sg", [128, 512], BF16)
    su = b.sbuf("su", [128, 512], BF16)
    sq = b.sbuf("sq", [128, 512], BF16)
    x0c = b.sbuf("x0c", [128, 512], F32)
    g0c = b.sbuf("g0c", [128, 512], BF16)
    t2c = b.sbuf("t2c", [128, 512], BF16)
    rstdb = b.sbuf("rstdb", [128, 512], BF16)
    rsr = b.sbuf("rsr", [1, 512], BF16)
    lnv = b.sbuf("lnv", [1, 512], F32)
    stats = b.sbuf("stats", [1, 6 * 512], F32)
    rr = b.sbuf("rr", [1, 512], BF16)
    rm = b.sbuf("rm", [1, 512], BF16)
    ones1 = b.sbuf("ones1", [1, 128], BF16)
    ones128 = b.sbuf("ones128", [128, 1], BF16)
    ps = [b.psum(f"ps{i}", (128, 512), F32) for i in range(6)]
    b.pool(lambda: nc.gpsimd.memset(ones1[:, :], 1.0), [], ["ones1"])
    b.pool(lambda: nc.gpsimd.memset(ones128[:, :], 1.0), [], ["ones128"])
    epsf_c, epsf_k = b.constcol(EPSF32, 1)
    eps5_c, eps5_k = b.constcol(1e-5, 1)

    ntile = ceil_div(nt, 512)
    for ti in range(ntile):
        off = ti * 512
        w = min(512, nt - off)
        b.dma("qpool", xns[:, 0:16 * w].rearrange("p (c t) -> p c t", c=16),
              xn_h.rearrange("(c p) t -> p c t", p=128)[:, :, off:off + w],
              [("dram", "xn")], ["xns"], tag=f"lxn{ti}")
        for g in range(64):
            wgb_, wub_ = wg[g % 2], wu[g % 2]
            kwg, kwu = ("wg", g % 2), ("wu", g % 2)
            b.dma("qsp", wgb_[:, :].rearrange("p (c n) -> p c n", c=16),
                  gu_src(g * 128).rearrange("(c p) n -> p c n", p=128),
                  gk, [kwg], tag=f"lwg{ti}_{g}")
            b.dma("qact", wub_[:, :].rearrange("p (c n) -> p c n", c=16),
                  gu_src(8192 + g * 128).rearrange("(c p) n -> p c n", p=128),
                  gk, [kwu], tag=f"lwu{ti}_{g}")
            pg, kpg = (ps[2], ("ps", 2)) if g % 2 == 0 else (ps[0], ("ps", 0))
            pu, kpu = (ps[3], ("ps", 3)) if g % 2 == 0 else (ps[1], ("ps", 1))
            for c in range(16):
                b.mm(pg[:, 0:w], wgb_[:, c * 128:(c + 1) * 128], xns[:, c * w:(c + 1) * w],
                     start=(c == 0), stop=(c == 15), reads=[kwg, "xns"], writes=[kpg],
                     tag=f"gm{ti}_{g}_{c}")
                b.mm(pu[:, 0:w], wub_[:, c * 128:(c + 1) * 128], xns[:, c * w:(c + 1) * w],
                     start=(c == 0), stop=(c == 15), reads=[kwu, "xns"], writes=[kpu],
                     tag=f"um{ti}_{g}_{c}")
            b.act(b.lp(lambda w=w, pg=pg: nc.scalar.activation(out=sg[:, 0:w], in_=pg[:, 0:w],
                  func=AF.Silu)), [kpg], ["sg"], tag=f"sg{ti}_{g}")
            b.dve(b.lp(lambda w=w, pu=pu: nc.vector.tensor_copy(su[:, 0:w], pu[:, 0:w])),
                  [kpu], ["su"], tag=f"sue{ti}_{g}")
            b.pool(b.lp(lambda g=g, w=w: nc.gpsimd.tensor_mul(hsb[:, g * w:(g + 1) * w],
                   sg[:, 0:w], su[:, 0:w])), ["sg", "su"], [("h", g)], tag=f"hm{ti}_{g}")
            b.pool(b.lp(lambda g=g, w=w: nc.gpsimd.tensor_mul(sq[:, 0:w],
                   hsb[:, g * w:(g + 1) * w], hsb[:, g * w:(g + 1) * w])),
                   [("h", g)], ["sqc"], tag=f"hs{ti}_{g}")
            b.mm(ps[4][0:1, 0:w], ones128[:, :], sq[:, 0:w], start=(g == 0), stop=(g == 63),
                 reads=["sqc", "ones128"], writes=[("ps", 4)], tag=f"hq{ti}_{g}")
        b.act(lambda w=w: nc.scalar.activation(out=lnv[:, 0:w], in_=ps[4][0:1, 0:w],
              func=AF.Ln, scale=1.0 / 8192, bias=epsf_c[:, :]),
              [("ps", 4), epsf_k], ["lnv"], tag=f"rl{ti}")
        b.act(b.lp(lambda w=w: nc.scalar.activation(out=rsr[:, 0:w], in_=lnv[:, 0:w],
              func=AF.Exp, scale=-0.5)), ["lnv"], ["rsr"], tag=f"re{ti}")
        b.mm(ps[5][:, 0:w], ones1[:, :], rsr[:, 0:w], True, True, ["ones1", "rsr"],
             [("ps", 5)], tag=f"rb{ti}")
        b.dve(b.lp(lambda w=w: nc.vector.tensor_copy(rstdb[:, 0:w], ps[5][:, 0:w])),
              [("ps", 5)], ["rstdb"], tag=f"rc{ti}")
        for f in range(16):
            b.dma("qpool", s0c[:, 0:w], s0d[f * 128:(f + 1) * 128, off:off + w],
                  [], ["s0c"], tag=f"ls0{ti}_{f}")
            wpb = wp[f % 2]
            for hh in range(2):
                b.dma("qsp" if hh == 0 else "qact",
                      wpb[:, hh * 32 * 128:(hh + 1) * 32 * 128].rearrange(
                          "p (c n) -> p c n", c=32),
                      wp_src(f * 128, hh * 4096, 4096).rearrange("(c p) n -> p c n", p=128),
                      gk, [("wp", f % 2, hh)], tag=f"lwp{ti}_{f}_{hh}")
            pb, kp = ps[f % 2], ("ps", f % 2)
            for c in range(64):
                b.mm(pb[:, 0:w], wpb[:, c * 128:(c + 1) * 128], hsb[:, c * w:(c + 1) * w],
                     start=(c == 0), stop=(c == 63),
                     reads=[("wp", f % 2, 0 if c < 32 else 1), ("h", c)], writes=[kp],
                     tag=f"dm{ti}_{f}_{c}")
            b.dve(b.lp(lambda w=w, pb=pb: nc.vector.scalar_tensor_tensor(out=t2c[:, 0:w],
                  in0=pb[:, 0:w], scalar=1.0, in1=rstdb[:, 0:w], op0=ALU.mult, op1=ALU.mult)),
                  [kp, "rstdb"], ["t2c"], tag=f"dn{ti}_{f}")
            b.act(b.lp(lambda w=w: nc.scalar.activation(out=g0c[:, 0:w], in_=t2c[:, 0:w],
                  func=AF.Silu)), ["t2c"], ["g0c"], tag=f"g0{ti}_{f}")
            b.dma("qpool", x0c[:, 0:w], x0_h[f * 128:(f + 1) * 128, off:off + w], [], ["x0c"],
                  tag=f"lx0{ti}_{f}")
            b.dve(b.lp(lambda w=w: nc.vector.tensor_mul(t2c[:, 0:w],
                  s0c[:, 0:w], g0c[:, 0:w])), ["s0c", "g0c"], ["t2c"],
                  tag=f"sgm{ti}_{f}")
            b.dve(lambda w=w: nc.vector.tensor_add(x1c[:, 0:w],
                  x0c[:, 0:w], t2c[:, 0:w]), ["x0c", "t2c"], ["x1c"], tag=f"x1{ti}_{f}")
            b.pool(b.lp(lambda f=f, w=w: nc.gpsimd.tensor_copy(x1sb[:, f * w:(f + 1) * w],
                   x1c[:, 0:w])), ["x1c"], [("x1", f)], tag=f"x1b{ti}_{f}")
            b.dma("qsp", x1_h[f * 128:(f + 1) * 128, off:off + w], x1c[:, 0:w],
                  ["x1c"], [("dram", "x1")], tag=f"sx1{ti}_{f}")
        pm, pq = ps[2], ps[3]
        for c in range(16):
            xc = x1sb[:, c * w:(c + 1) * w]
            b.mm(pm[0:1, 0:w], ones128[:, :], xc, start=(c == 0), stop=(c == 15),
                 reads=[("x1", c), "ones128"], writes=[("ps", 2)], tag=f"nsum{ti}_{c}")
            b.pool(b.lp(lambda xc=xc, w=w: nc.gpsimd.tensor_mul(sq[:, 0:w], xc, xc)),
                   [("x1", c)], ["sqc"], tag=f"nsq{ti}_{c}")
            b.mm(pq[0:1, 0:w], ones128[:, :], sq[:, 0:w], start=(c == 0), stop=(c == 15),
                 reads=["sqc", "ones128"], writes=[("ps", 3)], tag=f"nsq2{ti}_{c}")
        mean = stats[:, 0:w]
        ex2 = stats[:, 512:512 + w]
        msq = stats[:, 1024:1024 + w]
        var = stats[:, 1536:1536 + w]
        sd = stats[:, 2048:2048 + w]
        b.dve(lambda w=w, o=mean: nc.vector.tensor_scalar(out=o, in0=pm[0:1, 0:w],
              scalar1=1.0 / 2048, scalar2=None, op0=ALU.mult), [("ps", 2)], [("st", "m")])
        b.dve(lambda w=w, o=ex2: nc.vector.tensor_scalar(out=o, in0=pq[0:1, 0:w],
              scalar1=1.0 / 2048, scalar2=None, op0=ALU.mult), [("ps", 3)], [("st", "e")])
        b.dve(lambda o=msq, a=mean: nc.vector.tensor_mul(o, a, a), [("st", "m")], [("st", "ms")])
        b.dve(lambda o=var, a=ex2, c2=msq: nc.vector.tensor_sub(o, a, c2),
              [("st", "e"), ("st", "ms")], [("st", "v")])
        b.act(lambda o=sd, a=var: nc.scalar.activation(out=o, in_=a, func=AF.Sqrt,
              bias=eps5_c[:, :]), [("st", "v"), eps5_k], [("st", "sd")])
        b.dve(b.lp(lambda w=w, a=sd: nc.vector.reciprocal(rr[:, 0:w], a)), [("st", "sd")], ["rr"])
        b.dve(b.lp(lambda w=w, a=mean: nc.vector.tensor_mul(rm[:, 0:w], a, rr[:, 0:w])),
              [("st", "m"), "rr"], ["rm"])
        b.mm(ps[4][:, 0:w], ones1[:, :], rr[:, 0:w], True, True, ["ones1", "rr"], [("ps", 4)],
             tag=f"nbr{ti}")
        b.mm(ps[5][:, 0:w], ones1[:, :], rm[:, 0:w], True, True, ["ones1", "rm"], [("ps", 5)],
             tag=f"nbm{ti}")
        for c in range(16):
            xc = x1sb[:, c * w:(c + 1) * w]
            b.dve(b.lp(lambda xc=xc, w=w: nc.vector.tensor_mul(sq[:, 0:w], xc, ps[4][0:128, 0:w])),
                  [("x1", c), ("ps", 4)], ["sqc"], tag=f"xnm{ti}_{c}")
            b.dve(b.lp(lambda w=w: nc.vector.tensor_sub(t2c[:, 0:w], sq[:, 0:w], ps[5][0:128, 0:w])),
                  ["sqc", ("ps", 5)], ["t2c"], tag=f"xns{ti}_{c}")
            b.dma("qsp", xn1_h[c * 128:(c + 1) * 128, off:off + w], t2c[:, 0:w],
                  ["t2c"], [("dram", "xn1")], tag=f"sxn1{ti}_{c}")


def build_combine0(nt=NT_FULL):
    b = B("combine0")
    H = decl_combine0(b, nt, fused=False)
    emit_combine0_s0(b, H, nt)
    b.bar()
    emit_combine0(b, H, nt)
    return b


def stage_combine0(core, artsA, artsB, prep, inputs, nt=NT_FULL):
    bidx, half = core // 2, core % 2
    x = inputs["L0_attn_hidden_states"][bidx]
    lo = half * nt
    return {
        "hg": artsB["hg"], "xn": artsA["xn"],
        "x0T": np.ascontiguousarray(x[lo:lo + nt].T),
        "woutT": np.ascontiguousarray(inputs["L0_attn_W_out"].T),
        "wq_gu_t": prep["wq_gu_t"], "wq_p_t": prep["wq_p_t"],
    }


def decl_l1mix(b, nt, fused):
    mid = "Internal" if fused else "ExternalOutput"
    H = {
        "convd1": b.dram("convd1", [16 * 4 * 128, 128], F32, kind="ExternalInput"),
        "convb": b.dram("convb", [128, 16], F32, kind="ExternalInput"),
        "inwr_t": b.dram("inwr_t", [2048, 64], F32, kind="ExternalInput"),
        "inwk_t": b.dram("inwk_t", [2048, 64], F32, kind="ExternalInput"),
        "mgws_t": b.dram("mgws_t", [2048, 64], F32, kind="ExternalInput"),
        "mgwt_t": b.dram("mgwt_t", [2048, 64], F32, kind="ExternalInput"),
        "mgbs": b.dram("mgbs", [64, 1], F32, kind="ExternalInput"),
        "mgbt": b.dram("mgbt", [64, 1], F32, kind="ExternalInput"),
        "g0w_t": b.dram("g0w_t", [2048, 128], F32, kind="ExternalInput"),
        "g1w_t": b.dram("g1w_t", [128, 2048], F32, kind="ExternalInput"),
        "b1r": b.dram("b1r", [1, 2048], F32, kind="ExternalInput"),
        "resw": b.dram("resw", [128, 16], F32, kind="ExternalInput"),
        "mask01": b.dram("mask01", [128, 128], mybir.dt.int32, kind="ExternalInput"),
        "id128b": b.dram("id128b", [128, 128], F32, kind="ExternalInput"),
        "s1T": b.dram("s1T", [2048, nt], BF16, kind=mid),
        "s1fin": b.dram("s1fin", [64, 2048], F32, kind=mid),
    }
    if fused:
        H["rdall"] = b.dram("rdall", [64, nt], BF16)
        H["halo3"] = b.dram("halo3", [2048, HALO], BF16)
        H["cc3in"] = b.dram("cc3in", [64, 2048], F32)
        H["cc3out"] = b.dram("cc3out", [128, 2048], F32)
        H["sprevd"] = b.dram("sprevd", [64, 2048], F32)
    else:
        H["xn1f"] = b.dram("xn1f", [2048, nt + HALO], BF16, kind="ExternalInput")
        H["s1init"] = b.dram("s1init", [64, 2048], F32, kind="ExternalInput")
    return H


def emit_l1mix(b, H, nt, fused):
    nch = nt // 128
    nc = b.nc
    convd_h, convb_h = H["convd1"], H["convb"]
    inwr_h, inwk_h = H["inwr_t"], H["inwk_t"]
    mgws_h, mgwt_h, mgbs_h, mgbt_h = H["mgws_t"], H["mgwt_t"], H["mgbs"], H["mgbt"]
    g0w_h, g1w_h, b1r_h = H["g0w_t"], H["g1w_t"], H["b1r"]
    resw_h, mask_h, id128_h = H["resw"], H["mask01"], H["id128b"]
    s1_h, sfin_h = H["s1T"], H["s1fin"]
    if not fused:
        xf_h = H["xn1f"]
        sin_h = H["s1init"]
    else:
        xf_h = H["xn1"]
    xf = b.sbuf("xf", [128, 16 * (nt + HALO)], BF16)
    shift = b.sbuf("shift", [128, 16 * nt], BF16)
    rs = b.sbuf("rs", [64, nt], BF16)
    ks = b.sbuf("ks", [64, nt], BF16)
    sel = b.sbuf("sel", [64, 512], F32)
    tau = b.sbuf("tau", [64, 512], F32)
    itg = b.sbuf("itg", [64, 512], F32)
    gg = b.sbuf("gg", [64, nt], F32)
    kh = b.sbuf("kh", [64, nt], BF16)
    t64 = b.sbuf("t64", [64, 512], F32)
    S = b.sbuf("S1", [64, 2048], BF16)
    cw = b.sbuf("cw", [128, 4 * 128], BF16)
    cb = b.sbuf("cb", [128, 16], F32)
    inwr = b.sbuf("inwr", [128, 16 * 64], BF16)
    inwk = b.sbuf("inwk", [128, 16 * 64], BF16)
    mgws = b.sbuf("mgws", [128, 16 * 64], BF16)
    mgwt = b.sbuf("mgwt", [128, 16 * 64], BF16)
    mgbs = b.sbuf("mgbs_s", [64, 1], F32)
    mgbt = b.sbuf("mgbt_s", [64, 1], F32)
    g0w = b.sbuf("g0w", [128, 16 * 128], BF16)
    g1w = b.sbuf("g1w", [128, 2048], BF16)
    b1b = b.sbuf("b1b", [128, 2048], BF16)
    resw = b.sbuf("resw_s", [128, 16], F32)
    mask01 = b.sbuf("mask01_s", [128, 128], mybir.dt.int32)
    zeros = b.sbuf("zeros", [128, 128], BF16)
    zero64 = b.sbuf("zero64", [64, 128], F32)
    ones1 = b.sbuf("ones1", [1, 128], BF16)
    ones1f = b.sbuf("ones1f", [1, 128], F32)
    ones64 = b.sbuf("ones64", [64, 1], F32)
    Gc = b.sbuf("Gc1", [64, 128], F32)
    eGc = b.sbuf("eGc", [64, 128], F32)
    rti = b.sbuf("rti", [64, 128], BF16)
    d2 = b.sbuf("d2", [64, 128], F32)
    er = b.sbuf("er", [64, 128], F32)
    rh = b.sbuf("rh", [64, 128], BF16)
    d1 = b.sbuf("d1", [64, 32], F32)
    ek = b.sbuf("ek", [64, 32], F32)
    ktb = b.sbuf("ktb", [64, 32], BF16)
    Asb = b.sbuf("Asb", [128, 128], BF16)
    xtm = b.sbuf("xtm", [128, 2048], BF16)
    utm = b.sbuf("utm", [128, 2048], BF16)
    tmpc = b.sbuf("tmpc", [128, 128], BF16)
    chc = b.sbuf("chc", [128, 512], BF16)
    t2c = b.sbuf("t2c", [128, 512], F32)
    kbar = b.sbuf("kbar", [64, 128], BF16)
    kbt = b.sbuf("kbt", [128, 64], BF16)
    egl1 = b.sbuf("egl1", [64, 1], F32)
    s1stg = b.sbuf("s1stg", [128, 16 * 128], BF16)
    ocp = b.sbuf("ocp", [128, 128], F32)
    srow = b.sbuf("srow", [1, 512], F32)
    ps = [b.psum(f"ps{i}", (128, 512), F32) for i in range(6)]
    pbf = [b.psum(f"pbf{i}", (128, 1024), BF16) for i in range(2)]
    id128 = b.sbuf("id128s", [128, 128], BF16)

    b.pool(lambda: nc.gpsimd.memset(ones1[:, :], 1.0), [], ["ones1"])
    b.pool(lambda: nc.gpsimd.memset(ones1f[:, :], 1.0), [], ["ones1f"])
    b.pool(lambda: nc.gpsimd.memset(ones64[:, :], 1.0), [], ["ones64"])
    b.pool(lambda: nc.gpsimd.memset(zeros[:, :], 0.0), [], ["zeros"])
    b.pool(lambda: nc.gpsimd.memset(zero64[:, :], 0.0), [], ["zero64"])
    one_c, one_k = b.constcol(1.0, 64)
    for t, h, kk in ((cb, convb_h, "cb"), (mgbs, mgbs_h, "mgbs"), (mgbt, mgbt_h, "mgbt"),
                     (resw, resw_h, "resw"), (mask01, mask_h, "mask01"), (id128, id128_h, "id128")):
        b.dma("qpool", t[:, :], h, [], [kk])
    if fused:
        b.dma("qpool", xf[:, :].rearrange("p (c t) -> p c t", c=16)[:, :, 0:HALO],
              H["halo3"].rearrange("(c p) t -> p c t", p=128), [("dram", "halo3")],
              ["xf"], tag="lxfh")
        b.dma("qact", xf[:, :].rearrange("p (c t) -> p c t", c=16)[:, :, HALO:],
              H["xn1"].rearrange("(c p) t -> p c t", p=128), [("dram", "xn1")],
              ["xf"], tag="lxfm")
    else:
        b.dma("qpool", xf[:, :].rearrange("p (c t) -> p c t", c=16),
              xf_h.rearrange("(c p) t -> p c t", p=128), [], ["xf"])
    b.dma("qpool", cw[:, :], convd_h.rearrange("(z j p) n -> z p j n", j=4, p=128)[0], [], ["cwz"])
    b.dma("qpool", inwr[:, :].rearrange("p (c n) -> p c n", c=16),
          inwr_h.rearrange("(c p) n -> p c n", p=128), [], ["inwr"])
    b.dma("qpool", inwk[:, :].rearrange("p (c n) -> p c n", c=16),
          inwk_h.rearrange("(c p) n -> p c n", p=128), [], ["inwk"])
    b.dma("qpool", mgws[:, :].rearrange("p (c n) -> p c n", c=16),
          mgws_h.rearrange("(c p) n -> p c n", p=128), [], ["mgws"])
    b.dma("qpool", mgwt[:, :].rearrange("p (c n) -> p c n", c=16),
          mgwt_h.rearrange("(c p) n -> p c n", p=128), [], ["mgwt"])
    b.dma("qpool", g0w[:, :].rearrange("p (c n) -> p c n", c=16),
          g0w_h.rearrange("(c p) n -> p c n", p=128), [], ["g0w"])
    b.dma("qpool", g1w[:, :], g1w_h, [], ["g1w"])
    if fused:
        b.pool(lambda: nc.gpsimd.memset(S[:, :], 0.0), [], [("S1k",)])
        glsum = b.sbuf("glsum", [64, 1], F32)
        dpref = b.sbuf("dpref", [64, 1], F32)
        rdc = b.sbuf("rdc", [64, 128], BF16)
        b.pool(lambda: nc.gpsimd.memset(glsum[:, :], 0.0), [], ["glsum"])
    else:
        b.dma("qpool", S[:, :], sin_h, [], [("S1k",)])
    b1r_s = b.sbuf("b1r_s", [1, 512], F32)
    for q4 in range(4):
        b.dma("qpool", b1r_s[:, :], b1r_h[0:1, q4 * 512:(q4 + 1) * 512], ["b1r"], ["b1r"],
              tag=f"b1l{q4}")
        b.mm(ps[5][:, 0:512], ones1f[:, :], b1r_s[0:1, 0:512], True, True,
             ["ones1f", "b1r"], [("ps", 5)], tag=f"b1b{q4}")
        b.dve(b.lp(lambda q4=q4: nc.vector.tensor_copy(b1b[:, q4 * 512:(q4 + 1) * 512],
              ps[5][:, 0:512])), [("ps", 5)], ["b1b"], tag=f"b1c{q4}")

    ntile = ceil_div(nt, 512)
    for ci in range(16):
        for ti in range(ntile):
            off = ti * 512
            w = min(512, nt - off)
            pb, kp = ps[ti % 2], ("ps", ti % 2)
            for j in range(4):
                b.mm(pb[:, 0:w], cw[:, (4 * ci + j) * 0 + j * 128:(j + 1) * 128] if False else cw[:, j * 128:(j + 1) * 128],
                     xf[:, ci * (nt + HALO) + off + j:ci * (nt + HALO) + off + j + w],
                     start=(j == 0), stop=(j == 3), reads=["cwz", "xf"], writes=[kp],
                     tag=f"cv{ci}_{ti}_{j}")
            b.act(b.lp(lambda pb=pb, w=w, ci=ci, off=off: nc.scalar.activation(
                  out=shift[:, ci * nt + off:ci * nt + off + w], in_=pb[:, 0:w], func=AF.Silu,
                  bias=cb[:, ci:ci + 1])), [kp, "cb"], [("sh", ci)], tag=f"cs{ci}_{ti}")
        if ci < 15:
            b.dma("qpool", cw[:, :], convd_h.rearrange("(z j p) n -> z p j n", j=4, p=128)[ci + 1],
                  ["cwz"], ["cwz"], tag=f"lcw{ci}")
    SH = [("sh", i) for i in range(16)]
    for ti in range(ntile):
        off = ti * 512
        w = min(512, nt - off)
        slt = slice(off, off + w)
        wsl = slice(0, w)
        for name, wtile, kw, dst, kd, bias in (("r", inwr, "inwr", rs, "rs", None),
                                               ("k", inwk, "inwk", ks, "ks", None),
                                               ("s", mgws, "mgws", sel, "sel0", mgbs),
                                               ("t", mgwt, "mgwt", tau, "tau0", mgbt)):
            pb, kp = ps[2], ("ps", 2)
            for c in range(16):
                b.mm(pb[0:64, 0:w], wtile[:, c * 64:(c + 1) * 64],
                     shift[:, c * nt + off:c * nt + off + w], start=(c == 0), stop=(c == 15),
                     reads=[kw] + SH, writes=[kp], tag=f"pj{name}{ti}_{c}")
            if bias is None:
                sc = 0.125 if name == "r" else 1.0
                b.dve(b.lp(lambda pb=pb, w=w, dst=dst, off=off, sc=sc: nc.vector.tensor_scalar(
                      out=dst[:, off:off + w], in0=pb[0:64, 0:w], scalar1=sc, scalar2=None,
                      op0=ALU.mult)), [kp], [(kd, ti)], tag=f"pe{name}{ti}")
            else:
                kb = "mgbs" if name == "s" else "mgbt"
                b.act(lambda pb=pb, w=w, dst=dst, bias=bias: nc.scalar.activation(
                      out=dst[:, 0:w], in_=pb[0:64, 0:w], func=AF.Identity,
                      bias=bias[:, :]), [kp, kb], [(kd, ti)], tag=f"pe{name}{ti}")
        b.act(lambda wsl=wsl: nc.scalar.activation(out=t64[:, wsl], in_=sel[:, wsl], func=AF.Abs),
              [("sel0", ti)], ["t64a"], tag=f"sp1{ti}")
        b.act(lambda wsl=wsl: nc.scalar.activation(out=t64[:, wsl], in_=t64[:, wsl], func=AF.Exp,
              scale=-1.0), ["t64a"], ["t64a"], tag=f"sp2{ti}")
        b.act(lambda wsl=wsl: nc.scalar.activation(out=t64[:, wsl], in_=t64[:, wsl], func=AF.Ln,
              bias=one_c[:, :]), ["t64a", one_k], ["t64a"], tag=f"sp3{ti}")
        b.act(lambda wsl=wsl: nc.scalar.activation(out=sel[:, wsl], in_=sel[:, wsl], func=AF.Relu),
              [("sel0", ti)], [("selr", ti)], tag=f"sp4{ti}")
        b.dve(lambda wsl=wsl: nc.vector.tensor_add(sel[:, wsl], sel[:, wsl], t64[:, wsl]),
              [("selr", ti), "t64a"], [("sel", ti)], tag=f"sp5{ti}")
        b.act(lambda wsl=wsl: nc.scalar.activation(out=tau[:, wsl], in_=tau[:, wsl], func=AF.Sigmoid),
              [("tau0", ti)], [("tau", ti)], tag=f"sg{ti}")
        b.dve(lambda wsl=wsl: nc.vector.tensor_scalar(out=t64[:, wsl], in0=sel[:, wsl],
              scalar1=1e-19, scalar2=None, op0=ALU.max), [("sel", ti)], ["t64a"], tag=f"it1{ti}")
        b.act(lambda wsl=wsl: nc.scalar.activation(out=t64[:, wsl], in_=t64[:, wsl], func=AF.Ln),
              ["t64a"], ["t64a"], tag=f"it2{ti}")
        b.dve(lambda wsl=wsl: nc.vector.tensor_mul(t64[:, wsl], t64[:, wsl], tau[:, wsl]),
              ["t64a", ("tau", ti)], ["t64a"], tag=f"it3{ti}")
        b.act(lambda wsl=wsl: nc.scalar.activation(out=itg[:, wsl], in_=t64[:, wsl], func=AF.Exp),
              ["t64a"], [("itg", ti)], tag=f"it4{ti}")
        b.dve(lambda slt=slt, wsl=wsl: nc.vector.scalar_tensor_tensor(out=gg[:, slt], in0=sel[:, wsl],
              scalar=-1.0, in1=tau[:, wsl], op0=ALU.mult, op1=ALU.mult),
              [("sel", ti), ("tau", ti)], [("gg", ti)], tag=f"gm{ti}")
        b.pool(lambda slt=slt, wsl=wsl: nc.gpsimd.tensor_mul(t64[:, wsl], ks[:, slt], ks[:, slt]),
               [("ks", ti)], ["t64a"], tag=f"kn0{ti}")
        b.mm(ps[4][0:1, 0:w], ones64[:, :], t64[:, 0:w], True, True, ["ones64", "t64a"],
             [("ps", 4)], tag=f"kn1{ti}")
        b.act(lambda w=w: nc.scalar.activation(out=srow[:, 0:w], in_=ps[4][0:1, 0:w],
              func=AF.Sqrt), [("ps", 4)], ["srow"], tag=f"kn2{ti}")
        b.dve(lambda w=w: nc.vector.tensor_scalar(out=srow[:, 0:w], in0=srow[:, 0:w],
              scalar1=1e-12, scalar2=None, op0=ALU.max), ["srow"], ["srow"], tag=f"kn3{ti}")
        b.dve(lambda w=w: nc.vector.reciprocal(srow[:, 0:w], srow[:, 0:w]), ["srow"], ["srow"],
              tag=f"kn4{ti}")
        b.mm(ps[4][0:64, 0:w], ones1f[0:1, 0:64], srow[0:1, 0:w], True, True, ["ones1f", "srow"],
             [("ps", 4)], tag=f"kn5{ti}")
        b.dve(lambda wsl=wsl, w=w: nc.vector.tensor_mul(t64[:, wsl], itg[:, wsl], ps[4][0:64, 0:w]),
              [("itg", ti), ("ps", 4)], ["t64a"], tag=f"kn6{ti}")
        b.dve(b.lp(lambda slt=slt, wsl=wsl: nc.vector.tensor_mul(kh[:, slt], ks[:, slt], t64[:, wsl])),
              [("ks", ti), "t64a"], [("kh", ti)], tag=f"kn7{ti}")

    KH = [("kh", i) for i in range(ntile)]
    GGk = [("gg", i) for i in range(ntile)]
    RS = [("rs", i) for i in range(ntile)]
    for c in range(nch):
        sl = slice(c * 128, (c + 1) * 128)
        b.dve(lambda sl=sl: nc.vector.tensor_tensor_scan(out=Gc[:, :], data0=gg[:, sl],
              data1=zero64[:, :], initial=0.0, op0=ALU.add, op1=ALU.add),
              GGk + ["zero64"], ["Gc1"], tag=f"sc{c}")
        b.act(lambda: nc.scalar.activation(out=eGc[:, :], in_=Gc[:, :], func=AF.Exp),
              ["Gc1"], ["eGc"], tag=f"eg{c}")
        b.dve(b.lp(lambda sl=sl: nc.vector.tensor_mul(rti[:, :], rs[:, sl], eGc[:, :])),
              RS + ["eGc"], ["rti"], tag=f"rt{c}")
        if fused:
            b.act(lambda: nc.scalar.activation(out=dpref[:, :], in_=glsum[:, :], func=AF.Exp),
                  ["glsum"], ["dpref"], tag=f"dp{c}")
            b.dve(b.lp(lambda: nc.vector.tensor_scalar(out=rdc[:, :], in0=rti[:, :],
                  scalar1=dpref[:, :], scalar2=None, op0=ALU.mult)),
                  ["rti", "dpref"], ["rdc"], tag=f"rd{c}")
            b.dma("qsp", H["rdall"][:, sl], rdc[:, :], ["rdc"], [("dram", "rdall")],
                  tag=f"srd{c}")
            b.dve(lambda: nc.vector.tensor_add(glsum[:, :], glsum[:, :], Gc[:, 127:128]),
                  ["glsum", "Gc1"], ["glsum"], tag=f"gs{c}")
        hoff = 0 if fused else HALO
        b.dma("qsp", xtm[:, :], xf_h[:, hoff + c * 128:hoff + (c + 1) * 128],
              [("dram", "xn1")] if fused else [], ["xtm"], tag=f"xt{c}", transpose=True)
        pt, kpt = ps[3], ("ps", 3)
        for k16 in range(16):
            b.mm(pt[:, 0:128], g0w[:, k16 * 128:(k16 + 1) * 128],
                 xf[:, k16 * (nt + HALO) + HALO + c * 128:k16 * (nt + HALO) + HALO + (c + 1) * 128],
                 start=(k16 == 0), stop=(k16 == 15), reads=["g0w", "xf"], writes=[kpt],
                 tag=f"ch1{c}_{k16}")
        b.dve(b.lp(lambda: nc.vector.tensor_copy(tmpc[:, :], ps[3][:, 0:128])),
              [kpt], ["tmpc"], tag=f"ch2{c}")
        for q4 in range(4):
            b.mm(ps[2][:, 0:512], tmpc[:, :], g1w[:, q4 * 512:(q4 + 1) * 512], True, True,
                 ["tmpc", "g1w"], [("ps", 2)], tag=f"ch3{c}_{q4}")
            b.dve(lambda q4=q4: nc.vector.tensor_add(t2c[:, 0:512], ps[2][:, 0:512],
                  b1b[:, q4 * 512:(q4 + 1) * 512]), [("ps", 2), "b1b"], ["t2c1"], tag=f"ch4{c}_{q4}")
            b.act(b.lp(lambda q4=q4: nc.scalar.activation(out=chc[:, 0:512], in_=t2c[:, 0:512],
                  func=AF.Sigmoid)), ["t2c1"], ["chc"], tag=f"ch5{c}_{q4}")
            b.pool(b.lp(lambda q4=q4: nc.gpsimd.tensor_mul(utm[:, q4 * 512:(q4 + 1) * 512],
                   chc[:, 0:512], xtm[:, q4 * 512:(q4 + 1) * 512])),
                   ["chc", "xtm"], [("utm", q4)], tag=f"ch6{c}_{q4}")
        UT = [("utm", i) for i in range(4)]
        for blk in range(4):
            bsl = slice(blk * 32, (blk + 1) * 32)
            refc = Gc[:, blk * 32 + 16:blk * 32 + 17]
            b.dve(lambda bsl=bsl, refc=refc: nc.vector.tensor_scalar(out=d1[:, :],
                  in0=Gc[:, bsl], scalar1=refc, scalar2=-85.0, op0=ALU.subtract, op1=ALU.max),
                  ["Gc1"], ["d1"], tag=f"kb1{c}_{blk}")
            b.act(lambda: nc.scalar.activation(out=ek[:, :], in_=d1[:, :], func=AF.Exp,
                  scale=-1.0), ["d1"], ["ek"], tag=f"kb2{c}_{blk}")
            b.dve(b.lp(lambda c=c, blk=blk: nc.vector.tensor_mul(ktb[:, :],
                  kh[:, c * 128 + blk * 32:c * 128 + (blk + 1) * 32], ek[:, :])),
                  KH + ["ek"], ["ktb"], tag=f"kb3{c}_{blk}")
            b.dve(lambda refc=refc: nc.vector.tensor_scalar(out=d2[:, :], in0=Gc[:, :],
                  scalar1=refc, scalar2=85.0, op0=ALU.subtract, op1=ALU.min),
                  ["Gc1"], ["d2"], tag=f"rh1{c}_{blk}")
            b.act(lambda: nc.scalar.activation(out=er[:, :], in_=d2[:, :], func=AF.Exp),
                  ["d2"], ["er"], tag=f"rh2{c}_{blk}")
            b.dve(b.lp(lambda sl=sl: nc.vector.tensor_mul(rh[:, :], rs[:, sl], er[:, :])),
                  RS + ["er"], ["rh"], tag=f"rh3{c}_{blk}")
            if blk > 0:
                b.dve(b.lp(lambda blk=blk: nc.vector.memset(rh[:, 0:blk * 32], 0.0)),
                      ["rh"], ["rh"], tag=f"rh4{c}_{blk}")
            b.mm(ps[0][blk * 32:(blk + 1) * 32, 0:128], ktb[:, :], rh[:, :], True, True,
                 ["ktb", "rh"], [("ps", 0)], tag=f"am{c}_{blk}", tile_position=(0, blk * 32))
        b.dve(b.lp(lambda: nc.vector.tensor_copy(Asb[:, :], zeros[:, :])),
              ["zeros"], ["Asb"], tag=f"az{c}")
        b.dve(b.lp(lambda: nc.vector.copy_predicated(out=Asb[:, :], mask=mask01[:, :],
              data=ps[0][:, 0:128])), [("ps", 0), "mask01", "Asb"], ["Asb"], tag=f"asel{c}")
        if False and c == 0:
            for nm, t, rd in (("Asb0", Asb, ["Asb"]), ("utm0", utm, UT), ("rti0", rti, ["rti"]),
                              ("Gc0", Gc, ["Gc1"])):
                hh = b.dram("tap_" + nm, list(t.shape), F32, kind="ExternalOutput")
                b.dma("qpool", hh, t[:, :], rd, [], tag="tap" + nm)
        for dc in range(16):
            psl = ps[1]
            b.mm(psl[:, 0:128], utm[:, dc * 128:(dc + 1) * 128], Asb[:, :], True, False,
                 UT + ["Asb"], [("ps", 1)], tag=f"o1{c}_{dc}")
            b.mm(psl[:, 0:128], S[:, dc * 128:(dc + 1) * 128], rti[:, :], False, True,
                 [("S1k",), "rti"], [("ps", 1)], tag=f"o2{c}_{dc}")
            b.dve(b.lp(lambda: nc.vector.tensor_copy(ocp[:, :], ps[1][:, 0:128])),
                  [("ps", 1)], ["ocp"], tag=f"ocp{c}_{dc}")
            b.dve(b.lp(lambda dc=dc, c=c: nc.vector.scalar_tensor_tensor(
                  out=s1stg[:, dc * 128:(dc + 1) * 128],
                  in0=shift[:, dc * nt + c * 128:dc * nt + (c + 1) * 128],
                  scalar=resw[:, dc:dc + 1], in1=ocp[:, :], op0=ALU.mult, op1=ALU.add)),
                  [("sh", dc), "resw", "ocp"], [("s1s", dc)], tag=f"o3{c}_{dc}")
            if False and c == 0:
                otap = b.sbuf(f"otap{dc}", [128, 128], F32) if dc == 0 and not hasattr(b, "_otap") else getattr(b, "_otap", None)
                if dc == 0:
                    b._otap = otap
                    b._otap_h = b.dram("tap_ops", [128, 16 * 128], F32, kind="ExternalOutput")
                b.dve(lambda dc=dc: nc.vector.tensor_copy(b._otap[:, :], ps[1][:, 0:128]),
                      [("ps", 1)], ["otapb"], tag=f"ot{c}_{dc}")
                b.dma("qpool", b._otap_h[:, dc * 128:(dc + 1) * 128], b._otap[:, :],
                      ["otapb"], [], tag=f"otd{c}_{dc}")
                if dc == 0:
                    b._octap_h = b.dram("tap_ocp", [128, 128], F32, kind="ExternalOutput")
                    b.dma("qpool", b._octap_h, ocp[:, :], ["ocp"], [], tag="ocptap")
                    b._shtap_h = b.dram("tap_shslice", [128, 128], F32, kind="ExternalOutput")
                    b.dma("qpool", b._shtap_h, shift[:, 0 * nt + c * 128:0 * nt + (c + 1) * 128],
                          [("sh", 0)], [], tag="shslicetap")
        b.dma("qsp", s1_h.rearrange("(dc p) t -> p dc t", p=128)[:, :, sl],
              s1stg[:, :].rearrange("p (dc t) -> p dc t", dc=16),
              [("s1s", i) for i in range(16)], [("dram", "s1T")], tag=f"so{c}")
        if False and c == 0:
            hh = b.dram("tap_s1stg", [128, 2048], F32, kind="ExternalOutput")
            b.dma("qpool", hh, s1stg[:, :], [("s1s", i) for i in range(16)], [], tag="taps1stg")
        b.dve(lambda: nc.vector.tensor_scalar(out=d2[:, :], in0=Gc[:, :],
              scalar1=Gc[:, 127:128], scalar2=None, op0=ALU.subtract), ["Gc1"], ["d2"],
              tag=f"kb5{c}")
        b.act(lambda: nc.scalar.activation(out=er[:, :], in_=d2[:, :], func=AF.Exp, scale=-1.0),
              ["d2"], ["er"], tag=f"kb6{c}")
        b.dve(b.lp(lambda sl=sl: nc.vector.tensor_mul(kbar[:, :], kh[:, sl], er[:, :])),
              KH + ["er"], ["kbar"], tag=f"kb7{c}")
        b.transp(pbf[0][0:128, 0:64], kbar[:, :], id128[0:64, 0:64],
                 ["kbar", "id128"], [("pbf", 0)], tag=f"kt{c}")
        b.dve(b.lp(lambda: nc.vector.tensor_copy(kbt[:, :], pbf[0][0:128, 0:64])),
              [("pbf", 0)], ["kbt"], tag=f"ktc{c}")
        b.act(lambda: nc.scalar.activation(out=egl1[:, :], in_=Gc[:, 127:128], func=AF.Exp),
              ["Gc1"], ["eglc1"], tag=f"egl{c}")
        for q4 in range(4):
            b.mm(ps[2][0:64, 0:512], kbt[:, :], utm[:, q4 * 512:(q4 + 1) * 512], True, True,
                 ["kbt"] + UT, [("ps", 2)], tag=f"ds{c}_{q4}")
            b.dve(b.lp(lambda q4=q4: nc.vector.scalar_tensor_tensor(
                  out=S[:, q4 * 512:(q4 + 1) * 512], in0=S[:, q4 * 512:(q4 + 1) * 512],
                  scalar=egl1[:, :], in1=ps[2][0:64, 0:512], op0=ALU.mult, op1=ALU.add)),
                  [("S1k",), "eglc1", ("ps", 2)], [("S1k",)], tag=f"su{c}_{q4}")
    b.dma("qpool", H["cc3in"] if fused else sfin_h, S[:, :], [("S1k",)],
          [("dram", "cc3in" if fused else "s1fin")], tag="stS1")


def build_l1mix(nt=NT_FULL, debug_taps=False):
    b = B("l1mix")
    H = decl_l1mix(b, nt, fused=False)
    emit_l1mix(b, H, nt, fused=False)
    return b


def stage_l1mix(core, xn1, xn1_halo, inputs, nt=NT_FULL, s1init=None):
    cw = inputs["L1_attn_conv_w"]
    cd = np.zeros((16, 4, 128, 128), np.float32)
    idx = np.arange(128)
    for ci in range(16):
        for j in range(4):
            cd[ci, j, idx, idx] = cw[ci * 128:(ci + 1) * 128, j]
    inw = inputs["L1_attn_in_proj_w"]
    mgw = inputs["L1_attn_mem_gate_w"]
    mgb = inputs["L1_attn_mem_gate_b"]
    xn1f = np.concatenate([np.asarray(xn1_halo), np.asarray(xn1)], 1)
    return {
        "xn1f": np.ascontiguousarray(xn1f),
        "convd1": cd.reshape(16 * 4 * 128, 128),
        "convb": np.ascontiguousarray(inputs["L1_attn_conv_b"].reshape(16, 128).T),
        "inwr_t": np.ascontiguousarray(inw[:64].T), "inwk_t": np.ascontiguousarray(inw[64:].T),
        "mgws_t": np.ascontiguousarray(mgw[:64].T), "mgwt_t": np.ascontiguousarray(mgw[64:].T),
        "mgbs": mgb[:64].reshape(64, 1).astype(np.float32),
        "mgbt": mgb[64:].reshape(64, 1).astype(np.float32),
        "g0w_t": np.ascontiguousarray(inputs["L1_attn_ch_gate_w0"].T),
        "g1w_t": np.ascontiguousarray(inputs["L1_attn_ch_gate_w1"].T),
        "b1r": inputs["L1_attn_ch_gate_b1"].reshape(1, 2048).astype(np.float32),
        "resw": np.ascontiguousarray(inputs["L1_attn_residual_weight"].reshape(16, 128).T),
        "mask01": np.where(np.arange(128)[:, None] <= np.arange(128)[None, :], 1, 0).astype(np.int32),
        "id128b": np.eye(128, dtype=np.float32),
        "s1init": np.zeros((64, 2048), np.float32) if s1init is None else np.asarray(s1init, np.float32),
    }


def decl_l1ffn(b, nt, fused):
    H = {
        "wscg": b.dram("wscg", [128, 64], F32, kind="ExternalInput"),
        "wscu": b.dram("wscu", [128, 64], F32, kind="ExternalInput"),
        "wscd": b.dram("wscd", [128, 16], F32, kind="ExternalInput"),
        "outT": b.dram("outT", [2048, nt], F32, kind="ExternalOutput"),
    }
    if not fused:
        H["xn1"] = b.dram("xn1", [2048, nt], BF16, kind="ExternalInput")
        H["x1T"] = b.dram("x1T", [2048, nt], F32, kind="ExternalInput")
        H["s1T"] = b.dram("s1T", [2048, nt], BF16, kind="ExternalInput")
        H["fgt"] = b.dram("wgt_t", [2048, 8192], BF16, kind="ExternalInput")
        H["fup"] = b.dram("wup_t", [2048, 8192], BF16, kind="ExternalInput")
        H["fdn"] = b.dram("wdn_t", [8192, 2048], BF16, kind="ExternalInput")
    return H


def emit_l1ffn(b, H, nt, fused):
    nc = b.nc
    xn_h, x1_h, s1_h = H["xn1"], H["x1T"], H["s1T"]
    wgt_h, wup_h, wdn_h = H["fgt"], H["fup"], H["fdn"]
    wscg_h, wscu_h, wscd_h = H["wscg"], H["wscu"], H["wscd"]
    out_h = H["outT"]
    gk = list(getattr(b, "gatekeys", [])) if fused else []

    def ug_src(h, m0):
        if h.shape[0] == 2048:
            return h[:, m0:m0 + 128]
        s, oc = divmod(m0, 1024)
        return h[s * 2048:(s + 1) * 2048, oc:oc + 128]

    def dn_src(m0, r0, rn):
        if wdn_h.shape[0] == 8192:
            return wdn_h[r0:r0 + rn, m0:m0 + 128]
        s, oc = divmod(m0, 256)
        return wdn_h[s * 8192 + r0:s * 8192 + r0 + rn, oc:oc + 128]

    xns = b.sbuf("xns", [128, 16 * 512], BF16)
    hsb = b.sbuf("hsb", [128, 64 * 512], BF16)
    wg = [b.sbuf(f"wg{i}", [128, 16 * 128], BF16) for i in range(2)]
    wu = [b.sbuf(f"wu{i}", [128, 16 * 128], BF16) for i in range(2)]
    wp = [b.sbuf(f"wp{i}", [128, 64 * 128], BF16) for i in range(2)]
    wscg = b.sbuf("wscg_s", [128, 64], F32)
    wscu = b.sbuf("wscu_s", [128, 64], F32)
    wscd = b.sbuf("wscd_s", [128, 16], F32)
    sg = b.sbuf("sg", [128, 512], BF16)
    su = b.sbuf("su", [128, 512], BF16)
    g1c = b.sbuf("g1c", [128, 512], BF16)
    s1c = b.sbuf("s1c", [128, 512], BF16)
    x1c = b.sbuf("x1c", [128, 512], F32)
    oc = b.sbuf("oc", [128, 512], F32)
    t2c = b.sbuf("t2c", [128, 512], BF16)
    ps = [b.psum(f"ps{i}", (128, 512), F32) for i in range(6)]
    b.dma("qpool", wscg[:, :], wscg_h, [], ["wscg"])
    b.dma("qpool", wscu[:, :], wscu_h, [], ["wscu"])
    b.dma("qpool", wscd[:, :], wscd_h, [], ["wscd"])
    if fused:
        sprev = b.sbuf("sprev", [64, 2048], BF16)
        rdt = b.sbuf("rdt", [64, 512], BF16)
        s1u = b.sbuf("s1u", [128, 512], BF16)
        b.dma("qpool", sprev[:, :], H["sprevd"], [("dram", "sprevd")], ["sprev"])

    ntile = ceil_div(nt, 512)
    for ti in range(ntile):
        off = ti * 512
        w = min(512, nt - off)
        b.dma("qpool", xns[:, 0:16 * w].rearrange("p (c t) -> p c t", c=16),
              xn_h.rearrange("(c p) t -> p c t", p=128)[:, :, off:off + w],
              [("dram", "xn1")], ["xns"], tag=f"lxn{ti}")
        for g in range(64):
            wgb_, wub_ = wg[g % 2], wu[g % 2]
            kwg, kwu = ("wg", g % 2), ("wu", g % 2)
            b.dma("qsp", wgb_[:, :].rearrange("p (c n) -> p c n", c=16),
                  ug_src(wgt_h, g * 128).rearrange("(c p) n -> p c n", p=128),
                  gk, [kwg], tag=f"lwg{ti}_{g}")
            b.dma("qact", wub_[:, :].rearrange("p (c n) -> p c n", c=16),
                  ug_src(wup_h, g * 128).rearrange("(c p) n -> p c n", p=128),
                  gk, [kwu], tag=f"lwu{ti}_{g}")
            pg, kpg = (ps[2], ("ps", 2)) if g % 2 == 0 else (ps[0], ("ps", 0))
            pu, kpu = (ps[3], ("ps", 3)) if g % 2 == 0 else (ps[1], ("ps", 1))
            for cc in range(16):
                b.mm(pg[:, 0:w], wgb_[:, cc * 128:(cc + 1) * 128], xns[:, cc * w:(cc + 1) * w],
                     start=(cc == 0), stop=(cc == 15), reads=[kwg, "xns"], writes=[kpg],
                     tag=f"gm{ti}_{g}_{cc}")
                b.mm(pu[:, 0:w], wub_[:, cc * 128:(cc + 1) * 128], xns[:, cc * w:(cc + 1) * w],
                     start=(cc == 0), stop=(cc == 15), reads=[kwu, "xns"], writes=[kpu],
                     tag=f"um{ti}_{g}_{cc}")
            b.dve(b.lp(lambda w=w, g=g, pg=pg: nc.vector.tensor_scalar(
                  out=sg[:, 0:w], in0=pg[:, 0:w],
                  scalar1=wscg[:, g:g + 1], scalar2=None, op0=ALU.mult)),
                  [kpg, "wscg"], ["sg"], tag=f"gs{ti}_{g}")
            b.act(b.lp(lambda w=w: nc.scalar.activation(out=sg[:, 0:w], in_=sg[:, 0:w],
                  func=AF.Silu)), ["sg"], ["sg"], tag=f"gsl{ti}_{g}")
            b.dve(b.lp(lambda w=w, g=g, pu=pu: nc.vector.tensor_scalar(
                  out=su[:, 0:w], in0=pu[:, 0:w],
                  scalar1=wscu[:, g:g + 1], scalar2=None, op0=ALU.mult)),
                  [kpu, "wscu"], ["su"], tag=f"us{ti}_{g}")
            b.pool(b.lp(lambda g=g, w=w: nc.gpsimd.tensor_mul(hsb[:, g * w:(g + 1) * w],
                   sg[:, 0:w], su[:, 0:w])), ["sg", "su"], [("h", g)], tag=f"hm{ti}_{g}")
        for f in range(16):
            wpb = wp[f % 2]
            for hh in range(2):
                b.dma("qsp" if hh == 0 else "qact",
                      wpb[:, hh * 32 * 128:(hh + 1) * 32 * 128].rearrange(
                          "p (c n) -> p c n", c=32),
                      dn_src(f * 128, hh * 4096, 4096).rearrange("(c p) n -> p c n", p=128),
                      gk, [("wp", f % 2, hh)], tag=f"lwp{ti}_{f}_{hh}")
            pb, kp = ps[f % 2], ("ps", f % 2)
            for cc in range(64):
                b.mm(pb[:, 0:w], wpb[:, cc * 128:(cc + 1) * 128], hsb[:, cc * w:(cc + 1) * w],
                     start=(cc == 0), stop=(cc == 63),
                     reads=[("wp", f % 2, 0 if cc < 32 else 1), ("h", cc)], writes=[kp],
                     tag=f"dm{ti}_{f}_{cc}")
            b.dve(b.lp(lambda w=w, pb=pb, f=f: nc.vector.tensor_scalar(out=t2c[:, 0:w],
                  in0=pb[:, 0:w], scalar1=wscd[:, f:f + 1], scalar2=None, op0=ALU.mult)),
                  [kp, "wscd"], ["t2cp"], tag=f"dn{ti}_{f}")
            b.act(b.lp(lambda w=w: nc.scalar.activation(out=g1c[:, 0:w], in_=t2c[:, 0:w],
                  func=AF.Silu)), ["t2cp"], ["g1c"], tag=f"g1{ti}_{f}")
            b.dma("qpool", s1c[:, 0:w], s1_h[f * 128:(f + 1) * 128, off:off + w],
                  [("dram", "s1T")], ["s1c"], tag=f"ls1{ti}_{f}")
            b.dma("qpool", x1c[:, 0:w], x1_h[f * 128:(f + 1) * 128, off:off + w],
                  [("dram", "x1")], ["x1c"], tag=f"lx1{ti}_{f}")
            if fused:
                if f == 0:
                    b.dma("qpool", rdt[:, 0:w], H["rdall"][:, off:off + w],
                          [("dram", "rdall")], ["rdt"], tag=f"lrd{ti}")
                b.mm(ps[4][:, 0:w], sprev[:, f * 128:(f + 1) * 128], rdt[:, 0:w], True, True,
                     ["sprev", "rdt"], [("ps", 4)], tag=f"cor{ti}_{f}")
                b.dve(b.lp(lambda w=w: nc.vector.scalar_tensor_tensor(out=s1u[:, 0:w],
                      in0=ps[4][:, 0:w], scalar=1.0, in1=s1c[:, 0:w], op0=ALU.mult,
                      op1=ALU.add)), [("ps", 4), "s1c"], ["s1uk"], tag=f"corr{ti}_{f}")
                s1use, s1key = s1u, "s1uk"
            else:
                s1use, s1key = s1c, "s1c"
            b.dve(b.lp(lambda w=w, s1use=s1use: nc.vector.tensor_mul(t2c[:, 0:w],
                  s1use[:, 0:w], g1c[:, 0:w])), [s1key, "g1c"], ["t2cp"], tag=f"sm{ti}_{f}")
            b.dve(lambda w=w: nc.vector.tensor_add(oc[:, 0:w], x1c[:, 0:w], t2c[:, 0:w]),
                  ["x1c", "t2cp"], ["oc"], tag=f"oa{ti}_{f}")
            b.dma("qsp", out_h[f * 128:(f + 1) * 128, off:off + w], oc[:, 0:w], ["oc"], [],
                  tag=f"so{ti}_{f}")


def build_l1ffn(nt=NT_FULL):
    b = B("l1ffn")
    H = decl_l1ffn(b, nt, fused=False)
    emit_l1ffn(b, H, nt, fused=False)
    return b


def stage_l1ffn(core, xn1, x1T, s1T, prep, inputs, nt=NT_FULL):
    wsc = inputs["L1_ffn_Wscale_ug"].astype(np.float32)
    wd = {} if prep is None else {"wgt_t": prep["wgt_t"], "wup_t": prep["wup_t"],
                                  "wdn_t": prep["wdn_t"],
                                  "xn1": np.asarray(xn1), "x1T": np.asarray(x1T),
                                  "s1T": np.asarray(s1T)}
    return {
        **wd,
        "wscg": np.ascontiguousarray((wsc[8192:] / np.sqrt(2048.0)).reshape(64, 128).T),
        "wscu": np.ascontiguousarray((wsc[:8192] / np.sqrt(2048.0)).reshape(64, 128).T),
        "wscd": np.ascontiguousarray((inputs["L1_ffn_Wscale_dp"].astype(np.float32)
                                      / np.sqrt(8192.0)).reshape(16, 128).T),
    }


def stage_l0pre(core, inputs, nt=NT_FULL):
    bidx, half = core // 2, core % 2
    x = inputs["L0_attn_hidden_states"][bidx]
    lo = half * nt
    sl = x[lo:lo + nt]
    if half == 0:
        halov = np.zeros((HALO, 2048), np.float32)
    else:
        halov = x[lo - HALO:lo]
    xT = np.ascontiguousarray(np.concatenate([halov, sl], 0).T)
    cw = inputs["L0_attn_conv_w"][:, 0, :]
    cd = np.zeros((64, 4, 128, 128), np.float32)
    idx = np.arange(128)
    for c in range(64):
        for j in range(4):
            cd[c, j, idx, idx] = cw[c * 128:(c + 1) * 128, j]
    return {
        "xT": xT,
        "wqkvzT": np.ascontiguousarray(inputs["L0_attn_W_qkvz"].T),
        "wbaT": np.ascontiguousarray(inputs["L0_attn_W_ba"].T),
        "convw_diag": cd.reshape(64 * 4 * 128, 128),
    }


_PROGS = {}

GROUPS = [[0, 1], [2, 3], [4, 5], [6, 7]]


def build_fused(nt=NT_FULL):
    b = B("fused")
    nc = b.nc
    H = {"sel": b.dram("sel", [128, 1], F32, kind="ExternalInput")}
    b.prefix = "d_"
    H.update(decl_prep(b, fused=True))
    H.update(decl_l0pre(b, nt, fused=True))
    H.update(decl_l0delta(b, nt, fused=True))
    H.update(decl_combine0(b, nt, fused=True))
    H.update(decl_l1mix(b, nt, fused=True))
    H.update(decl_l1ffn(b, nt, fused=True))
    H["cc2in"] = b.dram("cc2in", [2048, HALO], F32)
    H["cc2out"] = b.dram("cc2out", [4096, HALO], F32)

    b.prefix = "p0_"
    b.phase_start()
    emit_prep(b, H, fused=True)
    b.phase_end()
    b.prefix = "p1_"
    b.phase_start()
    emit_l0pre(b, H, nt)
    b.phase_end()
    b.prefix = "p2_"
    b.phase_start()
    emit_l0delta(b, H, nt, fused=True, groups=GROUPS)
    b.phase_end()
    b.prefix = "p3s_"
    b.phase_start()
    b.gatekeys = ["prepgate"]
    prep_gate(b)
    emit_combine0_s0(b, H, nt)
    b.phase_end()
    b.prefix = "p3_"
    b.phase_start()
    prep_gate(b)
    emit_combine0(b, H, nt)
    t3 = b.sbuf("t3", [128, 48], F32)
    selg = b.sbuf("selg", [128, 1], F32)
    b.dma("qpool", selg[:, :], H["sel"], [], ["selg"])
    b.dma("qpool", t3[:, :].rearrange("p (c t) -> p c t", c=16),
          H["xn1"].rearrange("(c p) t -> p c t", p=128)[:, :, nt - HALO:nt],
          [("dram", "xn1")], ["t3"], tag="g2l")
    b.dma("qsp", H["cc2in"].rearrange("(c p) t -> p c t", p=128), t3[:, :].rearrange(
          "p (c t) -> p c t", c=16), ["t3"], [("dram", "cc2in")], tag="g2s")
    b.cc("AllGather", ALU.bypass, GROUPS, [H["cc2in"].opt()], [H["cc2out"].opt()],
         [("dram", "cc2in")], [("dram", "cc2out")], tag="ccG2")
    b.dma("qpool", t3[:, :].rearrange("p (c t) -> p c t", c=16),
          H["cc2out"][0:2048, :].rearrange("(c p) t -> p c t", p=128),
          [("dram", "cc2out")], ["t3"], tag="g2l2")
    b.dve(b.lp(lambda: nc.vector.tensor_scalar(out=t3[:, :], in0=t3[:, :], scalar1=selg[:, :],
          scalar2=None, op0=ALU.mult)), ["t3", "selg"], ["t3"], tag="g2m")
    b.dma("qpool", H["halo3"].rearrange("(c p) t -> p c t", p=128),
          t3[:, :].rearrange("p (c t) -> p c t", c=16), ["t3"], [("dram", "halo3")], tag="g2o")
    b.phase_end()
    b.prefix = "p4_"
    b.phase_start()
    emit_l1mix(b, H, nt, fused=True)
    b.phase_end()
    b.prefix = "p4b_"
    b.phase_start()
    sst = b.sbuf("sst", [64, 512], F32)
    selg3 = b.sbuf("selg3", [128, 1], F32)
    b.dma("qpool", selg3[:, :], H["sel"], [], ["selg3"])
    b.cc("AllGather", ALU.bypass, GROUPS, [H["cc3in"].opt()], [H["cc3out"].opt()],
         [("dram", "cc3in")], [("dram", "cc3out")], tag="ccG3")
    for q4 in range(4):
        b.dma("qpool", sst[:, :], H["cc3out"][0:64, q4 * 512:(q4 + 1) * 512],
              [("dram", "cc3out")], ["sst"], tag=f"g3l{q4}")
        b.dve(lambda: nc.vector.tensor_scalar(out=sst[:, :], in0=sst[:, :], scalar1=selg3[0:64, :],
              scalar2=None, op0=ALU.mult), ["sst", "selg3"], ["sst"], tag=f"g3m{q4}")
        b.dma("qsp", H["sprevd"][:, q4 * 512:(q4 + 1) * 512], sst[:, :], ["sst"],
              [("dram", "sprevd")], tag=f"g3o{q4}")
    b.phase_end()
    b.prefix = "p5_"
    b.phase_start()
    prep_gate(b)
    emit_l1ffn(b, H, nt, fused=True)
    b.phase_end()
    return b


def stage_fused(core, inputs, nt=NT_FULL):
    d = {}
    d.update(stage_prep(core, inputs))
    d.update(stage_l0pre(core, inputs, nt=nt))
    dd = stage_l0delta(core, {"mixed": 0, "zs": 0, "bb_f": 0, "aa_f": 0}, inputs, nt=nt)
    for k in ("mixed", "zs", "bb_f", "aa_f", "sinit"):
        dd.pop(k, None)
    d.update(dd)
    bidx, half = core // 2, core % 2
    x = inputs["L0_attn_hidden_states"][bidx]
    lo = half * nt
    d.update({"x0T": np.ascontiguousarray(x[lo:lo + nt].T)})
    dm = stage_l1mix(core, np.zeros((2048, 1), NPBF16), np.zeros((2048, 1), NPBF16),
                     inputs, nt=nt)
    for k in ("xn1f", "s1init"):
        dm.pop(k, None)
    d.update(dm)
    df = stage_l1ffn(core, 0, 0, 0, None, inputs, nt=nt)
    for k in ("xn1", "x1T", "s1T", "wgt_t", "wup_t", "wdn_t"):
        df.pop(k, None)
    d.update(df)
    d["sel"] = np.full((128, 1), float(core % 2), np.float32)
    return d


def kernel(**inputs):
    cores = list(range(8))
    if "fused" not in _PROGS:
        b = build_fused(NT_FULL)
        b.done()
        _PROGS["fused"] = b.nc
    nc = _PROGS["fused"]
    res = run_bass_kernel_spmd(nc, [stage_fused(c, inputs) for c in cores], core_ids=cores)
    out = np.empty((4, 4096, 2048), np.float32)
    for c in cores:
        bi, half = c // 2, c % 2
        out[bi, half * 2048:(half + 1) * 2048, :] = np.asarray(res.results[c]["outT"],
                                                              np.float32).T
    return out


